# version 01
# baseline (speedup 1.0000x reference)
import numpy as np
import concourse.bass as bass
import concourse.mybir as mybir
from concourse.bass_utils import run_bass_kernel_spmd

F32 = mybir.dt.float32
I32 = mybir.dt.int32
AF = mybir.ActivationFunctionType
ALU = mybir.AluOpType

NCORES = 8
T = 2048
D = 1024
DIN = 2048
NH = 32
HP = 64
DSTATE = 128
CONV_DIM = DIN + 2 * DSTATE
DPROJ = 2 * DIN + 2 * DSTATE + NH
DPROJ_PAD = 35 * 128
LO = 256
LC = 128
NO = T // LO
IFF = 3072
QMAX = 32767.0
EPS = 1e-5

_FULL = 4


class RecInstr:
    __slots__ = ("incs",)

    def __init__(self):
        self.incs = []

    def then_inc(self, sem_h, n):
        self.incs.append((sem_h, n))
        return self


_NOCHAIN = {"wait_ge", "dma_start", "indirect_dma_start", "collective_compute",
            "sem_inc", "sem_clear", "nop"}


class Rec:

    def __init__(self, name, chain_sem=None):
        self.name = name
        self.cmds = []
        self.chain = chain_sem
        self.ccount = 0

    def __getattr__(self, attr):
        def call(*a, **k):
            h = RecInstr()
            if self.chain is not None and attr not in _NOCHAIN:
                if self.ccount > 0:
                    self.cmds.append(("wait_ge", (self.chain, self.ccount), {}, RecInstr()))
                h.incs.append((self.chain, 1))
                self.ccount += 1
            self.cmds.append((attr, a, k, h))
            return h
        return call


def _replay(eng, cmds):
    for attr, a, k, h in cmds:
        r = getattr(eng, attr)(*a, **k)
        for sem_h, n in h.incs:
            r.then_inc(sem_h, n)


class Sem:
    def __init__(self, nc, name):
        self.h = nc.alloc_semaphore(name)
        self.v = 0

    def inc(self, instr, n=1):
        instr.then_inc(self.h, n)
        self.v += n
        return self.v

    def dma(self, instr):
        return self.inc(instr, 16)

    def wait(self, eng, val=None):
        eng.wait_ge(self.h, self.v if val is None else val)


class ChainSem:

    def __init__(self, rec):
        self.rec = rec
        self.h = rec.chain

    @property
    def v(self):
        return self.rec.ccount

    def inc(self, instr, n=1):
        return self.rec.ccount

    def wait(self, eng, val=None):
        eng.wait_ge(self.h, self.rec.ccount if val is None else val)


def rep(ap, k):
    return ap.unsqueeze(2).broadcast_to([ap.shape[0], ap.shape[1], k])


class P:
    pass


def _alloc(p, upto, dbg):
    nc = p.nc
    p.upto = upto
    p.dbg = set(dbg)

    def ein(name, shape, dt=F32):
        return nc.dram_tensor(name, shape, dt, kind="ExternalInput").ap()

    p.in_names = []

    def reg(name, shape, dt=F32):
        p.in_names.append(name)
        return ein(name, shape, dt)

    p.u = reg("u", [D, T])
    p.w_in_t = reg("w_in_t", [D, DPROJ_PAD])
    p.conv_w = reg("conv_w", [CONV_DIM, 4])
    p.conv_b = reg("conv_b", [CONV_DIM, 1])
    p.dt_bias = reg("dt_bias", [NH, 1])
    p.a_log = reg("a_log", [NH, 1])
    p.d_rep = reg("d_rep", [DIN, 1])
    p.nw_in = reg("nw_in", [DIN, 1])
    p.w_out_t = reg("w_out_t", [DIN, D])
    p.ident = reg("ident", [128, 128])
    p.mask01 = reg("mask01", [128, 128])
    if upto >= 2:
        p.w_up = reg("w_up", [IFF, D])
        p.w_gate = reg("w_gate", [IFF, D])
        p.w_down = reg("w_down", [D, IFF])
    if upto >= 3:
        p.wz = reg("wz", [D, D])
        p.wh = reg("wh", [D, D])
        p.bz = reg("bz", [D, 1])
        p.bh = reg("bh", [D, 1])
        p.log_d = reg("log_d", [D, 1])
        p.mdiff = reg("mdiff", [128, 128])
        p.mwp = reg("mwp", [128, 128])
        p.mwn = reg("mwn", [128, 128])
    if upto >= 4:
        p.qw_t = reg("qw_t", [6, D, D], I32)
        p.srep_t = reg("srep_t", [6, D, D])
        p.qb = reg("qb", [D, 6])
        p.la_t = reg("la_t", [6, D, 32])
        p.lb_t = reg("lb_t", [6, 32, D])
        p.gam = reg("gam", [D, 3])
        p.bet = reg("bet", [D, 3])

    p.out = nc.dram_tensor("out", [D, T], F32, kind="ExternalOutput").ap()

    p.z_hbm = nc.dram_tensor("z_hbm", [16, 128, T], F32).ap()
    p.xbc_hbm = nc.dram_tensor("xbc_hbm", [18, 128, T], F32).ap()
    p.y_hbm = nc.dram_tensor("y_hbm", [16, 128, T], F32).ap()
    p.bscr = nc.dram_tensor("bscr", [34, 512], F32).ap()
    if upto >= 2:
        p.up_hbm = nc.dram_tensor("up_hbm", [24, 128, T], F32).ap()
        p.gate_hbm = nc.dram_tensor("gate_hbm", [24, 128, T], F32).ap()
        p.g2_hbm = nc.dram_tensor("g2_hbm", [24, 128, T], F32).ap()
        p.o_hbm = nc.dram_tensor("o_hbm", [24, 128, T], F32).ap()
        p.wdq_hbm = nc.dram_tensor("wdq_hbm", [IFF, D], F32).ap()
        p.cc_in = nc.dram_tensor("cc_in", [1, 4], F32)
        p.cc_out = nc.dram_tensor("cc_out", [1, 4], F32)

    def sb(name, free):
        return nc.alloc_sbuf_tensor(name, [128, free], F32).ap()

    p.R = sb("R", 8 * T)
    p.big = sb("big", 16384)
    p.gb0 = sb("gb0", 1024)
    p.gb1 = sb("gb1", 1024)
    p.gb2 = sb("gb2", 1024)
    p.xd = sb("xd", 512)
    p.state = sb("state", DIN)
    p.dt_raw = sb("dt_raw", T)
    p.rstd_all = sb("rstd_all", T)
    p.ysq = sb("ysq", DIN)
    p.yoct = sb("yoct", 16 * LO)
    p.idn = sb("idn", 128)
    p.msk = sb("msk", 128)
    p.nbt = sb("nbt", 32)
    p.dst = sb("dst", 32)
    p.btm = sb("btm", 128)
    p.cbm = sb("cbm", 128)
    p.ones = sb("ones", 8)
    p.sml = sb("sml", 64)
    p.rows = sb("rows", 1024)
    p.bc1 = sb("bc1", 512)
    p.bc2 = sb("bc2", 512)
    p.tmpv = sb("tmpv", 1024)
    p.cw = sb("cw", 96)
    p.dr = sb("dr", 16)
    p.nw = sb("nw", 16)

    p.pb = [nc.alloc_psum_tensor(f"pb{i}", [128, 512], F32).ap() for i in range(8)]

    for s in ["ld", "t", "o2", "wa", "wb", "ea", "eb", "ra", "rb",
              "zc", "bw", "bc", "yw", "yb", "cc", "ci", "co"]:
        setattr(p, "s_" + s, Sem(nc, "sem_" + s))


def build(upto=_FULL, dbg=()):
    p = P()
    p.nc = bass.Bass("TRN2", target_bir_lowering=False)
    _alloc(p, upto, dbg)
    vch = p.nc.alloc_semaphore("sem_vch")
    sch = p.nc.alloc_semaphore("sem_sch")
    p.sy, p.t, p.v, p.s, p.g = (Rec("sy"), Rec("t"), Rec("v", vch), Rec("s", sch), Rec("g"))
    p.s_v = ChainSem(p.v)
    p.s_s = ChainSem(p.s)
    _emit(p)
    with p.nc.Block() as block:
        @block.sync
        def _(e):
            _replay(e, p.sy.cmds)

        @block.tensor
        def _(e):
            _replay(e, p.t.cmds)

        @block.vector
        def _(e):
            _replay(e, p.v.cmds)

        @block.scalar
        def _(e):
            _replay(e, p.s.cmds)

        @block.gpsimd
        def _(e):
            _replay(e, p.g.cmds)
    return p


def _dbg(p, name, ap, shape):
    if name in p.dbg:
        o = p.nc.dram_tensor("dbg_" + name, shape, F32, kind="ExternalOutput").ap()
        p.s_v.wait(p.sy)
        p.s_s.wait(p.sy)
        p.s_o2.dma(p.sy.dma_start(out=o, in_=ap))


def _fence_compute(p):
    for e in (p.t, p.v, p.s):
        for s in (p.s_t, p.s_v, p.s_s, p.s_ld, p.s_wa, p.s_wb, p.s_ea, p.s_eb,
                  p.s_ra, p.s_rb, p.s_zc, p.s_bw, p.s_bc, p.s_yw, p.s_yb):
            s.wait(e)


def _emit(p):
    sy, t, v, s, g = p.sy, p.t, p.v, p.s, p.g

    p.s_ld.dma(sy.dma_start(out=p.idn, in_=p.ident))
    p.s_ld.dma(sy.dma_start(out=p.msk, in_=p.mask01))
    for fc in range(8):
        p.s_ld.dma(sy.dma_start(out=p.R[:, fc * T : (fc + 1) * T],
                                in_=p.u[fc * 128 : (fc + 1) * 128, :]))
    p.s_ld.dma(sy.dma_start(out=p.sml[0:NH, 0:1], in_=p.dt_bias))
    p.s_ld.dma(sy.dma_start(out=p.sml[0:NH, 1:2], in_=p.a_log))
    p.s_ld.dma(sy.dma_start(
        out=p.cw[:, 0:72].rearrange("p (j w) -> p j w", j=18),
        in_=p.conv_w.rearrange("(j p) w -> p j w", p=128),
        allow_slow_non_contiguous=True))
    p.s_ld.dma(sy.dma_start(
        out=p.cw[:, 72:90].rearrange("p (j o) -> p j o", o=1),
        in_=p.conv_b.rearrange("(j p) o -> p j o", p=128),
        allow_slow_non_contiguous=True))
    p.s_ld.dma(sy.dma_start(
        out=p.dr.rearrange("p (j o) -> p j o", o=1),
        in_=p.d_rep.rearrange("(j p) o -> p j o", p=128),
        allow_slow_non_contiguous=True))
    p.s_ld.dma(sy.dma_start(
        out=p.nw.rearrange("p (j o) -> p j o", o=1),
        in_=p.nw_in.rearrange("(j p) o -> p j o", p=128),
        allow_slow_non_contiguous=True))
    p.s_ld.wait(v)
    p.s_v.inc(v.memset(p.ones, 1.0))
    p.s_ld.wait(s)
    xv = p.s_s.inc(s.activation(p.sml[0:NH, 2:3], p.sml[0:NH, 1:2], AF.Exp))
    p.s_s.wait(s, xv)
    p.s_s.inc(s.mul(p.sml[0:NH, 2:3], p.sml[0:NH, 2:3], -1.0))
    p.negA = p.sml[0:NH, 2:3]
    p.dtbv = p.sml[0:NH, 0:1]
    p.s_v.wait(t)
    p.s_s.wait(t)

    _mamba(p)
    if p.upto >= 2:
        _ffn0(p)
    if p.upto >= 3:
        _l1_attn(p)
    if p.upto >= 4:
        _l1_ffn(p)

    p.s_v.wait(sy)
    p.s_s.wait(sy)
    for fc in range(8):
        p.s_o2.dma(sy.dma_start(out=p.out[fc * 128 : (fc + 1) * 128, :],
                                in_=p.R[:, fc * T : (fc + 1) * T]))
    p.s_o2.wait(sy)


def _mamba(p):
    sy, t, v, s, g = p.sy, p.t, p.v, p.s, p.g

    wbuf = p.big[:, 0:2048]
    tmpe = p.big[:, 2048:3072]
    wl_ck = []
    tm_ck = [0] * 200
    ev_ck = [0] * 200
    sp_ck = [0] * 200
    wsem = [p.s_wa, p.s_wb]
    esem = [p.s_ea, p.s_eb]
    p.s_v.inc(v.memset(p.state, 0.0))
    for oc in range(35):
        wb = wbuf[:, (oc % 2) * 1024 : (oc % 2) * 1024 + 1024]
        if oc >= 2:
            sy.wait_ge(p.s_t.h, tm_ck[(oc - 2) * 4 + 3])
        src = p.w_in_t[:, oc * 128 : (oc + 1) * 128].rearrange("(kc kp) o -> kp kc o", kp=128)
        wv = wsem[oc % 2].dma(sy.dma_start(out=wb.rearrange("kp (kc o) -> kp kc o", kc=8), in_=src))
        wl_ck.append(wv)
        for tt in range(4):
            i = oc * 4 + tt
            bank = p.pb[i % 2]
            wsem[oc % 2].wait(t, wl_ck[oc])
            if i >= 2:
                p.s_v.wait(t, ev_ck[i - 2])
            for kc in range(8):
                mm = t.matmul(bank, wb[:, kc * 128 : (kc + 1) * 128],
                              p.R[:, kc * T + tt * 512 : kc * T + tt * 512 + 512],
                              start=(kc == 0), stop=(kc == 7))
            tm_ck[i] = p.s_t.inc(mm)
            p.s_t.wait(v, tm_ck[i])
            if oc < 34:
                slot = i % 2
                te = tmpe[:, slot * 512 : slot * 512 + 512]
                if i >= 2:
                    esem[slot].wait(v, sp_ck[i - 2])
                ev_ck[i] = p.s_v.inc(v.tensor_copy(te, bank))
                p.s_v.wait(g, ev_ck[i])
                dst = (p.z_hbm if oc < 16 else p.xbc_hbm)
                row = oc if oc < 16 else oc - 16
                sp_ck[i] = esem[slot].dma(
                    g.dma_start(out=dst[row, :, tt * 512 : tt * 512 + 512], in_=te))
            else:
                ev_ck[i] = p.s_v.inc(v.tensor_copy(p.dt_raw[:, tt * 512 : tt * 512 + 512], bank))

    p.s_ea.wait(sy)
    p.s_eb.wait(sy)
    _dbg(p, "dt_raw", p.dt_raw, [128, T])

    xbcs = p.big[:, 0 : 18 * LO]
    zc = p.big[:, 4608 : 4608 + 2048]
    xtm = p.big[:, 6656 : 6656 + 2048]
    dtb = p.big[0:NH, 8704 : 8704 + 8 * LO]
    raw = p.big[:, 10752 : 10752 + 2 * 260]
    for o in range(NO):
        to = o * LO
        dts = dtb[:, 0 * LO : 1 * LO]
        lndt = dtb[:, 1 * LO : 2 * LO]
        dA = dtb[:, 2 * LO : 3 * LO]
        cA = dtb[:, 3 * LO : 4 * LO]
        nb = dtb[:, 4 * LO : 5 * LO]
        dsb = dtb[:, 5 * LO : 6 * LO]
        t32 = dtb[:, 6 * LO : 7 * LO]
        p.s_v.wait(s)
        p.s_t.wait(s)
        av = p.s_s.inc(s.activation(t32, p.dt_raw[0:NH, to : to + LO], AF.Exp, bias=p.dtbv))
        p.s_s.wait(s, av)
        av = p.s_s.inc(s.activation(dts, t32, AF.Ln, bias=1.0))
        p.s_s.wait(s, av)
        sv = p.s_s.inc(s.activation(lndt, dts, AF.Ln))
        p.s_s.wait(v, sv)
        vv = p.s_v.inc(v.tensor_scalar(dA, dts, p.negA, None, ALU.mult))
        p.s_v.wait(v, vv)
        for cc in range(2):
            sl = slice(cc * LC, (cc + 1) * LC)
            vv = p.s_v.inc(v.tensor_tensor_scan(
                cA[:, sl], p.ones[0:NH, 0:1].broadcast_to([NH, LC]),
                dA[:, sl], 0.0, ALU.mult, ALU.add))
        p.s_v.wait(v, vv)
        v.tensor_sub(nb, lndt, cA)
        for cc in range(2):
            sl = slice(cc * LC, (cc + 1) * LC)
            v.tensor_scalar(t32[:, sl], cA[:, sl], -1.0,
                            cA[:, cc * LC + LC - 1 : cc * LC + LC], ALU.mult, ALU.add)
            v.tensor_add(t32[:, sl], t32[:, sl], lndt[:, sl])
        dv = p.s_v.v
        p.s_v.wait(s, dv)
        p.s_s.inc(s.activation(dsb, t32, AF.Exp))

        cv_ck = [0] * 18
        rsem = [p.s_ra, p.s_rb]
        for j in range(18):
            slot = j % 2
            rb = raw[:, slot * 260 : slot * 260 + 260]
            if j >= 2:
                p.s_v.wait(sy, cv_ck[j - 2])
            if o == 0:
                lv = rsem[slot].dma(sy.dma_start(out=rb[:, 3:259], in_=p.xbc_hbm[j, :, 0:LO]))
            else:
                lv = rsem[slot].dma(sy.dma_start(out=rb[:, 0:259],
                                                 in_=p.xbc_hbm[j, :, to - 3 : to + LO]))
            rsem[slot].wait(v, lv)
            if o == 0:
                v.memset(rb[:, 0:3], 0.0)
            xj = xbcs[:, j * LO : (j + 1) * LO]
            v.tensor_scalar(xj, rb[:, 0:LO], p.cw[:, 4 * j : 4 * j + 1],
                            p.cw[:, 72 + j : 73 + j], ALU.mult, ALU.add)
            for w in range(1, 4):
                v.scalar_tensor_tensor(xj, rb[:, w : w + LO], p.cw[:, 4 * j + w : 4 * j + w + 1],
                                       xj, ALU.mult, ALU.add)
            cv_ck[j] = p.s_v.v
            p.s_v.wait(s, cv_ck[j])
            p.s_s.inc(s.activation(xj, xj, AF.Silu))

        _dbg(p, f"xbcs_o{o}", xbcs, [128, 18 * LO])

        for cc in range(2):
            _mamba_chunk(p, o, cc, xbcs, zc, xtm, cA, nb, dsb)

        p.s_v.wait(g)
        p.s_yw.dma(g.dma_start(
            out=p.y_hbm[:, :, to : to + LO].transpose([1, 0, 2]),
            in_=p.yoct.rearrange("p (fc l) -> p fc l", fc=16)))

    p.s_yw.wait(sy)
    _dbg(p, "rstd_all", p.rstd_all, [128, T])

    _outproj_ln(p)


def _mamba_chunk(p, o, cc, xbcs, zc, xtm, cA, nb, dsb):
    sy, t, v, s, g = p.sy, p.t, p.v, p.s, p.g
    tg = o * LO + cc * LC
    sl = slice(cc * LC, (cc + 1) * LC)

    p.s_v.wait(sy)
    p.s_s.wait(sy)
    zv = p.s_zc.dma(sy.dma_start(
        out=zc.rearrange("p (fc l) -> p fc l", fc=16),
        in_=p.z_hbm[:, :, tg : tg + LC].transpose([1, 0, 2])))

    p.s_v.wait(g)
    p.s_s.wait(g)
    bflat = p.bscr.rearrange("a b -> (a b)")
    wv = p.s_bw.dma(g.dma_start(
        out=bflat[0 : NH * LC].rearrange("(h l) -> h l", l=LC), in_=cA[:, sl]))

    bank_tr = p.pb[2]
    bank_cb = p.pb[3]
    xb16 = xbcs[:, 16 * LO : 16 * LO + LO]
    xb17 = xbcs[:, 17 * LO : 17 * LO + LO]
    p.s_s.wait(t)
    p.s_v.wait(t)
    p.s_t.inc(t.matmul(bank_cb[:, 0:128], xb16[:, sl], xb17[:, sl], start=True, stop=True))
    p.s_t.inc(t.transpose(bank_tr[:, 0:32], nb[:, sl], p.idn[0:32, 0:32]))
    p.s_t.inc(t.transpose(bank_tr[:, 32:64], dsb[:, sl], p.idn[0:32, 0:32]))
    tv = p.s_t.inc(t.transpose(bank_tr[:, 128:256], xb16[:, sl], p.idn))
    p.s_t.wait(v, tv)
    v.tensor_copy(p.nbt, bank_tr[:, 0:32])
    v.tensor_copy(p.dst, bank_tr[:, 32:64])
    v.tensor_copy(p.btm, bank_tr[:, 128:256])
    ev = p.s_v.inc(v.tensor_mul(p.cbm, bank_cb[:, 0:128], p.msk))
    for g4 in range(4):
        p.s_v.wait(t, ev)
        for k in range(4):
            fc = g4 * 4 + k
            tv = p.s_t.inc(t.transpose(
                bank_tr[:, k * 128 : (k + 1) * 128],
                xbcs[:, fc * LO + cc * LC : fc * LO + cc * LC + LC], p.idn))
        p.s_t.wait(v, tv)
        ev = p.s_v.inc(v.tensor_copy(xtm[:, g4 * 512 : (g4 + 1) * 512], bank_tr))

    for gi in range(4):
        h0 = gi * 8
        b0, b1, b2 = p.gb0, p.gb1, p.gb2
        p.s_bw.wait(g, wv)
        if gi > 0:
            p.s_t.wait(g)
            p.s_t.wait(v)
            p.s_t.wait(s)
        bflat = p.bscr.rearrange("a b -> (a b)")
        bv = p.s_bc.dma(g.dma_start(
            out=b0,
            in_=bflat[h0 * LC : h0 * LC + 1024].unsqueeze(0).broadcast_to([128, 1024])))
        p.s_bc.wait(v, bv)
        b0v = b0.rearrange("p (h l) -> p h l", h=8)
        v.tensor_tensor(b0v, b0v, rep(p.nbt[:, h0 : h0 + 8], LC), ALU.add)
        mv = p.s_v.inc(v.tensor_scalar(b2, b0, 30.0, None, ALU.min))
        p.s_v.wait(s, mv)
        sv = p.s_s.inc(s.activation(b1, b2, AF.Exp))
        p.s_s.wait(v, sv)
        b1v = b1.rearrange("p (h l) -> p h l", h=8)
        v.tensor_tensor(b1v, b1v, rep(p.cbm, 8).transpose([0, 2, 1]), ALU.mult)
        v.tensor_tensor(b0v, b0v, rep(p.nbt[:, h0 : h0 + 8], LC), ALU.subtract)
        mv = p.s_v.v
        p.s_v.wait(s, mv)
        sv = p.s_s.inc(s.activation(b0, b0, AF.Exp))
        p.s_s.wait(v, sv)
        for hh in range(8):
            h = h0 + hh
            v.tensor_scalar(p.xd[:, hh * 64 : hh * 64 + 64],
                            xtm[:, h * 64 : h * 64 + 64],
                            p.dst[:, h : h + 1], None, ALU.mult)
        b2v = b2.rearrange("p (h l) -> p h l", h=8)
        xb17 = xbcs[:, 17 * LO : 17 * LO + LO]
        v.tensor_tensor(b2v, rep(xb17[:, sl], 8).transpose([0, 2, 1]),
                        b0.rearrange("p (h l) -> p h l", h=8), ALU.mult)
        vv = p.s_v.v

        p.s_v.wait(t, vv)
        sbank = p.pb[0]
        ybank = p.pb[4 + gi]
        for hh in range(8):
            h = h0 + hh
            ypart = (hh % 2) * 64
            yfree = (hh // 2) * 128
            yout = ybank[ypart : ypart + 64, yfree : yfree + 128]
            t.matmul(yout, xtm[:, h * 64 : h * 64 + 64],
                     b1[:, hh * 128 : hh * 128 + 128],
                     start=(hh < 2), stop=False, skip_group_check=True)
            t.matmul(yout, p.state[:, h * 64 : h * 64 + 64],
                     b2[:, hh * 128 : hh * 128 + 128],
                     start=False, stop=(hh >= 6), skip_group_check=True)
            mm = t.matmul(sbank[:, hh * 64 : hh * 64 + 64], p.btm,
                          p.xd[:, hh * 64 : hh * 64 + 64],
                          start=(hh == 0), stop=(hh == 7))
        tv = p.s_t.inc(mm)
        p.s_t.wait(v, tv)
        for hh in range(8):
            h = h0 + hh
            stsl = p.state[:, h * 64 : h * 64 + 64]
            v.tensor_scalar(stsl, stsl, b0[:, hh * 128 + 127 : hh * 128 + 128],
                            None, ALU.mult)
        gv = p.s_v.inc(v.tensor_add(p.state[:, h0 * 64 : h0 * 64 + 512],
                                    p.state[:, h0 * 64 : h0 * 64 + 512], sbank))
        p.s_v.wait(g, gv)

    p.s_zc.wait(s, zv)
    p.s_zc.wait(v, zv)
    p.s_yw.wait(v)
    sv = p.s_s.inc(s.activation(zc, zc, AF.Silu))
    p.s_s.wait(v, sv)
    for fc in range(16):
        ybank = p.pb[4 + fc // 4]
        yfree = (fc % 4) * 128
        dst_y = p.yoct[:, fc * LO + cc * LC : fc * LO + cc * LC + LC]
        v.scalar_tensor_tensor(p.tmpv[:, 0:128],
                               xbcs[:, fc * LO + cc * LC : fc * LO + cc * LC + LC],
                               p.dr[:, fc : fc + 1],
                               ybank[:, yfree : yfree + 128], ALU.mult, ALU.add)
        v.tensor_mul(dst_y, p.tmpv[:, 0:128], zc[:, fc * 128 : fc * 128 + 128])
    yv = p.s_v.v
    p.s_v.wait(s, yv)
    sv = p.s_s.inc(s.activation(p.ysq.rearrange("p (fc l) -> p fc l", fc=16),
                                p.yoct.rearrange("p (fc l) -> p fc l", fc=16)[:, :, cc * LC : cc * LC + LC],
                                AF.Square))
    p.s_s.wait(t, sv)
    stats = p.pb[1]
    for fc in range(16):
        mm = t.matmul(stats[0:1, 0:128], p.ones[:, 0:1],
                      p.ysq[:, fc * 128 : fc * 128 + 128],
                      start=(fc == 0), stop=(fc == 15))
    tv = p.s_t.inc(mm)
    p.s_t.wait(v, tv)
    v.tensor_scalar(p.rows[0:1, 8:136], stats[0:1, 0:128], 1.0 / DIN, EPS,
                    ALU.mult, ALU.add)
    rv = p.s_v.inc(v.reciprocal(p.rows[0:1, 8:136], p.rows[0:1, 8:136]))
    p.s_v.wait(s, rv)
    sv = p.s_s.inc(s.activation(p.rows[0:1, 8:136], p.rows[0:1, 8:136], AF.Sqrt))
    p.s_s.wait(v, sv)
    p.s_v.inc(v.tensor_copy(p.rstd_all[0:1, tg : tg + LC], p.rows[0:1, 8:136]))


def _outproj_ln(p):
    sy, t, v, s, g = p.sy, p.t, p.v, p.s, p.g
    _fence_compute(p)
    wo4 = p.big[:, 0:8192]
    ybuf = p.big[:, 8192:12288]
    prev_tv = 0
    for ocg in range(2):
        wl = 0
        if ocg == 1:
            sy.wait_ge(p.s_t.h, prev_tv)
        for j in range(4):
            oc = ocg * 4 + j
            wj = wo4[:, j * 2048 : j * 2048 + 2048]
            src = p.w_out_t[:, oc * 128 : (oc + 1) * 128].rearrange(
                "(kc kp) o -> kp kc o", kp=128)
            wl = p.s_wa.dma(sy.dma_start(
                out=wj.rearrange("kp (kc o) -> kp kc o", kc=16), in_=src))
        p.s_wa.wait(v, wl)
        for j in range(4):
            wj = wo4[:, j * 2048 : j * 2048 + 2048]
            v.tensor_tensor(wj.rearrange("p (kc o) -> p kc o", kc=16),
                            wj.rearrange("p (kc o) -> p kc o", kc=16),
                            rep(p.nw, 128), ALU.mult)
        wsv = p.s_v.v

        for sub in range(8):
            ts0 = sub * 256
            if not (ocg == 0 and sub == 0):
                sy.wait_ge(p.s_t.h, prev_tv)
            yv = p.s_yb.dma(sy.dma_start(
                out=ybuf.rearrange("p (fc l) -> p fc l", fc=16),
                in_=p.y_hbm[:, :, ts0 : ts0 + 256].transpose([1, 0, 2])))
            p.s_v.wait(g)
            wv = p.s_bw.dma(g.dma_start(out=p.bscr[17:18, 0:256],
                                        in_=p.rstd_all[0:1, ts0 : ts0 + 256]))
            p.s_bw.wait(g, wv)
            bv = p.s_bc.dma(g.dma_start(
                out=p.bc1[:, 0:256], in_=p.bscr[17:18, 0:256].broadcast_to([128, 256])))
            for j in range(4):
                oc = ocg * 4 + j
                bank = p.pb[2 + (j % 2)]
                p.s_v.wait(t, wsv)
                p.s_yb.wait(t, yv)
                if not (ocg == 0 and sub == 0 and j < 2):
                    p.s_v.wait(t)
                for kc in range(16):
                    mm = t.matmul(bank[:, 0:256],
                                  wo4[:, j * 2048 + kc * 128 : j * 2048 + kc * 128 + 128],
                                  ybuf[:, kc * 256 : (kc + 1) * 256],
                                  start=(kc == 0), stop=(kc == 15))
                tv = p.s_t.inc(mm)
                prev_tv = tv
                p.s_t.wait(v, tv)
                p.s_bc.wait(v, bv)
                rsl = p.R[:, oc * T + ts0 : oc * T + ts0 + 256]
                v.tensor_mul(p.tmpv[:, 0:256], bank[:, 0:256], p.bc1[:, 0:256])
                p.s_v.inc(v.tensor_add(rsl, rsl, p.tmpv[:, 0:256]))

    _dbg(p, "x1pre", p.R, [128, 8 * T])
    _post_ln(p, ln_idx=0, absmax_acc=(p.upto >= 2))


def _post_ln(p, ln_idx, absmax_acc=False, gamma=None, beta=None):
    sy, t, v, s, g = p.sy, p.t, p.v, p.s, p.g
    _fence_compute(p)
    if absmax_acc:
        p.s_v.inc(v.memset(p.sml[:, 3:4], 0.0))
    stats = p.pb[1]
    for sub in range(T // 256):
        ts0 = sub * 256
        if sub > 0:
            p.s_t.wait(s)
        for oc in range(8):
            rsl = p.R[:, oc * T + ts0 : oc * T + ts0 + 256]
            s.activation(p.ysq[:, oc * 256 : oc * 256 + 256], rsl, AF.Square)
        sv = p.s_s.v
        p.s_s.wait(t, sv)
        mm = None
        for oc in range(8):
            rsl = p.R[:, oc * T + ts0 : oc * T + ts0 + 256]
            t.matmul(stats[0:1, 0:256], p.ones[:, 0:1], rsl,
                     start=(oc == 0), stop=False)
            mm = t.matmul(stats[0:1, 256:512], p.ones[:, 0:1],
                          p.ysq[:, oc * 256 : oc * 256 + 256],
                          start=False, stop=(oc == 7))
        tv = p.s_t.inc(mm)
        p.s_t.wait(v, tv)
        m = p.rows[0:1, 0:256]
        msq = p.rows[0:1, 256:512]
        var = p.rows[0:1, 512:768]
        v.tensor_scalar(m, stats[0:1, 0:256], 1.0 / D, None, ALU.mult)
        v.tensor_scalar(msq, stats[0:1, 256:512], 1.0 / D, None, ALU.mult)
        v.tensor_mul(var, m, m)
        v.tensor_sub(var, msq, var)
        v.tensor_scalar(var, var, 1.0, EPS, ALU.mult, ALU.add)
        rv = p.s_v.inc(v.reciprocal(var, var))
        p.s_v.wait(s, rv)
        sv = p.s_s.inc(s.activation(var, var, AF.Sqrt))
        p.s_s.wait(g, sv)
        p.s_v.wait(g)
        wv = p.s_bw.dma(g.dma_start(out=p.bscr[16:17, 0:256], in_=m))
        wv = p.s_bw.dma(g.dma_start(out=p.bscr[17:18, 0:256], in_=var))
        p.s_bw.wait(g, wv)
        p.s_bc.dma(g.dma_start(out=p.bc1[:, 0:256],
                               in_=p.bscr[16:17, 0:256].broadcast_to([128, 256])))
        bv = p.s_bc.dma(g.dma_start(out=p.bc2[:, 0:256],
                                    in_=p.bscr[17:18, 0:256].broadcast_to([128, 256])))
        p.s_bc.wait(v, bv)
        for oc in range(8):
            rsl = p.R[:, oc * T + ts0 : oc * T + ts0 + 256]
            v.tensor_sub(rsl, rsl, p.bc1[:, 0:256])
            v.tensor_mul(rsl, rsl, p.bc2[:, 0:256])
            if gamma is not None:
                v.tensor_scalar(rsl, rsl, gamma[:, oc : oc + 1], beta[:, oc : oc + 1],
                                ALU.mult, ALU.add)
            if absmax_acc:
                v.tensor_reduce(p.tmpv[:, 0:1], rsl, mybir.AxisListType.X, ALU.max,
                                apply_absolute_value=True)
                v.tensor_max(p.sml[:, 3:4], p.sml[:, 3:4], p.tmpv[:, 0:1])
        p.s_v.v


def _allreduce_max(p, cols, out_cols):
    sy, t, v, s, g = p.sy, p.t, p.v, p.s, p.g
    k = len(cols)
    stats = p.pb[1]
    p.s_v.wait(t)
    for i, c in enumerate(cols):
        tv = p.s_t.inc(t.transpose(stats[0:1, i * 128 : i * 128 + 128],
                                   p.sml[:, c : c + 1], p.idn))
    p.s_t.wait(v, tv)
    v.memset(p.rows[0:1, 640:644], 0.0)
    for i in range(k):
        v.tensor_reduce(p.rows[0:1, 640 + i : 641 + i], stats[0:1, i * 128 : i * 128 + 128],
                        mybir.AxisListType.X, ALU.max)
    vv = p.s_v.v
    p.s_v.wait(g, vv)
    p.s_cc.wait(g)
    p.s_ci.dma(g.dma_start(out=p.cc_in.ap()[0:1, 0:4], in_=p.rows[0:1, 640:644]))
    p.s_ci.wait(g)
    p.s_co.wait(g)
    cc = g.collective_compute(
        "AllReduce", ALU.max, replica_groups=[list(range(NCORES))],
        ins=[p.cc_in.ap().opt()], outs=[p.cc_out.ap().opt()])
    p.s_cc.inc(cc)
    p.s_cc.wait(g)
    bv = p.s_co.dma(g.dma_start(
        out=p.tmpv[:, 0:4], in_=p.cc_out.ap()[0:1, 0:4].broadcast_to([128, 4])))
    p.s_co.wait(v, bv)
    for i, oc in enumerate(out_cols):
        v.tensor_copy(p.sml[:, oc : oc + 1], p.tmpv[:, i : i + 1])


def _scale_recip(p, mx_col, s_col, r_col):
    v = p.v
    v.tensor_scalar(p.sml[:, s_col : s_col + 1], p.sml[:, mx_col : mx_col + 1],
                    float(np.float32(1.0 / QMAX)), 1e-12, ALU.mult, ALU.max)
    v.reciprocal(p.sml[:, r_col : r_col + 1], p.sml[:, s_col : s_col + 1])


def _lpbq_dequant(p, wraw, ncols, sbuf, rbuf):
    v = p.v
    nb = ncols // 32
    v3 = wraw.rearrange("p (b e) -> p b e", b=nb)
    v.tensor_reduce(sbuf, v3, mybir.AxisListType.X, ALU.max, apply_absolute_value=True)
    v.tensor_scalar(sbuf, sbuf, float(np.float32(1.0 / 7.0)), 1e-12, ALU.mult, ALU.max)
    v.reciprocal(rbuf, sbuf)
    v.tensor_tensor(v3.bitcast(I32), v3, rep(rbuf, 32), ALU.mult)
    v.tensor_tensor(v3, v3.bitcast(I32), rep(sbuf, 32), ALU.mult)


def _ffn0(p):
    sy, t, v, s, g = p.sy, p.t, p.v, p.s, p.g
    _fence_compute(p)
    _allreduce_max(p, [3], [3])
    _scale_recip(p, 3, 4, 5)

    xq = p.big
    vv0 = p.s_v.v
    for kc in range(8):
        xr = p.R[:, kc * T : (kc + 1) * T]
        dst = xq[:, kc * T : (kc + 1) * T]
        v.tensor_scalar(dst.bitcast(I32), xr, p.sml[:, 5:6], None, ALU.mult)
        v.tensor_scalar(dst, dst.bitcast(I32), p.sml[:, 4:5], None, ALU.mult)
    xqv = p.s_v.v

    v.memset(p.sml[:, 6:7], 0.0)
    v.memset(p.sml[:, 7:8], 0.0)
    wraw2 = p.yoct[:, 0:2048]
    wbuf = p.dt_raw[:, 0:2048]
    sc2 = p.state[:, 0:128]
    stg = p.ysq[:, 0:2048]
    wsem = [p.s_wa, p.s_wb]
    esem = [p.s_ea, p.s_eb]
    wl_ck = {}
    tm_ck = {}
    ev_ck = {}
    sp_ck = {}
    dq_ck = {}
    units = [(m, oc) for oc in range(24) for m in range(2)]
    bank_wt = p.pb[4]
    for ui, (m, oc) in enumerate(units):
        wsl = ui % 2
        wraw = wraw2[:, wsl * 1024 : wsl * 1024 + 1024]
        wb = wbuf[:, wsl * 1024 : wsl * 1024 + 1024]
        wsrc = (p.w_up if m == 0 else p.w_gate)
        if ui >= 2:
            p.s_v.wait(sy, dq_ck[ui - 2])
        wl_ck[ui] = wsem[wsl].dma(sy.dma_start(
            out=wraw, in_=wsrc[oc * 128 : (oc + 1) * 128, :]))
        wsem[wsl].wait(v, wl_ck[ui])
        _lpbq_dequant(p, wraw, 1024, sc2[:, wsl * 64 : wsl * 64 + 32],
                      sc2[:, wsl * 64 + 32 : wsl * 64 + 64])
        dqv = p.s_v.v
        p.s_v.wait(t, dqv)
        for kk in range(2):
            p.s_v.wait(t)
            for k4 in range(4):
                kc = kk * 4 + k4
                tv = p.s_t.inc(t.transpose(bank_wt[:, k4 * 128 : k4 * 128 + 128],
                                           wraw[:, kc * 128 : kc * 128 + 128], p.idn))
            p.s_t.wait(v, tv)
            v.tensor_copy(wb[:, kk * 512 : kk * 512 + 512], bank_wt)
        dq_ck[ui] = p.s_v.v
        for tt in range(4):
            i = ui * 4 + tt
            bank = p.pb[i % 2]
            p.s_v.wait(t, dq_ck[ui])
            if ui == 0 and tt == 0:
                p.s_v.wait(t, xqv)
            if i >= 2:
                p.s_v.wait(t, ev_ck[i - 2])
            for kc in range(8):
                mm = t.matmul(bank, wb[:, kc * 128 : (kc + 1) * 128],
                              xq[:, kc * T + tt * 512 : kc * T + tt * 512 + 512],
                              start=(kc == 0), stop=(kc == 7))
            tm_ck[(ui, tt)] = p.s_t.inc(mm)
            p.s_t.wait(v, tm_ck[(ui, tt)])
            ssl = i % 2
            te = stg[:, ssl * 512 : ssl * 512 + 512]
            if i >= 2:
                esem[ssl].wait(v, sp_ck[i - 2])
            v.tensor_copy(te, bank)
            v.tensor_reduce(p.tmpv[:, 0:1], te, mybir.AxisListType.X, ALU.max,
                            apply_absolute_value=True)
            acc = p.sml[:, 6 + m : 7 + m]
            v.tensor_max(acc, acc, p.tmpv[:, 0:1])
            ev_ck[i] = p.s_v.v
            p.s_v.wait(g, ev_ck[i])
            dst = (p.up_hbm if m == 0 else p.gate_hbm)
            sp_ck[i] = esem[ssl].dma(
                g.dma_start(out=dst[oc, :, tt * 512 : tt * 512 + 512], in_=te))

    p.s_ea.wait(v)
    p.s_eb.wait(v)
    for oc in range(8):
        wsl = oc % 2
        wraw = wraw2[:, wsl * 1024 : wsl * 1024 + 1024]
        for cp in range(3):
            p.s_v.wait(sy)
            wl = wsem[wsl].dma(sy.dma_start(
                out=wraw, in_=p.w_down[oc * 128 : (oc + 1) * 128,
                                       cp * 1024 : cp * 1024 + 1024]))
            wsem[wsl].wait(v, wl)
            _lpbq_dequant(p, wraw, 1024, sc2[:, wsl * 64 : wsl * 64 + 32],
                          sc2[:, wsl * 64 + 32 : wsl * 64 + 64])
            dqv = p.s_v.v
            p.s_v.wait(t, dqv)
            p.s_ea.wait(v)
            p.s_eb.wait(v)
            for kk in range(2):
                p.s_v.wait(t)
                for k4 in range(4):
                    kc = kk * 4 + k4
                    tv = p.s_t.inc(t.transpose(bank_wt[:, k4 * 128 : k4 * 128 + 128],
                                               wraw[:, kc * 128 : kc * 128 + 128], p.idn))
                p.s_t.wait(v, tv)
                v.tensor_copy(stg[:, kk * 512 : kk * 512 + 512], bank_wt)
            vv = p.s_v.v
            p.s_v.wait(g, vv)
            esem[wsl].dma(g.dma_start(
                out=p.wdq_hbm[cp * 1024 : (cp + 1) * 1024, oc * 128 : (oc + 1) * 128]
                .rearrange("(kc kp) o -> kp kc o", kp=128),
                in_=stg[:, 0:1024].rearrange("kp (kc o) -> kp kc o", kc=8)))

    _allreduce_max(p, [6, 7], [6, 7])
    _scale_recip(p, 6, 10, 11)
    _scale_recip(p, 7, 12, 13)

    p.s_ea.wait(sy)
    p.s_eb.wait(sy)
    v.memset(p.sml[:, 8:9], 0.0)
    rsem = [p.s_ra, p.s_rb]
    ld_ck = {}
    cv_ck = {}
    for ti in range(96):
        oc, tt = divmod(ti, 4)
        ssl = ti % 2
        base = ssl * 2048
        graw = p.big[:, base : base + 512]
        gq = p.big[:, base + 512 : base + 1024]
        sq = p.big[:, base + 1024 : base + 1536]
        g2 = p.big[:, base + 1536 : base + 2048]
        if ti >= 2:
            p.s_v.wait(sy, cv_ck[ti - 2])
        ld_ck[ti] = rsem[ssl].dma(sy.dma_start(
            out=graw, in_=p.gate_hbm[oc, :, tt * 512 : tt * 512 + 512]))
        rsem[ssl].wait(v, ld_ck[ti])
        if ti >= 2:
            esem[ssl].wait(v, sp_ck[ti - 2])
        v.tensor_scalar(gq.bitcast(I32), graw, p.sml[:, 13:14], None, ALU.mult)
        v.tensor_scalar(gq, gq.bitcast(I32), p.sml[:, 12:13], None, ALU.mult)
        sv = p.s_v.v
        p.s_v.wait(s, sv)
        av = p.s_s.inc(s.activation(sq, gq, AF.Sigmoid))
        p.s_s.wait(v, av)
        v.tensor_scalar(sq.bitcast(I32), sq, 65536.0, None, ALU.mult)
        v.tensor_scalar(sq, sq.bitcast(I32), 65535.0, float(2.0 ** -16),
                        ALU.min, ALU.mult)
        v.tensor_mul(g2, gq, sq)
        v.tensor_reduce(p.tmpv[:, 0:1], g2, mybir.AxisListType.X, ALU.max,
                        apply_absolute_value=True)
        v.tensor_max(p.sml[:, 8:9], p.sml[:, 8:9], p.tmpv[:, 0:1])
        cv_ck[ti] = p.s_v.v
        p.s_v.wait(g, cv_ck[ti])
        sp_ck[ti] = esem[ssl].dma(
            g.dma_start(out=p.g2_hbm[oc, :, tt * 512 : tt * 512 + 512], in_=g2))

    _allreduce_max(p, [8], [8])
    _scale_recip(p, 8, 14, 15)

    p.s_ea.wait(sy)
    p.s_eb.wait(sy)
    v.memset(p.sml[:, 9:10], 0.0)
    for ti in range(96):
        oc, tt = divmod(ti, 4)
        ssl = ti % 2
        base = ssl * 2048
        g2r = p.big[:, base : base + 512]
        upr = p.big[:, base + 512 : base + 1024]
        orw = p.big[:, base + 1024 : base + 1536]
        if ti >= 2:
            p.s_v.wait(sy, cv_ck[ti - 2])
        rsem[ssl].dma(sy.dma_start(
            out=g2r, in_=p.g2_hbm[oc, :, tt * 512 : tt * 512 + 512]))
        ld_ck[ti] = rsem[ssl].dma(sy.dma_start(
            out=upr, in_=p.up_hbm[oc, :, tt * 512 : tt * 512 + 512]))
        rsem[ssl].wait(v, ld_ck[ti])
        if ti >= 2:
            esem[ssl].wait(v, sp_ck[ti - 2])
        v.tensor_scalar(g2r.bitcast(I32), g2r, p.sml[:, 15:16], None, ALU.mult)
        v.tensor_scalar(g2r, g2r.bitcast(I32), p.sml[:, 14:15], None, ALU.mult)
        v.tensor_scalar(upr.bitcast(I32), upr, p.sml[:, 11:12], None, ALU.mult)
        v.tensor_scalar(upr, upr.bitcast(I32), p.sml[:, 10:11], None, ALU.mult)
        v.tensor_mul(orw, g2r, upr)
        v.tensor_reduce(p.tmpv[:, 0:1], orw, mybir.AxisListType.X, ALU.max,
                        apply_absolute_value=True)
        v.tensor_max(p.sml[:, 9:10], p.sml[:, 9:10], p.tmpv[:, 0:1])
        cv_ck[ti] = p.s_v.v
        p.s_v.wait(g, cv_ck[ti])
        sp_ck[ti] = esem[ssl].dma(
            g.dma_start(out=p.o_hbm[oc, :, tt * 512 : tt * 512 + 512], in_=orw))

    _allreduce_max(p, [9], [9])
    _scale_recip(p, 9, 16, 17)

    p.s_ea.wait(sy)
    p.s_eb.wait(sy)
    oq = p.big[:, 0:12288]
    wdb = p.big[:, 12288 : 12288 + 3072]
    prev_tv = p.s_t.v
    for tt in range(4):
        if tt > 0:
            p.s_v.wait(sy, cv_ck[(tt - 1, 23)])
            sy.wait_ge(p.s_t.h, prev_tv)
        for kc in range(24):
            ssl = kc % 2
            rsem[ssl].dma(sy.dma_start(
                out=oq[:, kc * 512 : kc * 512 + 512],
                in_=p.o_hbm[kc, :, tt * 512 : tt * 512 + 512]))
        rsem[0].wait(v)
        rsem[1].wait(v)
        for kc in range(24):
            dst = oq[:, kc * 512 : kc * 512 + 512]
            v.tensor_scalar(dst.bitcast(I32), dst, p.sml[:, 17:18], None, ALU.mult)
            v.tensor_scalar(dst, dst.bitcast(I32), p.sml[:, 16:17], None, ALU.mult)
        cv_ck[(tt, 23)] = p.s_v.v
        qv = p.s_v.v
        for oc in range(8):
            sy.wait_ge(p.s_t.h, prev_tv)
            src = p.wdq_hbm[:, oc * 128 : (oc + 1) * 128].rearrange(
                "(kc kp) o -> kp kc o", kp=128)
            wl = p.s_wa.dma(sy.dma_start(
                out=wdb.rearrange("kp (kc o) -> kp kc o", kc=24), in_=src))
            bank = p.pb[2 + oc % 2]
            p.s_wa.wait(t, wl)
            p.s_v.wait(t, qv)
            if not (tt == 0 and oc < 2):
                p.s_v.wait(t)
            for kc in range(24):
                mm = t.matmul(bank[:, 0:512],
                              wdb[:, kc * 128 : kc * 128 + 128],
                              oq[:, kc * 512 : kc * 512 + 512],
                              start=(kc == 0), stop=(kc == 23))
            tv = p.s_t.inc(mm)
            prev_tv = tv
            p.s_t.wait(v, tv)
            rsl = p.R[:, oc * T + tt * 512 : oc * T + tt * 512 + 512]
            v.tensor_add(rsl, rsl, bank[:, 0:512])

    _dbg(p, "x2pre", p.R, [128, 8 * T])
    _post_ln(p, ln_idx=1)


def _l1_attn(p):
    sy, t, v, s, g = p.sy, p.t, p.v, p.s, p.g
    _fence_compute(p)
    LQ4 = 512
    abuf = p.big[:, 0:4096]
    bbuf = p.big[:, 4096:8192]
    hbuf = p.big[:, 8192:12288]
    wzb = p.big[:, 12288:13312]
    whb = p.big[:, 13312:14336]

    def shslot(fc, d):
        if d == 0:
            if fc < 8:
                return (p.yoct[:, fc * 512 : fc * 512 + 512] if fc < 8 else None)
        return None

    bzv = p.cw[:, 0:8]
    bhv = p.cw[:, 8:16]
    dcv = p.cw[:, 16:24]

    lv = p.s_ld.dma(sy.dma_start(
        out=bzv.rearrange("p (j o) -> p j o", o=1),
        in_=p.bz.rearrange("(j p) o -> p j o", p=128),
        allow_slow_non_contiguous=True))
    lv = p.s_ld.dma(sy.dma_start(
        out=bhv.rearrange("p (j o) -> p j o", o=1),
        in_=p.bh.rearrange("(j p) o -> p j o", p=128),
        allow_slow_non_contiguous=True))
    lv = p.s_ld.dma(sy.dma_start(
        out=dcv.rearrange("p (j o) -> p j o", o=1),
        in_=p.log_d.rearrange("(j p) o -> p j o", p=128),
        allow_slow_non_contiguous=True))
    p.mdf = p.gb0[:, 0:128]
    p.mwpb = p.gb0[:, 128:256]
    p.mwnb = p.gb0[:, 256:384]
    p.s_ld.dma(sy.dma_start(out=p.mdf, in_=p.mdiff))
    p.s_ld.dma(sy.dma_start(out=p.mwpb, in_=p.mwp))
    p.s_ld.dma(sy.dma_start(out=p.mwnb, in_=p.mwn))
    p.s_ld.wait(t)
    p.s_ld.wait(s)
    av = p.s_s.inc(s.activation(dcv, dcv, AF.Exp))
    v.memset(p.sml[:, 20:28], 0.0)
    wsem = [p.s_wa, p.s_wb]
    rsem = [p.s_ra, p.s_rb]
    prev_tv = p.s_t.v
    for q in range(4):
        tq = q * LQ4
        for oc in range(8):
            sy.wait_ge(p.s_t.h, prev_tv)
            p.s_s.wait(sy)
            wl = wsem[0].dma(sy.dma_start(
                out=wzb.rearrange("kp (kc o) -> kp kc o", kc=8),
                in_=p.wz[:, oc * 128 : (oc + 1) * 128].rearrange("(kc kp) o -> kp kc o", kp=128)))
            wl2 = wsem[1].dma(sy.dma_start(
                out=whb.rearrange("kp (kc o) -> kp kc o", kc=8),
                in_=p.wh[:, oc * 128 : (oc + 1) * 128].rearrange("(kc kp) o -> kp kc o", kp=128)))
            wsem[0].wait(t, wl)
            wsem[1].wait(t, wl2)
            p.s_s.wait(t)
            p.s_v.wait(t)
            for kc in range(8):
                mm = t.matmul(p.pb[2], wzb[:, kc * 128 : kc * 128 + 128],
                              p.R[:, kc * T + tq : kc * T + tq + LQ4],
                              start=(kc == 0), stop=(kc == 7))
            tvz = p.s_t.inc(mm)
            for kc in range(8):
                mm = t.matmul(p.pb[3], whb[:, kc * 128 : kc * 128 + 128],
                              p.R[:, kc * T + tq : kc * T + tq + LQ4],
                              start=(kc == 0), stop=(kc == 7))
            prev_tv = tvh = p.s_t.inc(mm)
            zsl = abuf[:, oc * 512 : oc * 512 + 512]
            hsl = bbuf[:, oc * 512 : oc * 512 + 512]
            p.s_t.wait(s, tvz)
            s.activation(zsl, p.pb[2], AF.Sigmoid, bias=bzv[:, oc : oc + 1])
            p.s_t.wait(s, tvh)
            sv = p.s_s.inc(s.activation(hsl, p.pb[3], AF.Identity, bias=bhv[:, oc : oc + 1]))
            p.s_s.wait(v, sv)
            v.tensor_mul(hsl, zsl, hsl)
            v.tensor_scalar(zsl, zsl, -1.0, 1.0, ALU.mult, ALU.add)
        if "no_scan" in p.dbg:
            for fc in range(8):
                v.tensor_copy(hbuf[:, fc * 512 : fc * 512 + 512],
                              bbuf[:, fc * 512 : fc * 512 + 512])
        rsem[0].wait(v)
        rsem[1].wait(v)
        for fc in range(8 if "no_scan" not in p.dbg else 0):
            v.tensor_tensor_scan(hbuf[:, fc * 512 : fc * 512 + 512],
                                 abuf[:, fc * 512 : fc * 512 + 512],
                                 bbuf[:, fc * 512 : fc * 512 + 512],
                                 p.sml[:, 20 + fc : 21 + fc], ALU.mult, ALU.add)
            v.tensor_copy(p.sml[:, 20 + fc : 21 + fc],
                          hbuf[:, fc * 512 + 511 : fc * 512 + 512])
        hv = p.s_v.v
        p.s_v.wait(t, hv)
        p.s_s.wait(t)
        for fc in range(8):
            hsl = hbuf[:, fc * 512 : fc * 512 + 512]
            prv = hbuf[:, ((fc - 1) % 8) * 512 : ((fc - 1) % 8) * 512 + 512]
            nxt = hbuf[:, ((fc + 1) % 8) * 512 : ((fc + 1) % 8) * 512 + 512]
            bank = p.pb[fc]
            t.matmul(bank, p.mdf, hsl, start=True, stop=False)
            t.matmul(bank, p.mwpb, prv, start=False, stop=False)
            dtv = p.s_t.inc(t.matmul(bank, p.mwnb, nxt, start=False, stop=True))
        p.s_t.wait(v, dtv)
        for fc in range(8):
            hsl = hbuf[:, fc * 512 : fc * 512 + 512]
            v.scalar_tensor_tensor(hsl, p.pb[fc], dcv[:, fc : fc + 1], hsl,
                                   ALU.mult, ALU.add)
            rsl = p.R[:, fc * T + tq : fc * T + tq + LQ4]
            v.tensor_add(rsl, rsl, hsl)

    _dbg(p, "x3pre", p.R, [128, 8 * T])
    _post_ln(p, ln_idx=2)


def _l1_ffn(p):
    sy, t, v, s, g = p.sy, p.t, p.v, p.s, p.g
    _fence_compute(p)
    wd1 = p.big[:, 0:8192]
    y1 = p.big[:, 8192:12288]
    srst = p.big[:, 12288:14336]
    lra = p.big[:, 14336:14848]
    lab = p.big[:, 14848:15104]
    lbb = p.big[:, 15104:16128]

    def wd2(kc):
        if kc < 4:
            return p.yoct[:, kc * 1024 : kc * 1024 + 1024]
        if kc < 6:
            return p.ysq[:, (kc - 4) * 1024 : (kc - 4) * 1024 + 1024]
        return p.state[:, (kc - 6) * 1024 : (kc - 6) * 1024 + 1024]

    qb1v = p.cw[:, 24:32]
    qb2v = p.cw[:, 32:40]
    gamv = p.cw[:, 40:48]
    betv = p.cw[:, 48:56]
    wsem = [p.s_wa, p.s_wb]
    stats = p.pb[1]
    p.s_v.wait(g)
    p.s_s.wait(g)
    for fc in range(8):
        p.s_ea.dma(g.dma_start(out=p.z_hbm[fc, :, :], in_=p.R[:, fc * T : (fc + 1) * T]))
    for blk in range(3):
        i0, i1 = 2 * blk, 2 * blk + 1
        p.s_s.wait(sy)
        p.s_v.wait(sy)
        p.s_t.wait(sy)
        lv = p.s_ld.dma(sy.dma_start(
            out=qb1v.rearrange("p (j o) -> p j o", o=1),
            in_=p.qb[:, i0 : i0 + 1].rearrange("(j p) o -> p j o", p=128),
            allow_slow_non_contiguous=True))
        lv = p.s_ld.dma(sy.dma_start(
            out=qb2v.rearrange("p (j o) -> p j o", o=1),
            in_=p.qb[:, i1 : i1 + 1].rearrange("(j p) o -> p j o", p=128),
            allow_slow_non_contiguous=True))
        lv = p.s_ld.dma(sy.dma_start(
            out=gamv.rearrange("p (j o) -> p j o", o=1),
            in_=p.gam[:, blk : blk + 1].rearrange("(j p) o -> p j o", p=128),
            allow_slow_non_contiguous=True))
        lv = p.s_ld.dma(sy.dma_start(
            out=betv.rearrange("p (j o) -> p j o", o=1),
            in_=p.bet[:, blk : blk + 1].rearrange("(j p) o -> p j o", p=128),
            allow_slow_non_contiguous=True))
        lv = p.s_ld.dma(sy.dma_start(
            out=lab.rearrange("p (kc r) -> p kc r", kc=8),
            in_=p.la_t[i1].rearrange("(kc p) r -> p kc r", p=128)))
        lv = p.s_ld.dma(sy.dma_start(out=lbb[0:32, :], in_=p.lb_t[i1]))
        lab1 = p.dt_raw[:, 0:256]
        lbb1 = p.dt_raw[:, 256:1280]
        lv = p.s_ld.dma(sy.dma_start(
            out=lab1.rearrange("p (kc r) -> p kc r", kc=8),
            in_=p.la_t[i0].rearrange("(kc p) r -> p kc r", p=128)))
        lv = p.s_ld.dma(sy.dma_start(out=lbb1[0:32, :], in_=p.lb_t[i0]))

        p.s_t.wait(sy)
        for un in range(16):
            mat, kc = divmod(un, 8)
            dstw = wd1[:, kc * 1024 : kc * 1024 + 1024] if mat == 0 else wd2(kc)
            ssl = un % 2
            srb = srst[:, ssl * 1024 : ssl * 1024 + 1024]
            qi = i0 if mat == 0 else i1
            if un >= 2:
                p.s_v.wait(sy)
            wl = wsem[ssl].dma(sy.dma_start(
                out=dstw.bitcast(I32), in_=p.qw_t[qi, kc * 128 : (kc + 1) * 128, :]))
            wl = wsem[ssl].dma(sy.dma_start(
                out=srb, in_=p.srep_t[qi, kc * 128 : (kc + 1) * 128, :]))
            wsem[ssl].wait(v, wl)
            v.tensor_scalar(dstw, dstw.bitcast(I32), float(np.float32(1.0 / 7.5)),
                            -1.0, ALU.mult, ALU.add)
            v.tensor_mul(dstw, dstw, srb)
        dqv = p.s_v.v
        p.s_ld.wait(t)
        p.s_ld.wait(v)
        p.s_ld.wait(s)
        p.s_v.wait(t, dqv)

        prev_tv = p.s_t.v
        for tt in range(4):
            ts0 = tt * 512
            for kc in range(8):
                mm = t.matmul(p.pb[5][0:32, :], lab1[:, kc * 32 : kc * 32 + 32],
                              p.R[:, kc * T + ts0 : kc * T + ts0 + 512],
                              start=(kc == 0), stop=(kc == 7))
            ltv = p.s_t.inc(mm)
            p.s_t.wait(v, ltv)
            v.tensor_copy(lra[0:32, :], p.pb[5][0:32, :])
            lrv = p.s_v.v
            p.s_v.wait(t, lrv)
            for oc in range(8):
                bank = p.pb[2 + oc % 2]
                if not (tt == 0 and oc < 2):
                    p.s_s.wait(t)
                for kc in range(8):
                    t.matmul(bank, wd1[:, kc * 1024 + oc * 128 : kc * 1024 + oc * 128 + 128],
                             p.R[:, kc * T + ts0 : kc * T + ts0 + 512],
                             start=(kc == 0), stop=False)
                mm = t.matmul(bank, lbb1[0:32, oc * 128 : oc * 128 + 128],
                              lra[0:32, :], start=False, stop=True)
                tv = p.s_t.inc(mm)
                p.s_t.wait(s, tv)
                sv = p.s_s.inc(s.activation(y1[:, oc * 512 : oc * 512 + 512], bank,
                                            AF.Identity, bias=qb1v[:, oc : oc + 1]))
            if blk == 0 and tt == 0:
                _dbg(p, "y1b0", y1, [128, 4096])
            stats2 = p.pb[4]
            sq_ck = {}
            for oc in range(8):
                ysl = y1[:, oc * 512 : oc * 512 + 512]
                ssl = oc % 2
                sq = srst[:, ssl * 1024 : ssl * 1024 + 512]
                if oc >= 2:
                    p.s_t.wait(s, sq_ck[oc - 2])
                sv = p.s_s.inc(s.activation(sq, ysl, AF.Square))
                p.s_t.inc(t.matmul(stats[0:1, 0:512], p.ones[:, 0:1], ysl,
                                   start=(oc == 0), stop=(oc == 7)))
                p.s_s.wait(t, sv)
                sq_ck[oc] = tv = p.s_t.inc(t.matmul(
                    stats2[0:1, 0:512], p.ones[:, 0:1], sq,
                    start=(oc == 0), stop=(oc == 7)))
            p.s_t.wait(v, tv)
            m = p.rows[0:1, 0:512]
            var = p.rows[0:1, 512:1024]
            v.tensor_scalar(m, stats[0:1, 0:512], 1.0 / D, None, ALU.mult)
            v.tensor_scalar(var, stats2[0:1, 0:512], 1.0 / D, None, ALU.mult)
            v.tensor_mul(p.tmpv[0:1, 0:512], m, m)
            v.tensor_sub(var, var, p.tmpv[0:1, 0:512])
            v.tensor_scalar(var, var, 1.0, EPS, ALU.mult, ALU.add)
            rv = p.s_v.inc(v.reciprocal(var, var))
            p.s_v.wait(s, rv)
            sv = p.s_s.inc(s.activation(var, var, AF.Sqrt))
            p.s_s.wait(g, sv)
            p.s_v.wait(g)
            wv = p.s_bw.dma(g.dma_start(out=p.bscr[16:17, 0:512], in_=m))
            wv = p.s_bw.dma(g.dma_start(out=p.bscr[18:19, 0:512], in_=var))
            p.s_bw.wait(g, wv)
            p.s_bc.dma(g.dma_start(out=p.bc1[:, 0:512],
                                   in_=p.bscr[16:17, 0:512].broadcast_to([128, 512])))
            bv = p.s_bc.dma(g.dma_start(out=p.bc2[:, 0:512],
                                        in_=p.bscr[18:19, 0:512].broadcast_to([128, 512])))
            p.s_bc.wait(v, bv)
            for oc in range(8):
                ysl = y1[:, oc * 512 : oc * 512 + 512]
                v.tensor_sub(ysl, ysl, p.bc1[:, 0:512])
                v.tensor_mul(ysl, ysl, p.bc2[:, 0:512])
                v.tensor_scalar(ysl, ysl, gamv[:, oc : oc + 1], betv[:, oc : oc + 1],
                                ALU.mult, ALU.add)
            gv = p.s_v.v
            p.s_v.wait(s, gv)
            for oc in range(8):
                ysl = y1[:, oc * 512 : oc * 512 + 512]
                sv = p.s_s.inc(s.activation(ysl, ysl, AF.Gelu))
            p.s_s.wait(t, sv)
            p.s_s.wait(v, sv)
            if blk == 0 and tt == 0:
                _dbg(p, "y1g_b0", y1, [128, 4096])

            for kc in range(8):
                mm = t.matmul(p.pb[5][0:32, :], lab[:, kc * 32 : kc * 32 + 32],
                              y1[:, kc * 512 : kc * 512 + 512],
                              start=(kc == 0), stop=(kc == 7))
            ltv = p.s_t.inc(mm)
            p.s_t.wait(v, ltv)
            v.tensor_copy(lra[0:32, :], p.pb[5][0:32, :])
            lrv = p.s_v.v
            p.s_v.wait(t, lrv)
            for oc in range(8):
                bank = p.pb[2 + oc % 2]
                if not (tt == 0 and oc < 2):
                    p.s_v.wait(t)
                for kc in range(8):
                    t.matmul(bank, wd2(kc)[:, oc * 128 : oc * 128 + 128],
                             y1[:, kc * 512 : kc * 512 + 512],
                             start=(kc == 0), stop=False)
                mm = t.matmul(bank, lbb[0:32, oc * 128 : oc * 128 + 128],
                              lra[0:32, :], start=False, stop=True)
                tv = p.s_t.inc(mm)
                prev_tv = tv
                p.s_t.wait(v, tv)
                rsl = p.R[:, oc * T + ts0 : oc * T + ts0 + 512]
                v.scalar_tensor_tensor(p.tmpv[:, 0:512], bank, 1.0, rsl,
                                       ALU.mult, ALU.add)
                v.tensor_scalar(p.tmpv[:, 0:512], p.tmpv[:, 0:512],
                                qb2v[:, oc : oc + 1], None, ALU.add)
                v.tensor_copy(rsl, p.tmpv[:, 0:512])
        _dbg(p, f"rb{blk}", p.R, [128, 8 * T])

    p.s_t.wait(sy)
    p.s_v.wait(sy)
    p.s_s.wait(sy)
    for fc in range(8):
        lv = p.s_ra.dma(sy.dma_start(out=p.big[:, fc * T : fc * T + T],
                                     in_=p.z_hbm[fc, :, :]))
    p.s_ra.wait(v)
    for fc in range(8):
        rsl = p.R[:, fc * T : (fc + 1) * T]
        v.tensor_add(rsl, rsl, p.big[:, fc * T : fc * T + T])

    _post_ln(p, ln_idx=3)


def _stage_inputs(inputs, names):
    u = np.ascontiguousarray(np.asarray(inputs["L0_attn_u"], dtype=np.float32))
    w_in_t = np.zeros((D, DPROJ_PAD), np.float32)
    w_in_t[:, :DPROJ] = np.asarray(inputs["L0_attn_W_in"], np.float32).T
    common = {
        "w_in_t": w_in_t,
        "conv_w": np.ascontiguousarray(np.asarray(inputs["L0_attn_conv_w"], np.float32)),
        "conv_b": np.asarray(inputs["L0_attn_conv_b"], np.float32)[:, None].copy(),
        "dt_bias": np.asarray(inputs["L0_attn_dt_bias"], np.float32)[:, None].copy(),
        "a_log": np.asarray(inputs["L0_attn_A_log"], np.float32)[:, None].copy(),
        "d_rep": np.repeat(np.asarray(inputs["L0_attn_D"], np.float32), HP)[:, None].copy(),
        "nw_in": np.asarray(inputs["L0_attn_norm_weight"], np.float32)[:, None].copy(),
        "w_out_t": np.ascontiguousarray(np.asarray(inputs["L0_attn_W_out"], np.float32).T),
        "ident": np.eye(128, dtype=np.float32),
        "mask01": np.triu(np.ones((128, 128), np.float32)).copy(),
    }
    if "w_up" in names:
        common["w_up"] = np.ascontiguousarray(np.asarray(inputs["L0_ffn_w_up"], np.float32))
        common["w_gate"] = np.ascontiguousarray(np.asarray(inputs["L0_ffn_w_gate"], np.float32))
        common["w_down"] = np.ascontiguousarray(np.asarray(inputs["L0_ffn_w_down"], np.float32))
    if "wz" in names:
        common["wz"] = np.ascontiguousarray(np.asarray(inputs["L1_attn_Wz"], np.float32))
        common["wh"] = np.ascontiguousarray(np.asarray(inputs["L1_attn_Wh"], np.float32))
        common["bz"] = np.asarray(inputs["L1_attn_bz"], np.float32)[:, None].copy()
        common["bh"] = np.asarray(inputs["L1_attn_bh"], np.float32)[:, None].copy()
        common["log_d"] = np.asarray(inputs["L1_attn_log_D"], np.float32)[:, None].copy()
        md = np.zeros((128, 128), np.float32)
        for i_ in range(128):
            md[i_, i_] = -2.0
            if i_ + 1 < 128:
                md[i_, i_ + 1] = 1.0
                md[i_ + 1, i_] = 1.0
        common["mdiff"] = md
        mwp_ = np.zeros((128, 128), np.float32)
        mwp_[127, 0] = 1.0
        common["mwp"] = mwp_
        mwn_ = np.zeros((128, 128), np.float32)
        mwn_[0, 127] = 1.0
        common["mwn"] = mwn_
    if "qw_t" in names:
        qw = np.asarray(inputs["L1_ffn_qweights"], np.int32)
        common["qw_t"] = np.ascontiguousarray(qw.transpose(0, 2, 1))
        sc = np.asarray(inputs["L1_ffn_scales"], np.float32)
        common["srep_t"] = np.ascontiguousarray(
            np.repeat(sc, 16, axis=2).transpose(0, 2, 1))
        common["qb"] = np.ascontiguousarray(np.asarray(inputs["L1_ffn_biases"], np.float32).T)
        common["la_t"] = np.ascontiguousarray(
            np.asarray(inputs["L1_ffn_lora_a"], np.float32).transpose(0, 2, 1))
        common["lb_t"] = np.ascontiguousarray(
            np.asarray(inputs["L1_ffn_lora_b"], np.float32).transpose(0, 2, 1))
        common["gam"] = np.ascontiguousarray(np.asarray(inputs["L1_ffn_ln_gamma"], np.float32).T)
        common["bet"] = np.ascontiguousarray(np.asarray(inputs["L1_ffn_ln_beta"], np.float32).T)
    in_maps = []
    for c in range(NCORES):
        m = {k: vv for k, vv in common.items() if k in names}
        m["u"] = np.ascontiguousarray(u[c].T)
        in_maps.append(m)
    return in_maps


def kernel(**inputs):
    p = build(_FULL)
    in_maps = _stage_inputs(inputs, set(p.in_names))
    res = run_bass_kernel_spmd(p.nc, in_maps, core_ids=list(range(NCORES)))
    outs = [np.ascontiguousarray(res.results[c]["out"].T) for c in range(NCORES)]
    return np.stack(outs, axis=0)


# version 03
# speedup vs baseline: 1.1274x; 1.1274x over previous
import numpy as np
import concourse.bass as bass
import concourse.mybir as mybir
from concourse.bass_utils import run_bass_kernel_spmd

F32 = mybir.dt.float32
I32 = mybir.dt.int32
AF = mybir.ActivationFunctionType
ALU = mybir.AluOpType

NCORES = 8
T = 2048
D = 1024
DIN = 2048
NH = 32
HP = 64
DSTATE = 128
CONV_DIM = DIN + 2 * DSTATE
DPROJ = 2 * DIN + 2 * DSTATE + NH
DPROJ_PAD = 35 * 128
LO = 256
LC = 128
NO = T // LO
IFF = 3072
QMAX = 32767.0
EPS = 1e-5

_FULL = 4


class RecInstr:
    __slots__ = ("incs",)

    def __init__(self):
        self.incs = []

    def then_inc(self, sem_h, n):
        self.incs.append((sem_h, n))
        return self


_NOCHAIN = {"wait_ge", "dma_start", "indirect_dma_start", "collective_compute",
            "sem_inc", "sem_clear", "nop"}


def scope(p, name):
    for rec in (p.sy, p.t, p.v, p.s, p.g):
        rec.cmds.append(("__scope__", (name,), {}, RecInstr()))


class Rec:

    def __init__(self, name, chain_sem=None):
        self.name = name
        self.cmds = []
        self.chain = chain_sem
        self.ccount = 0

    def __getattr__(self, attr):
        def call(*a, **k):
            h = RecInstr()
            if self.chain is not None and attr not in _NOCHAIN:
                if self.ccount > 0:
                    self.cmds.append(("wait_ge", (self.chain, self.ccount), {}, RecInstr()))
                h.incs.append((self.chain, 1))
                self.ccount += 1
            self.cmds.append((attr, a, k, h))
            return h
        return call


def _replay(nc, eng, cmds):
    import contextlib
    stack = contextlib.ExitStack()
    with stack:
        for attr, a, k, h in cmds:
            if attr == "__scope__":
                stack.close()
                stack = contextlib.ExitStack()
                if a[0] is not None:
                    stack.enter_context(nc.named_scope(a[0]))
                continue
            r = getattr(eng, attr)(*a, **k)
            for sem_h, n in h.incs:
                r.then_inc(sem_h, n)


class Sem:
    def __init__(self, nc, name):
        self.h = nc.alloc_semaphore(name)
        self.v = 0

    def inc(self, instr, n=1):
        instr.then_inc(self.h, n)
        self.v += n
        return self.v

    def dma(self, instr):
        return self.inc(instr, 16)

    def wait(self, eng, val=None):
        eng.wait_ge(self.h, self.v if val is None else val)


class ChainSem:

    def __init__(self, rec):
        self.rec = rec
        self.h = rec.chain

    @property
    def v(self):
        return self.rec.ccount

    def inc(self, instr, n=1):
        return self.rec.ccount

    def wait(self, eng, val=None):
        eng.wait_ge(self.h, self.rec.ccount if val is None else val)


def rep(ap, k):
    return ap.unsqueeze(2).broadcast_to([ap.shape[0], ap.shape[1], k])


class P:
    pass


def _alloc(p, upto, dbg):
    nc = p.nc
    p.upto = upto
    p.dbg = set(dbg)

    def ein(name, shape, dt=F32):
        return nc.dram_tensor(name, shape, dt, kind="ExternalInput").ap()

    p.in_names = []

    def reg(name, shape, dt=F32):
        p.in_names.append(name)
        return ein(name, shape, dt)

    p.u = reg("u", [D, T])
    p.w_in_t = reg("w_in_t", [D, DPROJ_PAD])
    p.conv_w = reg("conv_w", [CONV_DIM, 4])
    p.conv_b = reg("conv_b", [CONV_DIM, 1])
    p.dt_bias = reg("dt_bias", [NH, 1])
    p.a_log = reg("a_log", [NH, 1])
    p.d_rep = reg("d_rep", [DIN, 1])
    p.nw_in = reg("nw_in", [DIN, 1])
    p.w_out_t = reg("w_out_t", [DIN, D])
    p.ident = reg("ident", [128, 128])
    p.mask01 = reg("mask01", [128, 128])
    if upto >= 2:
        p.w_up = reg("w_up", [IFF, D])
        p.w_gate = reg("w_gate", [IFF, D])
        p.w_down = reg("w_down", [D, IFF])
    if upto >= 3:
        p.wz = reg("wz", [D, D])
        p.wh = reg("wh", [D, D])
        p.bz = reg("bz", [D, 1])
        p.bh = reg("bh", [D, 1])
        p.log_d = reg("log_d", [D, 1])
        p.mdiff = reg("mdiff", [128, 128])
        p.mwp = reg("mwp", [128, 128])
        p.mwn = reg("mwn", [128, 128])
    if upto >= 4:
        p.qw_t = reg("qw_t", [6, D, D], I32)
        p.srep_t = reg("srep_t", [6, D, D])
        p.qb = reg("qb", [D, 6])
        p.la_t = reg("la_t", [6, D, 32])
        p.lb_t = reg("lb_t", [6, 32, D])
        p.gam = reg("gam", [D, 3])
        p.bet = reg("bet", [D, 3])

    p.out = nc.dram_tensor("out", [D, T], F32, kind="ExternalOutput").ap()

    p.z_hbm = nc.dram_tensor("z_hbm", [16, 128, T], F32).ap()
    p.xbc_hbm = nc.dram_tensor("xbc_hbm", [18, 128, T], F32).ap()
    p.y_hbm = nc.dram_tensor("y_hbm", [16, 128, T], F32).ap()
    p.bscr = nc.dram_tensor("bscr", [34, 512], F32).ap()
    if upto >= 2:
        p.up_hbm = nc.dram_tensor("up_hbm", [24, 128, T], F32).ap()
        p.gate_hbm = nc.dram_tensor("gate_hbm", [24, 128, T], F32).ap()
        p.g2_hbm = nc.dram_tensor("g2_hbm", [24, 128, T], F32).ap()
        p.o_hbm = nc.dram_tensor("o_hbm", [24, 128, T], F32).ap()
        p.wdq_hbm = nc.dram_tensor("wdq_hbm", [IFF, D], F32).ap()
        p.cc_in = nc.dram_tensor("cc_in", [1, 4], F32)
        p.cc_out = nc.dram_tensor("cc_out", [1, 4], F32)

    def sb(name, free):
        return nc.alloc_sbuf_tensor(name, [128, free], F32).ap()

    p.R = sb("R", 8 * T)
    p.big = sb("big", 16384)
    p.gbs = sb("gbs", 4096)
    p.gb0 = p.gbs[:, 0:1024]
    p.gb1 = p.gbs[:, 1024:2048]
    p.gb2 = p.gbs[:, 2048:3072]
    p.xd = sb("xd", 512)
    p.state = sb("state", DIN)
    p.dt_raw = sb("dt_raw", T)
    p.rstd_all = sb("rstd_all", T)
    p.ysq = sb("ysq", DIN)
    p.yoct = sb("yoct", 16 * LO)
    p.idn = sb("idn", 128)
    p.msk = sb("msk", 128)
    p.nbt = sb("nbt", 32)
    p.dst = sb("dst", 32)
    p.btm = sb("btm", 128)
    p.cbm = sb("cbm", 128)
    p.ones = sb("ones", 8)
    p.sml = sb("sml", 64)
    p.rows = sb("rows", 1024)
    p.bc1 = p.gbs[:, 0:2048]
    p.bc2 = p.gbs[:, 2048:4096]
    p.tmpv = sb("tmpv", 1024)
    p.cw = sb("cw", 96)
    p.dr = sb("dr", 16)
    p.nw = sb("nw", 16)

    p.pb = [nc.alloc_psum_tensor(f"pb{i}", [128, 512], F32).ap() for i in range(8)]

    for s in ["ld", "t", "o2", "wa", "wb", "wc", "wd", "ea", "eb", "ra", "rb",
              "zc", "bw", "bc", "be", "yw", "yb", "cc", "ci", "co"]:
        setattr(p, "s_" + s, Sem(nc, "sem_" + s))


def build(upto=_FULL, dbg=()):
    p = P()
    p.nc = bass.Bass("TRN2", target_bir_lowering=False)
    _alloc(p, upto, dbg)
    vch = p.nc.alloc_semaphore("sem_vch")
    sch = p.nc.alloc_semaphore("sem_sch")
    gch = p.nc.alloc_semaphore("sem_gch")
    p.sy, p.t, p.v, p.s, p.g = (Rec("sy"), Rec("t"), Rec("v", vch), Rec("s", sch),
                                Rec("g", gch))
    p.s_v = ChainSem(p.v)
    p.s_s = ChainSem(p.s)
    p.s_gc = ChainSem(p.g)
    _emit(p)
    with p.nc.Block() as block:
        @block.sync
        def _(e):
            _replay(p.nc, e, p.sy.cmds)

        @block.tensor
        def _(e):
            _replay(p.nc, e, p.t.cmds)

        @block.vector
        def _(e):
            _replay(p.nc, e, p.v.cmds)

        @block.scalar
        def _(e):
            _replay(p.nc, e, p.s.cmds)

        @block.gpsimd
        def _(e):
            _replay(p.nc, e, p.g.cmds)
    return p


def _dbg(p, name, ap, shape):
    if name in p.dbg:
        o = p.nc.dram_tensor("dbg_" + name, shape, F32, kind="ExternalOutput").ap()
        p.s_v.wait(p.sy)
        p.s_s.wait(p.sy)
        p.s_o2.dma(p.sy.dma_start(out=o, in_=ap))


def _fence_compute(p):
    for e in (p.t, p.v, p.s, p.sy, p.g):
        for s in (p.s_t, p.s_v, p.s_s, p.s_gc, p.s_ld, p.s_wa, p.s_wb, p.s_wc,
                  p.s_wd, p.s_ea, p.s_eb, p.s_ra, p.s_rb, p.s_zc, p.s_bw, p.s_bc,
                  p.s_yw, p.s_yb):
            s.wait(e)


def _emit(p):
    sy, t, v, s, g = p.sy, p.t, p.v, p.s, p.g

    p.s_ld.dma(sy.dma_start(out=p.idn, in_=p.ident))
    p.s_ld.dma(sy.dma_start(out=p.msk, in_=p.mask01))
    for fc in range(8):
        p.s_ld.dma(sy.dma_start(out=p.R[:, fc * T : (fc + 1) * T],
                                in_=p.u[fc * 128 : (fc + 1) * 128, :]))
    p.s_ld.dma(sy.dma_start(out=p.sml[0:NH, 0:1], in_=p.dt_bias))
    p.s_ld.dma(sy.dma_start(out=p.sml[0:NH, 1:2], in_=p.a_log))
    p.s_ld.dma(sy.dma_start(
        out=p.cw[:, 0:72].rearrange("p (j w) -> p j w", j=18),
        in_=p.conv_w.rearrange("(j p) w -> p j w", p=128),
        allow_slow_non_contiguous=True))
    p.s_ld.dma(sy.dma_start(
        out=p.cw[:, 72:90].rearrange("p (j o) -> p j o", o=1),
        in_=p.conv_b.rearrange("(j p) o -> p j o", p=128),
        allow_slow_non_contiguous=True))
    p.s_ld.dma(sy.dma_start(
        out=p.dr.rearrange("p (j o) -> p j o", o=1),
        in_=p.d_rep.rearrange("(j p) o -> p j o", p=128),
        allow_slow_non_contiguous=True))
    p.s_ld.dma(sy.dma_start(
        out=p.nw.rearrange("p (j o) -> p j o", o=1),
        in_=p.nw_in.rearrange("(j p) o -> p j o", p=128),
        allow_slow_non_contiguous=True))
    p.s_ld.wait(v)
    p.s_v.inc(v.memset(p.ones, 1.0))
    p.s_ld.wait(s)
    xv = p.s_s.inc(s.activation(p.sml[0:NH, 2:3], p.sml[0:NH, 1:2], AF.Exp))
    p.s_s.wait(s, xv)
    p.s_s.inc(s.mul(p.sml[0:NH, 2:3], p.sml[0:NH, 2:3], -1.0))
    p.negA = p.sml[0:NH, 2:3]
    p.dtbv = p.sml[0:NH, 0:1]
    p.s_v.wait(t)
    p.s_s.wait(t)

    _mamba(p)
    if p.upto >= 2:
        _ffn0(p)
    if p.upto >= 3:
        _l1_attn(p)
    if p.upto >= 4:
        _l1_ffn(p)

    p.s_v.wait(sy)
    p.s_s.wait(sy)
    for fc in range(8):
        p.s_o2.dma(sy.dma_start(out=p.out[fc * 128 : (fc + 1) * 128, :],
                                in_=p.R[:, fc * T : (fc + 1) * T]))
    p.s_o2.wait(sy)


def _mamba(p):
    sy, t, v, s, g = p.sy, p.t, p.v, p.s, p.g

    scope(p, 'mamba_inproj')
    wbuf = p.big[:, 0:2048]
    tmpe = p.big[:, 2048:3072]
    wl_ck = []
    tm_ck = [0] * 200
    ev_ck = [0] * 200
    sp_ck = [0] * 200
    wsem = [p.s_wa, p.s_wb]
    esem = [p.s_ea, p.s_eb]
    p.s_v.inc(v.memset(p.state, 0.0))
    for oc in range(35):
        wb = wbuf[:, (oc % 2) * 1024 : (oc % 2) * 1024 + 1024]
        if oc >= 2:
            sy.wait_ge(p.s_t.h, tm_ck[(oc - 2) * 4 + 3])
        src = p.w_in_t[:, oc * 128 : (oc + 1) * 128].rearrange("(kc kp) o -> kp kc o", kp=128)
        wv = wsem[oc % 2].dma(sy.dma_start(out=wb.rearrange("kp (kc o) -> kp kc o", kc=8), in_=src))
        wl_ck.append(wv)
        for tt in range(4):
            i = oc * 4 + tt
            bank = p.pb[i % 2]
            wsem[oc % 2].wait(t, wl_ck[oc])
            if i >= 2:
                p.s_v.wait(t, ev_ck[i - 2])
            for kc in range(8):
                mm = t.matmul(bank, wb[:, kc * 128 : (kc + 1) * 128],
                              p.R[:, kc * T + tt * 512 : kc * T + tt * 512 + 512],
                              start=(kc == 0), stop=(kc == 7))
            tm_ck[i] = p.s_t.inc(mm)
            p.s_t.wait(v, tm_ck[i])
            if oc < 34:
                slot = i % 2
                te = tmpe[:, slot * 512 : slot * 512 + 512]
                if i >= 2:
                    esem[slot].wait(v, sp_ck[i - 2])
                ev_ck[i] = p.s_v.inc(v.tensor_copy(te, bank))
                p.s_v.wait(g, ev_ck[i])
                dst = (p.z_hbm if oc < 16 else p.xbc_hbm)
                row = oc if oc < 16 else oc - 16
                sp_ck[i] = esem[slot].dma(
                    g.dma_start(out=dst[row, :, tt * 512 : tt * 512 + 512], in_=te))
            else:
                ev_ck[i] = p.s_v.inc(v.tensor_copy(p.dt_raw[:, tt * 512 : tt * 512 + 512], bank))

    p.s_ea.wait(sy)
    p.s_eb.wait(sy)
    _dbg(p, "dt_raw", p.dt_raw, [128, T])

    scope(p, 'mamba_chunks')
    xbcs = p.big[:, 0 : 18 * LO]
    zc = p.big[:, 4608 : 4608 + 2048]
    xtm = p.big[:, 6656 : 6656 + 2048]
    dtb = p.big[0:NH, 8704 : 8704 + 8 * LO]
    raw = p.big[:, 10752 : 10752 + 2 * 260]
    for o in range(NO):
        to = o * LO
        dts = dtb[:, 0 * LO : 1 * LO]
        lndt = dtb[:, 1 * LO : 2 * LO]
        dA = dtb[:, 2 * LO : 3 * LO]
        cA = dtb[:, 3 * LO : 4 * LO]
        nb = dtb[:, 4 * LO : 5 * LO]
        dsb = dtb[:, 5 * LO : 6 * LO]
        t32 = dtb[:, 6 * LO : 7 * LO]
        p.s_v.wait(s)
        p.s_t.wait(s)
        av = p.s_s.inc(s.activation(t32, p.dt_raw[0:NH, to : to + LO], AF.Exp, bias=p.dtbv))
        p.s_s.wait(s, av)
        av = p.s_s.inc(s.activation(dts, t32, AF.Ln, bias=1.0))
        p.s_s.wait(s, av)
        sv = p.s_s.inc(s.activation(lndt, dts, AF.Ln))
        p.s_s.wait(v, sv)
        vv = p.s_v.inc(v.tensor_scalar(dA, dts, p.negA, None, ALU.mult))
        p.s_v.wait(v, vv)
        for cc in range(2):
            sl = slice(cc * LC, (cc + 1) * LC)
            vv = p.s_v.inc(v.tensor_tensor_scan(
                cA[:, sl], p.ones[0:NH, 0:1].broadcast_to([NH, LC]),
                dA[:, sl], 0.0, ALU.mult, ALU.add))
        p.s_v.wait(v, vv)
        v.tensor_sub(nb, lndt, cA)
        for cc in range(2):
            sl = slice(cc * LC, (cc + 1) * LC)
            v.tensor_scalar(t32[:, sl], cA[:, sl], -1.0,
                            cA[:, cc * LC + LC - 1 : cc * LC + LC], ALU.mult, ALU.add)
            v.tensor_add(t32[:, sl], t32[:, sl], lndt[:, sl])
        dv = p.s_v.v
        p.s_v.wait(s, dv)
        p.s_s.inc(s.activation(dsb, t32, AF.Exp))

        cv_ck = [0] * 18
        rsem = [p.s_ra, p.s_rb]
        for j in range(18):
            slot = j % 2
            rb = raw[:, slot * 260 : slot * 260 + 260]
            if j >= 2:
                p.s_v.wait(sy, cv_ck[j - 2])
            if o == 0:
                lv = rsem[slot].dma(sy.dma_start(out=rb[:, 3:259], in_=p.xbc_hbm[j, :, 0:LO]))
            else:
                lv = rsem[slot].dma(sy.dma_start(out=rb[:, 0:259],
                                                 in_=p.xbc_hbm[j, :, to - 3 : to + LO]))
            rsem[slot].wait(v, lv)
            if o == 0:
                v.memset(rb[:, 0:3], 0.0)
            xj = xbcs[:, j * LO : (j + 1) * LO]
            v.tensor_scalar(xj, rb[:, 0:LO], p.cw[:, 4 * j : 4 * j + 1],
                            p.cw[:, 72 + j : 73 + j], ALU.mult, ALU.add)
            for w in range(1, 4):
                v.scalar_tensor_tensor(xj, rb[:, w : w + LO], p.cw[:, 4 * j + w : 4 * j + w + 1],
                                       xj, ALU.mult, ALU.add)
            cv_ck[j] = p.s_v.v
            p.s_v.wait(s, cv_ck[j])
            p.s_s.inc(s.activation(xj, xj, AF.Silu))

        _dbg(p, f"xbcs_o{o}", xbcs, [128, 18 * LO])

        for cc in range(2):
            _mamba_chunk(p, o, cc, xbcs, zc, xtm, cA, nb, dsb)

        p.s_v.wait(g)
        p.s_yw.dma(g.dma_start(
            out=p.y_hbm[:, :, to : to + LO].transpose([1, 0, 2]),
            in_=p.yoct.rearrange("p (fc l) -> p fc l", fc=16)))

    p.s_yw.wait(sy)
    _dbg(p, "rstd_all", p.rstd_all, [128, T])

    scope(p, 'mamba_outproj')
    _outproj_ln(p)


def _mamba_chunk(p, o, cc, xbcs, zc, xtm, cA, nb, dsb):
    sy, t, v, s, g = p.sy, p.t, p.v, p.s, p.g
    tg = o * LO + cc * LC
    sl = slice(cc * LC, (cc + 1) * LC)

    p.s_v.wait(sy)
    p.s_s.wait(sy)
    zv = p.s_zc.dma(sy.dma_start(
        out=zc.rearrange("p (fc l) -> p fc l", fc=16),
        in_=p.z_hbm[:, :, tg : tg + LC].transpose([1, 0, 2])))

    p.s_v.wait(g)
    p.s_s.wait(g)
    bflat = p.bscr.rearrange("a b -> (a b)")
    wv = p.s_bw.dma(g.dma_start(
        out=bflat[0 : NH * LC].rearrange("(h l) -> h l", l=LC), in_=cA[:, sl]))

    bank_tr = p.pb[2]
    bank_cb = p.pb[3]
    xb16 = xbcs[:, 16 * LO : 16 * LO + LO]
    xb17 = xbcs[:, 17 * LO : 17 * LO + LO]
    p.s_s.wait(t)
    p.s_v.wait(t)
    p.s_t.inc(t.matmul(bank_cb[:, 0:128], xb16[:, sl], xb17[:, sl], start=True, stop=True))
    p.s_t.inc(t.transpose(bank_tr[:, 0:32], nb[:, sl], p.idn[0:32, 0:32]))
    p.s_t.inc(t.transpose(bank_tr[:, 32:64], dsb[:, sl], p.idn[0:32, 0:32]))
    tv = p.s_t.inc(t.transpose(bank_tr[:, 128:256], xb16[:, sl], p.idn))
    p.s_t.wait(v, tv)
    v.tensor_copy(p.nbt, bank_tr[:, 0:32])
    v.tensor_copy(p.dst, bank_tr[:, 32:64])
    v.tensor_copy(p.btm, bank_tr[:, 128:256])
    ev = p.s_v.inc(v.tensor_mul(p.cbm, bank_cb[:, 0:128], p.msk))
    for g4 in range(4):
        p.s_v.wait(t, ev)
        for k in range(4):
            fc = g4 * 4 + k
            tv = p.s_t.inc(t.transpose(
                bank_tr[:, k * 128 : (k + 1) * 128],
                xbcs[:, fc * LO + cc * LC : fc * LO + cc * LC + LC], p.idn))
        p.s_t.wait(v, tv)
        ev = p.s_v.inc(v.tensor_copy(xtm[:, g4 * 512 : (g4 + 1) * 512], bank_tr))

    bufsets = [
        (p.gb0, p.gb1, p.gb2, p.xd[:, 0:512], p.pb[0], p.s_bc),
        (p.big[:, 12288:13312], p.big[:, 13312:14336], p.big[:, 14336:15360],
         p.big[:, 15360:15872], p.pb[1], p.s_be),
    ]
    xb17 = xbcs[:, 17 * LO : 17 * LO + LO]
    bflat = p.bscr.rearrange("a b -> (a b)")
    g_tv = {}
    g_gv = {}
    g_vv = {}

    def build_group(gi):
        h0 = gi * 8
        b0, b1, b2, xd, sbank, bsem = bufsets[gi % 2]
        p.s_bw.wait(g, wv)
        if gi >= 2:
            p.s_t.wait(g, g_tv[gi - 2])
            p.s_v.wait(g, g_gv[gi - 2])
        bv = bsem.dma(g.dma_start(
            out=b0,
            in_=bflat[h0 * LC : h0 * LC + 1024].unsqueeze(0).broadcast_to([128, 1024])))
        bsem.wait(v, bv)
        if gi >= 2:
            p.s_t.wait(v, g_tv[gi - 2])
        b0v = b0.rearrange("p (h l) -> p h l", h=8)
        v.tensor_tensor(b0v, b0v, rep(p.nbt[:, h0 : h0 + 8], LC), ALU.add)
        mv = p.s_v.inc(v.tensor_scalar(b2, b0, 30.0, None, ALU.min))
        p.s_v.wait(s, mv)
        if gi >= 2:
            p.s_t.wait(s, g_tv[gi - 2])
        sv = p.s_s.inc(s.activation(b1, b2, AF.Exp))
        p.s_s.wait(v, sv)
        b1v = b1.rearrange("p (h l) -> p h l", h=8)
        v.tensor_tensor(b1v, b1v, rep(p.cbm, 8).transpose([0, 2, 1]), ALU.mult)
        v.tensor_tensor(b0v, b0v, rep(p.nbt[:, h0 : h0 + 8], LC), ALU.subtract)
        mv = p.s_v.v
        p.s_v.wait(s, mv)
        sv = p.s_s.inc(s.activation(b0, b0, AF.Exp))
        p.s_s.wait(v, sv)
        for hh in range(8):
            h = h0 + hh
            v.tensor_scalar(xd[:, hh * 64 : hh * 64 + 64],
                            xtm[:, h * 64 : h * 64 + 64],
                            p.dst[:, h : h + 1], None, ALU.mult)
        b2v = b2.rearrange("p (h l) -> p h l", h=8)
        v.tensor_tensor(b2v, rep(xb17[:, sl], 8).transpose([0, 2, 1]),
                        b0.rearrange("p (h l) -> p h l", h=8), ALU.mult)
        g_vv[gi] = p.s_v.v

    def mm_group(gi):
        h0 = gi * 8
        b0, b1, b2, xd, sbank, bsem = bufsets[gi % 2]
        ybank = p.pb[4 + gi]
        p.s_v.wait(t, g_vv[gi])
        for hh in range(8):
            h = h0 + hh
            ypart = (hh % 2) * 64
            yfree = (hh // 2) * 128
            yout = ybank[ypart : ypart + 64, yfree : yfree + 128]
            t.matmul(yout, xtm[:, h * 64 : h * 64 + 64],
                     b1[:, hh * 128 : hh * 128 + 128],
                     start=(hh < 2), stop=False, skip_group_check=True)
            t.matmul(yout, p.state[:, h * 64 : h * 64 + 64],
                     b2[:, hh * 128 : hh * 128 + 128],
                     start=False, stop=(hh >= 6), skip_group_check=True)
            mm = t.matmul(sbank[:, hh * 64 : hh * 64 + 64], p.btm,
                          xd[:, hh * 64 : hh * 64 + 64],
                          start=(hh == 0), stop=(hh == 7))
        g_tv[gi] = tv = p.s_t.inc(mm)
        p.s_t.wait(v, tv)
        for hh in range(8):
            h = h0 + hh
            stsl = p.state[:, h * 64 : h * 64 + 64]
            v.tensor_scalar(stsl, stsl, b0[:, hh * 128 + 127 : hh * 128 + 128],
                            None, ALU.mult)
        g_gv[gi] = p.s_v.inc(v.tensor_add(p.state[:, h0 * 64 : h0 * 64 + 512],
                                          p.state[:, h0 * 64 : h0 * 64 + 512], sbank))

    for gi in range(4):
        build_group(gi)
        if gi >= 1:
            mm_group(gi - 1)
    mm_group(3)

    p.s_zc.wait(s, zv)
    p.s_zc.wait(v, zv)
    p.s_yw.wait(v)
    sv = p.s_s.inc(s.activation(zc, zc, AF.Silu))
    p.s_s.wait(v, sv)
    for fc in range(16):
        ybank = p.pb[4 + fc // 4]
        yfree = (fc % 4) * 128
        dst_y = p.yoct[:, fc * LO + cc * LC : fc * LO + cc * LC + LC]
        v.scalar_tensor_tensor(p.tmpv[:, 0:128],
                               xbcs[:, fc * LO + cc * LC : fc * LO + cc * LC + LC],
                               p.dr[:, fc : fc + 1],
                               ybank[:, yfree : yfree + 128], ALU.mult, ALU.add)
        v.tensor_mul(dst_y, p.tmpv[:, 0:128], zc[:, fc * 128 : fc * 128 + 128])
    yv = p.s_v.v
    p.s_v.wait(s, yv)
    sv = p.s_s.inc(s.activation(p.ysq.rearrange("p (fc l) -> p fc l", fc=16),
                                p.yoct.rearrange("p (fc l) -> p fc l", fc=16)[:, :, cc * LC : cc * LC + LC],
                                AF.Square))
    p.s_s.wait(t, sv)
    p.s_v.wait(t)
    stats = p.pb[1]
    for fc in range(16):
        mm = t.matmul(stats[0:1, 0:128], p.ones[:, 0:1],
                      p.ysq[:, fc * 128 : fc * 128 + 128],
                      start=(fc == 0), stop=(fc == 15))
    tv = p.s_t.inc(mm)
    p.s_t.wait(v, tv)
    v.tensor_scalar(p.rows[0:1, 8:136], stats[0:1, 0:128], 1.0 / DIN, EPS,
                    ALU.mult, ALU.add)
    rv = p.s_v.inc(v.reciprocal(p.rows[0:1, 8:136], p.rows[0:1, 8:136]))
    p.s_v.wait(s, rv)
    sv = p.s_s.inc(s.activation(p.rows[0:1, 8:136], p.rows[0:1, 8:136], AF.Sqrt))
    p.s_s.wait(v, sv)
    p.s_v.inc(v.tensor_copy(p.rstd_all[0:1, tg : tg + LC], p.rows[0:1, 8:136]))


def _outproj_ln(p):
    sy, t, v, s, g = p.sy, p.t, p.v, p.s, p.g
    _fence_compute(p)
    wo4 = p.big[:, 0:8192]
    ybuf = p.big[:, 8192:12288]
    prev_tv = 0
    for ocg in range(2):
        wl = 0
        if ocg == 1:
            sy.wait_ge(p.s_t.h, prev_tv)
        for j in range(4):
            oc = ocg * 4 + j
            wj = wo4[:, j * 2048 : j * 2048 + 2048]
            src = p.w_out_t[:, oc * 128 : (oc + 1) * 128].rearrange(
                "(kc kp) o -> kp kc o", kp=128)
            wl = p.s_wa.dma(sy.dma_start(
                out=wj.rearrange("kp (kc o) -> kp kc o", kc=16), in_=src))
        p.s_wa.wait(v, wl)
        for j in range(4):
            wj = wo4[:, j * 2048 : j * 2048 + 2048]
            v.tensor_tensor(wj.rearrange("p (kc o) -> p kc o", kc=16),
                            wj.rearrange("p (kc o) -> p kc o", kc=16),
                            rep(p.nw, 128), ALU.mult)
        wsv = p.s_v.v

        for sub in range(8):
            ts0 = sub * 256
            if not (ocg == 0 and sub == 0):
                sy.wait_ge(p.s_t.h, prev_tv)
            yv = p.s_yb.dma(sy.dma_start(
                out=ybuf.rearrange("p (fc l) -> p fc l", fc=16),
                in_=p.y_hbm[:, :, ts0 : ts0 + 256].transpose([1, 0, 2])))
            p.s_v.wait(g)
            wv = p.s_bw.dma(g.dma_start(out=p.bscr[17:18, 0:256],
                                        in_=p.rstd_all[0:1, ts0 : ts0 + 256]))
            p.s_bw.wait(g, wv)
            bv = p.s_bc.dma(g.dma_start(
                out=p.bc1[:, 0:256], in_=p.bscr[17:18, 0:256].broadcast_to([128, 256])))
            for j in range(4):
                oc = ocg * 4 + j
                bank = p.pb[2 + (j % 2)]
                p.s_v.wait(t, wsv)
                p.s_yb.wait(t, yv)
                if not (ocg == 0 and sub == 0 and j < 2):
                    p.s_v.wait(t)
                for kc in range(16):
                    mm = t.matmul(bank[:, 0:256],
                                  wo4[:, j * 2048 + kc * 128 : j * 2048 + kc * 128 + 128],
                                  ybuf[:, kc * 256 : (kc + 1) * 256],
                                  start=(kc == 0), stop=(kc == 15))
                tv = p.s_t.inc(mm)
                prev_tv = tv
                p.s_t.wait(v, tv)
                p.s_bc.wait(v, bv)
                rsl = p.R[:, oc * T + ts0 : oc * T + ts0 + 256]
                v.tensor_mul(p.tmpv[:, 0:256], bank[:, 0:256], p.bc1[:, 0:256])
                p.s_v.inc(v.tensor_add(rsl, rsl, p.tmpv[:, 0:256]))

    _dbg(p, "x1pre", p.R, [128, 8 * T])
    _post_ln(p, ln_idx=0, absmax_acc=(p.upto >= 2))


def _post_ln(p, ln_idx, absmax_acc=False, gamma=None, beta=None):
    sy, t, v, s, g = p.sy, p.t, p.v, p.s, p.g
    scope(p, f"post_ln{ln_idx}")
    _fence_compute(p)
    if absmax_acc:
        v.memset(p.sml[:, 3:4], 0.0)
    sbanks = [p.pb[1], p.pb[4]]
    mrow = p.dt_raw[0:1, 0:2048]
    rrow = p.rstd_all[0:1, 0:2048]
    st_ck = {}
    ev_ck = {}
    for sub in range(8):
        ts0 = sub * 256
        stats = sbanks[sub % 2]
        ssl = sub % 2
        sqb = p.ysq[:, ssl * 1024 : ssl * 1024 + 1024]
        if sub >= 2:
            p.s_t.wait(s, st_ck[sub - 2])
        for half in range(2):
            if half == 1:
                p.s_t.wait(s, tv)
            for k in range(4):
                oc = half * 4 + k
                rsl = p.R[:, oc * T + ts0 : oc * T + ts0 + 256]
                s.activation(sqb[:, k * 256 : k * 256 + 256], rsl, AF.Square)
            sv = p.s_s.v
            p.s_s.wait(t, sv)
            if sub >= 2 and half == 0:
                p.s_v.wait(t, ev_ck[sub - 2])
            for k in range(4):
                oc = half * 4 + k
                rsl2 = p.R[:, oc * T + ts0 : oc * T + ts0 + 256]
                p.s_t.inc(t.matmul(stats[0:1, 0:256], p.ones[:, 0:1], rsl2,
                                   start=(oc == 0), stop=False,
                                   skip_group_check=True))
                tv = p.s_t.inc(t.matmul(
                    stats[0:1, 256:512], p.ones[:, 0:1],
                    sqb[:, k * 256 : k * 256 + 256],
                    start=False, stop=(oc == 7), skip_group_check=True))
        st_ck[sub] = tv
        p.s_t.wait(v, tv)
        v.tensor_scalar(mrow[:, ts0 : ts0 + 256], stats[0:1, 0:256], 1.0 / D,
                        None, ALU.mult)
        v.tensor_scalar(rrow[:, ts0 : ts0 + 256], stats[0:1, 256:512], 1.0 / D,
                        None, ALU.mult)
        ev_ck[sub] = p.s_v.v
    v.tensor_mul(p.tmpv[0:1, 0:1024], mrow[:, 0:1024], mrow[:, 0:1024])
    v.tensor_sub(rrow[:, 0:1024], rrow[:, 0:1024], p.tmpv[0:1, 0:1024])
    v.tensor_mul(p.tmpv[0:1, 0:1024], mrow[:, 1024:2048], mrow[:, 1024:2048])
    v.tensor_sub(rrow[:, 1024:2048], rrow[:, 1024:2048], p.tmpv[0:1, 0:1024])
    v.tensor_scalar(rrow, rrow, 1.0, EPS, ALU.mult, ALU.add)
    rv = p.s_v.inc(v.reciprocal(rrow, rrow))
    p.s_v.wait(s, rv)
    sv = p.s_s.inc(s.activation(rrow, rrow, AF.Sqrt))
    p.s_s.wait(g, sv)
    p.s_v.wait(g)
    bflat = p.bscr.rearrange("a b -> (a b)")
    wv = p.s_bw.dma(g.dma_start(out=bflat[16 * 512 : 16 * 512 + 2048], in_=mrow))
    wv = p.s_bw.dma(g.dma_start(out=bflat[20 * 512 : 20 * 512 + 2048], in_=rrow))
    p.s_bw.wait(g, wv)
    p.s_bc.dma(g.dma_start(
        out=p.bc1, in_=bflat[16 * 512 : 16 * 512 + 2048].unsqueeze(0).broadcast_to([128, 2048])))
    bv = p.s_bc.dma(g.dma_start(
        out=p.bc2, in_=bflat[20 * 512 : 20 * 512 + 2048].unsqueeze(0).broadcast_to([128, 2048])))
    p.s_bc.wait(v, bv)
    for oc in range(8):
        rsl = p.R[:, oc * T : (oc + 1) * T]
        v.tensor_sub(rsl, rsl, p.bc1)
        v.tensor_mul(rsl, rsl, p.bc2)
        if gamma is not None:
            v.tensor_scalar(rsl, rsl, gamma[:, oc : oc + 1], beta[:, oc : oc + 1],
                            ALU.mult, ALU.add)
        if absmax_acc:
            v.tensor_reduce(p.tmpv[:, 0:1], rsl, mybir.AxisListType.X, ALU.max,
                            apply_absolute_value=True)
            v.tensor_max(p.sml[:, 3:4], p.sml[:, 3:4], p.tmpv[:, 0:1])



def _allreduce_max(p, cols, out_cols):
    sy, t, v, s, g = p.sy, p.t, p.v, p.s, p.g
    k = len(cols)
    stats = p.pb[1]
    p.s_v.wait(t)
    p.s_gc.wait(t)
    for i, c in enumerate(cols):
        tv = p.s_t.inc(t.transpose(stats[0:1, i * 128 : i * 128 + 128],
                                   p.sml[:, c : c + 1], p.idn))
    p.s_t.wait(v, tv)
    v.memset(p.rows[0:1, 640:644], 0.0)
    for i in range(k):
        v.tensor_reduce(p.rows[0:1, 640 + i : 641 + i], stats[0:1, i * 128 : i * 128 + 128],
                        mybir.AxisListType.X, ALU.max)
    vv = p.s_v.v
    p.s_v.wait(g, vv)
    p.s_cc.wait(g)
    p.s_ci.dma(g.dma_start(out=p.cc_in.ap()[0:1, 0:4], in_=p.rows[0:1, 640:644]))
    p.s_ci.wait(g)
    p.s_co.wait(g)
    cc = g.collective_compute(
        "AllReduce", ALU.max, replica_groups=[list(range(NCORES))],
        ins=[p.cc_in.ap().opt()], outs=[p.cc_out.ap().opt()])
    p.s_cc.inc(cc)
    p.s_cc.wait(g)
    bv = p.s_co.dma(g.dma_start(
        out=p.tmpv[:, 0:4], in_=p.cc_out.ap()[0:1, 0:4].broadcast_to([128, 4])))
    p.s_co.wait(v, bv)
    for i, oc in enumerate(out_cols):
        v.tensor_copy(p.sml[:, oc : oc + 1], p.tmpv[:, i : i + 1])


def _scale_recip(p, mx_col, s_col, r_col):
    v = p.v
    v.tensor_scalar(p.sml[:, s_col : s_col + 1], p.sml[:, mx_col : mx_col + 1],
                    float(np.float32(1.0 / QMAX)), 1e-12, ALU.mult, ALU.max)
    v.reciprocal(p.sml[:, r_col : r_col + 1], p.sml[:, s_col : s_col + 1])


def _lpbq_dequant(p, wraw, ncols, sbuf, rbuf):
    v = p.v
    nb = ncols // 32
    v3 = wraw.rearrange("p (b e) -> p b e", b=nb)
    v.tensor_reduce(sbuf, v3, mybir.AxisListType.X, ALU.max, apply_absolute_value=True)
    v.tensor_scalar(sbuf, sbuf, float(np.float32(1.0 / 7.0)), 1e-12, ALU.mult, ALU.max)
    v.reciprocal(rbuf, sbuf)
    v.tensor_tensor(v3.bitcast(I32), v3, rep(rbuf, 32), ALU.mult)
    v.tensor_tensor(v3, v3.bitcast(I32), rep(sbuf, 32), ALU.mult)


def _ffn0(p):
    sy, t, v, s, g = p.sy, p.t, p.v, p.s, p.g
    _fence_compute(p)
    scope(p, 'ffn0_arA')
    _allreduce_max(p, [3], [3])
    _scale_recip(p, 3, 4, 5)

    xq = p.big
    vv0 = p.s_v.v
    for kc in range(8):
        xr = p.R[:, kc * T : (kc + 1) * T]
        dst = xq[:, kc * T : (kc + 1) * T]
        v.tensor_scalar(dst.bitcast(I32), xr, p.sml[:, 5:6], None, ALU.mult)
        v.tensor_scalar(dst, dst.bitcast(I32), p.sml[:, 4:5], None, ALU.mult)
    xqv = p.s_v.v

    scope(p, 'ffn0_p1')
    v.memset(p.sml[:, 6:7], 0.0)
    v.memset(p.sml[:, 7:8], 0.0)
    wraw2 = p.yoct[:, 0:2048]
    wbuf = p.dt_raw[:, 0:2048]
    sc2 = p.state[:, 0:128]
    stg = p.ysq[:, 0:2048]
    wsem = [p.s_wa, p.s_wb]
    esem = [p.s_ea, p.s_eb]
    tm_ck = {}
    sc_tr = {}
    sc_ev = {}
    sp_ck = {}
    dq_ck = {}
    tr_ck = {}
    units = [(m, oc) for oc in range(24) for m in range(2)]
    bank_wt = p.pb[4]

    def p1_stage_a(ui):
        m, oc = units[ui]
        wsl = ui % 2
        wraw = wraw2[:, wsl * 1024 : wsl * 1024 + 1024]
        wb = wbuf[:, wsl * 1024 : wsl * 1024 + 1024]
        wsrc = (p.w_up if m == 0 else p.w_gate)
        if ui >= 2:
            p.s_v.wait(sy, dq_ck[ui - 2])
            sy.wait_ge(p.s_t.h, tr_ck[ui - 2])
        wl = wsem[wsl].dma(sy.dma_start(
            out=wraw, in_=wsrc[oc * 128 : (oc + 1) * 128, :]))
        wsem[wsl].wait(v, wl)
        _lpbq_dequant(p, wraw, 1024, sc2[:, wsl * 64 : wsl * 64 + 32],
                      sc2[:, wsl * 64 + 32 : wsl * 64 + 64])
        dq_ck[ui] = p.s_v.v
        p.s_v.wait(t, dq_ck[ui])
        if ui >= 2:
            tmp = tm_ck.get((ui - 2, 3))
            p.s_s.wait(t, sc_tr.get(ui - 1, 0))
        for kk in range(2):
            if ui >= 1 or kk == 1:
                p.s_s.wait(t, sc_tr.get(ui if kk == 1 else ui - 1, 0))
            for k4 in range(4):
                kc = kk * 4 + k4
                tv = p.s_t.inc(t.transpose(bank_wt[:, k4 * 128 : k4 * 128 + 128],
                                           wraw[:, kc * 128 : kc * 128 + 128], p.idn))
            tr_ck[ui] = tv
            p.s_t.wait(s, tv)
            if ui >= 2 and kk == 0:
                sy2 = tm_ck.get((ui - 2, 3), 0)
                t.nop() if False else None
            if ui >= 2 and kk == 0:
                p.s_t.wait(s, tm_ck[(ui - 2, 3)])
            s.copy(wb[:, kk * 512 : kk * 512 + 512], bank_wt[:, 0:512])
            sc_tr[ui] = p.s_s.v

    def p1_stage_b(ui):
        m, oc = units[ui]
        wsl = ui % 2
        wb = wbuf[:, wsl * 1024 : wsl * 1024 + 1024]
        for tt in range(4):
            i = ui * 4 + tt
            bank = p.pb[i % 2]
            p.s_s.wait(t, sc_tr[ui])
            if ui == 0 and tt == 0:
                p.s_v.wait(t, xqv)
            if i >= 2:
                p.s_v.wait(t, vr_ck[i - 2])
            for kc in range(8):
                mm = t.matmul(bank, wb[:, kc * 128 : (kc + 1) * 128],
                              xq[:, kc * T + tt * 512 : kc * T + tt * 512 + 512],
                              start=(kc == 0), stop=(kc == 7))
            tm_ck[(ui, tt)] = p.s_t.inc(mm)
            ssl = i % 2
            te = stg[:, ssl * 512 : ssl * 512 + 512]
            p.s_t.wait(v, tm_ck[(ui, tt)])
            if i >= 2:
                esem[ssl].wait(v, sp_ck[i - 2])
            v.tensor_copy(te, bank)
            v.tensor_reduce(p.tmpv[:, 4 + ssl : 5 + ssl], bank, mybir.AxisListType.X,
                            ALU.max, apply_absolute_value=True)
            acc = p.sml[:, 6 + m : 7 + m]
            v.tensor_max(acc, acc, p.tmpv[:, 4 + ssl : 5 + ssl])
            vr_ck[i] = sc_ev[i] = p.s_v.v
            p.s_v.wait(g, sc_ev[i])
            dst = (p.up_hbm if m == 0 else p.gate_hbm)
            sp_ck[i] = esem[ssl].dma(
                g.dma_start(out=dst[oc, :, tt * 512 : tt * 512 + 512], in_=te))

    gr_ck = {}
    vr_ck = {}
    for ui in range(48):
        p1_stage_a(ui)
        if ui >= 1:
            p1_stage_b(ui - 1)
    p1_stage_b(47)

    scope(p, 'ffn0_wdq')
    p.s_ea.wait(v)
    p.s_eb.wait(v)
    for oc in range(8):
        wsl = oc % 2
        wraw = wraw2[:, wsl * 1024 : wsl * 1024 + 1024]
        for cp in range(3):
            p.s_v.wait(sy)
            wl = wsem[wsl].dma(sy.dma_start(
                out=wraw, in_=p.w_down[oc * 128 : (oc + 1) * 128,
                                       cp * 1024 : cp * 1024 + 1024]))
            wsem[wsl].wait(v, wl)
            _lpbq_dequant(p, wraw, 1024, sc2[:, wsl * 64 : wsl * 64 + 32],
                          sc2[:, wsl * 64 + 32 : wsl * 64 + 64])
            dqv = p.s_v.v
            p.s_v.wait(t, dqv)
            p.s_ea.wait(v)
            p.s_eb.wait(v)
            for kk in range(2):
                p.s_v.wait(t)
                for k4 in range(4):
                    kc = kk * 4 + k4
                    tv = p.s_t.inc(t.transpose(bank_wt[:, k4 * 128 : k4 * 128 + 128],
                                               wraw[:, kc * 128 : kc * 128 + 128], p.idn))
                p.s_t.wait(v, tv)
                v.tensor_copy(stg[:, kk * 512 : kk * 512 + 512], bank_wt)
            vv = p.s_v.v
            p.s_v.wait(g, vv)
            esem[wsl].dma(g.dma_start(
                out=p.wdq_hbm[cp * 1024 : (cp + 1) * 1024, oc * 128 : (oc + 1) * 128]
                .rearrange("(kc kp) o -> kp kc o", kp=128),
                in_=stg[:, 0:1024].rearrange("kp (kc o) -> kp kc o", kc=8)))

    scope(p, 'ffn0_arB')
    _allreduce_max(p, [6, 7], [6, 7])
    _scale_recip(p, 6, 10, 11)
    _scale_recip(p, 7, 12, 13)

    scope(p, 'ffn0_p2')
    p.s_ea.wait(sy)
    p.s_eb.wait(sy)
    v.memset(p.sml[:, 8:9], 0.0)
    rsem = [p.s_ra, p.s_rb]
    ld_ck = {}
    cv_ck = {}
    for ti in range(96):
        oc, tt = divmod(ti, 4)
        ssl = ti % 2
        base = ssl * 2048
        graw = p.big[:, base : base + 512]
        gq = p.big[:, base + 512 : base + 1024]
        sq = p.big[:, base + 1024 : base + 1536]
        g2 = p.big[:, base + 1536 : base + 2048]
        if ti >= 2:
            p.s_v.wait(sy, cv_ck[ti - 2])
        ld_ck[ti] = rsem[ssl].dma(sy.dma_start(
            out=graw, in_=p.gate_hbm[oc, :, tt * 512 : tt * 512 + 512]))
        rsem[ssl].wait(v, ld_ck[ti])
        if ti >= 2:
            esem[ssl].wait(v, sp_ck[ti - 2])
        v.tensor_scalar(gq.bitcast(I32), graw, p.sml[:, 13:14], None, ALU.mult)
        v.tensor_scalar(gq, gq.bitcast(I32), p.sml[:, 12:13], None, ALU.mult)
        sv = p.s_v.v
        p.s_v.wait(s, sv)
        av = p.s_s.inc(s.activation(sq, gq, AF.Sigmoid))
        p.s_s.wait(v, av)
        v.tensor_scalar(sq.bitcast(I32), sq, 65536.0, None, ALU.mult)
        v.tensor_scalar(sq, sq.bitcast(I32), 65535.0, float(2.0 ** -16),
                        ALU.min, ALU.mult)
        v.tensor_mul(g2, gq, sq)
        v.tensor_reduce(p.tmpv[:, 0:1], g2, mybir.AxisListType.X, ALU.max,
                        apply_absolute_value=True)
        v.tensor_max(p.sml[:, 8:9], p.sml[:, 8:9], p.tmpv[:, 0:1])
        cv_ck[ti] = p.s_v.v
        p.s_v.wait(g, cv_ck[ti])
        sp_ck[ti] = esem[ssl].dma(
            g.dma_start(out=p.g2_hbm[oc, :, tt * 512 : tt * 512 + 512], in_=g2))

    scope(p, 'ffn0_arC')
    _allreduce_max(p, [8], [8])
    _scale_recip(p, 8, 14, 15)

    scope(p, 'ffn0_p3')
    p.s_ea.wait(sy)
    p.s_eb.wait(sy)
    v.memset(p.sml[:, 9:10], 0.0)
    for ti in range(96):
        oc, tt = divmod(ti, 4)
        ssl = ti % 2
        base = ssl * 2048
        g2r = p.big[:, base : base + 512]
        upr = p.big[:, base + 512 : base + 1024]
        orw = p.big[:, base + 1024 : base + 1536]
        if ti >= 2:
            p.s_v.wait(sy, cv_ck[ti - 2])
        rsem[ssl].dma(sy.dma_start(
            out=g2r, in_=p.g2_hbm[oc, :, tt * 512 : tt * 512 + 512]))
        ld_ck[ti] = rsem[ssl].dma(sy.dma_start(
            out=upr, in_=p.up_hbm[oc, :, tt * 512 : tt * 512 + 512]))
        rsem[ssl].wait(v, ld_ck[ti])
        if ti >= 2:
            esem[ssl].wait(v, sp_ck[ti - 2])
        v.tensor_scalar(g2r.bitcast(I32), g2r, p.sml[:, 15:16], None, ALU.mult)
        v.tensor_scalar(g2r, g2r.bitcast(I32), p.sml[:, 14:15], None, ALU.mult)
        v.tensor_scalar(upr.bitcast(I32), upr, p.sml[:, 11:12], None, ALU.mult)
        v.tensor_scalar(upr, upr.bitcast(I32), p.sml[:, 10:11], None, ALU.mult)
        v.tensor_mul(orw, g2r, upr)
        v.tensor_reduce(p.tmpv[:, 0:1], orw, mybir.AxisListType.X, ALU.max,
                        apply_absolute_value=True)
        v.tensor_max(p.sml[:, 9:10], p.sml[:, 9:10], p.tmpv[:, 0:1])
        cv_ck[ti] = p.s_v.v
        p.s_v.wait(g, cv_ck[ti])
        sp_ck[ti] = esem[ssl].dma(
            g.dma_start(out=p.o_hbm[oc, :, tt * 512 : tt * 512 + 512], in_=orw))

    scope(p, 'ffn0_arD')
    _allreduce_max(p, [9], [9])
    _scale_recip(p, 9, 16, 17)

    scope(p, 'ffn0_p4')
    p.s_ea.wait(sy)
    p.s_eb.wait(sy)
    oq = p.big[:, 0:12288]
    wdbs = [p.big[:, 12288 : 12288 + 3072], p.yoct[:, 0:3072]]
    prev_tv = p.s_t.v
    dtm_ck = {}
    for tt in range(4):
        if tt > 0:
            p.s_v.wait(sy, cv_ck[(tt - 1, 23)])
            sy.wait_ge(p.s_t.h, prev_tv)
        for kc in range(24):
            ssl = kc % 2
            rsem[ssl].dma(sy.dma_start(
                out=oq[:, kc * 512 : kc * 512 + 512],
                in_=p.o_hbm[kc, :, tt * 512 : tt * 512 + 512]))
        rsem[0].wait(v)
        rsem[1].wait(v)
        for kc in range(24):
            dst = oq[:, kc * 512 : kc * 512 + 512]
            v.tensor_scalar(dst.bitcast(I32), dst, p.sml[:, 17:18], None, ALU.mult)
            v.tensor_scalar(dst, dst.bitcast(I32), p.sml[:, 16:17], None, ALU.mult)
        cv_ck[(tt, 23)] = p.s_v.v
        qv = p.s_v.v
        for oc in range(8):
            u4 = tt * 8 + oc
            wdb = wdbs[u4 % 2]
            wsm = [p.s_wa, p.s_wb][u4 % 2]
            if u4 >= 2:
                sy.wait_ge(p.s_t.h, dtm_ck[u4 - 2])
            src = p.wdq_hbm[:, oc * 128 : (oc + 1) * 128].rearrange(
                "(kc kp) o -> kp kc o", kp=128)
            wl = wsm.dma(sy.dma_start(
                out=wdb.rearrange("kp (kc o) -> kp kc o", kc=24), in_=src))
            bank = p.pb[2 + oc % 2]
            wsm.wait(t, wl)
            p.s_v.wait(t, qv)
            if not (tt == 0 and oc < 2):
                p.s_v.wait(t)
            for kc in range(24):
                mm = t.matmul(bank[:, 0:512],
                              wdb[:, kc * 128 : kc * 128 + 128],
                              oq[:, kc * 512 : kc * 512 + 512],
                              start=(kc == 0), stop=(kc == 23))
            tv = p.s_t.inc(mm)
            dtm_ck[u4] = tv
            prev_tv = tv
            p.s_t.wait(v, tv)
            rsl = p.R[:, oc * T + tt * 512 : oc * T + tt * 512 + 512]
            v.tensor_add(rsl, rsl, bank[:, 0:512])

    _dbg(p, "x2pre", p.R, [128, 8 * T])
    _post_ln(p, ln_idx=1)


def _l1_attn(p):
    sy, t, v, s, g = p.sy, p.t, p.v, p.s, p.g
    scope(p, "l1_attn")
    _fence_compute(p)
    LQ4 = 512
    abuf = p.big[:, 0:4096]
    bbuf = p.big[:, 4096:8192]
    hbuf = p.big[:, 8192:12288]
    wzb = p.big[:, 12288:13312]
    whb = p.big[:, 13312:14336]

    def shslot(fc, d):
        if d == 0:
            if fc < 8:
                return (p.yoct[:, fc * 512 : fc * 512 + 512] if fc < 8 else None)
        return None

    bzv = p.cw[:, 0:8]
    bhv = p.cw[:, 8:16]
    dcv = p.cw[:, 16:24]

    lv = p.s_ld.dma(sy.dma_start(
        out=bzv.rearrange("p (j o) -> p j o", o=1),
        in_=p.bz.rearrange("(j p) o -> p j o", p=128),
        allow_slow_non_contiguous=True))
    lv = p.s_ld.dma(sy.dma_start(
        out=bhv.rearrange("p (j o) -> p j o", o=1),
        in_=p.bh.rearrange("(j p) o -> p j o", p=128),
        allow_slow_non_contiguous=True))
    lv = p.s_ld.dma(sy.dma_start(
        out=dcv.rearrange("p (j o) -> p j o", o=1),
        in_=p.log_d.rearrange("(j p) o -> p j o", p=128),
        allow_slow_non_contiguous=True))
    p.mdf = p.gb0[:, 0:128]
    p.mwpb = p.gb0[:, 128:256]
    p.mwnb = p.gb0[:, 256:384]
    p.s_ld.dma(sy.dma_start(out=p.mdf, in_=p.mdiff))
    p.s_ld.dma(sy.dma_start(out=p.mwpb, in_=p.mwp))
    p.s_ld.dma(sy.dma_start(out=p.mwnb, in_=p.mwn))
    p.s_ld.wait(t)
    p.s_ld.wait(s)
    av = p.s_s.inc(s.activation(dcv, dcv, AF.Exp))
    v.memset(p.sml[:, 20:28], 0.0)
    wsem = [p.s_wa, p.s_wb]
    rsem = [p.s_ra, p.s_rb]
    wz2 = p.big[:, 14336:15360]
    wh2 = p.big[:, 15360:16384]
    tm_ck = {}
    sc_ck = {}
    scan_ck = 0
    diff_ck = 0
    for q in range(4):
        tq = q * LQ4
        for oc in range(8):
            u = q * 8 + oc
            wzsl = wzb if u % 2 == 0 else wz2
            whsl = whb if u % 2 == 0 else wh2
            if u >= 2:
                sy.wait_ge(p.s_t.h, tm_ck[u - 2])
            zsem = [p.s_wa, p.s_wc][u % 2]
            hsem = [p.s_wb, p.s_wd][u % 2]
            wl = zsem.dma(sy.dma_start(
                out=wzsl.rearrange("kp (kc o) -> kp kc o", kc=8),
                in_=p.wz[:, oc * 128 : (oc + 1) * 128].rearrange("(kc kp) o -> kp kc o", kp=128)))
            wl2 = hsem.dma(sy.dma_start(
                out=whsl.rearrange("kp (kc o) -> kp kc o", kc=8),
                in_=p.wh[:, oc * 128 : (oc + 1) * 128].rearrange("(kc kp) o -> kp kc o", kp=128)))
            zsem.wait(t, wl)
            hsem.wait(t, wl2)
            bz_bank = p.pb[2 + (u % 2)]
            bh_bank = p.pb[6 + (u % 2)]
            if u >= 2:
                p.s_s.wait(t, sc_ck[u - 2])
            if oc == 0:
                p.s_v.wait(t, diff_ck)
                p.s_v.wait(s, scan_ck)
            for kc in range(8):
                mm = t.matmul(bz_bank, wzsl[:, kc * 128 : kc * 128 + 128],
                              p.R[:, kc * T + tq : kc * T + tq + LQ4],
                              start=(kc == 0), stop=(kc == 7))
            tvz = p.s_t.inc(mm)
            for kc in range(8):
                mm = t.matmul(bh_bank, whsl[:, kc * 128 : kc * 128 + 128],
                              p.R[:, kc * T + tq : kc * T + tq + LQ4],
                              start=(kc == 0), stop=(kc == 7))
            tm_ck[u] = tvh = p.s_t.inc(mm)
            zsl = abuf[:, oc * 512 : oc * 512 + 512]
            hsl = bbuf[:, oc * 512 : oc * 512 + 512]
            p.s_t.wait(s, tvz)
            s.activation(zsl, bz_bank, AF.Sigmoid, bias=bzv[:, oc : oc + 1])
            p.s_t.wait(s, tvh)
            sv = p.s_s.inc(s.activation(hsl, bh_bank, AF.Identity, bias=bhv[:, oc : oc + 1]))
            sc_ck[u] = sv
            p.s_s.wait(v, sv)
            v.tensor_mul(hsl, zsl, hsl)
            v.tensor_scalar(zsl, zsl, -1.0, 1.0, ALU.mult, ALU.add)
        if "no_scan" in p.dbg:
            for fc in range(8):
                v.tensor_copy(hbuf[:, fc * 512 : fc * 512 + 512],
                              bbuf[:, fc * 512 : fc * 512 + 512])
        rsem[0].wait(v)
        rsem[1].wait(v)
        for fc in range(8 if "no_scan" not in p.dbg else 0):
            v.tensor_tensor_scan(hbuf[:, fc * 512 : fc * 512 + 512],
                                 abuf[:, fc * 512 : fc * 512 + 512],
                                 bbuf[:, fc * 512 : fc * 512 + 512],
                                 p.sml[:, 20 + fc : 21 + fc], ALU.mult, ALU.add)
            v.tensor_copy(p.sml[:, 20 + fc : 21 + fc],
                          hbuf[:, fc * 512 + 511 : fc * 512 + 512])
        hv = p.s_v.v
        scan_ck = hv
        p.s_v.wait(t, hv)
        p.s_s.wait(t)
        for fc in range(8):
            hsl = hbuf[:, fc * 512 : fc * 512 + 512]
            prv = hbuf[:, ((fc - 1) % 8) * 512 : ((fc - 1) % 8) * 512 + 512]
            nxt = hbuf[:, ((fc + 1) % 8) * 512 : ((fc + 1) % 8) * 512 + 512]
            bank = p.pb[fc]
            t.matmul(bank, p.mdf, hsl, start=True, stop=False)
            t.matmul(bank, p.mwpb, prv, start=False, stop=False)
            dtv = p.s_t.inc(t.matmul(bank, p.mwnb, nxt, start=False, stop=True))
        p.s_t.wait(v, dtv)
        for fc in range(8):
            hsl = hbuf[:, fc * 512 : fc * 512 + 512]
            v.scalar_tensor_tensor(hsl, p.pb[fc], dcv[:, fc : fc + 1], hsl,
                                   ALU.mult, ALU.add)
            rsl = p.R[:, fc * T + tq : fc * T + tq + LQ4]
            v.tensor_add(rsl, rsl, hsl)
        diff_ck = p.s_v.v

    _dbg(p, "x3pre", p.R, [128, 8 * T])
    _post_ln(p, ln_idx=2)


def _l1_ffn(p):
    sy, t, v, s, g = p.sy, p.t, p.v, p.s, p.g
    scope(p, "l1_ffn")
    _fence_compute(p)
    wd1 = p.big[:, 0:8192]
    y1 = p.big[:, 8192:12288]
    srst = p.big[:, 12288:14336]
    lra = p.big[:, 14336:14848]
    lab = p.big[:, 14848:15104]
    lbb = p.big[:, 15104:16128]

    def wd2(kc):
        if kc < 4:
            return p.yoct[:, kc * 1024 : kc * 1024 + 1024]
        if kc < 6:
            return p.ysq[:, (kc - 4) * 1024 : (kc - 4) * 1024 + 1024]
        return p.state[:, (kc - 6) * 1024 : (kc - 6) * 1024 + 1024]

    qb1v = p.cw[:, 24:32]
    qb2v = p.cw[:, 32:40]
    gamv = p.cw[:, 40:48]
    betv = p.cw[:, 48:56]
    wsem = [p.s_wa, p.s_wb]
    stats = p.pb[1]
    p.s_v.wait(g)
    p.s_s.wait(g)
    for fc in range(8):
        p.s_ea.dma(g.dma_start(out=p.z_hbm[fc, :, :], in_=p.R[:, fc * T : (fc + 1) * T]))
    p.s_ea.wait(v)
    for blk in range(3):
        i0, i1 = 2 * blk, 2 * blk + 1
        p.s_s.wait(sy)
        p.s_v.wait(sy)
        p.s_t.wait(sy)
        lv = p.s_ld.dma(sy.dma_start(
            out=qb1v.rearrange("p (j o) -> p j o", o=1),
            in_=p.qb[:, i0 : i0 + 1].rearrange("(j p) o -> p j o", p=128),
            allow_slow_non_contiguous=True))
        lv = p.s_ld.dma(sy.dma_start(
            out=qb2v.rearrange("p (j o) -> p j o", o=1),
            in_=p.qb[:, i1 : i1 + 1].rearrange("(j p) o -> p j o", p=128),
            allow_slow_non_contiguous=True))
        lv = p.s_ld.dma(sy.dma_start(
            out=gamv.rearrange("p (j o) -> p j o", o=1),
            in_=p.gam[:, blk : blk + 1].rearrange("(j p) o -> p j o", p=128),
            allow_slow_non_contiguous=True))
        lv = p.s_ld.dma(sy.dma_start(
            out=betv.rearrange("p (j o) -> p j o", o=1),
            in_=p.bet[:, blk : blk + 1].rearrange("(j p) o -> p j o", p=128),
            allow_slow_non_contiguous=True))
        lv = p.s_ld.dma(sy.dma_start(
            out=lab.rearrange("p (kc r) -> p kc r", kc=8),
            in_=p.la_t[i1].rearrange("(kc p) r -> p kc r", p=128)))
        lv = p.s_ld.dma(sy.dma_start(out=lbb[0:32, :], in_=p.lb_t[i1]))
        lab1 = p.dt_raw[:, 0:256]
        lbb1 = p.dt_raw[:, 256:1280]
        lv = p.s_ld.dma(sy.dma_start(
            out=lab1.rearrange("p (kc r) -> p kc r", kc=8),
            in_=p.la_t[i0].rearrange("(kc p) r -> p kc r", p=128)))
        lv = p.s_ld.dma(sy.dma_start(out=lbb1[0:32, :], in_=p.lb_t[i0]))

        p.s_t.wait(sy)
        for un in range(16):
            mat, kc = divmod(un, 8)
            dstw = wd1[:, kc * 1024 : kc * 1024 + 1024] if mat == 0 else wd2(kc)
            ssl = un % 2
            srb = srst[:, ssl * 1024 : ssl * 1024 + 1024]
            qi = i0 if mat == 0 else i1
            if un >= 2:
                p.s_v.wait(sy)
            wl = wsem[ssl].dma(sy.dma_start(
                out=dstw.bitcast(I32), in_=p.qw_t[qi, kc * 128 : (kc + 1) * 128, :]))
            wl = wsem[ssl].dma(sy.dma_start(
                out=srb, in_=p.srep_t[qi, kc * 128 : (kc + 1) * 128, :]))
            wsem[ssl].wait(v, wl)
            v.tensor_scalar(dstw, dstw.bitcast(I32), float(np.float32(1.0 / 7.5)),
                            -1.0, ALU.mult, ALU.add)
            v.tensor_mul(dstw, dstw, srb)
        dqv = p.s_v.v
        p.s_ld.wait(t)
        p.s_ld.wait(v)
        p.s_ld.wait(s)
        p.s_v.wait(t, dqv)

        prev_tv = p.s_t.v
        l1sc_ck = {}
        l2v_ck = {}
        for tt in range(4):
            ts0 = tt * 512
            for kc in range(8):
                mm = t.matmul(p.pb[5][0:32, :], lab1[:, kc * 32 : kc * 32 + 32],
                              p.R[:, kc * T + ts0 : kc * T + ts0 + 512],
                              start=(kc == 0), stop=(kc == 7))
            ltv = p.s_t.inc(mm)
            p.s_t.wait(v, ltv)
            v.tensor_copy(lra[0:32, :], p.pb[5][0:32, :])
            lrv = p.s_v.v
            p.s_v.wait(t, lrv)
            for oc in range(8):
                bank = p.pb[2 + oc % 2]
                if oc >= 2:
                    p.s_s.wait(t, l1sc_ck[oc - 2])
                elif tt > 0:
                    p.s_v.wait(t)
                for kc in range(8):
                    t.matmul(bank, wd1[:, kc * 1024 + oc * 128 : kc * 1024 + oc * 128 + 128],
                             p.R[:, kc * T + ts0 : kc * T + ts0 + 512],
                             start=(kc == 0), stop=False)
                mm = t.matmul(bank, lbb1[0:32, oc * 128 : oc * 128 + 128],
                              lra[0:32, :], start=False, stop=True)
                tv = p.s_t.inc(mm)
                p.s_t.wait(s, tv)
                sv = p.s_s.inc(s.activation(y1[:, oc * 512 : oc * 512 + 512], bank,
                                            AF.Identity, bias=qb1v[:, oc : oc + 1]))
                l1sc_ck[oc] = sv
            if blk == 0 and tt == 0:
                _dbg(p, "y1b0", y1, [128, 4096])
            stats2 = p.pb[4]
            sq_ck = {}
            for oc in range(8):
                ysl = y1[:, oc * 512 : oc * 512 + 512]
                ssl = oc % 2
                sq = srst[:, ssl * 1024 : ssl * 1024 + 512]
                if oc >= 2:
                    p.s_t.wait(s, sq_ck[oc - 2])
                sv = p.s_s.inc(s.activation(sq, ysl, AF.Square))
                p.s_t.inc(t.matmul(stats[0:1, 0:512], p.ones[:, 0:1], ysl,
                                   start=(oc == 0), stop=(oc == 7)))
                p.s_s.wait(t, sv)
                sq_ck[oc] = tv = p.s_t.inc(t.matmul(
                    stats2[0:1, 0:512], p.ones[:, 0:1], sq,
                    start=(oc == 0), stop=(oc == 7)))
            p.s_t.wait(v, tv)
            m = p.rows[0:1, 0:512]
            var = p.rows[0:1, 512:1024]
            v.tensor_scalar(m, stats[0:1, 0:512], 1.0 / D, None, ALU.mult)
            v.tensor_scalar(var, stats2[0:1, 0:512], 1.0 / D, None, ALU.mult)
            v.tensor_mul(p.tmpv[0:1, 0:512], m, m)
            v.tensor_sub(var, var, p.tmpv[0:1, 0:512])
            v.tensor_scalar(var, var, 1.0, EPS, ALU.mult, ALU.add)
            rv = p.s_v.inc(v.reciprocal(var, var))
            p.s_v.wait(s, rv)
            sv = p.s_s.inc(s.activation(var, var, AF.Sqrt))
            p.s_s.wait(g, sv)
            p.s_v.wait(g)
            wv = p.s_bw.dma(g.dma_start(out=p.bscr[16:17, 0:512], in_=m))
            wv = p.s_bw.dma(g.dma_start(out=p.bscr[18:19, 0:512], in_=var))
            p.s_bw.wait(g, wv)
            p.s_bc.dma(g.dma_start(out=p.bc1[:, 0:512],
                                   in_=p.bscr[16:17, 0:512].broadcast_to([128, 512])))
            bv = p.s_bc.dma(g.dma_start(out=p.bc2[:, 0:512],
                                        in_=p.bscr[18:19, 0:512].broadcast_to([128, 512])))
            p.s_bc.wait(v, bv)
            for oc in range(8):
                ysl = y1[:, oc * 512 : oc * 512 + 512]
                v.tensor_sub(ysl, ysl, p.bc1[:, 0:512])
                v.tensor_mul(ysl, ysl, p.bc2[:, 0:512])
                v.tensor_scalar(ysl, ysl, gamv[:, oc : oc + 1], betv[:, oc : oc + 1],
                                ALU.mult, ALU.add)
            gv = p.s_v.v
            p.s_v.wait(s, gv)
            for oc in range(8):
                ysl = y1[:, oc * 512 : oc * 512 + 512]
                sv = p.s_s.inc(s.activation(ysl, ysl, AF.Gelu))
            p.s_s.wait(t, sv)
            p.s_s.wait(v, sv)
            if blk == 0 and tt == 0:
                _dbg(p, "y1g_b0", y1, [128, 4096])

            for kc in range(8):
                mm = t.matmul(p.pb[5][0:32, :], lab[:, kc * 32 : kc * 32 + 32],
                              y1[:, kc * 512 : kc * 512 + 512],
                              start=(kc == 0), stop=(kc == 7))
            ltv = p.s_t.inc(mm)
            p.s_t.wait(v, ltv)
            v.tensor_copy(lra[0:32, :], p.pb[5][0:32, :])
            lrv = p.s_v.v
            p.s_v.wait(t, lrv)
            for oc in range(8):
                bank = p.pb[2 + oc % 2]
                if oc >= 2:
                    p.s_v.wait(t, l2v_ck[oc - 2])
                else:
                    p.s_s.wait(t, l1sc_ck[6 + oc])
                for kc in range(8):
                    t.matmul(bank, wd2(kc)[:, oc * 128 : oc * 128 + 128],
                             y1[:, kc * 512 : kc * 512 + 512],
                             start=(kc == 0), stop=False)
                mm = t.matmul(bank, lbb[0:32, oc * 128 : oc * 128 + 128],
                              lra[0:32, :], start=False, stop=True)
                tv = p.s_t.inc(mm)
                prev_tv = tv
                p.s_t.wait(v, tv)
                rsl = p.R[:, oc * T + ts0 : oc * T + ts0 + 512]
                v.scalar_tensor_tensor(p.tmpv[:, 0:512], bank, 1.0, rsl,
                                       ALU.mult, ALU.add)
                v.tensor_scalar(p.tmpv[:, 0:512], p.tmpv[:, 0:512],
                                qb2v[:, oc : oc + 1], None, ALU.add)
                v.tensor_copy(rsl, p.tmpv[:, 0:512])
                l2v_ck[oc] = p.s_v.v
        _dbg(p, f"rb{blk}", p.R, [128, 8 * T])

    p.s_t.wait(sy)
    p.s_v.wait(sy)
    p.s_s.wait(sy)
    for fc in range(8):
        lv = p.s_ra.dma(sy.dma_start(out=p.big[:, fc * T : fc * T + T],
                                     in_=p.z_hbm[fc, :, :]))
    p.s_ra.wait(v)
    for fc in range(8):
        rsl = p.R[:, fc * T : (fc + 1) * T]
        v.tensor_add(rsl, rsl, p.big[:, fc * T : fc * T + T])

    _post_ln(p, ln_idx=3)


def _stage_inputs(inputs, names):
    u = np.ascontiguousarray(np.asarray(inputs["L0_attn_u"], dtype=np.float32))
    w_in_t = np.zeros((D, DPROJ_PAD), np.float32)
    w_in_t[:, :DPROJ] = np.asarray(inputs["L0_attn_W_in"], np.float32).T
    common = {
        "w_in_t": w_in_t,
        "conv_w": np.ascontiguousarray(np.asarray(inputs["L0_attn_conv_w"], np.float32)),
        "conv_b": np.asarray(inputs["L0_attn_conv_b"], np.float32)[:, None].copy(),
        "dt_bias": np.asarray(inputs["L0_attn_dt_bias"], np.float32)[:, None].copy(),
        "a_log": np.asarray(inputs["L0_attn_A_log"], np.float32)[:, None].copy(),
        "d_rep": np.repeat(np.asarray(inputs["L0_attn_D"], np.float32), HP)[:, None].copy(),
        "nw_in": np.asarray(inputs["L0_attn_norm_weight"], np.float32)[:, None].copy(),
        "w_out_t": np.ascontiguousarray(np.asarray(inputs["L0_attn_W_out"], np.float32).T),
        "ident": np.eye(128, dtype=np.float32),
        "mask01": np.triu(np.ones((128, 128), np.float32)).copy(),
    }
    if "w_up" in names:
        common["w_up"] = np.ascontiguousarray(np.asarray(inputs["L0_ffn_w_up"], np.float32))
        common["w_gate"] = np.ascontiguousarray(np.asarray(inputs["L0_ffn_w_gate"], np.float32))
        common["w_down"] = np.ascontiguousarray(np.asarray(inputs["L0_ffn_w_down"], np.float32))
    if "wz" in names:
        common["wz"] = np.ascontiguousarray(np.asarray(inputs["L1_attn_Wz"], np.float32))
        common["wh"] = np.ascontiguousarray(np.asarray(inputs["L1_attn_Wh"], np.float32))
        common["bz"] = np.asarray(inputs["L1_attn_bz"], np.float32)[:, None].copy()
        common["bh"] = np.asarray(inputs["L1_attn_bh"], np.float32)[:, None].copy()
        common["log_d"] = np.asarray(inputs["L1_attn_log_D"], np.float32)[:, None].copy()
        md = np.zeros((128, 128), np.float32)
        for i_ in range(128):
            md[i_, i_] = -2.0
            if i_ + 1 < 128:
                md[i_, i_ + 1] = 1.0
                md[i_ + 1, i_] = 1.0
        common["mdiff"] = md
        mwp_ = np.zeros((128, 128), np.float32)
        mwp_[127, 0] = 1.0
        common["mwp"] = mwp_
        mwn_ = np.zeros((128, 128), np.float32)
        mwn_[0, 127] = 1.0
        common["mwn"] = mwn_
    if "qw_t" in names:
        qw = np.asarray(inputs["L1_ffn_qweights"], np.int32)
        common["qw_t"] = np.ascontiguousarray(qw.transpose(0, 2, 1))
        sc = np.asarray(inputs["L1_ffn_scales"], np.float32)
        common["srep_t"] = np.ascontiguousarray(
            np.repeat(sc, 16, axis=2).transpose(0, 2, 1))
        common["qb"] = np.ascontiguousarray(np.asarray(inputs["L1_ffn_biases"], np.float32).T)
        common["la_t"] = np.ascontiguousarray(
            np.asarray(inputs["L1_ffn_lora_a"], np.float32).transpose(0, 2, 1))
        common["lb_t"] = np.ascontiguousarray(
            np.asarray(inputs["L1_ffn_lora_b"], np.float32).transpose(0, 2, 1))
        common["gam"] = np.ascontiguousarray(np.asarray(inputs["L1_ffn_ln_gamma"], np.float32).T)
        common["bet"] = np.ascontiguousarray(np.asarray(inputs["L1_ffn_ln_beta"], np.float32).T)
    in_maps = []
    for c in range(NCORES):
        m = {k: vv for k, vv in common.items() if k in names}
        m["u"] = np.ascontiguousarray(u[c].T)
        in_maps.append(m)
    return in_maps


def kernel(**inputs):
    p = build(_FULL)
    in_maps = _stage_inputs(inputs, set(p.in_names))
    res = run_bass_kernel_spmd(p.nc, in_maps, core_ids=list(range(NCORES)))
    outs = [np.ascontiguousarray(res.results[c]["out"].T) for c in range(NCORES)]
    return np.stack(outs, axis=0)


# version 04
# speedup vs baseline: 1.1687x; 1.0366x over previous
import numpy as np
import concourse.bass as bass
import concourse.mybir as mybir
from concourse.bass_utils import run_bass_kernel_spmd

F32 = mybir.dt.float32
I32 = mybir.dt.int32
AF = mybir.ActivationFunctionType
ALU = mybir.AluOpType

NCORES = 8
T = 2048
D = 1024
DIN = 2048
NH = 32
HP = 64
DSTATE = 128
CONV_DIM = DIN + 2 * DSTATE
DPROJ = 2 * DIN + 2 * DSTATE + NH
DPROJ_PAD = 35 * 128
LO = 256
LC = 128
NO = T // LO
IFF = 3072
QMAX = 32767.0
EPS = 1e-5

_FULL = 4


class RecInstr:
    __slots__ = ("incs",)

    def __init__(self):
        self.incs = []

    def then_inc(self, sem_h, n):
        self.incs.append((sem_h, n))
        return self


_NOCHAIN = {"wait_ge", "dma_start", "indirect_dma_start", "collective_compute",
            "sem_inc", "sem_clear", "nop"}


def scope(p, name):
    for rec in (p.sy, p.t, p.v, p.s, p.g):
        rec.cmds.append(("__scope__", (name,), {}, RecInstr()))


class Rec:

    def __init__(self, name, chain_sem=None):
        self.name = name
        self.cmds = []
        self.chain = chain_sem
        self.ccount = 0

    def __getattr__(self, attr):
        def call(*a, **k):
            h = RecInstr()
            if self.chain is not None and attr not in _NOCHAIN:
                if self.ccount > 0:
                    self.cmds.append(("wait_ge", (self.chain, self.ccount), {}, RecInstr()))
                h.incs.append((self.chain, 1))
                self.ccount += 1
            self.cmds.append((attr, a, k, h))
            return h
        return call


def _replay(nc, eng, cmds):
    import contextlib
    stack = contextlib.ExitStack()
    with stack:
        for attr, a, k, h in cmds:
            if attr == "__scope__":
                stack.close()
                stack = contextlib.ExitStack()
                if a[0] is not None:
                    stack.enter_context(nc.named_scope(a[0]))
                continue
            r = getattr(eng, attr)(*a, **k)
            for sem_h, n in h.incs:
                r.then_inc(sem_h, n)


class Sem:
    def __init__(self, nc, name):
        self.h = nc.alloc_semaphore(name)
        self.v = 0

    def inc(self, instr, n=1):
        instr.then_inc(self.h, n)
        self.v += n
        return self.v

    def dma(self, instr):
        return self.inc(instr, 16)

    def wait(self, eng, val=None):
        eng.wait_ge(self.h, self.v if val is None else val)


class ChainSem:

    def __init__(self, rec):
        self.rec = rec
        self.h = rec.chain

    @property
    def v(self):
        return self.rec.ccount

    def inc(self, instr, n=1):
        return self.rec.ccount

    def wait(self, eng, val=None):
        eng.wait_ge(self.h, self.rec.ccount if val is None else val)


def rep(ap, k):
    return ap.unsqueeze(2).broadcast_to([ap.shape[0], ap.shape[1], k])


class P:
    pass


def _alloc(p, upto, dbg):
    nc = p.nc
    p.upto = upto
    p.dbg = set(dbg)

    def ein(name, shape, dt=F32):
        return nc.dram_tensor(name, shape, dt, kind="ExternalInput").ap()

    p.in_names = []

    def reg(name, shape, dt=F32):
        p.in_names.append(name)
        return ein(name, shape, dt)

    p.u = reg("u", [D, T])
    p.w_in_t = reg("w_in_t", [D, DPROJ_PAD])
    p.conv_w = reg("conv_w", [CONV_DIM, 4])
    p.conv_b = reg("conv_b", [CONV_DIM, 1])
    p.dt_bias = reg("dt_bias", [NH, 1])
    p.a_log = reg("a_log", [NH, 1])
    p.d_rep = reg("d_rep", [DIN, 1])
    p.nw_in = reg("nw_in", [DIN, 1])
    p.w_out_t = reg("w_out_t", [DIN, D])
    p.ident = reg("ident", [128, 128])
    p.mask01 = reg("mask01", [128, 128])
    if upto >= 2:
        p.w_up = reg("w_up", [IFF, D])
        p.w_gate = reg("w_gate", [IFF, D])
        p.w_down = reg("w_down", [D, IFF])
    if upto >= 3:
        p.wz = reg("wz", [D, D])
        p.wh = reg("wh", [D, D])
        p.bz = reg("bz", [D, 1])
        p.bh = reg("bh", [D, 1])
        p.log_d = reg("log_d", [D, 1])
        p.mdiff = reg("mdiff", [128, 128])
        p.mwp = reg("mwp", [128, 128])
        p.mwn = reg("mwn", [128, 128])
    if upto >= 4:
        p.qw_t = reg("qw_t", [6, D, D], I32)
        p.srep_t = reg("srep_t", [6, D, D])
        p.qb = reg("qb", [D, 6])
        p.la_t = reg("la_t", [6, D, 32])
        p.lb_t = reg("lb_t", [6, 32, D])
        p.gam = reg("gam", [D, 3])
        p.bet = reg("bet", [D, 3])

    p.out = nc.dram_tensor("out", [D, T], F32, kind="ExternalOutput").ap()

    p.z_hbm = nc.dram_tensor("z_hbm", [16, 128, T], F32).ap()
    p.xbc_hbm = nc.dram_tensor("xbc_hbm", [18, 128, T], F32).ap()
    p.y_hbm = nc.dram_tensor("y_hbm", [16, 128, T], F32).ap()
    p.bscr = nc.dram_tensor("bscr", [34, 512], F32).ap()
    if upto >= 2:
        p.up_hbm = nc.dram_tensor("up_hbm", [24, 128, T], F32).ap()
        p.gate_hbm = nc.dram_tensor("gate_hbm", [24, 128, T], F32).ap()
        p.g2_hbm = nc.dram_tensor("g2_hbm", [24, 128, T], F32).ap()
        p.o_hbm = nc.dram_tensor("o_hbm", [24, 128, T], F32).ap()
        p.wdq_hbm = nc.dram_tensor("wdq_hbm", [IFF, D], F32).ap()
        p.cc_in = nc.dram_tensor("cc_in", [1, 4], F32)
        p.cc_out = nc.dram_tensor("cc_out", [1, 4], F32)

    def sb(name, free):
        return nc.alloc_sbuf_tensor(name, [128, free], F32).ap()

    p.R = sb("R", 8 * T)
    p.big = sb("big", 16384)
    p.gbs = sb("gbs", 4096)
    p.gb0 = p.gbs[:, 0:1024]
    p.gb1 = p.gbs[:, 1024:2048]
    p.gb2 = p.gbs[:, 2048:3072]
    p.xd = sb("xd", 512)
    p.state = sb("state", DIN)
    p.dt_raw = sb("dt_raw", T)
    p.rstd_all = sb("rstd_all", T)
    p.ysq = sb("ysq", DIN)
    p.yoct = sb("yoct", 16 * LO)
    p.idn = sb("idn", 128)
    p.msk = sb("msk", 128)
    p.nbt = sb("nbt", 32)
    p.dst = sb("dst", 32)
    p.btm = sb("btm", 128)
    p.cbm = sb("cbm", 128)
    p.ones = sb("ones", 8)
    p.sml = sb("sml", 64)
    p.rows = sb("rows", 1024)
    p.bc1 = p.gbs[:, 0:2048]
    p.bc2 = p.gbs[:, 2048:4096]
    p.tmpv = sb("tmpv", 1024)
    p.cw = sb("cw", 96)
    p.dr = sb("dr", 16)
    p.nw = sb("nw", 16)

    p.pb = [nc.alloc_psum_tensor(f"pb{i}", [128, 512], F32).ap() for i in range(8)]

    for s in ["ld", "t", "o2", "wa", "wb", "wc", "wd", "ea", "eb", "ra", "rb",
              "zc", "bw", "bc", "be", "yw", "yb", "cc", "ci", "co"]:
        setattr(p, "s_" + s, Sem(nc, "sem_" + s))


def build(upto=_FULL, dbg=()):
    p = P()
    p.nc = bass.Bass("TRN2", target_bir_lowering=False)
    _alloc(p, upto, dbg)
    vch = p.nc.alloc_semaphore("sem_vch")
    sch = p.nc.alloc_semaphore("sem_sch")
    gch = p.nc.alloc_semaphore("sem_gch")
    p.sy, p.t, p.v, p.s, p.g = (Rec("sy"), Rec("t"), Rec("v", vch), Rec("s", sch),
                                Rec("g", gch))
    p.s_v = ChainSem(p.v)
    p.s_s = ChainSem(p.s)
    p.s_gc = ChainSem(p.g)
    _emit(p)
    with p.nc.Block() as block:
        @block.sync
        def _(e):
            _replay(p.nc, e, p.sy.cmds)

        @block.tensor
        def _(e):
            _replay(p.nc, e, p.t.cmds)

        @block.vector
        def _(e):
            _replay(p.nc, e, p.v.cmds)

        @block.scalar
        def _(e):
            _replay(p.nc, e, p.s.cmds)

        @block.gpsimd
        def _(e):
            _replay(p.nc, e, p.g.cmds)
    return p


def _dbg(p, name, ap, shape):
    if name in p.dbg:
        o = p.nc.dram_tensor("dbg_" + name, shape, F32, kind="ExternalOutput").ap()
        p.s_v.wait(p.sy)
        p.s_s.wait(p.sy)
        p.s_o2.dma(p.sy.dma_start(out=o, in_=ap))


def _fence_compute(p):
    for e in (p.t, p.v, p.s, p.sy, p.g):
        for s in (p.s_t, p.s_v, p.s_s, p.s_gc, p.s_ld, p.s_wa, p.s_wb, p.s_wc,
                  p.s_wd, p.s_ea, p.s_eb, p.s_ra, p.s_rb, p.s_zc, p.s_bw, p.s_bc,
                  p.s_yw, p.s_yb):
            s.wait(e)


def _emit(p):
    sy, t, v, s, g = p.sy, p.t, p.v, p.s, p.g

    p.s_ld.dma(sy.dma_start(out=p.idn, in_=p.ident))
    p.s_ld.dma(sy.dma_start(out=p.msk, in_=p.mask01))
    for fc in range(8):
        p.s_ld.dma(sy.dma_start(out=p.R[:, fc * T : (fc + 1) * T],
                                in_=p.u[fc * 128 : (fc + 1) * 128, :]))
    p.s_ld.dma(sy.dma_start(out=p.sml[0:NH, 0:1], in_=p.dt_bias))
    p.s_ld.dma(sy.dma_start(out=p.sml[0:NH, 1:2], in_=p.a_log))
    p.s_ld.dma(sy.dma_start(
        out=p.cw[:, 0:72].rearrange("p (j w) -> p j w", j=18),
        in_=p.conv_w.rearrange("(j p) w -> p j w", p=128),
        allow_slow_non_contiguous=True))
    p.s_ld.dma(sy.dma_start(
        out=p.cw[:, 72:90].rearrange("p (j o) -> p j o", o=1),
        in_=p.conv_b.rearrange("(j p) o -> p j o", p=128),
        allow_slow_non_contiguous=True))
    p.s_ld.dma(sy.dma_start(
        out=p.dr.rearrange("p (j o) -> p j o", o=1),
        in_=p.d_rep.rearrange("(j p) o -> p j o", p=128),
        allow_slow_non_contiguous=True))
    p.s_ld.dma(sy.dma_start(
        out=p.nw.rearrange("p (j o) -> p j o", o=1),
        in_=p.nw_in.rearrange("(j p) o -> p j o", p=128),
        allow_slow_non_contiguous=True))
    p.s_ld.wait(v)
    p.s_v.inc(v.memset(p.ones, 1.0))
    p.s_ld.wait(s)
    xv = p.s_s.inc(s.activation(p.sml[0:NH, 2:3], p.sml[0:NH, 1:2], AF.Exp))
    p.s_s.wait(s, xv)
    p.s_s.inc(s.mul(p.sml[0:NH, 2:3], p.sml[0:NH, 2:3], -1.0))
    p.negA = p.sml[0:NH, 2:3]
    p.dtbv = p.sml[0:NH, 0:1]
    p.s_v.wait(t)
    p.s_s.wait(t)

    _mamba(p)
    if p.upto >= 2:
        _ffn0(p)
    if p.upto >= 3:
        _l1_attn(p)
    if p.upto >= 4:
        _l1_ffn(p)

    p.s_v.wait(sy)
    p.s_s.wait(sy)
    for fc in range(8):
        p.s_o2.dma(sy.dma_start(out=p.out[fc * 128 : (fc + 1) * 128, :],
                                in_=p.R[:, fc * T : (fc + 1) * T]))
    p.s_o2.wait(sy)


def _mamba(p):
    sy, t, v, s, g = p.sy, p.t, p.v, p.s, p.g

    scope(p, 'mamba_inproj')
    wbuf = p.big[:, 0:2048]
    tmpe = p.big[:, 2048:3072]
    wl_ck = []
    tm_ck = [0] * 200
    ev_ck = [0] * 200
    sp_ck = [0] * 200
    wsem = [p.s_wa, p.s_wb]
    esem = [p.s_ea, p.s_eb]
    p.s_v.inc(v.memset(p.state, 0.0))
    for oc in range(35):
        wb = wbuf[:, (oc % 2) * 1024 : (oc % 2) * 1024 + 1024]
        if oc >= 2:
            sy.wait_ge(p.s_t.h, tm_ck[(oc - 2) * 4 + 3])
        src = p.w_in_t[:, oc * 128 : (oc + 1) * 128].rearrange("(kc kp) o -> kp kc o", kp=128)
        wv = wsem[oc % 2].dma(sy.dma_start(out=wb.rearrange("kp (kc o) -> kp kc o", kc=8), in_=src))
        wl_ck.append(wv)
        for tt in range(4):
            i = oc * 4 + tt
            bank = p.pb[i % 2]
            wsem[oc % 2].wait(t, wl_ck[oc])
            if i >= 2:
                p.s_v.wait(t, ev_ck[i - 2])
            for kc in range(8):
                mm = t.matmul(bank, wb[:, kc * 128 : (kc + 1) * 128],
                              p.R[:, kc * T + tt * 512 : kc * T + tt * 512 + 512],
                              start=(kc == 0), stop=(kc == 7))
            tm_ck[i] = p.s_t.inc(mm)
            p.s_t.wait(v, tm_ck[i])
            if oc < 34:
                slot = i % 2
                te = tmpe[:, slot * 512 : slot * 512 + 512]
                if i >= 2:
                    esem[slot].wait(v, sp_ck[i - 2])
                ev_ck[i] = p.s_v.inc(v.tensor_copy(te, bank))
                p.s_v.wait(g, ev_ck[i])
                dst = (p.z_hbm if oc < 16 else p.xbc_hbm)
                row = oc if oc < 16 else oc - 16
                sp_ck[i] = esem[slot].dma(
                    g.dma_start(out=dst[row, :, tt * 512 : tt * 512 + 512], in_=te))
            else:
                ev_ck[i] = p.s_v.inc(v.tensor_copy(p.dt_raw[:, tt * 512 : tt * 512 + 512], bank))

    p.s_ea.wait(sy)
    p.s_eb.wait(sy)
    _dbg(p, "dt_raw", p.dt_raw, [128, T])

    scope(p, 'mamba_chunks')
    xbcs = p.big[:, 0 : 18 * LO]
    zc = p.big[:, 4608 : 4608 + 2048]
    xtm = p.big[:, 6656 : 6656 + 2048]
    dtb = p.big[0:NH, 8704 : 8704 + 8 * LO]
    raw = p.big[:, 10752 : 10752 + 2 * 260]
    for o in range(NO):
        to = o * LO
        dts = dtb[:, 0 * LO : 1 * LO]
        lndt = dtb[:, 1 * LO : 2 * LO]
        dA = dtb[:, 2 * LO : 3 * LO]
        cA = dtb[:, 3 * LO : 4 * LO]
        nb = dtb[:, 4 * LO : 5 * LO]
        dsb = dtb[:, 5 * LO : 6 * LO]
        t32 = dtb[:, 6 * LO : 7 * LO]
        p.s_v.wait(s)
        p.s_t.wait(s)
        av = p.s_s.inc(s.activation(t32, p.dt_raw[0:NH, to : to + LO], AF.Exp, bias=p.dtbv))
        p.s_s.wait(s, av)
        av = p.s_s.inc(s.activation(dts, t32, AF.Ln, bias=1.0))
        p.s_s.wait(s, av)
        sv = p.s_s.inc(s.activation(lndt, dts, AF.Ln))
        p.s_s.wait(v, sv)
        vv = p.s_v.inc(v.tensor_scalar(dA, dts, p.negA, None, ALU.mult))
        p.s_v.wait(v, vv)
        for cc in range(2):
            sl = slice(cc * LC, (cc + 1) * LC)
            vv = p.s_v.inc(v.tensor_tensor_scan(
                cA[:, sl], p.ones[0:NH, 0:1].broadcast_to([NH, LC]),
                dA[:, sl], 0.0, ALU.mult, ALU.add))
        p.s_v.wait(v, vv)
        v.tensor_sub(nb, lndt, cA)
        for cc in range(2):
            sl = slice(cc * LC, (cc + 1) * LC)
            v.tensor_scalar(t32[:, sl], cA[:, sl], -1.0,
                            cA[:, cc * LC + LC - 1 : cc * LC + LC], ALU.mult, ALU.add)
            v.tensor_add(t32[:, sl], t32[:, sl], lndt[:, sl])
        dv = p.s_v.v
        p.s_v.wait(s, dv)
        p.s_s.inc(s.activation(dsb, t32, AF.Exp))

        cv_ck = [0] * 18
        rsem = [p.s_ra, p.s_rb]
        for j in range(18):
            slot = j % 2
            rb = raw[:, slot * 260 : slot * 260 + 260]
            if j >= 2:
                p.s_v.wait(sy, cv_ck[j - 2])
            if o == 0:
                lv = rsem[slot].dma(sy.dma_start(out=rb[:, 3:259], in_=p.xbc_hbm[j, :, 0:LO]))
            else:
                lv = rsem[slot].dma(sy.dma_start(out=rb[:, 0:259],
                                                 in_=p.xbc_hbm[j, :, to - 3 : to + LO]))
            rsem[slot].wait(v, lv)
            if o == 0:
                v.memset(rb[:, 0:3], 0.0)
            xj = xbcs[:, j * LO : (j + 1) * LO]
            v.tensor_scalar(xj, rb[:, 0:LO], p.cw[:, 4 * j : 4 * j + 1],
                            p.cw[:, 72 + j : 73 + j], ALU.mult, ALU.add)
            for w in range(1, 4):
                v.scalar_tensor_tensor(xj, rb[:, w : w + LO], p.cw[:, 4 * j + w : 4 * j + w + 1],
                                       xj, ALU.mult, ALU.add)
            cv_ck[j] = p.s_v.v
            p.s_v.wait(s, cv_ck[j])
            p.s_s.inc(s.activation(xj, xj, AF.Silu))

        _dbg(p, f"xbcs_o{o}", xbcs, [128, 18 * LO])

        for cc in range(2):
            _mamba_chunk(p, o, cc, xbcs, zc, xtm, cA, nb, dsb)

        p.s_v.wait(g)
        p.s_yw.dma(g.dma_start(
            out=p.y_hbm[:, :, to : to + LO].transpose([1, 0, 2]),
            in_=p.yoct.rearrange("p (fc l) -> p fc l", fc=16)))

    p.s_yw.wait(sy)
    _dbg(p, "rstd_all", p.rstd_all, [128, T])

    scope(p, 'mamba_outproj')
    _outproj_ln(p)


def _mamba_chunk(p, o, cc, xbcs, zc, xtm, cA, nb, dsb):
    sy, t, v, s, g = p.sy, p.t, p.v, p.s, p.g
    tg = o * LO + cc * LC
    sl = slice(cc * LC, (cc + 1) * LC)

    p.s_v.wait(sy)
    p.s_s.wait(sy)
    zv = p.s_zc.dma(sy.dma_start(
        out=zc.rearrange("p (fc l) -> p fc l", fc=16),
        in_=p.z_hbm[:, :, tg : tg + LC].transpose([1, 0, 2])))

    p.s_v.wait(g)
    p.s_s.wait(g)
    bflat = p.bscr.rearrange("a b -> (a b)")
    wv = p.s_bw.dma(g.dma_start(
        out=bflat[0 : NH * LC].rearrange("(h l) -> h l", l=LC), in_=cA[:, sl]))

    bank_tr = p.pb[2]
    bank_cb = p.pb[3]
    xb16 = xbcs[:, 16 * LO : 16 * LO + LO]
    xb17 = xbcs[:, 17 * LO : 17 * LO + LO]
    p.s_s.wait(t)
    p.s_v.wait(t)
    p.s_t.inc(t.matmul(bank_cb[:, 0:128], xb16[:, sl], xb17[:, sl], start=True, stop=True))
    p.s_t.inc(t.transpose(bank_tr[:, 0:32], nb[:, sl], p.idn[0:32, 0:32]))
    p.s_t.inc(t.transpose(bank_tr[:, 32:64], dsb[:, sl], p.idn[0:32, 0:32]))
    tv = p.s_t.inc(t.transpose(bank_tr[:, 128:256], xb16[:, sl], p.idn))
    p.s_t.wait(v, tv)
    v.tensor_copy(p.nbt, bank_tr[:, 0:32])
    v.tensor_copy(p.dst, bank_tr[:, 32:64])
    v.tensor_copy(p.btm, bank_tr[:, 128:256])
    ev = p.s_v.inc(v.tensor_mul(p.cbm, bank_cb[:, 0:128], p.msk))
    for g4 in range(4):
        p.s_v.wait(t, ev)
        for k in range(4):
            fc = g4 * 4 + k
            tv = p.s_t.inc(t.transpose(
                bank_tr[:, k * 128 : (k + 1) * 128],
                xbcs[:, fc * LO + cc * LC : fc * LO + cc * LC + LC], p.idn))
        p.s_t.wait(v, tv)
        ev = p.s_v.inc(v.tensor_copy(xtm[:, g4 * 512 : (g4 + 1) * 512], bank_tr))

    bufsets = [
        (p.gb0, p.gb1, p.gb2, p.xd[:, 0:512], p.pb[0], p.s_bc),
        (p.big[:, 12288:13312], p.big[:, 13312:14336], p.big[:, 14336:15360],
         p.big[:, 15360:15872], p.pb[1], p.s_be),
    ]
    xb17 = xbcs[:, 17 * LO : 17 * LO + LO]
    bflat = p.bscr.rearrange("a b -> (a b)")
    g_tv = {}
    g_gv = {}
    g_vv = {}

    def build_group(gi):
        h0 = gi * 8
        b0, b1, b2, xd, sbank, bsem = bufsets[gi % 2]
        p.s_bw.wait(g, wv)
        if gi >= 2:
            p.s_t.wait(g, g_tv[gi - 2])
            p.s_v.wait(g, g_gv[gi - 2])
        bv = bsem.dma(g.dma_start(
            out=b0,
            in_=bflat[h0 * LC : h0 * LC + 1024].unsqueeze(0).broadcast_to([128, 1024])))
        bsem.wait(v, bv)
        if gi >= 2:
            p.s_t.wait(v, g_tv[gi - 2])
        b0v = b0.rearrange("p (h l) -> p h l", h=8)
        v.tensor_tensor(b0v, b0v, rep(p.nbt[:, h0 : h0 + 8], LC), ALU.add)
        mv = p.s_v.inc(v.tensor_scalar(b2, b0, 30.0, None, ALU.min))
        p.s_v.wait(s, mv)
        if gi >= 2:
            p.s_t.wait(s, g_tv[gi - 2])
        sv = p.s_s.inc(s.activation(b1, b2, AF.Exp))
        p.s_s.wait(v, sv)
        b1v = b1.rearrange("p (h l) -> p h l", h=8)
        v.tensor_tensor(b1v, b1v, rep(p.cbm, 8).transpose([0, 2, 1]), ALU.mult)
        v.tensor_tensor(b0v, b0v, rep(p.nbt[:, h0 : h0 + 8], LC), ALU.subtract)
        mv = p.s_v.v
        p.s_v.wait(s, mv)
        sv = p.s_s.inc(s.activation(b0, b0, AF.Exp))
        p.s_s.wait(v, sv)
        for hh in range(8):
            h = h0 + hh
            v.tensor_scalar(xd[:, hh * 64 : hh * 64 + 64],
                            xtm[:, h * 64 : h * 64 + 64],
                            p.dst[:, h : h + 1], None, ALU.mult)
        b2v = b2.rearrange("p (h l) -> p h l", h=8)
        v.tensor_tensor(b2v, rep(xb17[:, sl], 8).transpose([0, 2, 1]),
                        b0.rearrange("p (h l) -> p h l", h=8), ALU.mult)
        g_vv[gi] = p.s_v.v

    def mm_group(gi):
        h0 = gi * 8
        b0, b1, b2, xd, sbank, bsem = bufsets[gi % 2]
        ybank = p.pb[4 + gi]
        p.s_v.wait(t, g_vv[gi])
        for hh in range(8):
            h = h0 + hh
            ypart = (hh % 2) * 64
            yfree = (hh // 2) * 128
            yout = ybank[ypart : ypart + 64, yfree : yfree + 128]
            t.matmul(yout, xtm[:, h * 64 : h * 64 + 64],
                     b1[:, hh * 128 : hh * 128 + 128],
                     start=(hh < 2), stop=False, skip_group_check=True)
            t.matmul(yout, p.state[:, h * 64 : h * 64 + 64],
                     b2[:, hh * 128 : hh * 128 + 128],
                     start=False, stop=(hh >= 6), skip_group_check=True)
            mm = t.matmul(sbank[:, hh * 64 : hh * 64 + 64], p.btm,
                          xd[:, hh * 64 : hh * 64 + 64],
                          start=(hh == 0), stop=(hh == 7))
        g_tv[gi] = tv = p.s_t.inc(mm)
        p.s_t.wait(v, tv)
        for hh in range(8):
            h = h0 + hh
            stsl = p.state[:, h * 64 : h * 64 + 64]
            v.tensor_scalar(stsl, stsl, b0[:, hh * 128 + 127 : hh * 128 + 128],
                            None, ALU.mult)
        g_gv[gi] = p.s_v.inc(v.tensor_add(p.state[:, h0 * 64 : h0 * 64 + 512],
                                          p.state[:, h0 * 64 : h0 * 64 + 512], sbank))

    for gi in range(4):
        build_group(gi)
        if gi >= 1:
            mm_group(gi - 1)
    mm_group(3)

    p.s_zc.wait(s, zv)
    p.s_zc.wait(v, zv)
    p.s_yw.wait(v)
    sv = p.s_s.inc(s.activation(zc, zc, AF.Silu))
    p.s_s.wait(v, sv)
    for fc in range(16):
        ybank = p.pb[4 + fc // 4]
        yfree = (fc % 4) * 128
        dst_y = p.yoct[:, fc * LO + cc * LC : fc * LO + cc * LC + LC]
        v.scalar_tensor_tensor(p.tmpv[:, 0:128],
                               xbcs[:, fc * LO + cc * LC : fc * LO + cc * LC + LC],
                               p.dr[:, fc : fc + 1],
                               ybank[:, yfree : yfree + 128], ALU.mult, ALU.add)
        v.tensor_mul(dst_y, p.tmpv[:, 0:128], zc[:, fc * 128 : fc * 128 + 128])
    yv = p.s_v.v
    p.s_v.wait(s, yv)
    sv = p.s_s.inc(s.activation(p.ysq.rearrange("p (fc l) -> p fc l", fc=16),
                                p.yoct.rearrange("p (fc l) -> p fc l", fc=16)[:, :, cc * LC : cc * LC + LC],
                                AF.Square))
    p.s_s.wait(t, sv)
    p.s_v.wait(t)
    stats = p.pb[1]
    for fc in range(16):
        mm = t.matmul(stats[0:1, 0:128], p.ones[:, 0:1],
                      p.ysq[:, fc * 128 : fc * 128 + 128],
                      start=(fc == 0), stop=(fc == 15))
    tv = p.s_t.inc(mm)
    p.s_t.wait(v, tv)
    v.tensor_scalar(p.rows[0:1, 8:136], stats[0:1, 0:128], 1.0 / DIN, EPS,
                    ALU.mult, ALU.add)
    rv = p.s_v.inc(v.reciprocal(p.rows[0:1, 8:136], p.rows[0:1, 8:136]))
    p.s_v.wait(s, rv)
    sv = p.s_s.inc(s.activation(p.rows[0:1, 8:136], p.rows[0:1, 8:136], AF.Sqrt))
    p.s_s.wait(v, sv)
    p.s_v.inc(v.tensor_copy(p.rstd_all[0:1, tg : tg + LC], p.rows[0:1, 8:136]))


def _outproj_ln(p):
    sy, t, v, s, g = p.sy, p.t, p.v, p.s, p.g
    _fence_compute(p)
    wo4 = p.big[:, 0:8192]
    ybuf = p.big[:, 8192:12288]
    prev_tv = 0
    for ocg in range(2):
        wl = 0
        if ocg == 1:
            sy.wait_ge(p.s_t.h, prev_tv)
        for j in range(4):
            oc = ocg * 4 + j
            wj = wo4[:, j * 2048 : j * 2048 + 2048]
            src = p.w_out_t[:, oc * 128 : (oc + 1) * 128].rearrange(
                "(kc kp) o -> kp kc o", kp=128)
            wl = p.s_wa.dma(sy.dma_start(
                out=wj.rearrange("kp (kc o) -> kp kc o", kc=16), in_=src))
        p.s_wa.wait(v, wl)
        for j in range(4):
            wj = wo4[:, j * 2048 : j * 2048 + 2048]
            v.tensor_tensor(wj.rearrange("p (kc o) -> p kc o", kc=16),
                            wj.rearrange("p (kc o) -> p kc o", kc=16),
                            rep(p.nw, 128), ALU.mult)
        wsv = p.s_v.v

        for sub in range(8):
            ts0 = sub * 256
            if not (ocg == 0 and sub == 0):
                sy.wait_ge(p.s_t.h, prev_tv)
            yv = p.s_yb.dma(sy.dma_start(
                out=ybuf.rearrange("p (fc l) -> p fc l", fc=16),
                in_=p.y_hbm[:, :, ts0 : ts0 + 256].transpose([1, 0, 2])))
            p.s_v.wait(g)
            wv = p.s_bw.dma(g.dma_start(out=p.bscr[17:18, 0:256],
                                        in_=p.rstd_all[0:1, ts0 : ts0 + 256]))
            p.s_bw.wait(g, wv)
            bv = p.s_bc.dma(g.dma_start(
                out=p.bc1[:, 0:256], in_=p.bscr[17:18, 0:256].broadcast_to([128, 256])))
            for j in range(4):
                oc = ocg * 4 + j
                bank = p.pb[2 + (j % 2)]
                p.s_v.wait(t, wsv)
                p.s_yb.wait(t, yv)
                if not (ocg == 0 and sub == 0 and j < 2):
                    p.s_v.wait(t)
                for kc in range(16):
                    mm = t.matmul(bank[:, 0:256],
                                  wo4[:, j * 2048 + kc * 128 : j * 2048 + kc * 128 + 128],
                                  ybuf[:, kc * 256 : (kc + 1) * 256],
                                  start=(kc == 0), stop=(kc == 15))
                tv = p.s_t.inc(mm)
                prev_tv = tv
                p.s_t.wait(v, tv)
                p.s_bc.wait(v, bv)
                rsl = p.R[:, oc * T + ts0 : oc * T + ts0 + 256]
                v.tensor_mul(p.tmpv[:, 0:256], bank[:, 0:256], p.bc1[:, 0:256])
                p.s_v.inc(v.tensor_add(rsl, rsl, p.tmpv[:, 0:256]))

    _dbg(p, "x1pre", p.R, [128, 8 * T])
    _post_ln(p, ln_idx=0, absmax_acc=(p.upto >= 2))


def _post_ln(p, ln_idx, absmax_acc=False, gamma=None, beta=None):
    sy, t, v, s, g = p.sy, p.t, p.v, p.s, p.g
    scope(p, f"post_ln{ln_idx}")
    _fence_compute(p)
    if absmax_acc:
        v.memset(p.sml[:, 3:4], 0.0)
    sbanks = [p.pb[1], p.pb[4]]
    mrow = p.dt_raw[0:1, 0:2048]
    rrow = p.rstd_all[0:1, 0:2048]
    st_ck = {}
    ev_ck = {}
    for sub in range(8):
        ts0 = sub * 256
        stats = sbanks[sub % 2]
        ssl = sub % 2
        sqb = p.ysq[:, ssl * 1024 : ssl * 1024 + 1024]
        if sub >= 2:
            p.s_t.wait(s, st_ck[sub - 2])
        for half in range(2):
            if half == 1:
                p.s_t.wait(s, tv)
            for k in range(4):
                oc = half * 4 + k
                rsl = p.R[:, oc * T + ts0 : oc * T + ts0 + 256]
                s.activation(sqb[:, k * 256 : k * 256 + 256], rsl, AF.Square)
            sv = p.s_s.v
            p.s_s.wait(t, sv)
            if sub >= 2 and half == 0:
                p.s_v.wait(t, ev_ck[sub - 2])
            for k in range(4):
                oc = half * 4 + k
                rsl2 = p.R[:, oc * T + ts0 : oc * T + ts0 + 256]
                p.s_t.inc(t.matmul(stats[0:1, 0:256], p.ones[:, 0:1], rsl2,
                                   start=(oc == 0), stop=False,
                                   skip_group_check=True))
                tv = p.s_t.inc(t.matmul(
                    stats[0:1, 256:512], p.ones[:, 0:1],
                    sqb[:, k * 256 : k * 256 + 256],
                    start=False, stop=(oc == 7), skip_group_check=True))
        st_ck[sub] = tv
        p.s_t.wait(v, tv)
        v.tensor_scalar(mrow[:, ts0 : ts0 + 256], stats[0:1, 0:256], 1.0 / D,
                        None, ALU.mult)
        v.tensor_scalar(rrow[:, ts0 : ts0 + 256], stats[0:1, 256:512], 1.0 / D,
                        None, ALU.mult)
        ev_ck[sub] = p.s_v.v
    v.tensor_mul(p.tmpv[0:1, 0:1024], mrow[:, 0:1024], mrow[:, 0:1024])
    v.tensor_sub(rrow[:, 0:1024], rrow[:, 0:1024], p.tmpv[0:1, 0:1024])
    v.tensor_mul(p.tmpv[0:1, 0:1024], mrow[:, 1024:2048], mrow[:, 1024:2048])
    v.tensor_sub(rrow[:, 1024:2048], rrow[:, 1024:2048], p.tmpv[0:1, 0:1024])
    v.tensor_scalar(rrow, rrow, 1.0, EPS, ALU.mult, ALU.add)
    rv = p.s_v.inc(v.reciprocal(rrow, rrow))
    p.s_v.wait(s, rv)
    sv = p.s_s.inc(s.activation(rrow, rrow, AF.Sqrt))
    p.s_s.wait(g, sv)
    p.s_v.wait(g)
    bflat = p.bscr.rearrange("a b -> (a b)")
    wv = p.s_bw.dma(g.dma_start(out=bflat[16 * 512 : 16 * 512 + 2048], in_=mrow))
    wv = p.s_bw.dma(g.dma_start(out=bflat[20 * 512 : 20 * 512 + 2048], in_=rrow))
    p.s_bw.wait(g, wv)
    p.s_bc.dma(g.dma_start(
        out=p.bc1, in_=bflat[16 * 512 : 16 * 512 + 2048].unsqueeze(0).broadcast_to([128, 2048])))
    bv = p.s_bc.dma(g.dma_start(
        out=p.bc2, in_=bflat[20 * 512 : 20 * 512 + 2048].unsqueeze(0).broadcast_to([128, 2048])))
    p.s_bc.wait(v, bv)
    for oc in range(8):
        rsl = p.R[:, oc * T : (oc + 1) * T]
        v.tensor_sub(rsl, rsl, p.bc1)
        v.tensor_mul(rsl, rsl, p.bc2)
        if gamma is not None:
            v.tensor_scalar(rsl, rsl, gamma[:, oc : oc + 1], beta[:, oc : oc + 1],
                            ALU.mult, ALU.add)
        if absmax_acc:
            v.tensor_reduce(p.tmpv[:, 0:1], rsl, mybir.AxisListType.X, ALU.max,
                            apply_absolute_value=True)
            v.tensor_max(p.sml[:, 3:4], p.sml[:, 3:4], p.tmpv[:, 0:1])



def _allreduce_max(p, cols, out_cols):
    sy, t, v, s, g = p.sy, p.t, p.v, p.s, p.g
    k = len(cols)
    stats = p.pb[1]
    p.s_v.wait(t)
    p.s_gc.wait(t)
    for i, c in enumerate(cols):
        tv = p.s_t.inc(t.transpose(stats[0:1, i * 128 : i * 128 + 128],
                                   p.sml[:, c : c + 1], p.idn))
    p.s_t.wait(v, tv)
    v.memset(p.rows[0:1, 640:644], 0.0)
    for i in range(k):
        v.tensor_reduce(p.rows[0:1, 640 + i : 641 + i], stats[0:1, i * 128 : i * 128 + 128],
                        mybir.AxisListType.X, ALU.max)
    vv = p.s_v.v
    p.s_v.wait(g, vv)
    p.s_cc.wait(g)
    p.s_ci.dma(g.dma_start(out=p.cc_in.ap()[0:1, 0:4], in_=p.rows[0:1, 640:644]))
    p.s_ci.wait(g)
    p.s_co.wait(g)
    cc = g.collective_compute(
        "AllReduce", ALU.max, replica_groups=[list(range(NCORES))],
        ins=[p.cc_in.ap().opt()], outs=[p.cc_out.ap().opt()])
    p.s_cc.inc(cc)
    p.s_cc.wait(g)
    bv = p.s_co.dma(g.dma_start(
        out=p.tmpv[:, 0:4], in_=p.cc_out.ap()[0:1, 0:4].broadcast_to([128, 4])))
    p.s_co.wait(v, bv)
    for i, oc in enumerate(out_cols):
        v.tensor_copy(p.sml[:, oc : oc + 1], p.tmpv[:, i : i + 1])


def _scale_recip(p, mx_col, s_col, r_col):
    v = p.v
    v.tensor_scalar(p.sml[:, s_col : s_col + 1], p.sml[:, mx_col : mx_col + 1],
                    float(np.float32(1.0 / QMAX)), 1e-12, ALU.mult, ALU.max)
    v.reciprocal(p.sml[:, r_col : r_col + 1], p.sml[:, s_col : s_col + 1])


def _lpbq_dequant(p, wraw, ncols, sbuf, rbuf):
    v = p.v
    nb = ncols // 32
    v3 = wraw.rearrange("p (b e) -> p b e", b=nb)
    v.tensor_reduce(sbuf, v3, mybir.AxisListType.X, ALU.max, apply_absolute_value=True)
    v.tensor_scalar(sbuf, sbuf, float(np.float32(1.0 / 7.0)), 1e-12, ALU.mult, ALU.max)
    v.reciprocal(rbuf, sbuf)
    v.tensor_tensor(v3.bitcast(I32), v3, rep(rbuf, 32), ALU.mult)
    v.tensor_tensor(v3, v3.bitcast(I32), rep(sbuf, 32), ALU.mult)


def _ffn0(p):
    sy, t, v, s, g = p.sy, p.t, p.v, p.s, p.g
    _fence_compute(p)
    scope(p, 'ffn0_arA')
    _allreduce_max(p, [3], [3])
    _scale_recip(p, 3, 4, 5)

    xq = p.big
    vv0 = p.s_v.v
    for kc in range(8):
        xr = p.R[:, kc * T : (kc + 1) * T]
        dst = xq[:, kc * T : (kc + 1) * T]
        v.tensor_scalar(dst.bitcast(I32), xr, p.sml[:, 5:6], None, ALU.mult)
        v.tensor_scalar(dst, dst.bitcast(I32), p.sml[:, 4:5], None, ALU.mult)
    xqv = p.s_v.v

    scope(p, 'ffn0_p1')
    v.memset(p.sml[:, 6:7], 0.0)
    v.memset(p.sml[:, 7:8], 0.0)
    wraw2 = p.yoct[:, 0:2048]
    wbuf = p.dt_raw[:, 0:2048]
    sc2 = p.state[:, 0:128]
    stg = p.ysq[:, 0:2048]
    wsem = [p.s_wa, p.s_wb]
    esem = [p.s_ea, p.s_eb]
    tm_ck = {}
    sc_tr = {}
    sc_ev = {}
    sp_ck = {}
    dq_ck = {}
    tr_ck = {}
    units = [(m, oc) for oc in range(24) for m in range(2)]
    bank_wt = p.pb[4]

    def p1_stage_a(ui):
        m, oc = units[ui]
        wsl = ui % 2
        wraw = wraw2[:, wsl * 1024 : wsl * 1024 + 1024]
        wb = wbuf[:, wsl * 1024 : wsl * 1024 + 1024]
        wsrc = (p.w_up if m == 0 else p.w_gate)
        if ui >= 2:
            p.s_v.wait(sy, dq_ck[ui - 2])
            sy.wait_ge(p.s_t.h, tr_ck[ui - 2])
        wl = wsem[wsl].dma(sy.dma_start(
            out=wraw, in_=wsrc[oc * 128 : (oc + 1) * 128, :]))
        wsem[wsl].wait(v, wl)
        _lpbq_dequant(p, wraw, 1024, sc2[:, wsl * 64 : wsl * 64 + 32],
                      sc2[:, wsl * 64 + 32 : wsl * 64 + 64])
        dq_ck[ui] = p.s_v.v
        p.s_v.wait(t, dq_ck[ui])
        if ui >= 2:
            tmp = tm_ck.get((ui - 2, 3))
            p.s_s.wait(t, sc_tr.get(ui - 1, 0))
        for kk in range(2):
            if ui >= 1 or kk == 1:
                p.s_s.wait(t, sc_tr.get(ui if kk == 1 else ui - 1, 0))
            for k4 in range(4):
                kc = kk * 4 + k4
                tv = p.s_t.inc(t.transpose(bank_wt[:, k4 * 128 : k4 * 128 + 128],
                                           wraw[:, kc * 128 : kc * 128 + 128], p.idn))
            tr_ck[ui] = tv
            p.s_t.wait(s, tv)
            if ui >= 2 and kk == 0:
                sy2 = tm_ck.get((ui - 2, 3), 0)
                t.nop() if False else None
            if ui >= 2 and kk == 0:
                p.s_t.wait(s, tm_ck[(ui - 2, 3)])
            s.copy(wb[:, kk * 512 : kk * 512 + 512], bank_wt[:, 0:512])
            sc_tr[ui] = p.s_s.v

    def p1_stage_b(ui):
        m, oc = units[ui]
        wsl = ui % 2
        wb = wbuf[:, wsl * 1024 : wsl * 1024 + 1024]
        for tt in range(4):
            i = ui * 4 + tt
            bank = p.pb[i % 2]
            p.s_s.wait(t, sc_tr[ui])
            if ui == 0 and tt == 0:
                p.s_v.wait(t, xqv)
            if i >= 2:
                p.s_v.wait(t, vr_ck[i - 2])
            for kc in range(8):
                mm = t.matmul(bank, wb[:, kc * 128 : (kc + 1) * 128],
                              xq[:, kc * T + tt * 512 : kc * T + tt * 512 + 512],
                              start=(kc == 0), stop=(kc == 7))
            tm_ck[(ui, tt)] = p.s_t.inc(mm)
            ssl = i % 2
            te = stg[:, ssl * 512 : ssl * 512 + 512]
            p.s_t.wait(v, tm_ck[(ui, tt)])
            if i >= 2:
                esem[ssl].wait(v, sp_ck[i - 2])
            v.tensor_copy(te, bank)
            v.tensor_reduce(p.tmpv[:, 4 + ssl : 5 + ssl], bank, mybir.AxisListType.X,
                            ALU.max, apply_absolute_value=True)
            acc = p.sml[:, 6 + m : 7 + m]
            v.tensor_max(acc, acc, p.tmpv[:, 4 + ssl : 5 + ssl])
            vr_ck[i] = sc_ev[i] = p.s_v.v
            p.s_v.wait(g, sc_ev[i])
            dst = (p.up_hbm if m == 0 else p.gate_hbm)
            sp_ck[i] = esem[ssl].dma(
                g.dma_start(out=dst[oc, :, tt * 512 : tt * 512 + 512], in_=te))

    gr_ck = {}
    vr_ck = {}
    for ui in range(48):
        p1_stage_a(ui)
        if ui >= 1:
            p1_stage_b(ui - 1)
    p1_stage_b(47)

    scope(p, 'ffn0_wdq')
    p.s_ea.wait(v)
    p.s_eb.wait(v)
    for oc in range(8):
        wsl = oc % 2
        wraw = wraw2[:, wsl * 1024 : wsl * 1024 + 1024]
        for cp in range(3):
            p.s_v.wait(sy)
            wl = wsem[wsl].dma(sy.dma_start(
                out=wraw, in_=p.w_down[oc * 128 : (oc + 1) * 128,
                                       cp * 1024 : cp * 1024 + 1024]))
            wsem[wsl].wait(v, wl)
            _lpbq_dequant(p, wraw, 1024, sc2[:, wsl * 64 : wsl * 64 + 32],
                          sc2[:, wsl * 64 + 32 : wsl * 64 + 64])
            dqv = p.s_v.v
            p.s_v.wait(t, dqv)
            p.s_ea.wait(v)
            p.s_eb.wait(v)
            for kk in range(2):
                p.s_v.wait(t)
                for k4 in range(4):
                    kc = kk * 4 + k4
                    tv = p.s_t.inc(t.transpose(bank_wt[:, k4 * 128 : k4 * 128 + 128],
                                               wraw[:, kc * 128 : kc * 128 + 128], p.idn))
                p.s_t.wait(v, tv)
                v.tensor_copy(stg[:, kk * 512 : kk * 512 + 512], bank_wt)
            vv = p.s_v.v
            p.s_v.wait(g, vv)
            esem[wsl].dma(g.dma_start(
                out=p.wdq_hbm[cp * 1024 : (cp + 1) * 1024, oc * 128 : (oc + 1) * 128]
                .rearrange("(kc kp) o -> kp kc o", kp=128),
                in_=stg[:, 0:1024].rearrange("kp (kc o) -> kp kc o", kc=8)))

    scope(p, 'ffn0_arB')
    _allreduce_max(p, [6, 7], [6, 7])
    _scale_recip(p, 6, 10, 11)
    _scale_recip(p, 7, 12, 13)

    scope(p, 'ffn0_p2')
    p.s_ea.wait(sy)
    p.s_eb.wait(sy)
    v.memset(p.sml[:, 8:9], 0.0)
    rsem = [p.s_ra, p.s_rb]
    ld_ck = {}
    cv_ck = {}
    sp_ck = {}
    for oc in range(24):
        ssl = oc % 2
        base = ssl * 8192
        graw = p.big[:, base : base + 2048]
        gq = p.big[:, base + 2048 : base + 4096]
        sq = p.big[:, base + 4096 : base + 6144]
        g2 = p.big[:, base + 6144 : base + 8192]
        if oc >= 2:
            p.s_v.wait(sy, cv_ck[oc - 2])
        ld_ck[oc] = rsem[ssl].dma(sy.dma_start(out=graw, in_=p.gate_hbm[oc, :, :]))
        rsem[ssl].wait(v, ld_ck[oc])
        if oc >= 2:
            esem[ssl].wait(v, sp_ck[oc - 2])
        v.tensor_scalar(gq.bitcast(I32), graw, p.sml[:, 13:14], None, ALU.mult)
        v.tensor_scalar(gq, gq.bitcast(I32), p.sml[:, 12:13], None, ALU.mult)
        sv = p.s_v.v
        p.s_v.wait(s, sv)
        av = p.s_s.inc(s.activation(sq, gq, AF.Sigmoid))
        p.s_s.wait(v, av)
        v.tensor_scalar(sq.bitcast(I32), sq, 65536.0, None, ALU.mult)
        v.tensor_scalar(sq, sq.bitcast(I32), 65535.0, float(2.0 ** -16),
                        ALU.min, ALU.mult)
        v.tensor_mul(g2, gq, sq)
        v.tensor_reduce(p.tmpv[:, 0:1], g2, mybir.AxisListType.X, ALU.max,
                        apply_absolute_value=True)
        v.tensor_max(p.sml[:, 8:9], p.sml[:, 8:9], p.tmpv[:, 0:1])
        cv_ck[oc] = p.s_v.v
        p.s_v.wait(g, cv_ck[oc])
        sp_ck[oc] = esem[ssl].dma(g.dma_start(out=p.g2_hbm[oc, :, :], in_=g2))

    scope(p, 'ffn0_arC')
    scope(p, 'ffn0_arC')
    _allreduce_max(p, [8], [8])
    _scale_recip(p, 8, 14, 15)

    scope(p, 'ffn0_p3')
    p.s_ea.wait(sy)
    p.s_eb.wait(sy)
    v.memset(p.sml[:, 9:10], 0.0)
    for oc in range(24):
        ssl = oc % 2
        base = ssl * 8192
        g2r = p.big[:, base : base + 2048]
        upr = p.big[:, base + 2048 : base + 4096]
        orw = p.big[:, base + 4096 : base + 6144]
        if oc >= 2:
            p.s_v.wait(sy, cv_ck[oc - 2])
        rsem[ssl].dma(sy.dma_start(out=g2r, in_=p.g2_hbm[oc, :, :]))
        ld_ck[oc] = rsem[ssl].dma(sy.dma_start(out=upr, in_=p.up_hbm[oc, :, :]))
        rsem[ssl].wait(v, ld_ck[oc])
        if oc >= 2:
            esem[ssl].wait(v, sp_ck[oc - 2])
        v.tensor_scalar(g2r.bitcast(I32), g2r, p.sml[:, 15:16], None, ALU.mult)
        v.tensor_scalar(g2r, g2r.bitcast(I32), p.sml[:, 14:15], None, ALU.mult)
        v.tensor_scalar(upr.bitcast(I32), upr, p.sml[:, 11:12], None, ALU.mult)
        v.tensor_scalar(upr, upr.bitcast(I32), p.sml[:, 10:11], None, ALU.mult)
        v.tensor_mul(orw, g2r, upr)
        v.tensor_reduce(p.tmpv[:, 0:1], orw, mybir.AxisListType.X, ALU.max,
                        apply_absolute_value=True)
        v.tensor_max(p.sml[:, 9:10], p.sml[:, 9:10], p.tmpv[:, 0:1])
        cv_ck[oc] = p.s_v.v
        p.s_v.wait(g, cv_ck[oc])
        sp_ck[oc] = esem[ssl].dma(g.dma_start(out=p.o_hbm[oc, :, :], in_=orw))

    scope(p, 'ffn0_arD')
    scope(p, 'ffn0_arD')
    _allreduce_max(p, [9], [9])
    _scale_recip(p, 9, 16, 17)

    scope(p, 'ffn0_p4')
    p.s_ea.wait(sy)
    p.s_eb.wait(sy)
    oq = p.big[:, 0:12288]
    wdbs = [p.big[:, 12288 : 12288 + 3072], p.yoct[:, 0:3072]]
    prev_tv = p.s_t.v
    dtm_ck = {}
    for tt in range(4):
        if tt > 0:
            p.s_v.wait(sy)
            sy.wait_ge(p.s_t.h, prev_tv)
        for kc in range(24):
            ssl = kc % 2
            rsem[ssl].dma(sy.dma_start(
                out=oq[:, kc * 512 : kc * 512 + 512],
                in_=p.o_hbm[kc, :, tt * 512 : tt * 512 + 512]))
        rsem[0].wait(v)
        rsem[1].wait(v)
        for kc in range(24):
            dst = oq[:, kc * 512 : kc * 512 + 512]
            v.tensor_scalar(dst.bitcast(I32), dst, p.sml[:, 17:18], None, ALU.mult)
            v.tensor_scalar(dst, dst.bitcast(I32), p.sml[:, 16:17], None, ALU.mult)
        qv = p.s_v.v
        for oc in range(8):
            u4 = tt * 8 + oc
            wdb = wdbs[u4 % 2]
            wsm = [p.s_wa, p.s_wb][u4 % 2]
            if u4 >= 2:
                sy.wait_ge(p.s_t.h, dtm_ck[u4 - 2])
            src = p.wdq_hbm[:, oc * 128 : (oc + 1) * 128].rearrange(
                "(kc kp) o -> kp kc o", kp=128)
            wl = wsm.dma(sy.dma_start(
                out=wdb.rearrange("kp (kc o) -> kp kc o", kc=24), in_=src))
            bank = p.pb[2 + oc % 2]
            wsm.wait(t, wl)
            p.s_v.wait(t, qv)
            if not (tt == 0 and oc < 2):
                p.s_v.wait(t)
            for kc in range(24):
                mm = t.matmul(bank[:, 0:512],
                              wdb[:, kc * 128 : kc * 128 + 128],
                              oq[:, kc * 512 : kc * 512 + 512],
                              start=(kc == 0), stop=(kc == 23))
            tv = p.s_t.inc(mm)
            dtm_ck[u4] = tv
            prev_tv = tv
            p.s_t.wait(v, tv)
            rsl = p.R[:, oc * T + tt * 512 : oc * T + tt * 512 + 512]
            v.tensor_add(rsl, rsl, bank[:, 0:512])

    _dbg(p, "x2pre", p.R, [128, 8 * T])
    _post_ln(p, ln_idx=1)


def _l1_attn(p):
    sy, t, v, s, g = p.sy, p.t, p.v, p.s, p.g
    scope(p, "l1_attn")
    _fence_compute(p)
    LQ4 = 512
    abuf = p.big[:, 0:4096]
    bbuf = p.big[:, 4096:8192]
    hbuf = p.big[:, 8192:12288]
    wzb = p.big[:, 12288:13312]
    whb = p.big[:, 13312:14336]

    def shslot(fc, d):
        if d == 0:
            if fc < 8:
                return (p.yoct[:, fc * 512 : fc * 512 + 512] if fc < 8 else None)
        return None

    bzv = p.cw[:, 0:8]
    bhv = p.cw[:, 8:16]
    dcv = p.cw[:, 16:24]

    lv = p.s_ld.dma(sy.dma_start(
        out=bzv.rearrange("p (j o) -> p j o", o=1),
        in_=p.bz.rearrange("(j p) o -> p j o", p=128),
        allow_slow_non_contiguous=True))
    lv = p.s_ld.dma(sy.dma_start(
        out=bhv.rearrange("p (j o) -> p j o", o=1),
        in_=p.bh.rearrange("(j p) o -> p j o", p=128),
        allow_slow_non_contiguous=True))
    lv = p.s_ld.dma(sy.dma_start(
        out=dcv.rearrange("p (j o) -> p j o", o=1),
        in_=p.log_d.rearrange("(j p) o -> p j o", p=128),
        allow_slow_non_contiguous=True))
    p.mdf = p.gb0[:, 0:128]
    p.mwpb = p.gb0[:, 128:256]
    p.mwnb = p.gb0[:, 256:384]
    p.s_ld.dma(sy.dma_start(out=p.mdf, in_=p.mdiff))
    p.s_ld.dma(sy.dma_start(out=p.mwpb, in_=p.mwp))
    p.s_ld.dma(sy.dma_start(out=p.mwnb, in_=p.mwn))
    p.s_ld.wait(t)
    p.s_ld.wait(s)
    av = p.s_s.inc(s.activation(dcv, dcv, AF.Exp))
    v.memset(p.sml[:, 20:28], 0.0)
    wsem = [p.s_wa, p.s_wb]
    rsem = [p.s_ra, p.s_rb]
    wz2 = p.big[:, 14336:15360]
    wh2 = p.big[:, 15360:16384]
    tm_ck = {}
    sc_ck = {}
    scan_ck = 0
    diff_ck = 0
    for q in range(4):
        tq = q * LQ4
        for oc in range(8):
            u = q * 8 + oc
            wzsl = wzb if u % 2 == 0 else wz2
            whsl = whb if u % 2 == 0 else wh2
            if u >= 2:
                sy.wait_ge(p.s_t.h, tm_ck[u - 2])
            zsem = [p.s_wa, p.s_wc][u % 2]
            hsem = [p.s_wb, p.s_wd][u % 2]
            wl = zsem.dma(sy.dma_start(
                out=wzsl.rearrange("kp (kc o) -> kp kc o", kc=8),
                in_=p.wz[:, oc * 128 : (oc + 1) * 128].rearrange("(kc kp) o -> kp kc o", kp=128)))
            wl2 = hsem.dma(sy.dma_start(
                out=whsl.rearrange("kp (kc o) -> kp kc o", kc=8),
                in_=p.wh[:, oc * 128 : (oc + 1) * 128].rearrange("(kc kp) o -> kp kc o", kp=128)))
            zsem.wait(t, wl)
            hsem.wait(t, wl2)
            bz_bank = p.pb[2 + (u % 2)]
            bh_bank = p.pb[6 + (u % 2)]
            if u >= 2:
                p.s_s.wait(t, sc_ck[u - 2])
            if oc == 0:
                p.s_v.wait(t, diff_ck)
                p.s_v.wait(s, scan_ck)
            for kc in range(8):
                mm = t.matmul(bz_bank, wzsl[:, kc * 128 : kc * 128 + 128],
                              p.R[:, kc * T + tq : kc * T + tq + LQ4],
                              start=(kc == 0), stop=(kc == 7))
            tvz = p.s_t.inc(mm)
            for kc in range(8):
                mm = t.matmul(bh_bank, whsl[:, kc * 128 : kc * 128 + 128],
                              p.R[:, kc * T + tq : kc * T + tq + LQ4],
                              start=(kc == 0), stop=(kc == 7))
            tm_ck[u] = tvh = p.s_t.inc(mm)
            zsl = abuf[:, oc * 512 : oc * 512 + 512]
            hsl = bbuf[:, oc * 512 : oc * 512 + 512]
            p.s_t.wait(s, tvz)
            s.activation(zsl, bz_bank, AF.Sigmoid, bias=bzv[:, oc : oc + 1])
            p.s_t.wait(s, tvh)
            sv = p.s_s.inc(s.activation(hsl, bh_bank, AF.Identity, bias=bhv[:, oc : oc + 1]))
            sc_ck[u] = sv
            p.s_s.wait(v, sv)
            v.tensor_mul(hsl, zsl, hsl)
            v.tensor_scalar(zsl, zsl, -1.0, 1.0, ALU.mult, ALU.add)
        if "no_scan" in p.dbg:
            for fc in range(8):
                v.tensor_copy(hbuf[:, fc * 512 : fc * 512 + 512],
                              bbuf[:, fc * 512 : fc * 512 + 512])
        rsem[0].wait(v)
        rsem[1].wait(v)
        for fc in range(8 if "no_scan" not in p.dbg else 0):
            v.tensor_tensor_scan(hbuf[:, fc * 512 : fc * 512 + 512],
                                 abuf[:, fc * 512 : fc * 512 + 512],
                                 bbuf[:, fc * 512 : fc * 512 + 512],
                                 p.sml[:, 20 + fc : 21 + fc], ALU.mult, ALU.add)
            v.tensor_copy(p.sml[:, 20 + fc : 21 + fc],
                          hbuf[:, fc * 512 + 511 : fc * 512 + 512])
        hv = p.s_v.v
        scan_ck = hv
        p.s_v.wait(t, hv)
        p.s_s.wait(t)
        for fc in range(8):
            hsl = hbuf[:, fc * 512 : fc * 512 + 512]
            prv = hbuf[:, ((fc - 1) % 8) * 512 : ((fc - 1) % 8) * 512 + 512]
            nxt = hbuf[:, ((fc + 1) % 8) * 512 : ((fc + 1) % 8) * 512 + 512]
            bank = p.pb[fc]
            t.matmul(bank, p.mdf, hsl, start=True, stop=False)
            t.matmul(bank, p.mwpb, prv, start=False, stop=False)
            dtv = p.s_t.inc(t.matmul(bank, p.mwnb, nxt, start=False, stop=True))
        p.s_t.wait(v, dtv)
        for fc in range(8):
            hsl = hbuf[:, fc * 512 : fc * 512 + 512]
            v.scalar_tensor_tensor(hsl, p.pb[fc], dcv[:, fc : fc + 1], hsl,
                                   ALU.mult, ALU.add)
            rsl = p.R[:, fc * T + tq : fc * T + tq + LQ4]
            v.tensor_add(rsl, rsl, hsl)
        diff_ck = p.s_v.v

    _dbg(p, "x3pre", p.R, [128, 8 * T])
    _post_ln(p, ln_idx=2)


def _l1_ffn(p):
    sy, t, v, s, g = p.sy, p.t, p.v, p.s, p.g
    scope(p, "l1_ffn")
    _fence_compute(p)
    wd1 = p.big[:, 0:8192]
    y1 = p.big[:, 8192:12288]
    srst = p.big[:, 12288:14336]
    lra = p.big[:, 14336:14848]
    lab = p.big[:, 14848:15104]
    lbb = p.big[:, 15104:16128]

    def wd2(kc):
        if kc < 4:
            return p.yoct[:, kc * 1024 : kc * 1024 + 1024]
        if kc < 6:
            return p.ysq[:, (kc - 4) * 1024 : (kc - 4) * 1024 + 1024]
        return p.state[:, (kc - 6) * 1024 : (kc - 6) * 1024 + 1024]

    qb1v = p.cw[:, 24:32]
    qb2v = p.cw[:, 32:40]
    gamv = p.cw[:, 40:48]
    betv = p.cw[:, 48:56]
    wsem = [p.s_wa, p.s_wb]
    stats = p.pb[1]
    p.s_v.wait(g)
    p.s_s.wait(g)
    for fc in range(8):
        p.s_ea.dma(g.dma_start(out=p.z_hbm[fc, :, :], in_=p.R[:, fc * T : (fc + 1) * T]))
    p.s_ea.wait(v)
    for blk in range(3):
        i0, i1 = 2 * blk, 2 * blk + 1
        p.s_s.wait(sy)
        p.s_v.wait(sy)
        p.s_t.wait(sy)
        lv = p.s_ld.dma(sy.dma_start(
            out=qb1v.rearrange("p (j o) -> p j o", o=1),
            in_=p.qb[:, i0 : i0 + 1].rearrange("(j p) o -> p j o", p=128),
            allow_slow_non_contiguous=True))
        lv = p.s_ld.dma(sy.dma_start(
            out=qb2v.rearrange("p (j o) -> p j o", o=1),
            in_=p.qb[:, i1 : i1 + 1].rearrange("(j p) o -> p j o", p=128),
            allow_slow_non_contiguous=True))
        lv = p.s_ld.dma(sy.dma_start(
            out=gamv.rearrange("p (j o) -> p j o", o=1),
            in_=p.gam[:, blk : blk + 1].rearrange("(j p) o -> p j o", p=128),
            allow_slow_non_contiguous=True))
        lv = p.s_ld.dma(sy.dma_start(
            out=betv.rearrange("p (j o) -> p j o", o=1),
            in_=p.bet[:, blk : blk + 1].rearrange("(j p) o -> p j o", p=128),
            allow_slow_non_contiguous=True))
        lv = p.s_ld.dma(sy.dma_start(
            out=lab.rearrange("p (kc r) -> p kc r", kc=8),
            in_=p.la_t[i1].rearrange("(kc p) r -> p kc r", p=128)))
        lv = p.s_ld.dma(sy.dma_start(out=lbb[0:32, :], in_=p.lb_t[i1]))
        lab1 = p.dt_raw[:, 0:256]
        lbb1 = p.dt_raw[:, 256:1280]
        lv = p.s_ld.dma(sy.dma_start(
            out=lab1.rearrange("p (kc r) -> p kc r", kc=8),
            in_=p.la_t[i0].rearrange("(kc p) r -> p kc r", p=128)))
        lv = p.s_ld.dma(sy.dma_start(out=lbb1[0:32, :], in_=p.lb_t[i0]))

        p.s_t.wait(sy)
        for un in range(16):
            mat, kc = divmod(un, 8)
            dstw = wd1[:, kc * 1024 : kc * 1024 + 1024] if mat == 0 else wd2(kc)
            ssl = un % 2
            srb = srst[:, ssl * 1024 : ssl * 1024 + 1024]
            qi = i0 if mat == 0 else i1
            if un >= 2:
                p.s_v.wait(sy)
            wl = wsem[ssl].dma(sy.dma_start(
                out=dstw.bitcast(I32), in_=p.qw_t[qi, kc * 128 : (kc + 1) * 128, :]))
            wl = wsem[ssl].dma(sy.dma_start(
                out=srb, in_=p.srep_t[qi, kc * 128 : (kc + 1) * 128, :]))
            wsem[ssl].wait(v, wl)
            v.tensor_scalar(dstw, dstw.bitcast(I32), float(np.float32(1.0 / 7.5)),
                            -1.0, ALU.mult, ALU.add)
            v.tensor_mul(dstw, dstw, srb)
        dqv = p.s_v.v
        p.s_ld.wait(t)
        p.s_ld.wait(v)
        p.s_ld.wait(s)
        p.s_v.wait(t, dqv)

        prev_tv = p.s_t.v
        l1sc_ck = {}
        l2v_ck = {}
        for tt in range(4):
            ts0 = tt * 512
            for kc in range(8):
                mm = t.matmul(p.pb[5][0:32, :], lab1[:, kc * 32 : kc * 32 + 32],
                              p.R[:, kc * T + ts0 : kc * T + ts0 + 512],
                              start=(kc == 0), stop=(kc == 7))
            ltv = p.s_t.inc(mm)
            p.s_t.wait(v, ltv)
            v.tensor_copy(lra[0:32, :], p.pb[5][0:32, :])
            lrv = p.s_v.v
            p.s_v.wait(t, lrv)
            for oc in range(8):
                bank = p.pb[2 + oc % 2]
                if oc >= 2:
                    p.s_s.wait(t, l1sc_ck[oc - 2])
                elif tt > 0:
                    p.s_v.wait(t)
                for kc in range(8):
                    t.matmul(bank, wd1[:, kc * 1024 + oc * 128 : kc * 1024 + oc * 128 + 128],
                             p.R[:, kc * T + ts0 : kc * T + ts0 + 512],
                             start=(kc == 0), stop=False)
                mm = t.matmul(bank, lbb1[0:32, oc * 128 : oc * 128 + 128],
                              lra[0:32, :], start=False, stop=True)
                tv = p.s_t.inc(mm)
                p.s_t.wait(s, tv)
                sv = p.s_s.inc(s.activation(y1[:, oc * 512 : oc * 512 + 512], bank,
                                            AF.Identity, bias=qb1v[:, oc : oc + 1]))
                l1sc_ck[oc] = sv
            if blk == 0 and tt == 0:
                _dbg(p, "y1b0", y1, [128, 4096])
            stats2 = p.pb[4]
            sq_ck = {}
            for oc in range(8):
                ysl = y1[:, oc * 512 : oc * 512 + 512]
                ssl = oc % 2
                sq = srst[:, ssl * 1024 : ssl * 1024 + 512]
                if oc >= 2:
                    p.s_t.wait(s, sq_ck[oc - 2])
                sv = p.s_s.inc(s.activation(sq, ysl, AF.Square))
                p.s_t.inc(t.matmul(stats[0:1, 0:512], p.ones[:, 0:1], ysl,
                                   start=(oc == 0), stop=(oc == 7)))
                p.s_s.wait(t, sv)
                sq_ck[oc] = tv = p.s_t.inc(t.matmul(
                    stats2[0:1, 0:512], p.ones[:, 0:1], sq,
                    start=(oc == 0), stop=(oc == 7)))
            p.s_t.wait(v, tv)
            m = p.rows[0:1, 0:512]
            var = p.rows[0:1, 512:1024]
            v.tensor_scalar(m, stats[0:1, 0:512], 1.0 / D, None, ALU.mult)
            v.tensor_scalar(var, stats2[0:1, 0:512], 1.0 / D, None, ALU.mult)
            v.tensor_mul(p.tmpv[0:1, 0:512], m, m)
            v.tensor_sub(var, var, p.tmpv[0:1, 0:512])
            v.tensor_scalar(var, var, 1.0, EPS, ALU.mult, ALU.add)
            rv = p.s_v.inc(v.reciprocal(var, var))
            p.s_v.wait(s, rv)
            sv = p.s_s.inc(s.activation(var, var, AF.Sqrt))
            p.s_s.wait(g, sv)
            p.s_v.wait(g)
            wv = p.s_bw.dma(g.dma_start(out=p.bscr[16:17, 0:512], in_=m))
            wv = p.s_bw.dma(g.dma_start(out=p.bscr[18:19, 0:512], in_=var))
            p.s_bw.wait(g, wv)
            p.s_bc.dma(g.dma_start(out=p.bc1[:, 0:512],
                                   in_=p.bscr[16:17, 0:512].broadcast_to([128, 512])))
            bv = p.s_bc.dma(g.dma_start(out=p.bc2[:, 0:512],
                                        in_=p.bscr[18:19, 0:512].broadcast_to([128, 512])))
            p.s_bc.wait(v, bv)
            for oc in range(8):
                ysl = y1[:, oc * 512 : oc * 512 + 512]
                v.tensor_sub(ysl, ysl, p.bc1[:, 0:512])
                v.tensor_mul(ysl, ysl, p.bc2[:, 0:512])
                v.tensor_scalar(ysl, ysl, gamv[:, oc : oc + 1], betv[:, oc : oc + 1],
                                ALU.mult, ALU.add)
            gv = p.s_v.v
            p.s_v.wait(s, gv)
            for oc in range(8):
                ysl = y1[:, oc * 512 : oc * 512 + 512]
                sv = p.s_s.inc(s.activation(ysl, ysl, AF.Gelu))
            p.s_s.wait(t, sv)
            p.s_s.wait(v, sv)
            if blk == 0 and tt == 0:
                _dbg(p, "y1g_b0", y1, [128, 4096])

            for kc in range(8):
                mm = t.matmul(p.pb[5][0:32, :], lab[:, kc * 32 : kc * 32 + 32],
                              y1[:, kc * 512 : kc * 512 + 512],
                              start=(kc == 0), stop=(kc == 7))
            ltv = p.s_t.inc(mm)
            p.s_t.wait(v, ltv)
            v.tensor_copy(lra[0:32, :], p.pb[5][0:32, :])
            lrv = p.s_v.v
            p.s_v.wait(t, lrv)
            for oc in range(8):
                bank = p.pb[2 + oc % 2]
                if oc >= 2:
                    p.s_v.wait(t, l2v_ck[oc - 2])
                else:
                    p.s_s.wait(t, l1sc_ck[6 + oc])
                for kc in range(8):
                    t.matmul(bank, wd2(kc)[:, oc * 128 : oc * 128 + 128],
                             y1[:, kc * 512 : kc * 512 + 512],
                             start=(kc == 0), stop=False)
                mm = t.matmul(bank, lbb[0:32, oc * 128 : oc * 128 + 128],
                              lra[0:32, :], start=False, stop=True)
                tv = p.s_t.inc(mm)
                prev_tv = tv
                p.s_t.wait(v, tv)
                rsl = p.R[:, oc * T + ts0 : oc * T + ts0 + 512]
                v.scalar_tensor_tensor(p.tmpv[:, 0:512], bank, 1.0, rsl,
                                       ALU.mult, ALU.add)
                v.tensor_scalar(p.tmpv[:, 0:512], p.tmpv[:, 0:512],
                                qb2v[:, oc : oc + 1], None, ALU.add)
                v.tensor_copy(rsl, p.tmpv[:, 0:512])
                l2v_ck[oc] = p.s_v.v
        _dbg(p, f"rb{blk}", p.R, [128, 8 * T])

    p.s_t.wait(sy)
    p.s_v.wait(sy)
    p.s_s.wait(sy)
    for fc in range(8):
        lv = p.s_ra.dma(sy.dma_start(out=p.big[:, fc * T : fc * T + T],
                                     in_=p.z_hbm[fc, :, :]))
    p.s_ra.wait(v)
    for fc in range(8):
        rsl = p.R[:, fc * T : (fc + 1) * T]
        v.tensor_add(rsl, rsl, p.big[:, fc * T : fc * T + T])

    _post_ln(p, ln_idx=3)


def _stage_inputs(inputs, names):
    u = np.ascontiguousarray(np.asarray(inputs["L0_attn_u"], dtype=np.float32))
    w_in_t = np.zeros((D, DPROJ_PAD), np.float32)
    w_in_t[:, :DPROJ] = np.asarray(inputs["L0_attn_W_in"], np.float32).T
    common = {
        "w_in_t": w_in_t,
        "conv_w": np.ascontiguousarray(np.asarray(inputs["L0_attn_conv_w"], np.float32)),
        "conv_b": np.asarray(inputs["L0_attn_conv_b"], np.float32)[:, None].copy(),
        "dt_bias": np.asarray(inputs["L0_attn_dt_bias"], np.float32)[:, None].copy(),
        "a_log": np.asarray(inputs["L0_attn_A_log"], np.float32)[:, None].copy(),
        "d_rep": np.repeat(np.asarray(inputs["L0_attn_D"], np.float32), HP)[:, None].copy(),
        "nw_in": np.asarray(inputs["L0_attn_norm_weight"], np.float32)[:, None].copy(),
        "w_out_t": np.ascontiguousarray(np.asarray(inputs["L0_attn_W_out"], np.float32).T),
        "ident": np.eye(128, dtype=np.float32),
        "mask01": np.triu(np.ones((128, 128), np.float32)).copy(),
    }
    if "w_up" in names:
        common["w_up"] = np.ascontiguousarray(np.asarray(inputs["L0_ffn_w_up"], np.float32))
        common["w_gate"] = np.ascontiguousarray(np.asarray(inputs["L0_ffn_w_gate"], np.float32))
        common["w_down"] = np.ascontiguousarray(np.asarray(inputs["L0_ffn_w_down"], np.float32))
    if "wz" in names:
        common["wz"] = np.ascontiguousarray(np.asarray(inputs["L1_attn_Wz"], np.float32))
        common["wh"] = np.ascontiguousarray(np.asarray(inputs["L1_attn_Wh"], np.float32))
        common["bz"] = np.asarray(inputs["L1_attn_bz"], np.float32)[:, None].copy()
        common["bh"] = np.asarray(inputs["L1_attn_bh"], np.float32)[:, None].copy()
        common["log_d"] = np.asarray(inputs["L1_attn_log_D"], np.float32)[:, None].copy()
        md = np.zeros((128, 128), np.float32)
        for i_ in range(128):
            md[i_, i_] = -2.0
            if i_ + 1 < 128:
                md[i_, i_ + 1] = 1.0
                md[i_ + 1, i_] = 1.0
        common["mdiff"] = md
        mwp_ = np.zeros((128, 128), np.float32)
        mwp_[127, 0] = 1.0
        common["mwp"] = mwp_
        mwn_ = np.zeros((128, 128), np.float32)
        mwn_[0, 127] = 1.0
        common["mwn"] = mwn_
    if "qw_t" in names:
        qw = np.asarray(inputs["L1_ffn_qweights"], np.int32)
        common["qw_t"] = np.ascontiguousarray(qw.transpose(0, 2, 1))
        sc = np.asarray(inputs["L1_ffn_scales"], np.float32)
        common["srep_t"] = np.ascontiguousarray(
            np.repeat(sc, 16, axis=2).transpose(0, 2, 1))
        common["qb"] = np.ascontiguousarray(np.asarray(inputs["L1_ffn_biases"], np.float32).T)
        common["la_t"] = np.ascontiguousarray(
            np.asarray(inputs["L1_ffn_lora_a"], np.float32).transpose(0, 2, 1))
        common["lb_t"] = np.ascontiguousarray(
            np.asarray(inputs["L1_ffn_lora_b"], np.float32).transpose(0, 2, 1))
        common["gam"] = np.ascontiguousarray(np.asarray(inputs["L1_ffn_ln_gamma"], np.float32).T)
        common["bet"] = np.ascontiguousarray(np.asarray(inputs["L1_ffn_ln_beta"], np.float32).T)
    in_maps = []
    for c in range(NCORES):
        m = {k: vv for k, vv in common.items() if k in names}
        m["u"] = np.ascontiguousarray(u[c].T)
        in_maps.append(m)
    return in_maps


def kernel(**inputs):
    p = build(_FULL)
    in_maps = _stage_inputs(inputs, set(p.in_names))
    res = run_bass_kernel_spmd(p.nc, in_maps, core_ids=list(range(NCORES)))
    outs = [np.ascontiguousarray(res.results[c]["out"].T) for c in range(NCORES)]
    return np.stack(outs, axis=0)


# version 05
# speedup vs baseline: 1.1987x; 1.0257x over previous
import numpy as np
import concourse.bass as bass
import concourse.mybir as mybir
from concourse.bass_utils import run_bass_kernel_spmd

F32 = mybir.dt.float32
F32R = mybir.dt.float32r
I32 = mybir.dt.int32
AF = mybir.ActivationFunctionType
ALU = mybir.AluOpType

NCORES = 8
T = 2048
D = 1024
DIN = 2048
NH = 32
HP = 64
DSTATE = 128
CONV_DIM = DIN + 2 * DSTATE
DPROJ = 2 * DIN + 2 * DSTATE + NH
DPROJ_PAD = 35 * 128
LO = 256
LC = 128
NO = T // LO
IFF = 3072
QMAX = 32767.0
EPS = 1e-5

_FULL = 4


class RecInstr:
    __slots__ = ("incs",)

    def __init__(self):
        self.incs = []

    def then_inc(self, sem_h, n):
        self.incs.append((sem_h, n))
        return self


_NOCHAIN = {"wait_ge", "dma_start", "indirect_dma_start", "collective_compute",
            "sem_inc", "sem_clear", "nop"}


def scope(p, name):
    for rec in (p.sy, p.t, p.v, p.s, p.g):
        rec.cmds.append(("__scope__", (name,), {}, RecInstr()))


class Rec:

    def __init__(self, name, chain_sem=None):
        self.name = name
        self.cmds = []
        self.chain = chain_sem
        self.ccount = 0

    def __getattr__(self, attr):
        def call(*a, **k):
            h = RecInstr()
            if self.chain is not None and attr not in _NOCHAIN:
                if self.ccount > 0:
                    self.cmds.append(("wait_ge", (self.chain, self.ccount), {}, RecInstr()))
                h.incs.append((self.chain, 1))
                self.ccount += 1
            self.cmds.append((attr, a, k, h))
            return h
        return call


def _replay(nc, eng, cmds):
    import contextlib
    stack = contextlib.ExitStack()
    with stack:
        for attr, a, k, h in cmds:
            if attr == "__scope__":
                stack.close()
                stack = contextlib.ExitStack()
                if a[0] is not None:
                    stack.enter_context(nc.named_scope(a[0]))
                continue
            r = getattr(eng, attr)(*a, **k)
            for sem_h, n in h.incs:
                r.then_inc(sem_h, n)


class Sem:
    def __init__(self, nc, name):
        self.h = nc.alloc_semaphore(name)
        self.v = 0

    def inc(self, instr, n=1):
        instr.then_inc(self.h, n)
        self.v += n
        return self.v

    def dma(self, instr):
        return self.inc(instr, 16)

    def wait(self, eng, val=None):
        eng.wait_ge(self.h, self.v if val is None else val)


class ChainSem:

    def __init__(self, rec):
        self.rec = rec
        self.h = rec.chain

    @property
    def v(self):
        return self.rec.ccount

    def inc(self, instr, n=1):
        return self.rec.ccount

    def wait(self, eng, val=None):
        eng.wait_ge(self.h, self.rec.ccount if val is None else val)


def rep(ap, k):
    return ap.unsqueeze(2).broadcast_to([ap.shape[0], ap.shape[1], k])


class P:
    pass


def _alloc(p, upto, dbg):
    nc = p.nc
    p.upto = upto
    p.dbg = set(dbg)

    def ein(name, shape, dt=F32):
        return nc.dram_tensor(name, shape, dt, kind="ExternalInput").ap()

    p.in_names = []

    def reg(name, shape, dt=F32):
        p.in_names.append(name)
        return ein(name, shape, dt)

    p.u = reg("u", [D, T])
    p.w_in_t = reg("w_in_t", [D, DPROJ_PAD])
    p.conv_w = reg("conv_w", [CONV_DIM, 4])
    p.conv_b = reg("conv_b", [CONV_DIM, 1])
    p.dt_bias = reg("dt_bias", [NH, 1])
    p.a_log = reg("a_log", [NH, 1])
    p.d_rep = reg("d_rep", [DIN, 1])
    p.nw_in = reg("nw_in", [DIN, 1])
    p.w_out_t = reg("w_out_t", [DIN, D])
    p.ident = reg("ident", [128, 128])
    p.mask01 = reg("mask01", [128, 128])
    if upto >= 2:
        p.w_up = reg("w_up", [IFF, D])
        p.w_gate = reg("w_gate", [IFF, D])
        p.w_down = reg("w_down", [D, IFF])
    if upto >= 3:
        p.wz = reg("wz", [D, D])
        p.wh = reg("wh", [D, D])
        p.bz = reg("bz", [D, 1])
        p.bh = reg("bh", [D, 1])
        p.log_d = reg("log_d", [D, 1])
        p.mdiff = reg("mdiff", [128, 128])
        p.mwp = reg("mwp", [128, 128])
        p.mwn = reg("mwn", [128, 128])
    if upto >= 4:
        p.qw_t = reg("qw_t", [6, D, D], I32)
        p.srep_t = reg("srep_t", [6, D, D])
        p.qb = reg("qb", [D, 6])
        p.la_t = reg("la_t", [6, D, 32])
        p.lb_t = reg("lb_t", [6, 32, D])
        p.gam = reg("gam", [D, 3])
        p.bet = reg("bet", [D, 3])

    p.out = nc.dram_tensor("out", [D, T], F32, kind="ExternalOutput").ap()

    p.z_hbm = nc.dram_tensor("z_hbm", [16, 128, T], F32).ap()
    p.xbc_hbm = nc.dram_tensor("xbc_hbm", [18, 128, T], F32).ap()
    p.y_hbm = nc.dram_tensor("y_hbm", [16, 128, T], F32).ap()
    p.bscr = nc.dram_tensor("bscr", [34, 512], F32).ap()
    if upto >= 2:
        p.up_hbm = nc.dram_tensor("up_hbm", [24, 128, T], F32).ap()
        p.gate_hbm = nc.dram_tensor("gate_hbm", [24, 128, T], F32).ap()
        p.g2_hbm = nc.dram_tensor("g2_hbm", [24, 128, T], F32).ap()
        p.o_hbm = nc.dram_tensor("o_hbm", [24, 128, T], F32).ap()
        p.wdq_hbm = nc.dram_tensor("wdq_hbm", [IFF, D], F32).ap()
        p.cc_in = nc.dram_tensor("cc_in", [1, 4], F32)
        p.cc_out = nc.dram_tensor("cc_out", [1, 4], F32)

    def sb(name, free):
        return nc.alloc_sbuf_tensor(name, [128, free], F32).ap()

    def addr():
        return (nc.sbuf_base + 31) // 32 * 32

    a_R = addr()
    p.R = sb("R", 8 * T)
    a_big = addr()
    p.big = sb("big", 16384)
    p.gbs = sb("gbs", 4096)
    p.gb0 = p.gbs[:, 0:1024]
    p.gb1 = p.gbs[:, 1024:2048]
    p.gb2 = p.gbs[:, 2048:3072]
    p.xd = sb("xd", 512)
    p.state = sb("state", DIN)
    a_dtr = addr()
    p.dt_raw = sb("dt_raw", T)
    a_rsa = addr()
    p.rstd_all = sb("rstd_all", T)
    p.ysq = sb("ysq", DIN)
    p.yoct = sb("yoct", 16 * LO)
    p.idn = sb("idn", 128)
    p.msk = sb("msk", 128)
    p.nbt = sb("nbt", 32)
    p.dst = sb("dst", 32)
    p.btm = sb("btm", 128)
    p.cbm = sb("cbm", 128)
    p.ones = sb("ones", 8)
    p.sml = sb("sml", 64)
    p.rows = sb("rows", 1024)
    p.bc1 = p.gbs[:, 0:2048]
    p.bc2 = p.gbs[:, 2048:4096]
    p.tmpv = sb("tmpv", 1024)
    p.cw = sb("cw", 96)
    p.dr = sb("dr", 16)
    p.nw = sb("nw", 16)

    p.pb = [nc.alloc_psum_tensor(f"pb{i}", [128, 512], F32).ap() for i in range(8)]

    p.xqh_r = nc.alloc_sbuf_tensor_at("xqh_r", [128, 8 * T], F32R, offset=a_R).ap()
    p.xql_r = nc.alloc_sbuf_tensor_at("xql_r", [128, 16384], F32R, offset=a_big).ap()
    p.wlb_r = nc.alloc_sbuf_tensor_at("wlb_r", [128, T], F32R, offset=a_dtr).ap()
    p.whb_r = nc.alloc_sbuf_tensor_at("whb_r", [128, T], F32R, offset=a_rsa).ap()

    for s in ["ld", "t", "o2", "wa", "wb", "wc", "wd", "ea", "eb", "ra", "rb",
              "zc", "bw", "bc", "be", "yw", "yb", "cc", "ci", "co", "rr"]:
        setattr(p, "s_" + s, Sem(nc, "sem_" + s))


def build(upto=_FULL, dbg=()):
    p = P()
    p.nc = bass.Bass("TRN2", target_bir_lowering=False)
    _alloc(p, upto, dbg)
    vch = p.nc.alloc_semaphore("sem_vch")
    sch = p.nc.alloc_semaphore("sem_sch")
    gch = p.nc.alloc_semaphore("sem_gch")
    p.sy, p.t, p.v, p.s, p.g = (Rec("sy"), Rec("t"), Rec("v", vch), Rec("s", sch),
                                Rec("g", gch))
    p.s_v = ChainSem(p.v)
    p.s_s = ChainSem(p.s)
    p.s_gc = ChainSem(p.g)
    _emit(p)
    with p.nc.Block() as block:
        @block.sync
        def _(e):
            _replay(p.nc, e, p.sy.cmds)

        @block.tensor
        def _(e):
            _replay(p.nc, e, p.t.cmds)

        @block.vector
        def _(e):
            _replay(p.nc, e, p.v.cmds)

        @block.scalar
        def _(e):
            _replay(p.nc, e, p.s.cmds)

        @block.gpsimd
        def _(e):
            _replay(p.nc, e, p.g.cmds)
    return p


def _dbg(p, name, ap, shape):
    if name in p.dbg:
        o = p.nc.dram_tensor("dbg_" + name, shape, F32, kind="ExternalOutput").ap()
        p.s_v.wait(p.sy)
        p.s_s.wait(p.sy)
        p.s_o2.dma(p.sy.dma_start(out=o, in_=ap))


def _fence_compute(p):
    for e in (p.t, p.v, p.s, p.sy, p.g):
        for s in (p.s_t, p.s_v, p.s_s, p.s_gc, p.s_ld, p.s_wa, p.s_wb, p.s_wc,
                  p.s_wd, p.s_ea, p.s_eb, p.s_ra, p.s_rb, p.s_zc, p.s_bw, p.s_bc,
                  p.s_yw, p.s_yb):
            s.wait(e)


def _emit(p):
    sy, t, v, s, g = p.sy, p.t, p.v, p.s, p.g

    p.s_ld.dma(sy.dma_start(out=p.idn, in_=p.ident))
    p.s_ld.dma(sy.dma_start(out=p.msk, in_=p.mask01))
    for fc in range(8):
        p.s_ld.dma(sy.dma_start(out=p.R[:, fc * T : (fc + 1) * T],
                                in_=p.u[fc * 128 : (fc + 1) * 128, :]))
    p.s_ld.dma(sy.dma_start(out=p.sml[0:NH, 0:1], in_=p.dt_bias))
    p.s_ld.dma(sy.dma_start(out=p.sml[0:NH, 1:2], in_=p.a_log))
    p.s_ld.dma(sy.dma_start(
        out=p.cw[:, 0:72].rearrange("p (j w) -> p j w", j=18),
        in_=p.conv_w.rearrange("(j p) w -> p j w", p=128),
        allow_slow_non_contiguous=True))
    p.s_ld.dma(sy.dma_start(
        out=p.cw[:, 72:90].rearrange("p (j o) -> p j o", o=1),
        in_=p.conv_b.rearrange("(j p) o -> p j o", p=128),
        allow_slow_non_contiguous=True))
    p.s_ld.dma(sy.dma_start(
        out=p.dr.rearrange("p (j o) -> p j o", o=1),
        in_=p.d_rep.rearrange("(j p) o -> p j o", p=128),
        allow_slow_non_contiguous=True))
    p.s_ld.dma(sy.dma_start(
        out=p.nw.rearrange("p (j o) -> p j o", o=1),
        in_=p.nw_in.rearrange("(j p) o -> p j o", p=128),
        allow_slow_non_contiguous=True))
    p.s_ld.wait(v)
    p.s_v.inc(v.memset(p.ones, 1.0))
    p.s_ld.wait(s)
    xv = p.s_s.inc(s.activation(p.sml[0:NH, 2:3], p.sml[0:NH, 1:2], AF.Exp))
    p.s_s.wait(s, xv)
    p.s_s.inc(s.mul(p.sml[0:NH, 2:3], p.sml[0:NH, 2:3], -1.0))
    p.negA = p.sml[0:NH, 2:3]
    p.dtbv = p.sml[0:NH, 0:1]
    p.s_v.wait(t)
    p.s_s.wait(t)

    _mamba(p)
    if p.upto >= 2:
        _ffn0(p)
    if p.upto >= 3:
        _l1_attn(p)
    if p.upto >= 4:
        _l1_ffn(p)

    p.s_v.wait(sy)
    p.s_s.wait(sy)
    for fc in range(8):
        p.s_o2.dma(sy.dma_start(out=p.out[fc * 128 : (fc + 1) * 128, :],
                                in_=p.R[:, fc * T : (fc + 1) * T]))
    p.s_o2.wait(sy)


def _mamba(p):
    sy, t, v, s, g = p.sy, p.t, p.v, p.s, p.g

    scope(p, 'mamba_inproj')
    wbuf = p.big[:, 0:2048]
    tmpe = p.big[:, 2048:3072]
    wl_ck = []
    tm_ck = [0] * 200
    ev_ck = [0] * 200
    sp_ck = [0] * 200
    wsem = [p.s_wa, p.s_wb]
    esem = [p.s_ea, p.s_eb]
    p.s_v.inc(v.memset(p.state, 0.0))
    for oc in range(35):
        wb = wbuf[:, (oc % 2) * 1024 : (oc % 2) * 1024 + 1024]
        if oc >= 2:
            sy.wait_ge(p.s_t.h, tm_ck[(oc - 2) * 4 + 3])
        src = p.w_in_t[:, oc * 128 : (oc + 1) * 128].rearrange("(kc kp) o -> kp kc o", kp=128)
        wv = wsem[oc % 2].dma(sy.dma_start(out=wb.rearrange("kp (kc o) -> kp kc o", kc=8), in_=src))
        wl_ck.append(wv)
        for tt in range(4):
            i = oc * 4 + tt
            bank = p.pb[i % 2]
            wsem[oc % 2].wait(t, wl_ck[oc])
            if i >= 2:
                p.s_v.wait(t, ev_ck[i - 2])
            for kc in range(8):
                mm = t.matmul(bank, wb[:, kc * 128 : (kc + 1) * 128],
                              p.R[:, kc * T + tt * 512 : kc * T + tt * 512 + 512],
                              start=(kc == 0), stop=(kc == 7))
            tm_ck[i] = p.s_t.inc(mm)
            p.s_t.wait(v, tm_ck[i])
            if oc < 34:
                slot = i % 2
                te = tmpe[:, slot * 512 : slot * 512 + 512]
                if i >= 2:
                    esem[slot].wait(v, sp_ck[i - 2])
                ev_ck[i] = p.s_v.inc(v.tensor_copy(te, bank))
                p.s_v.wait(g, ev_ck[i])
                dst = (p.z_hbm if oc < 16 else p.xbc_hbm)
                row = oc if oc < 16 else oc - 16
                sp_ck[i] = esem[slot].dma(
                    g.dma_start(out=dst[row, :, tt * 512 : tt * 512 + 512], in_=te))
            else:
                ev_ck[i] = p.s_v.inc(v.tensor_copy(p.dt_raw[:, tt * 512 : tt * 512 + 512], bank))

    p.s_ea.wait(sy)
    p.s_eb.wait(sy)
    _dbg(p, "dt_raw", p.dt_raw, [128, T])

    scope(p, 'mamba_chunks')
    xbcs = p.big[:, 0 : 18 * LO]
    zc = p.big[:, 4608 : 4608 + 2048]
    xtm = p.big[:, 6656 : 6656 + 2048]
    dtb = p.big[0:NH, 8704 : 8704 + 8 * LO]
    raw = p.big[:, 10752 : 10752 + 2 * 260]
    for o in range(NO):
        to = o * LO
        dts = dtb[:, 0 * LO : 1 * LO]
        lndt = dtb[:, 1 * LO : 2 * LO]
        dA = dtb[:, 2 * LO : 3 * LO]
        cA = dtb[:, 3 * LO : 4 * LO]
        nb = dtb[:, 4 * LO : 5 * LO]
        dsb = dtb[:, 5 * LO : 6 * LO]
        t32 = dtb[:, 6 * LO : 7 * LO]
        p.s_v.wait(s)
        p.s_t.wait(s)
        av = p.s_s.inc(s.activation(t32, p.dt_raw[0:NH, to : to + LO], AF.Exp, bias=p.dtbv))
        p.s_s.wait(s, av)
        av = p.s_s.inc(s.activation(dts, t32, AF.Ln, bias=1.0))
        p.s_s.wait(s, av)
        sv = p.s_s.inc(s.activation(lndt, dts, AF.Ln))
        p.s_s.wait(v, sv)
        vv = p.s_v.inc(v.tensor_scalar(dA, dts, p.negA, None, ALU.mult))
        p.s_v.wait(v, vv)
        for cc in range(2):
            sl = slice(cc * LC, (cc + 1) * LC)
            vv = p.s_v.inc(v.tensor_tensor_scan(
                cA[:, sl], p.ones[0:NH, 0:1].broadcast_to([NH, LC]),
                dA[:, sl], 0.0, ALU.mult, ALU.add))
        p.s_v.wait(v, vv)
        v.tensor_sub(nb, lndt, cA)
        for cc in range(2):
            sl = slice(cc * LC, (cc + 1) * LC)
            v.tensor_scalar(t32[:, sl], cA[:, sl], -1.0,
                            cA[:, cc * LC + LC - 1 : cc * LC + LC], ALU.mult, ALU.add)
            v.tensor_add(t32[:, sl], t32[:, sl], lndt[:, sl])
        dv = p.s_v.v
        p.s_v.wait(s, dv)
        p.s_s.inc(s.activation(dsb, t32, AF.Exp))

        cv_ck = [0] * 18
        rsem = [p.s_ra, p.s_rb]
        for j in range(18):
            slot = j % 2
            rb = raw[:, slot * 260 : slot * 260 + 260]
            if j >= 2:
                p.s_v.wait(sy, cv_ck[j - 2])
            if o == 0:
                lv = rsem[slot].dma(sy.dma_start(out=rb[:, 3:259], in_=p.xbc_hbm[j, :, 0:LO]))
            else:
                lv = rsem[slot].dma(sy.dma_start(out=rb[:, 0:259],
                                                 in_=p.xbc_hbm[j, :, to - 3 : to + LO]))
            rsem[slot].wait(v, lv)
            if o == 0:
                v.memset(rb[:, 0:3], 0.0)
            xj = xbcs[:, j * LO : (j + 1) * LO]
            v.tensor_scalar(xj, rb[:, 0:LO], p.cw[:, 4 * j : 4 * j + 1],
                            p.cw[:, 72 + j : 73 + j], ALU.mult, ALU.add)
            for w in range(1, 4):
                v.scalar_tensor_tensor(xj, rb[:, w : w + LO], p.cw[:, 4 * j + w : 4 * j + w + 1],
                                       xj, ALU.mult, ALU.add)
            cv_ck[j] = p.s_v.v
            p.s_v.wait(s, cv_ck[j])
            p.s_s.inc(s.activation(xj, xj, AF.Silu))

        _dbg(p, f"xbcs_o{o}", xbcs, [128, 18 * LO])

        for cc in range(2):
            _mamba_chunk(p, o, cc, xbcs, zc, xtm, cA, nb, dsb)

        p.s_v.wait(g)
        p.s_yw.dma(g.dma_start(
            out=p.y_hbm[:, :, to : to + LO].transpose([1, 0, 2]),
            in_=p.yoct.rearrange("p (fc l) -> p fc l", fc=16)))

    p.s_yw.wait(sy)
    _dbg(p, "rstd_all", p.rstd_all, [128, T])

    scope(p, 'mamba_outproj')
    _outproj_ln(p)


def _mamba_chunk(p, o, cc, xbcs, zc, xtm, cA, nb, dsb):
    sy, t, v, s, g = p.sy, p.t, p.v, p.s, p.g
    tg = o * LO + cc * LC
    sl = slice(cc * LC, (cc + 1) * LC)

    p.s_v.wait(sy)
    p.s_s.wait(sy)
    zv = p.s_zc.dma(sy.dma_start(
        out=zc.rearrange("p (fc l) -> p fc l", fc=16),
        in_=p.z_hbm[:, :, tg : tg + LC].transpose([1, 0, 2])))

    p.s_v.wait(g)
    p.s_s.wait(g)
    bflat = p.bscr.rearrange("a b -> (a b)")
    wv = p.s_bw.dma(g.dma_start(
        out=bflat[0 : NH * LC].rearrange("(h l) -> h l", l=LC), in_=cA[:, sl]))

    bank_tr = p.pb[2]
    bank_cb = p.pb[3]
    xb16 = xbcs[:, 16 * LO : 16 * LO + LO]
    xb17 = xbcs[:, 17 * LO : 17 * LO + LO]
    p.s_s.wait(t)
    p.s_v.wait(t)
    p.s_t.inc(t.matmul(bank_cb[:, 0:128], xb16[:, sl], xb17[:, sl], start=True, stop=True))
    p.s_t.inc(t.transpose(bank_tr[:, 0:32], nb[:, sl], p.idn[0:32, 0:32]))
    p.s_t.inc(t.transpose(bank_tr[:, 32:64], dsb[:, sl], p.idn[0:32, 0:32]))
    tv = p.s_t.inc(t.transpose(bank_tr[:, 128:256], xb16[:, sl], p.idn))
    p.s_t.wait(v, tv)
    v.tensor_copy(p.nbt, bank_tr[:, 0:32])
    v.tensor_copy(p.dst, bank_tr[:, 32:64])
    v.tensor_copy(p.btm, bank_tr[:, 128:256])
    ev = p.s_v.inc(v.tensor_mul(p.cbm, bank_cb[:, 0:128], p.msk))
    for g4 in range(4):
        p.s_v.wait(t, ev)
        for k in range(4):
            fc = g4 * 4 + k
            tv = p.s_t.inc(t.transpose(
                bank_tr[:, k * 128 : (k + 1) * 128],
                xbcs[:, fc * LO + cc * LC : fc * LO + cc * LC + LC], p.idn))
        p.s_t.wait(v, tv)
        ev = p.s_v.inc(v.tensor_copy(xtm[:, g4 * 512 : (g4 + 1) * 512], bank_tr))

    bufsets = [
        (p.gb0, p.gb1, p.gb2, p.xd[:, 0:512], p.pb[0], p.s_bc),
        (p.big[:, 12288:13312], p.big[:, 13312:14336], p.big[:, 14336:15360],
         p.big[:, 15360:15872], p.pb[1], p.s_be),
    ]
    xb17 = xbcs[:, 17 * LO : 17 * LO + LO]
    bflat = p.bscr.rearrange("a b -> (a b)")
    g_tv = {}
    g_gv = {}
    g_vv = {}

    def build_group(gi):
        h0 = gi * 8
        b0, b1, b2, xd, sbank, bsem = bufsets[gi % 2]
        p.s_bw.wait(g, wv)
        if gi >= 2:
            p.s_t.wait(g, g_tv[gi - 2])
            p.s_v.wait(g, g_gv[gi - 2])
        bv = bsem.dma(g.dma_start(
            out=b0,
            in_=bflat[h0 * LC : h0 * LC + 1024].unsqueeze(0).broadcast_to([128, 1024])))
        bsem.wait(v, bv)
        if gi >= 2:
            p.s_t.wait(v, g_tv[gi - 2])
        b0v = b0.rearrange("p (h l) -> p h l", h=8)
        v.tensor_tensor(b0v, b0v, rep(p.nbt[:, h0 : h0 + 8], LC), ALU.add)
        mv = p.s_v.inc(v.tensor_scalar(b2, b0, 30.0, None, ALU.min))
        p.s_v.wait(s, mv)
        if gi >= 2:
            p.s_t.wait(s, g_tv[gi - 2])
        sv = p.s_s.inc(s.activation(b1, b2, AF.Exp))
        p.s_s.wait(v, sv)
        b1v = b1.rearrange("p (h l) -> p h l", h=8)
        v.tensor_tensor(b1v, b1v, rep(p.cbm, 8).transpose([0, 2, 1]), ALU.mult)
        v.tensor_tensor(b0v, b0v, rep(p.nbt[:, h0 : h0 + 8], LC), ALU.subtract)
        mv = p.s_v.v
        p.s_v.wait(s, mv)
        sv = p.s_s.inc(s.activation(b0, b0, AF.Exp))
        p.s_s.wait(v, sv)
        for hh in range(8):
            h = h0 + hh
            v.tensor_scalar(xd[:, hh * 64 : hh * 64 + 64],
                            xtm[:, h * 64 : h * 64 + 64],
                            p.dst[:, h : h + 1], None, ALU.mult)
        b2v = b2.rearrange("p (h l) -> p h l", h=8)
        v.tensor_tensor(b2v, rep(xb17[:, sl], 8).transpose([0, 2, 1]),
                        b0.rearrange("p (h l) -> p h l", h=8), ALU.mult)
        g_vv[gi] = p.s_v.v

    def mm_group(gi):
        h0 = gi * 8
        b0, b1, b2, xd, sbank, bsem = bufsets[gi % 2]
        ybank = p.pb[4 + gi]
        p.s_v.wait(t, g_vv[gi])
        for hh in range(8):
            h = h0 + hh
            ypart = (hh % 2) * 64
            yfree = (hh // 2) * 128
            yout = ybank[ypart : ypart + 64, yfree : yfree + 128]
            t.matmul(yout, xtm[:, h * 64 : h * 64 + 64],
                     b1[:, hh * 128 : hh * 128 + 128],
                     start=(hh < 2), stop=False, skip_group_check=True)
            t.matmul(yout, p.state[:, h * 64 : h * 64 + 64],
                     b2[:, hh * 128 : hh * 128 + 128],
                     start=False, stop=(hh >= 6), skip_group_check=True)
            mm = t.matmul(sbank[:, hh * 64 : hh * 64 + 64], p.btm,
                          xd[:, hh * 64 : hh * 64 + 64],
                          start=(hh == 0), stop=(hh == 7))
        g_tv[gi] = tv = p.s_t.inc(mm)
        p.s_t.wait(v, tv)
        for hh in range(8):
            h = h0 + hh
            stsl = p.state[:, h * 64 : h * 64 + 64]
            v.tensor_scalar(stsl, stsl, b0[:, hh * 128 + 127 : hh * 128 + 128],
                            None, ALU.mult)
        g_gv[gi] = p.s_v.inc(v.tensor_add(p.state[:, h0 * 64 : h0 * 64 + 512],
                                          p.state[:, h0 * 64 : h0 * 64 + 512], sbank))

    for gi in range(4):
        build_group(gi)
        if gi >= 1:
            mm_group(gi - 1)
    mm_group(3)

    p.s_zc.wait(s, zv)
    p.s_zc.wait(v, zv)
    p.s_yw.wait(v)
    sv = p.s_s.inc(s.activation(zc, zc, AF.Silu))
    p.s_s.wait(v, sv)
    for fc in range(16):
        ybank = p.pb[4 + fc // 4]
        yfree = (fc % 4) * 128
        dst_y = p.yoct[:, fc * LO + cc * LC : fc * LO + cc * LC + LC]
        v.scalar_tensor_tensor(p.tmpv[:, 0:128],
                               xbcs[:, fc * LO + cc * LC : fc * LO + cc * LC + LC],
                               p.dr[:, fc : fc + 1],
                               ybank[:, yfree : yfree + 128], ALU.mult, ALU.add)
        v.tensor_mul(dst_y, p.tmpv[:, 0:128], zc[:, fc * 128 : fc * 128 + 128])
    yv = p.s_v.v
    p.s_v.wait(s, yv)
    sv = p.s_s.inc(s.activation(p.ysq.rearrange("p (fc l) -> p fc l", fc=16),
                                p.yoct.rearrange("p (fc l) -> p fc l", fc=16)[:, :, cc * LC : cc * LC + LC],
                                AF.Square))
    p.s_s.wait(t, sv)
    p.s_v.wait(t)
    stats = p.pb[1]
    for fc in range(16):
        mm = t.matmul(stats[0:1, 0:128], p.ones[:, 0:1],
                      p.ysq[:, fc * 128 : fc * 128 + 128],
                      start=(fc == 0), stop=(fc == 15))
    tv = p.s_t.inc(mm)
    p.s_t.wait(v, tv)
    v.tensor_scalar(p.rows[0:1, 8:136], stats[0:1, 0:128], 1.0 / DIN, EPS,
                    ALU.mult, ALU.add)
    rv = p.s_v.inc(v.reciprocal(p.rows[0:1, 8:136], p.rows[0:1, 8:136]))
    p.s_v.wait(s, rv)
    sv = p.s_s.inc(s.activation(p.rows[0:1, 8:136], p.rows[0:1, 8:136], AF.Sqrt))
    p.s_s.wait(v, sv)
    p.s_v.inc(v.tensor_copy(p.rstd_all[0:1, tg : tg + LC], p.rows[0:1, 8:136]))


def _outproj_ln(p):
    sy, t, v, s, g = p.sy, p.t, p.v, p.s, p.g
    _fence_compute(p)
    wo4 = p.big[:, 0:8192]
    ybuf = p.big[:, 8192:12288]
    prev_tv = 0
    for ocg in range(2):
        wl = 0
        if ocg == 1:
            sy.wait_ge(p.s_t.h, prev_tv)
        for j in range(4):
            oc = ocg * 4 + j
            wj = wo4[:, j * 2048 : j * 2048 + 2048]
            src = p.w_out_t[:, oc * 128 : (oc + 1) * 128].rearrange(
                "(kc kp) o -> kp kc o", kp=128)
            wl = p.s_wa.dma(sy.dma_start(
                out=wj.rearrange("kp (kc o) -> kp kc o", kc=16), in_=src))
        p.s_wa.wait(v, wl)
        for j in range(4):
            wj = wo4[:, j * 2048 : j * 2048 + 2048]
            v.tensor_tensor(wj.rearrange("p (kc o) -> p kc o", kc=16),
                            wj.rearrange("p (kc o) -> p kc o", kc=16),
                            rep(p.nw, 128), ALU.mult)
        wsv = p.s_v.v

        for sub in range(8):
            ts0 = sub * 256
            if not (ocg == 0 and sub == 0):
                sy.wait_ge(p.s_t.h, prev_tv)
            yv = p.s_yb.dma(sy.dma_start(
                out=ybuf.rearrange("p (fc l) -> p fc l", fc=16),
                in_=p.y_hbm[:, :, ts0 : ts0 + 256].transpose([1, 0, 2])))
            p.s_v.wait(g)
            wv = p.s_bw.dma(g.dma_start(out=p.bscr[17:18, 0:256],
                                        in_=p.rstd_all[0:1, ts0 : ts0 + 256]))
            p.s_bw.wait(g, wv)
            bv = p.s_bc.dma(g.dma_start(
                out=p.bc1[:, 0:256], in_=p.bscr[17:18, 0:256].broadcast_to([128, 256])))
            for j in range(4):
                oc = ocg * 4 + j
                bank = p.pb[2 + (j % 2)]
                p.s_v.wait(t, wsv)
                p.s_yb.wait(t, yv)
                if not (ocg == 0 and sub == 0 and j < 2):
                    p.s_v.wait(t)
                for kc in range(16):
                    mm = t.matmul(bank[:, 0:256],
                                  wo4[:, j * 2048 + kc * 128 : j * 2048 + kc * 128 + 128],
                                  ybuf[:, kc * 256 : (kc + 1) * 256],
                                  start=(kc == 0), stop=(kc == 15))
                tv = p.s_t.inc(mm)
                prev_tv = tv
                p.s_t.wait(v, tv)
                p.s_bc.wait(v, bv)
                rsl = p.R[:, oc * T + ts0 : oc * T + ts0 + 256]
                v.tensor_mul(p.tmpv[:, 0:256], bank[:, 0:256], p.bc1[:, 0:256])
                p.s_v.inc(v.tensor_add(rsl, rsl, p.tmpv[:, 0:256]))

    _dbg(p, "x1pre", p.R, [128, 8 * T])
    _post_ln(p, ln_idx=0, absmax_acc=(p.upto >= 2))


def _post_ln(p, ln_idx, absmax_acc=False, gamma=None, beta=None):
    sy, t, v, s, g = p.sy, p.t, p.v, p.s, p.g
    scope(p, f"post_ln{ln_idx}")
    _fence_compute(p)
    if absmax_acc:
        v.memset(p.sml[:, 3:4], 0.0)
    sbanks = [p.pb[1], p.pb[4]]
    mrow = p.dt_raw[0:1, 0:2048]
    rrow = p.rstd_all[0:1, 0:2048]
    st_ck = {}
    ev_ck = {}
    for sub in range(8):
        ts0 = sub * 256
        stats = sbanks[sub % 2]
        ssl = sub % 2
        sqb = p.ysq[:, ssl * 1024 : ssl * 1024 + 1024]
        if sub >= 2:
            p.s_t.wait(s, st_ck[sub - 2])
        for half in range(2):
            if half == 1:
                p.s_t.wait(s, tv)
            for k in range(4):
                oc = half * 4 + k
                rsl = p.R[:, oc * T + ts0 : oc * T + ts0 + 256]
                s.activation(sqb[:, k * 256 : k * 256 + 256], rsl, AF.Square)
            sv = p.s_s.v
            p.s_s.wait(t, sv)
            if sub >= 2 and half == 0:
                p.s_v.wait(t, ev_ck[sub - 2])
            for k in range(4):
                oc = half * 4 + k
                rsl2 = p.R[:, oc * T + ts0 : oc * T + ts0 + 256]
                p.s_t.inc(t.matmul(stats[0:1, 0:256], p.ones[:, 0:1], rsl2,
                                   start=(oc == 0), stop=False,
                                   skip_group_check=True))
                tv = p.s_t.inc(t.matmul(
                    stats[0:1, 256:512], p.ones[:, 0:1],
                    sqb[:, k * 256 : k * 256 + 256],
                    start=False, stop=(oc == 7), skip_group_check=True))
        st_ck[sub] = tv
        p.s_t.wait(v, tv)
        v.tensor_scalar(mrow[:, ts0 : ts0 + 256], stats[0:1, 0:256], 1.0 / D,
                        None, ALU.mult)
        v.tensor_scalar(rrow[:, ts0 : ts0 + 256], stats[0:1, 256:512], 1.0 / D,
                        None, ALU.mult)
        ev_ck[sub] = p.s_v.v
    v.tensor_mul(p.tmpv[0:1, 0:1024], mrow[:, 0:1024], mrow[:, 0:1024])
    v.tensor_sub(rrow[:, 0:1024], rrow[:, 0:1024], p.tmpv[0:1, 0:1024])
    v.tensor_mul(p.tmpv[0:1, 0:1024], mrow[:, 1024:2048], mrow[:, 1024:2048])
    v.tensor_sub(rrow[:, 1024:2048], rrow[:, 1024:2048], p.tmpv[0:1, 0:1024])
    v.tensor_scalar(rrow, rrow, 1.0, EPS, ALU.mult, ALU.add)
    rv = p.s_v.inc(v.reciprocal(rrow, rrow))
    p.s_v.wait(s, rv)
    sv = p.s_s.inc(s.activation(rrow, rrow, AF.Sqrt))
    p.s_s.wait(g, sv)
    p.s_v.wait(g)
    bflat = p.bscr.rearrange("a b -> (a b)")
    wv = p.s_bw.dma(g.dma_start(out=bflat[16 * 512 : 16 * 512 + 2048], in_=mrow))
    wv = p.s_bw.dma(g.dma_start(out=bflat[20 * 512 : 20 * 512 + 2048], in_=rrow))
    p.s_bw.wait(g, wv)
    p.s_bc.dma(g.dma_start(
        out=p.bc1, in_=bflat[16 * 512 : 16 * 512 + 2048].unsqueeze(0).broadcast_to([128, 2048])))
    bv = p.s_bc.dma(g.dma_start(
        out=p.bc2, in_=bflat[20 * 512 : 20 * 512 + 2048].unsqueeze(0).broadcast_to([128, 2048])))
    p.s_bc.wait(v, bv)
    for oc in range(8):
        rsl = p.R[:, oc * T : (oc + 1) * T]
        v.tensor_sub(rsl, rsl, p.bc1)
        v.tensor_mul(rsl, rsl, p.bc2)
        if gamma is not None:
            v.tensor_scalar(rsl, rsl, gamma[:, oc : oc + 1], beta[:, oc : oc + 1],
                            ALU.mult, ALU.add)
        if absmax_acc:
            v.tensor_reduce(p.tmpv[:, 0:1], rsl, mybir.AxisListType.X, ALU.max,
                            apply_absolute_value=True)
            v.tensor_max(p.sml[:, 3:4], p.sml[:, 3:4], p.tmpv[:, 0:1])



def _allreduce_max(p, cols, out_cols):
    sy, t, v, s, g = p.sy, p.t, p.v, p.s, p.g
    k = len(cols)
    stats = p.pb[1]
    p.s_v.wait(t)
    p.s_gc.wait(t)
    for i, c in enumerate(cols):
        tv = p.s_t.inc(t.transpose(stats[0:1, i * 128 : i * 128 + 128],
                                   p.sml[:, c : c + 1], p.idn))
    p.s_t.wait(v, tv)
    v.memset(p.rows[0:1, 640:644], 0.0)
    for i in range(k):
        v.tensor_reduce(p.rows[0:1, 640 + i : 641 + i], stats[0:1, i * 128 : i * 128 + 128],
                        mybir.AxisListType.X, ALU.max)
    vv = p.s_v.v
    p.s_v.wait(g, vv)
    p.s_cc.wait(g)
    p.s_ci.dma(g.dma_start(out=p.cc_in.ap()[0:1, 0:4], in_=p.rows[0:1, 640:644]))
    p.s_ci.wait(g)
    p.s_co.wait(g)
    cc = g.collective_compute(
        "AllReduce", ALU.max, replica_groups=[list(range(NCORES))],
        ins=[p.cc_in.ap().opt()], outs=[p.cc_out.ap().opt()])
    p.s_cc.inc(cc)
    p.s_cc.wait(g)
    bv = p.s_co.dma(g.dma_start(
        out=p.tmpv[:, 0:4], in_=p.cc_out.ap()[0:1, 0:4].broadcast_to([128, 4])))
    p.s_co.wait(v, bv)
    for i, oc in enumerate(out_cols):
        v.tensor_copy(p.sml[:, oc : oc + 1], p.tmpv[:, i : i + 1])


def _scale_recip(p, mx_col, s_col, r_col):
    v = p.v
    v.tensor_scalar(p.sml[:, s_col : s_col + 1], p.sml[:, mx_col : mx_col + 1],
                    float(np.float32(1.0 / QMAX)), 1e-12, ALU.mult, ALU.max)
    v.reciprocal(p.sml[:, r_col : r_col + 1], p.sml[:, s_col : s_col + 1])


def _lpbq_dequant(p, wraw, ncols, sbuf, rbuf):
    v = p.v
    nb = ncols // 32
    v3 = wraw.rearrange("p (b e) -> p b e", b=nb)
    v.tensor_reduce(sbuf, v3, mybir.AxisListType.X, ALU.max, apply_absolute_value=True)
    v.tensor_scalar(sbuf, sbuf, float(np.float32(1.0 / 7.0)), 1e-12, ALU.mult, ALU.max)
    v.reciprocal(rbuf, sbuf)
    v.tensor_tensor(v3.bitcast(I32), v3, rep(rbuf, 32), ALU.mult)
    v.tensor_tensor(v3, v3.bitcast(I32), rep(sbuf, 32), ALU.mult)


def _ffn0(p):
    sy, t, v, s, g = p.sy, p.t, p.v, p.s, p.g
    _fence_compute(p)
    scope(p, 'ffn0_arA')
    _allreduce_max(p, [3], [3])
    _scale_recip(p, 3, 4, 5)

    p.s_v.wait(g)
    p.s_s.wait(g)
    for kc in range(8):
        p.s_ea.dma(g.dma_start(out=p.y_hbm[kc, :, :], in_=p.R[:, kc * T : (kc + 1) * T]))
    p.s_ea.wait(v)
    xqh = p.R
    xql = p.big
    for kc in range(8):
        xr = p.R[:, kc * T : (kc + 1) * T]
        dst = xql[:, kc * T : (kc + 1) * T]
        hi = p.xqh_r[:, kc * T : (kc + 1) * T]
        lo = p.xql_r[:, kc * T : (kc + 1) * T]
        v.tensor_scalar(dst.bitcast(I32), xr, p.sml[:, 5:6], None, ALU.mult)
        v.tensor_scalar(dst, dst.bitcast(I32), p.sml[:, 4:5], None, ALU.mult)
        v.tensor_copy(hi, dst)
        v.tensor_tensor(lo, dst, hi.bitcast(F32), ALU.subtract)
    xqv = p.s_v.v

    scope(p, 'ffn0_p1')
    v.memset(p.sml[:, 6:7], 0.0)
    v.memset(p.sml[:, 7:8], 0.0)
    wraw2 = p.yoct[:, 0:2048]
    wbuf = p.dt_raw[:, 0:2048]
    whbuf = p.rstd_all[:, 0:2048]
    sc2 = p.state[:, 0:128]
    stg = p.ysq[:, 0:2048]
    wsem = [p.s_wa, p.s_wb]
    esem = [p.s_ea, p.s_eb]
    tm_ck = {}
    sc_tr = {}
    sc_ev = {}
    sp_ck = {}
    dq_ck = {}
    tr_ck = {}
    wsp_ck = {}
    units = [(m, oc) for oc in range(24) for m in range(2)]
    bank_wt = p.pb[4]

    def p1_stage_a(ui):
        m, oc = units[ui]
        wsl = ui % 2
        wraw = wraw2[:, wsl * 1024 : wsl * 1024 + 1024]
        wb = wbuf[:, wsl * 1024 : wsl * 1024 + 1024]
        wsrc = (p.w_up if m == 0 else p.w_gate)
        if ui >= 2:
            p.s_v.wait(sy, dq_ck[ui - 2])
            sy.wait_ge(p.s_t.h, tr_ck[ui - 2])
        wl = wsem[wsl].dma(sy.dma_start(
            out=wraw, in_=wsrc[oc * 128 : (oc + 1) * 128, :]))
        wsem[wsl].wait(v, wl)
        _lpbq_dequant(p, wraw, 1024, sc2[:, wsl * 64 : wsl * 64 + 32],
                      sc2[:, wsl * 64 + 32 : wsl * 64 + 64])
        dq_ck[ui] = p.s_v.v
        p.s_v.wait(t, dq_ck[ui])
        if ui >= 2:
            tmp = tm_ck.get((ui - 2, 3))
            p.s_s.wait(t, sc_tr.get(ui - 1, 0))
        wh = p.whb_r[:, wsl * 1024 : wsl * 1024 + 1024]
        wlo = p.wlb_r[:, wsl * 1024 : wsl * 1024 + 1024]
        for kk in range(2):
            if ui >= 1 or kk == 1:
                p.s_s.wait(t, sc_tr.get(ui if kk == 1 else ui - 1, 0))
            for k4 in range(4):
                kc = kk * 4 + k4
                tv = p.s_t.inc(t.transpose(bank_wt[:, k4 * 128 : k4 * 128 + 128],
                                           wraw[:, kc * 128 : kc * 128 + 128], p.idn))
            tr_ck[ui] = tv
            p.s_t.wait(s, tv)
            if ui >= 2 and kk == 0:
                p.s_t.wait(s, tm_ck[(ui - 2, 3)])
            s.copy(wb[:, kk * 512 : kk * 512 + 512], bank_wt[:, 0:512])
            sc_tr[ui] = p.s_s.v
        p.s_s.wait(v, sc_tr[ui])
        if ui >= 2:
            p.s_t.wait(v, tm_ck[(ui - 2, 3)])
        v.tensor_copy(wh, wb)
        v.tensor_tensor(wlo, wb, wh.bitcast(F32), ALU.subtract)
        wsp_ck[ui] = p.s_v.v

    def p1_stage_b(ui):
        m, oc = units[ui]
        wsl = ui % 2
        wb = p.wlb_r[:, wsl * 1024 : wsl * 1024 + 1024]
        wh = p.whb_r[:, wsl * 1024 : wsl * 1024 + 1024]
        for tt in range(4):
            i = ui * 4 + tt
            bank = p.pb[i % 2]
            p.s_v.wait(t, wsp_ck[ui])
            if ui == 0 and tt == 0:
                p.s_v.wait(t, xqv)
            if i >= 2:
                p.s_v.wait(t, vr_ck[i - 2])
            for kc in range(8):
                xh = p.xqh_r[:, kc * T + tt * 512 : kc * T + tt * 512 + 512]
                xl = p.xql_r[:, kc * T + tt * 512 : kc * T + tt * 512 + 512]
                t.matmul(bank, wh[:, kc * 128 : (kc + 1) * 128], xh,
                         start=(kc == 0), stop=False)
                t.matmul(bank, wh[:, kc * 128 : (kc + 1) * 128], xl,
                         start=False, stop=False)
                mm = t.matmul(bank, wb[:, kc * 128 : (kc + 1) * 128], xh,
                              start=False, stop=(kc == 7))
            tm_ck[(ui, tt)] = p.s_t.inc(mm)
            ssl = i % 2
            te = stg[:, ssl * 512 : ssl * 512 + 512]
            p.s_t.wait(v, tm_ck[(ui, tt)])
            if i >= 2:
                esem[ssl].wait(v, sp_ck[i - 2])
            v.tensor_copy(te, bank)
            v.tensor_reduce(p.tmpv[:, 4 + ssl : 5 + ssl], bank, mybir.AxisListType.X,
                            ALU.max, apply_absolute_value=True)
            acc = p.sml[:, 6 + m : 7 + m]
            v.tensor_max(acc, acc, p.tmpv[:, 4 + ssl : 5 + ssl])
            vr_ck[i] = sc_ev[i] = p.s_v.v
            p.s_v.wait(g, sc_ev[i])
            dst = (p.up_hbm if m == 0 else p.gate_hbm)
            sp_ck[i] = esem[ssl].dma(
                g.dma_start(out=dst[oc, :, tt * 512 : tt * 512 + 512], in_=te))

    gr_ck = {}
    vr_ck = {}
    for ui in range(48):
        p1_stage_a(ui)
        if ui >= 1:
            p1_stage_b(ui - 1)
    p1_stage_b(47)

    scope(p, 'ffn0_wdq')
    p.s_ea.wait(v)
    p.s_eb.wait(v)
    for oc in range(8):
        wsl = oc % 2
        wraw = wraw2[:, wsl * 1024 : wsl * 1024 + 1024]
        for cp in range(3):
            p.s_v.wait(sy)
            wl = wsem[wsl].dma(sy.dma_start(
                out=wraw, in_=p.w_down[oc * 128 : (oc + 1) * 128,
                                       cp * 1024 : cp * 1024 + 1024]))
            wsem[wsl].wait(v, wl)
            _lpbq_dequant(p, wraw, 1024, sc2[:, wsl * 64 : wsl * 64 + 32],
                          sc2[:, wsl * 64 + 32 : wsl * 64 + 64])
            dqv = p.s_v.v
            p.s_v.wait(t, dqv)
            p.s_ea.wait(v)
            p.s_eb.wait(v)
            for kk in range(2):
                p.s_v.wait(t)
                for k4 in range(4):
                    kc = kk * 4 + k4
                    tv = p.s_t.inc(t.transpose(bank_wt[:, k4 * 128 : k4 * 128 + 128],
                                               wraw[:, kc * 128 : kc * 128 + 128], p.idn))
                p.s_t.wait(v, tv)
                v.tensor_copy(stg[:, kk * 512 : kk * 512 + 512], bank_wt)
            vv = p.s_v.v
            p.s_v.wait(g, vv)
            esem[wsl].dma(g.dma_start(
                out=p.wdq_hbm[cp * 1024 : (cp + 1) * 1024, oc * 128 : (oc + 1) * 128]
                .rearrange("(kc kp) o -> kp kc o", kp=128),
                in_=stg[:, 0:1024].rearrange("kp (kc o) -> kp kc o", kc=8)))

    scope(p, 'ffn0_arB')
    _allreduce_max(p, [6, 7], [6, 7])
    _scale_recip(p, 6, 10, 11)
    _scale_recip(p, 7, 12, 13)

    scope(p, 'ffn0_p2')
    p.s_ea.wait(sy)
    p.s_eb.wait(sy)
    v.memset(p.sml[:, 8:9], 0.0)
    rsem = [p.s_ra, p.s_rb]
    ld_ck = {}
    cv_ck = {}
    sp_ck = {}
    for oc in range(24):
        ssl = oc % 2
        base = ssl * 8192
        graw = p.big[:, base : base + 2048]
        gq = p.big[:, base + 2048 : base + 4096]
        sq = p.big[:, base + 4096 : base + 6144]
        g2 = p.big[:, base + 6144 : base + 8192]
        if oc >= 2:
            p.s_v.wait(sy, cv_ck[oc - 2])
        ld_ck[oc] = rsem[ssl].dma(sy.dma_start(out=graw, in_=p.gate_hbm[oc, :, :]))
        rsem[ssl].wait(v, ld_ck[oc])
        if oc >= 2:
            esem[ssl].wait(v, sp_ck[oc - 2])
        v.tensor_scalar(gq.bitcast(I32), graw, p.sml[:, 13:14], None, ALU.mult)
        v.tensor_scalar(gq, gq.bitcast(I32), p.sml[:, 12:13], None, ALU.mult)
        sv = p.s_v.v
        p.s_v.wait(s, sv)
        av = p.s_s.inc(s.activation(sq, gq, AF.Sigmoid))
        p.s_s.wait(v, av)
        v.tensor_scalar(sq.bitcast(I32), sq, 65536.0, None, ALU.mult)
        v.tensor_scalar(sq, sq.bitcast(I32), 65535.0, float(2.0 ** -16),
                        ALU.min, ALU.mult)
        v.tensor_mul(g2, gq, sq)
        v.tensor_reduce(p.tmpv[:, 0:1], g2, mybir.AxisListType.X, ALU.max,
                        apply_absolute_value=True)
        v.tensor_max(p.sml[:, 8:9], p.sml[:, 8:9], p.tmpv[:, 0:1])
        cv_ck[oc] = p.s_v.v
        p.s_v.wait(g, cv_ck[oc])
        sp_ck[oc] = esem[ssl].dma(g.dma_start(out=p.g2_hbm[oc, :, :], in_=g2))

    scope(p, 'ffn0_arC')
    scope(p, 'ffn0_arC')
    _allreduce_max(p, [8], [8])
    _scale_recip(p, 8, 14, 15)

    scope(p, 'ffn0_p3')
    p.s_ea.wait(sy)
    p.s_eb.wait(sy)
    v.memset(p.sml[:, 9:10], 0.0)
    for oc in range(24):
        ssl = oc % 2
        base = ssl * 8192
        g2r = p.big[:, base : base + 2048]
        upr = p.big[:, base + 2048 : base + 4096]
        orw = p.big[:, base + 4096 : base + 6144]
        if oc >= 2:
            p.s_v.wait(sy, cv_ck[oc - 2])
        rsem[ssl].dma(sy.dma_start(out=g2r, in_=p.g2_hbm[oc, :, :]))
        ld_ck[oc] = rsem[ssl].dma(sy.dma_start(out=upr, in_=p.up_hbm[oc, :, :]))
        rsem[ssl].wait(v, ld_ck[oc])
        if oc >= 2:
            esem[ssl].wait(v, sp_ck[oc - 2])
        v.tensor_scalar(g2r.bitcast(I32), g2r, p.sml[:, 15:16], None, ALU.mult)
        v.tensor_scalar(g2r, g2r.bitcast(I32), p.sml[:, 14:15], None, ALU.mult)
        v.tensor_scalar(upr.bitcast(I32), upr, p.sml[:, 11:12], None, ALU.mult)
        v.tensor_scalar(upr, upr.bitcast(I32), p.sml[:, 10:11], None, ALU.mult)
        v.tensor_mul(orw, g2r, upr)
        v.tensor_reduce(p.tmpv[:, 0:1], orw, mybir.AxisListType.X, ALU.max,
                        apply_absolute_value=True)
        v.tensor_max(p.sml[:, 9:10], p.sml[:, 9:10], p.tmpv[:, 0:1])
        cv_ck[oc] = p.s_v.v
        p.s_v.wait(g, cv_ck[oc])
        sp_ck[oc] = esem[ssl].dma(g.dma_start(out=p.o_hbm[oc, :, :], in_=orw))

    scope(p, 'ffn0_arD')
    scope(p, 'ffn0_arD')
    _allreduce_max(p, [9], [9])
    _scale_recip(p, 9, 16, 17)

    scope(p, 'ffn0_p4')
    p.s_ea.wait(sy)
    p.s_eb.wait(sy)
    p.s_v.wait(sy)
    for kc in range(8):
        p.s_rr.dma(sy.dma_start(out=p.R[:, kc * T : (kc + 1) * T], in_=p.y_hbm[kc, :, :]))
    p.s_rr.wait(v)
    oq = p.big[:, 0:12288]
    wdbs = [p.big[:, 12288 : 12288 + 3072], p.yoct[:, 0:3072]]
    prev_tv = p.s_t.v
    dtm_ck = {}
    for tt in range(4):
        if tt > 0:
            p.s_v.wait(sy)
            sy.wait_ge(p.s_t.h, prev_tv)
        for kc in range(24):
            ssl = kc % 2
            rsem[ssl].dma(sy.dma_start(
                out=oq[:, kc * 512 : kc * 512 + 512],
                in_=p.o_hbm[kc, :, tt * 512 : tt * 512 + 512]))
        rsem[0].wait(v)
        rsem[1].wait(v)
        for kc in range(24):
            dst = oq[:, kc * 512 : kc * 512 + 512]
            v.tensor_scalar(dst.bitcast(I32), dst, p.sml[:, 17:18], None, ALU.mult)
            v.tensor_scalar(dst, dst.bitcast(I32), p.sml[:, 16:17], None, ALU.mult)
        qv = p.s_v.v
        for oc in range(8):
            u4 = tt * 8 + oc
            wdb = wdbs[u4 % 2]
            wsm = [p.s_wa, p.s_wb][u4 % 2]
            if u4 >= 2:
                sy.wait_ge(p.s_t.h, dtm_ck[u4 - 2])
            src = p.wdq_hbm[:, oc * 128 : (oc + 1) * 128].rearrange(
                "(kc kp) o -> kp kc o", kp=128)
            wl = wsm.dma(sy.dma_start(
                out=wdb.rearrange("kp (kc o) -> kp kc o", kc=24), in_=src))
            bank = p.pb[2 + oc % 2]
            wsm.wait(t, wl)
            p.s_v.wait(t, qv)
            if not (tt == 0 and oc < 2):
                p.s_v.wait(t)
            for kc in range(24):
                mm = t.matmul(bank[:, 0:512],
                              wdb[:, kc * 128 : kc * 128 + 128],
                              oq[:, kc * 512 : kc * 512 + 512],
                              start=(kc == 0), stop=(kc == 23))
            tv = p.s_t.inc(mm)
            dtm_ck[u4] = tv
            prev_tv = tv
            p.s_t.wait(v, tv)
            rsl = p.R[:, oc * T + tt * 512 : oc * T + tt * 512 + 512]
            v.tensor_add(rsl, rsl, bank[:, 0:512])

    _dbg(p, "x2pre", p.R, [128, 8 * T])
    _post_ln(p, ln_idx=1)


def _l1_attn(p):
    sy, t, v, s, g = p.sy, p.t, p.v, p.s, p.g
    scope(p, "l1_attn")
    _fence_compute(p)
    LQ4 = 512
    abuf = p.big[:, 0:4096]
    bbuf = p.big[:, 4096:8192]
    hbuf = p.big[:, 8192:12288]
    wzb = p.big[:, 12288:13312]
    whb = p.big[:, 13312:14336]

    def shslot(fc, d):
        if d == 0:
            if fc < 8:
                return (p.yoct[:, fc * 512 : fc * 512 + 512] if fc < 8 else None)
        return None

    bzv = p.cw[:, 0:8]
    bhv = p.cw[:, 8:16]
    dcv = p.cw[:, 16:24]

    lv = p.s_ld.dma(sy.dma_start(
        out=bzv.rearrange("p (j o) -> p j o", o=1),
        in_=p.bz.rearrange("(j p) o -> p j o", p=128),
        allow_slow_non_contiguous=True))
    lv = p.s_ld.dma(sy.dma_start(
        out=bhv.rearrange("p (j o) -> p j o", o=1),
        in_=p.bh.rearrange("(j p) o -> p j o", p=128),
        allow_slow_non_contiguous=True))
    lv = p.s_ld.dma(sy.dma_start(
        out=dcv.rearrange("p (j o) -> p j o", o=1),
        in_=p.log_d.rearrange("(j p) o -> p j o", p=128),
        allow_slow_non_contiguous=True))
    p.mdf = p.gb0[:, 0:128]
    p.mwpb = p.gb0[:, 128:256]
    p.mwnb = p.gb0[:, 256:384]
    p.s_ld.dma(sy.dma_start(out=p.mdf, in_=p.mdiff))
    p.s_ld.dma(sy.dma_start(out=p.mwpb, in_=p.mwp))
    p.s_ld.dma(sy.dma_start(out=p.mwnb, in_=p.mwn))
    p.s_ld.wait(t)
    p.s_ld.wait(s)
    av = p.s_s.inc(s.activation(dcv, dcv, AF.Exp))
    v.memset(p.sml[:, 20:28], 0.0)
    wsem = [p.s_wa, p.s_wb]
    rsem = [p.s_ra, p.s_rb]
    wz2 = p.big[:, 14336:15360]
    wh2 = p.big[:, 15360:16384]
    tm_ck = {}
    sc_ck = {}
    scan_ck = 0
    diff_ck = 0
    for q in range(4):
        tq = q * LQ4
        for oc in range(8):
            u = q * 8 + oc
            wzsl = wzb if u % 2 == 0 else wz2
            whsl = whb if u % 2 == 0 else wh2
            if u >= 2:
                sy.wait_ge(p.s_t.h, tm_ck[u - 2])
            zsem = [p.s_wa, p.s_wc][u % 2]
            hsem = [p.s_wb, p.s_wd][u % 2]
            wl = zsem.dma(sy.dma_start(
                out=wzsl.rearrange("kp (kc o) -> kp kc o", kc=8),
                in_=p.wz[:, oc * 128 : (oc + 1) * 128].rearrange("(kc kp) o -> kp kc o", kp=128)))
            wl2 = hsem.dma(sy.dma_start(
                out=whsl.rearrange("kp (kc o) -> kp kc o", kc=8),
                in_=p.wh[:, oc * 128 : (oc + 1) * 128].rearrange("(kc kp) o -> kp kc o", kp=128)))
            zsem.wait(t, wl)
            hsem.wait(t, wl2)
            bz_bank = p.pb[2 + (u % 2)]
            bh_bank = p.pb[6 + (u % 2)]
            if u >= 2:
                p.s_s.wait(t, sc_ck[u - 2])
            if oc == 0:
                p.s_v.wait(t, diff_ck)
                p.s_v.wait(s, scan_ck)
            for kc in range(8):
                mm = t.matmul(bz_bank, wzsl[:, kc * 128 : kc * 128 + 128],
                              p.R[:, kc * T + tq : kc * T + tq + LQ4],
                              start=(kc == 0), stop=(kc == 7))
            tvz = p.s_t.inc(mm)
            for kc in range(8):
                mm = t.matmul(bh_bank, whsl[:, kc * 128 : kc * 128 + 128],
                              p.R[:, kc * T + tq : kc * T + tq + LQ4],
                              start=(kc == 0), stop=(kc == 7))
            tm_ck[u] = tvh = p.s_t.inc(mm)
            zsl = abuf[:, oc * 512 : oc * 512 + 512]
            hsl = bbuf[:, oc * 512 : oc * 512 + 512]
            p.s_t.wait(s, tvz)
            s.activation(zsl, bz_bank, AF.Sigmoid, bias=bzv[:, oc : oc + 1])
            p.s_t.wait(s, tvh)
            sv = p.s_s.inc(s.activation(hsl, bh_bank, AF.Identity, bias=bhv[:, oc : oc + 1]))
            sc_ck[u] = sv
            p.s_s.wait(v, sv)
            v.tensor_mul(hsl, zsl, hsl)
            v.tensor_scalar(zsl, zsl, -1.0, 1.0, ALU.mult, ALU.add)
        if "no_scan" in p.dbg:
            for fc in range(8):
                v.tensor_copy(hbuf[:, fc * 512 : fc * 512 + 512],
                              bbuf[:, fc * 512 : fc * 512 + 512])
        rsem[0].wait(v)
        rsem[1].wait(v)
        for fc in range(8 if "no_scan" not in p.dbg else 0):
            v.tensor_tensor_scan(hbuf[:, fc * 512 : fc * 512 + 512],
                                 abuf[:, fc * 512 : fc * 512 + 512],
                                 bbuf[:, fc * 512 : fc * 512 + 512],
                                 p.sml[:, 20 + fc : 21 + fc], ALU.mult, ALU.add)
            v.tensor_copy(p.sml[:, 20 + fc : 21 + fc],
                          hbuf[:, fc * 512 + 511 : fc * 512 + 512])
        hv = p.s_v.v
        scan_ck = hv
        p.s_v.wait(t, hv)
        p.s_s.wait(t)
        for fc in range(8):
            hsl = hbuf[:, fc * 512 : fc * 512 + 512]
            prv = hbuf[:, ((fc - 1) % 8) * 512 : ((fc - 1) % 8) * 512 + 512]
            nxt = hbuf[:, ((fc + 1) % 8) * 512 : ((fc + 1) % 8) * 512 + 512]
            bank = p.pb[fc]
            t.matmul(bank, p.mdf, hsl, start=True, stop=False)
            t.matmul(bank, p.mwpb, prv, start=False, stop=False)
            dtv = p.s_t.inc(t.matmul(bank, p.mwnb, nxt, start=False, stop=True))
        p.s_t.wait(v, dtv)
        for fc in range(8):
            hsl = hbuf[:, fc * 512 : fc * 512 + 512]
            v.scalar_tensor_tensor(hsl, p.pb[fc], dcv[:, fc : fc + 1], hsl,
                                   ALU.mult, ALU.add)
            rsl = p.R[:, fc * T + tq : fc * T + tq + LQ4]
            v.tensor_add(rsl, rsl, hsl)
        diff_ck = p.s_v.v

    _dbg(p, "x3pre", p.R, [128, 8 * T])
    _post_ln(p, ln_idx=2)


def _l1_ffn(p):
    sy, t, v, s, g = p.sy, p.t, p.v, p.s, p.g
    scope(p, "l1_ffn")
    _fence_compute(p)
    wd1 = p.big[:, 0:8192]
    y1 = p.big[:, 8192:12288]
    srst = p.big[:, 12288:14336]
    lra = p.big[:, 14336:14848]
    lab = p.big[:, 14848:15104]
    lbb = p.big[:, 15104:16128]

    def wd2(kc):
        if kc < 4:
            return p.yoct[:, kc * 1024 : kc * 1024 + 1024]
        if kc < 6:
            return p.ysq[:, (kc - 4) * 1024 : (kc - 4) * 1024 + 1024]
        return p.state[:, (kc - 6) * 1024 : (kc - 6) * 1024 + 1024]

    qb1v = p.cw[:, 24:32]
    qb2v = p.cw[:, 32:40]
    gamv = p.cw[:, 40:48]
    betv = p.cw[:, 48:56]
    wsem = [p.s_wa, p.s_wb]
    stats = p.pb[1]
    p.s_v.wait(g)
    p.s_s.wait(g)
    for fc in range(8):
        p.s_ea.dma(g.dma_start(out=p.z_hbm[fc, :, :], in_=p.R[:, fc * T : (fc + 1) * T]))
    p.s_ea.wait(v)
    for blk in range(3):
        i0, i1 = 2 * blk, 2 * blk + 1
        p.s_s.wait(sy)
        p.s_v.wait(sy)
        p.s_t.wait(sy)
        lv = p.s_ld.dma(sy.dma_start(
            out=qb1v.rearrange("p (j o) -> p j o", o=1),
            in_=p.qb[:, i0 : i0 + 1].rearrange("(j p) o -> p j o", p=128),
            allow_slow_non_contiguous=True))
        lv = p.s_ld.dma(sy.dma_start(
            out=qb2v.rearrange("p (j o) -> p j o", o=1),
            in_=p.qb[:, i1 : i1 + 1].rearrange("(j p) o -> p j o", p=128),
            allow_slow_non_contiguous=True))
        lv = p.s_ld.dma(sy.dma_start(
            out=gamv.rearrange("p (j o) -> p j o", o=1),
            in_=p.gam[:, blk : blk + 1].rearrange("(j p) o -> p j o", p=128),
            allow_slow_non_contiguous=True))
        lv = p.s_ld.dma(sy.dma_start(
            out=betv.rearrange("p (j o) -> p j o", o=1),
            in_=p.bet[:, blk : blk + 1].rearrange("(j p) o -> p j o", p=128),
            allow_slow_non_contiguous=True))
        lv = p.s_ld.dma(sy.dma_start(
            out=lab.rearrange("p (kc r) -> p kc r", kc=8),
            in_=p.la_t[i1].rearrange("(kc p) r -> p kc r", p=128)))
        lv = p.s_ld.dma(sy.dma_start(out=lbb[0:32, :], in_=p.lb_t[i1]))
        lab1 = p.dt_raw[:, 0:256]
        lbb1 = p.dt_raw[:, 256:1280]
        lv = p.s_ld.dma(sy.dma_start(
            out=lab1.rearrange("p (kc r) -> p kc r", kc=8),
            in_=p.la_t[i0].rearrange("(kc p) r -> p kc r", p=128)))
        lv = p.s_ld.dma(sy.dma_start(out=lbb1[0:32, :], in_=p.lb_t[i0]))

        p.s_t.wait(sy)
        for un in range(16):
            mat, kc = divmod(un, 8)
            dstw = wd1[:, kc * 1024 : kc * 1024 + 1024] if mat == 0 else wd2(kc)
            ssl = un % 2
            srb = srst[:, ssl * 1024 : ssl * 1024 + 1024]
            qi = i0 if mat == 0 else i1
            if un >= 2:
                p.s_v.wait(sy)
            wl = wsem[ssl].dma(sy.dma_start(
                out=dstw.bitcast(I32), in_=p.qw_t[qi, kc * 128 : (kc + 1) * 128, :]))
            wl = wsem[ssl].dma(sy.dma_start(
                out=srb, in_=p.srep_t[qi, kc * 128 : (kc + 1) * 128, :]))
            wsem[ssl].wait(v, wl)
            v.tensor_scalar(dstw, dstw.bitcast(I32), float(np.float32(1.0 / 7.5)),
                            -1.0, ALU.mult, ALU.add)
            v.tensor_mul(dstw, dstw, srb)
        dqv = p.s_v.v
        p.s_ld.wait(t)
        p.s_ld.wait(v)
        p.s_ld.wait(s)
        p.s_v.wait(t, dqv)

        prev_tv = p.s_t.v
        l1sc_ck = {}
        l2v_ck = {}
        for tt in range(4):
            ts0 = tt * 512
            for kc in range(8):
                mm = t.matmul(p.pb[5][0:32, :], lab1[:, kc * 32 : kc * 32 + 32],
                              p.R[:, kc * T + ts0 : kc * T + ts0 + 512],
                              start=(kc == 0), stop=(kc == 7))
            ltv = p.s_t.inc(mm)
            p.s_t.wait(v, ltv)
            v.tensor_copy(lra[0:32, :], p.pb[5][0:32, :])
            lrv = p.s_v.v
            p.s_v.wait(t, lrv)
            stats2 = p.pb[4]
            sq_ck = {}
            for oc in range(8):
                bank = p.pb[2 + oc % 2]
                if oc >= 2:
                    p.s_s.wait(t, l1sc_ck[oc - 2])
                elif tt > 0:
                    p.s_v.wait(t)
                for kc in range(8):
                    t.matmul(bank, wd1[:, kc * 1024 + oc * 128 : kc * 1024 + oc * 128 + 128],
                             p.R[:, kc * T + ts0 : kc * T + ts0 + 512],
                             start=(kc == 0), stop=False)
                mm = t.matmul(bank, lbb1[0:32, oc * 128 : oc * 128 + 128],
                              lra[0:32, :], start=False, stop=True)
                tv = p.s_t.inc(mm)
                p.s_t.wait(s, tv)
                ysl = y1[:, oc * 512 : oc * 512 + 512]
                sv = p.s_s.inc(s.activation(ysl, bank,
                                            AF.Identity, bias=qb1v[:, oc : oc + 1]))
                l1sc_ck[oc] = sv
                ssl = oc % 2
                sq = srst[:, ssl * 1024 : ssl * 1024 + 512]
                if oc >= 2:
                    p.s_t.wait(s, sq_ck[oc - 2])
                sv = p.s_s.inc(s.activation(sq, ysl, AF.Square))
                p.s_s.wait(t, l1sc_ck[oc])
                p.s_t.inc(t.matmul(stats[0:1, 0:512], p.ones[:, 0:1], ysl,
                                   start=(oc == 0), stop=(oc == 7)))
                p.s_s.wait(t, sv)
                sq_ck[oc] = tv = p.s_t.inc(t.matmul(
                    stats2[0:1, 0:512], p.ones[:, 0:1], sq,
                    start=(oc == 0), stop=(oc == 7)))
            if blk == 0 and tt == 0:
                _dbg(p, "y1b0", y1, [128, 4096])
            p.s_t.wait(v, tv)
            m = p.rows[0:1, 0:512]
            var = p.rows[0:1, 512:1024]
            v.tensor_scalar(m, stats[0:1, 0:512], 1.0 / D, None, ALU.mult)
            v.tensor_scalar(var, stats2[0:1, 0:512], 1.0 / D, None, ALU.mult)
            v.tensor_mul(p.tmpv[0:1, 0:512], m, m)
            v.tensor_sub(var, var, p.tmpv[0:1, 0:512])
            v.tensor_scalar(var, var, 1.0, EPS, ALU.mult, ALU.add)
            rv = p.s_v.inc(v.reciprocal(var, var))
            p.s_v.wait(s, rv)
            sv = p.s_s.inc(s.activation(var, var, AF.Sqrt))
            p.s_s.wait(g, sv)
            p.s_v.wait(g)
            wv = p.s_bw.dma(g.dma_start(out=p.bscr[16:17, 0:512], in_=m))
            wv = p.s_bw.dma(g.dma_start(out=p.bscr[18:19, 0:512], in_=var))
            p.s_bw.wait(g, wv)
            p.s_bc.dma(g.dma_start(out=p.bc1[:, 0:512],
                                   in_=p.bscr[16:17, 0:512].broadcast_to([128, 512])))
            bv = p.s_bc.dma(g.dma_start(out=p.bc2[:, 0:512],
                                        in_=p.bscr[18:19, 0:512].broadcast_to([128, 512])))
            p.s_bc.wait(v, bv)
            for oc in range(8):
                ysl = y1[:, oc * 512 : oc * 512 + 512]
                v.tensor_sub(ysl, ysl, p.bc1[:, 0:512])
                v.tensor_mul(ysl, ysl, p.bc2[:, 0:512])
                v.tensor_scalar(ysl, ysl, gamv[:, oc : oc + 1], betv[:, oc : oc + 1],
                                ALU.mult, ALU.add)
                gv = p.s_v.v
                p.s_v.wait(s, gv)
                sv = p.s_s.inc(s.activation(ysl, ysl, AF.Gelu))
                p.s_s.wait(t, sv)
                mm = t.matmul(p.pb[5][0:32, :], lab[:, oc * 32 : oc * 32 + 32],
                              ysl, start=(oc == 0), stop=(oc == 7))
            ltv = p.s_t.inc(mm)
            p.s_t.wait(v, ltv)
            v.tensor_copy(lra[0:32, :], p.pb[5][0:32, :])
            lrv = p.s_v.v
            p.s_v.wait(t, lrv)
            for oc in range(8):
                bank = p.pb[2 + oc % 2]
                if oc >= 2:
                    p.s_v.wait(t, l2v_ck[oc - 2])
                else:
                    p.s_s.wait(t, l1sc_ck[6 + oc])
                for kc in range(8):
                    t.matmul(bank, wd2(kc)[:, oc * 128 : oc * 128 + 128],
                             y1[:, kc * 512 : kc * 512 + 512],
                             start=(kc == 0), stop=False)
                mm = t.matmul(bank, lbb[0:32, oc * 128 : oc * 128 + 128],
                              lra[0:32, :], start=False, stop=True)
                tv = p.s_t.inc(mm)
                prev_tv = tv
                p.s_t.wait(v, tv)
                rsl = p.R[:, oc * T + ts0 : oc * T + ts0 + 512]
                v.scalar_tensor_tensor(p.tmpv[:, 0:512], bank, 1.0, rsl,
                                       ALU.mult, ALU.add)
                v.tensor_scalar(p.tmpv[:, 0:512], p.tmpv[:, 0:512],
                                qb2v[:, oc : oc + 1], None, ALU.add)
                v.tensor_copy(rsl, p.tmpv[:, 0:512])
                l2v_ck[oc] = p.s_v.v
        _dbg(p, f"rb{blk}", p.R, [128, 8 * T])

    p.s_t.wait(sy)
    p.s_v.wait(sy)
    p.s_s.wait(sy)
    for fc in range(8):
        lv = p.s_ra.dma(sy.dma_start(out=p.big[:, fc * T : fc * T + T],
                                     in_=p.z_hbm[fc, :, :]))
    p.s_ra.wait(v)
    for fc in range(8):
        rsl = p.R[:, fc * T : (fc + 1) * T]
        v.tensor_add(rsl, rsl, p.big[:, fc * T : fc * T + T])

    _post_ln(p, ln_idx=3)


def _stage_inputs(inputs, names):
    u = np.ascontiguousarray(np.asarray(inputs["L0_attn_u"], dtype=np.float32))
    w_in_t = np.zeros((D, DPROJ_PAD), np.float32)
    w_in_t[:, :DPROJ] = np.asarray(inputs["L0_attn_W_in"], np.float32).T
    common = {
        "w_in_t": w_in_t,
        "conv_w": np.ascontiguousarray(np.asarray(inputs["L0_attn_conv_w"], np.float32)),
        "conv_b": np.asarray(inputs["L0_attn_conv_b"], np.float32)[:, None].copy(),
        "dt_bias": np.asarray(inputs["L0_attn_dt_bias"], np.float32)[:, None].copy(),
        "a_log": np.asarray(inputs["L0_attn_A_log"], np.float32)[:, None].copy(),
        "d_rep": np.repeat(np.asarray(inputs["L0_attn_D"], np.float32), HP)[:, None].copy(),
        "nw_in": np.asarray(inputs["L0_attn_norm_weight"], np.float32)[:, None].copy(),
        "w_out_t": np.ascontiguousarray(np.asarray(inputs["L0_attn_W_out"], np.float32).T),
        "ident": np.eye(128, dtype=np.float32),
        "mask01": np.triu(np.ones((128, 128), np.float32)).copy(),
    }
    if "w_up" in names:
        common["w_up"] = np.ascontiguousarray(np.asarray(inputs["L0_ffn_w_up"], np.float32))
        common["w_gate"] = np.ascontiguousarray(np.asarray(inputs["L0_ffn_w_gate"], np.float32))
        common["w_down"] = np.ascontiguousarray(np.asarray(inputs["L0_ffn_w_down"], np.float32))
    if "wz" in names:
        common["wz"] = np.ascontiguousarray(np.asarray(inputs["L1_attn_Wz"], np.float32))
        common["wh"] = np.ascontiguousarray(np.asarray(inputs["L1_attn_Wh"], np.float32))
        common["bz"] = np.asarray(inputs["L1_attn_bz"], np.float32)[:, None].copy()
        common["bh"] = np.asarray(inputs["L1_attn_bh"], np.float32)[:, None].copy()
        common["log_d"] = np.asarray(inputs["L1_attn_log_D"], np.float32)[:, None].copy()
        md = np.zeros((128, 128), np.float32)
        for i_ in range(128):
            md[i_, i_] = -2.0
            if i_ + 1 < 128:
                md[i_, i_ + 1] = 1.0
                md[i_ + 1, i_] = 1.0
        common["mdiff"] = md
        mwp_ = np.zeros((128, 128), np.float32)
        mwp_[127, 0] = 1.0
        common["mwp"] = mwp_
        mwn_ = np.zeros((128, 128), np.float32)
        mwn_[0, 127] = 1.0
        common["mwn"] = mwn_
    if "qw_t" in names:
        qw = np.asarray(inputs["L1_ffn_qweights"], np.int32)
        common["qw_t"] = np.ascontiguousarray(qw.transpose(0, 2, 1))
        sc = np.asarray(inputs["L1_ffn_scales"], np.float32)
        common["srep_t"] = np.ascontiguousarray(
            np.repeat(sc, 16, axis=2).transpose(0, 2, 1))
        common["qb"] = np.ascontiguousarray(np.asarray(inputs["L1_ffn_biases"], np.float32).T)
        common["la_t"] = np.ascontiguousarray(
            np.asarray(inputs["L1_ffn_lora_a"], np.float32).transpose(0, 2, 1))
        common["lb_t"] = np.ascontiguousarray(
            np.asarray(inputs["L1_ffn_lora_b"], np.float32).transpose(0, 2, 1))
        common["gam"] = np.ascontiguousarray(np.asarray(inputs["L1_ffn_ln_gamma"], np.float32).T)
        common["bet"] = np.ascontiguousarray(np.asarray(inputs["L1_ffn_ln_beta"], np.float32).T)
    in_maps = []
    for c in range(NCORES):
        m = {k: vv for k, vv in common.items() if k in names}
        m["u"] = np.ascontiguousarray(u[c].T)
        in_maps.append(m)
    return in_maps


def kernel(**inputs):
    p = build(_FULL)
    in_maps = _stage_inputs(inputs, set(p.in_names))
    res = run_bass_kernel_spmd(p.nc, in_maps, core_ids=list(range(NCORES)))
    outs = [np.ascontiguousarray(res.results[c]["out"].T) for c in range(NCORES)]
    return np.stack(outs, axis=0)


# version 06
# speedup vs baseline: 1.2216x; 1.0191x over previous
import numpy as np
import concourse.bass as bass
import concourse.mybir as mybir
from concourse.bass_utils import run_bass_kernel_spmd

F32 = mybir.dt.float32
F32R = mybir.dt.float32r
I32 = mybir.dt.int32
AF = mybir.ActivationFunctionType
ALU = mybir.AluOpType

NCORES = 8
T = 2048
D = 1024
DIN = 2048
NH = 32
HP = 64
DSTATE = 128
CONV_DIM = DIN + 2 * DSTATE
DPROJ = 2 * DIN + 2 * DSTATE + NH
DPROJ_PAD = 35 * 128
LO = 256
LC = 128
NO = T // LO
IFF = 3072
QMAX = 32767.0
EPS = 1e-5

_FULL = 4


class RecInstr:
    __slots__ = ("incs",)

    def __init__(self):
        self.incs = []

    def then_inc(self, sem_h, n):
        self.incs.append((sem_h, n))
        return self


_NOCHAIN = {"wait_ge", "dma_start", "indirect_dma_start", "collective_compute",
            "sem_inc", "sem_clear", "nop"}


def scope(p, name):
    for rec in (p.sy, p.t, p.v, p.s, p.g):
        rec.cmds.append(("__scope__", (name,), {}, RecInstr()))


class Rec:

    def __init__(self, name, chain_sem=None):
        self.name = name
        self.cmds = []
        self.chain = chain_sem
        self.ccount = 0

    def __getattr__(self, attr):
        def call(*a, **k):
            h = RecInstr()
            if self.chain is not None and attr not in _NOCHAIN:
                if self.ccount > 0:
                    self.cmds.append(("wait_ge", (self.chain, self.ccount), {}, RecInstr()))
                h.incs.append((self.chain, 1))
                self.ccount += 1
            self.cmds.append((attr, a, k, h))
            return h
        return call


def _replay(nc, eng, cmds):
    import contextlib
    stack = contextlib.ExitStack()
    with stack:
        for attr, a, k, h in cmds:
            if attr == "__scope__":
                stack.close()
                stack = contextlib.ExitStack()
                if a[0] is not None:
                    stack.enter_context(nc.named_scope(a[0]))
                continue
            r = getattr(eng, attr)(*a, **k)
            for sem_h, n in h.incs:
                r.then_inc(sem_h, n)


class Sem:
    def __init__(self, nc, name):
        self.h = nc.alloc_semaphore(name)
        self.v = 0

    def inc(self, instr, n=1):
        instr.then_inc(self.h, n)
        self.v += n
        return self.v

    def dma(self, instr):
        return self.inc(instr, 16)

    def wait(self, eng, val=None):
        eng.wait_ge(self.h, self.v if val is None else val)


class ChainSem:

    def __init__(self, rec):
        self.rec = rec
        self.h = rec.chain

    @property
    def v(self):
        return self.rec.ccount

    def inc(self, instr, n=1):
        return self.rec.ccount

    def wait(self, eng, val=None):
        eng.wait_ge(self.h, self.rec.ccount if val is None else val)


def rep(ap, k):
    return ap.unsqueeze(2).broadcast_to([ap.shape[0], ap.shape[1], k])


class P:
    pass


def _alloc(p, upto, dbg):
    nc = p.nc
    p.upto = upto
    p.dbg = set(dbg)

    def ein(name, shape, dt=F32):
        return nc.dram_tensor(name, shape, dt, kind="ExternalInput").ap()

    p.in_names = []

    def reg(name, shape, dt=F32):
        p.in_names.append(name)
        return ein(name, shape, dt)

    p.u = reg("u", [D, T])
    p.w_in_t = reg("w_in_t", [D, DPROJ_PAD])
    p.conv_w = reg("conv_w", [CONV_DIM, 4])
    p.conv_b = reg("conv_b", [CONV_DIM, 1])
    p.dt_bias = reg("dt_bias", [NH, 1])
    p.a_log = reg("a_log", [NH, 1])
    p.d_rep = reg("d_rep", [DIN, 1])
    p.nw_in = reg("nw_in", [DIN, 1])
    p.w_out_t = reg("w_out_t", [DIN, D])
    p.ident = reg("ident", [128, 128])
    p.mask01 = reg("mask01", [128, 128])
    if upto >= 2:
        p.w_up = reg("w_up", [IFF, D])
        p.w_gate = reg("w_gate", [IFF, D])
        p.w_down = reg("w_down", [D, IFF])
    if upto >= 3:
        p.wz = reg("wz", [D, D])
        p.wh = reg("wh", [D, D])
        p.bz = reg("bz", [D, 1])
        p.bh = reg("bh", [D, 1])
        p.log_d = reg("log_d", [D, 1])
        p.mdiff = reg("mdiff", [128, 128])
        p.mwp = reg("mwp", [128, 128])
        p.mwn = reg("mwn", [128, 128])
    if upto >= 4:
        p.qw_t = reg("qw_t", [6, D, D], I32)
        p.srep_t = reg("srep_t", [6, D, D])
        p.qb = reg("qb", [D, 6])
        p.la_t = reg("la_t", [6, D, 32])
        p.lb_t = reg("lb_t", [6, 32, D])
        p.gam = reg("gam", [D, 3])
        p.bet = reg("bet", [D, 3])

    p.out = nc.dram_tensor("out", [D, T], F32, kind="ExternalOutput").ap()

    p.z_hbm = nc.dram_tensor("z_hbm", [16, 128, T], F32).ap()
    p.xbc_hbm = nc.dram_tensor("xbc_hbm", [18, 128, T], F32).ap()
    p.y_hbm = nc.dram_tensor("y_hbm", [16, 128, T], F32).ap()
    p.bscr = nc.dram_tensor("bscr", [34, 512], F32).ap()
    if upto >= 2:
        p.up_hbm = nc.dram_tensor("up_hbm", [24, 128, T], F32).ap()
        p.gate_hbm = nc.dram_tensor("gate_hbm", [24, 128, T], F32).ap()
        p.g2_hbm = nc.dram_tensor("g2_hbm", [24, 128, T], F32).ap()
        p.o_hbm = nc.dram_tensor("o_hbm", [24, 128, T], F32).ap()
        p.wdq_hbm = nc.dram_tensor("wdq_hbm", [IFF, D], F32).ap()
        p.cc_in = nc.dram_tensor("cc_in", [1, 4], F32)
        p.cc_out = nc.dram_tensor("cc_out", [1, 4], F32)

    def sb(name, free):
        return nc.alloc_sbuf_tensor(name, [128, free], F32).ap()

    def addr():
        return (nc.sbuf_base + 31) // 32 * 32

    a_R = addr()
    p.R = sb("R", 8 * T)
    a_big = addr()
    p.big = sb("big", 16384)
    p.gbs = sb("gbs", 4096)
    p.gb0 = p.gbs[:, 0:1024]
    p.gb1 = p.gbs[:, 1024:2048]
    p.gb2 = p.gbs[:, 2048:3072]
    p.xd = sb("xd", 512)
    p.state = sb("state", DIN)
    a_dtr = addr()
    p.dt_raw = sb("dt_raw", T)
    a_rsa = addr()
    p.rstd_all = sb("rstd_all", T)
    p.ysq = sb("ysq", DIN)
    p.yoct = sb("yoct", 16 * LO)
    p.idn = sb("idn", 128)
    p.msk = sb("msk", 128)
    p.nbt = sb("nbt", 32)
    p.dst = sb("dst", 32)
    p.btm = sb("btm", 128)
    p.cbm = sb("cbm", 128)
    p.ones = sb("ones", 8)
    p.sml = sb("sml", 64)
    p.rows = sb("rows", 1024)
    p.bc1 = p.gbs[:, 0:2048]
    p.bc2 = p.gbs[:, 2048:4096]
    p.tmpv = sb("tmpv", 1024)
    p.cw = sb("cw", 96)
    p.dr = sb("dr", 16)
    p.nw = sb("nw", 16)

    p.pb = [nc.alloc_psum_tensor(f"pb{i}", [128, 512], F32).ap() for i in range(8)]

    p.xqh_r = nc.alloc_sbuf_tensor_at("xqh_r", [128, 8 * T], F32R, offset=a_R).ap()
    p.xql_r = nc.alloc_sbuf_tensor_at("xql_r", [128, 16384], F32R, offset=a_big).ap()
    p.wlb_r = nc.alloc_sbuf_tensor_at("wlb_r", [128, T], F32R, offset=a_dtr).ap()
    p.whb_r = nc.alloc_sbuf_tensor_at("whb_r", [128, T], F32R, offset=a_rsa).ap()

    for s in ["ld", "t", "o2", "wa", "wb", "wc", "wd", "ea", "eb", "ra", "rb",
              "zc", "bw", "bc", "be", "yw", "yb", "cc", "ci", "co", "rr"]:
        setattr(p, "s_" + s, Sem(nc, "sem_" + s))


def build(upto=_FULL, dbg=()):
    p = P()
    p.nc = bass.Bass("TRN2", target_bir_lowering=False)
    _alloc(p, upto, dbg)
    vch = p.nc.alloc_semaphore("sem_vch")
    sch = p.nc.alloc_semaphore("sem_sch")
    gch = p.nc.alloc_semaphore("sem_gch")
    p.sy, p.t, p.v, p.s, p.g = (Rec("sy"), Rec("t"), Rec("v", vch), Rec("s", sch),
                                Rec("g", gch))
    p.s_v = ChainSem(p.v)
    p.s_s = ChainSem(p.s)
    p.s_gc = ChainSem(p.g)
    _emit(p)
    with p.nc.Block() as block:
        @block.sync
        def _(e):
            _replay(p.nc, e, p.sy.cmds)

        @block.tensor
        def _(e):
            _replay(p.nc, e, p.t.cmds)

        @block.vector
        def _(e):
            _replay(p.nc, e, p.v.cmds)

        @block.scalar
        def _(e):
            _replay(p.nc, e, p.s.cmds)

        @block.gpsimd
        def _(e):
            _replay(p.nc, e, p.g.cmds)
    return p


def _dbg(p, name, ap, shape):
    if name in p.dbg:
        o = p.nc.dram_tensor("dbg_" + name, shape, F32, kind="ExternalOutput").ap()
        p.s_v.wait(p.sy)
        p.s_s.wait(p.sy)
        p.s_o2.dma(p.sy.dma_start(out=o, in_=ap))


def _fence_compute(p):
    for e in (p.t, p.v, p.s, p.sy, p.g):
        for s in (p.s_t, p.s_v, p.s_s, p.s_gc, p.s_ld, p.s_wa, p.s_wb, p.s_wc,
                  p.s_wd, p.s_ea, p.s_eb, p.s_ra, p.s_rb, p.s_zc, p.s_bw, p.s_bc,
                  p.s_yw, p.s_yb):
            s.wait(e)


def _emit(p):
    sy, t, v, s, g = p.sy, p.t, p.v, p.s, p.g

    p.s_ld.dma(sy.dma_start(out=p.idn, in_=p.ident))
    p.s_ld.dma(sy.dma_start(out=p.msk, in_=p.mask01))
    for fc in range(8):
        p.s_ld.dma(sy.dma_start(out=p.R[:, fc * T : (fc + 1) * T],
                                in_=p.u[fc * 128 : (fc + 1) * 128, :]))
    p.s_ld.dma(sy.dma_start(out=p.sml[0:NH, 0:1], in_=p.dt_bias))
    p.s_ld.dma(sy.dma_start(out=p.sml[0:NH, 1:2], in_=p.a_log))
    p.s_ld.dma(sy.dma_start(
        out=p.cw[:, 0:72].rearrange("p (j w) -> p j w", j=18),
        in_=p.conv_w.rearrange("(j p) w -> p j w", p=128),
        allow_slow_non_contiguous=True))
    p.s_ld.dma(sy.dma_start(
        out=p.cw[:, 72:90].rearrange("p (j o) -> p j o", o=1),
        in_=p.conv_b.rearrange("(j p) o -> p j o", p=128),
        allow_slow_non_contiguous=True))
    p.s_ld.dma(sy.dma_start(
        out=p.dr.rearrange("p (j o) -> p j o", o=1),
        in_=p.d_rep.rearrange("(j p) o -> p j o", p=128),
        allow_slow_non_contiguous=True))
    p.s_ld.dma(sy.dma_start(
        out=p.nw.rearrange("p (j o) -> p j o", o=1),
        in_=p.nw_in.rearrange("(j p) o -> p j o", p=128),
        allow_slow_non_contiguous=True))
    p.s_ld.wait(v)
    p.s_v.inc(v.memset(p.ones, 1.0))
    p.s_ld.wait(s)
    xv = p.s_s.inc(s.activation(p.sml[0:NH, 2:3], p.sml[0:NH, 1:2], AF.Exp))
    p.s_s.wait(s, xv)
    p.s_s.inc(s.mul(p.sml[0:NH, 2:3], p.sml[0:NH, 2:3], -1.0))
    p.negA = p.sml[0:NH, 2:3]
    p.dtbv = p.sml[0:NH, 0:1]
    p.s_v.wait(t)
    p.s_s.wait(t)

    _mamba(p)
    if p.upto >= 2:
        _ffn0(p)
    if p.upto >= 3:
        _l1_attn(p)
    if p.upto >= 4:
        _l1_ffn(p)

    p.s_v.wait(sy)
    p.s_s.wait(sy)
    for fc in range(8):
        p.s_o2.dma(sy.dma_start(out=p.out[fc * 128 : (fc + 1) * 128, :],
                                in_=p.R[:, fc * T : (fc + 1) * T]))
    p.s_o2.wait(sy)


def _mamba(p):
    sy, t, v, s, g = p.sy, p.t, p.v, p.s, p.g

    scope(p, 'mamba_inproj')
    wbuf = p.big[:, 0:2048]
    tmpe = p.big[:, 2048:3072]
    wl_ck = []
    tm_ck = [0] * 200
    ev_ck = [0] * 200
    sp_ck = [0] * 200
    wsem = [p.s_wa, p.s_wb]
    esem = [p.s_ea, p.s_eb]
    p.s_v.inc(v.memset(p.state, 0.0))
    for oc in range(35):
        wb = wbuf[:, (oc % 2) * 1024 : (oc % 2) * 1024 + 1024]
        if oc >= 2:
            sy.wait_ge(p.s_t.h, tm_ck[(oc - 2) * 4 + 3])
        src = p.w_in_t[:, oc * 128 : (oc + 1) * 128].rearrange("(kc kp) o -> kp kc o", kp=128)
        wv = wsem[oc % 2].dma(sy.dma_start(out=wb.rearrange("kp (kc o) -> kp kc o", kc=8), in_=src))
        wl_ck.append(wv)
        for tt in range(4):
            i = oc * 4 + tt
            bank = p.pb[i % 2]
            wsem[oc % 2].wait(t, wl_ck[oc])
            if i >= 2:
                p.s_v.wait(t, ev_ck[i - 2])
            for kc in range(8):
                mm = t.matmul(bank, wb[:, kc * 128 : (kc + 1) * 128],
                              p.R[:, kc * T + tt * 512 : kc * T + tt * 512 + 512],
                              start=(kc == 0), stop=(kc == 7))
            tm_ck[i] = p.s_t.inc(mm)
            p.s_t.wait(v, tm_ck[i])
            if oc < 34:
                slot = i % 2
                te = tmpe[:, slot * 512 : slot * 512 + 512]
                if i >= 2:
                    esem[slot].wait(v, sp_ck[i - 2])
                ev_ck[i] = p.s_v.inc(v.tensor_copy(te, bank))
                p.s_v.wait(g, ev_ck[i])
                dst = (p.z_hbm if oc < 16 else p.xbc_hbm)
                row = oc if oc < 16 else oc - 16
                sp_ck[i] = esem[slot].dma(
                    g.dma_start(out=dst[row, :, tt * 512 : tt * 512 + 512], in_=te))
            else:
                ev_ck[i] = p.s_v.inc(v.tensor_copy(p.dt_raw[:, tt * 512 : tt * 512 + 512], bank))

    p.s_ea.wait(sy)
    p.s_eb.wait(sy)
    _dbg(p, "dt_raw", p.dt_raw, [128, T])

    scope(p, 'mamba_chunks')
    xbcs = p.big[:, 0 : 18 * LO]
    zc = p.big[:, 4608 : 4608 + 2048]
    xtm = p.big[:, 6656 : 6656 + 2048]
    dtb = p.big[0:NH, 8704 : 8704 + 8 * LO]
    raw = p.big[:, 10752 : 10752 + 2 * 260]
    for o in range(NO):
        to = o * LO
        dts = dtb[:, 0 * LO : 1 * LO]
        lndt = dtb[:, 1 * LO : 2 * LO]
        dA = dtb[:, 2 * LO : 3 * LO]
        cA = dtb[:, 3 * LO : 4 * LO]
        nb = dtb[:, 4 * LO : 5 * LO]
        dsb = dtb[:, 5 * LO : 6 * LO]
        t32 = dtb[:, 6 * LO : 7 * LO]
        p.s_v.wait(s)
        p.s_t.wait(s)
        av = p.s_s.inc(s.activation(t32, p.dt_raw[0:NH, to : to + LO], AF.Exp, bias=p.dtbv))
        p.s_s.wait(s, av)
        av = p.s_s.inc(s.activation(dts, t32, AF.Ln, bias=1.0))
        p.s_s.wait(s, av)
        sv = p.s_s.inc(s.activation(lndt, dts, AF.Ln))
        p.s_s.wait(v, sv)
        vv = p.s_v.inc(v.tensor_scalar(dA, dts, p.negA, None, ALU.mult))
        p.s_v.wait(v, vv)
        for cc in range(2):
            sl = slice(cc * LC, (cc + 1) * LC)
            vv = p.s_v.inc(v.tensor_tensor_scan(
                cA[:, sl], p.ones[0:NH, 0:1].broadcast_to([NH, LC]),
                dA[:, sl], 0.0, ALU.mult, ALU.add))
        p.s_v.wait(v, vv)
        v.tensor_sub(nb, lndt, cA)
        for cc in range(2):
            sl = slice(cc * LC, (cc + 1) * LC)
            v.tensor_scalar(t32[:, sl], cA[:, sl], -1.0,
                            cA[:, cc * LC + LC - 1 : cc * LC + LC], ALU.mult, ALU.add)
            v.tensor_add(t32[:, sl], t32[:, sl], lndt[:, sl])
        dv = p.s_v.v
        p.s_v.wait(s, dv)
        p.s_s.inc(s.activation(dsb, t32, AF.Exp))

        cv_ck = [0] * 18
        rsem = [p.s_ra, p.s_rb]
        for j in range(18):
            slot = j % 2
            rb = raw[:, slot * 260 : slot * 260 + 260]
            if j >= 2:
                p.s_v.wait(sy, cv_ck[j - 2])
            if o == 0:
                lv = rsem[slot].dma(sy.dma_start(out=rb[:, 3:259], in_=p.xbc_hbm[j, :, 0:LO]))
            else:
                lv = rsem[slot].dma(sy.dma_start(out=rb[:, 0:259],
                                                 in_=p.xbc_hbm[j, :, to - 3 : to + LO]))
            rsem[slot].wait(v, lv)
            if o == 0:
                v.memset(rb[:, 0:3], 0.0)
            xj = xbcs[:, j * LO : (j + 1) * LO]
            v.tensor_scalar(xj, rb[:, 0:LO], p.cw[:, 4 * j : 4 * j + 1],
                            p.cw[:, 72 + j : 73 + j], ALU.mult, ALU.add)
            for w in range(1, 4):
                v.scalar_tensor_tensor(xj, rb[:, w : w + LO], p.cw[:, 4 * j + w : 4 * j + w + 1],
                                       xj, ALU.mult, ALU.add)
            cv_ck[j] = p.s_v.v
            p.s_v.wait(s, cv_ck[j])
            p.s_s.inc(s.activation(xj, xj, AF.Silu))

        _dbg(p, f"xbcs_o{o}", xbcs, [128, 18 * LO])

        for cc in range(2):
            _mamba_chunk(p, o, cc, xbcs, zc, xtm, cA, nb, dsb)

        p.s_v.wait(g)
        p.s_yw.dma(g.dma_start(
            out=p.y_hbm[:, :, to : to + LO].transpose([1, 0, 2]),
            in_=p.yoct.rearrange("p (fc l) -> p fc l", fc=16)))

    p.s_yw.wait(sy)
    _dbg(p, "rstd_all", p.rstd_all, [128, T])

    scope(p, 'mamba_outproj')
    _outproj_ln(p)


def _mamba_chunk(p, o, cc, xbcs, zc, xtm, cA, nb, dsb):
    sy, t, v, s, g = p.sy, p.t, p.v, p.s, p.g
    tg = o * LO + cc * LC
    sl = slice(cc * LC, (cc + 1) * LC)

    p.s_v.wait(sy)
    p.s_s.wait(sy)
    zv = p.s_zc.dma(sy.dma_start(
        out=zc.rearrange("p (fc l) -> p fc l", fc=16),
        in_=p.z_hbm[:, :, tg : tg + LC].transpose([1, 0, 2])))

    p.s_v.wait(g)
    p.s_s.wait(g)
    bflat = p.bscr.rearrange("a b -> (a b)")
    wv = p.s_bw.dma(g.dma_start(
        out=bflat[0 : NH * LC].rearrange("(h l) -> h l", l=LC), in_=cA[:, sl]))

    bank_tr = p.pb[2]
    bank_cb = p.pb[3]
    xb16 = xbcs[:, 16 * LO : 16 * LO + LO]
    xb17 = xbcs[:, 17 * LO : 17 * LO + LO]
    p.s_s.wait(t)
    p.s_v.wait(t)
    p.s_t.inc(t.matmul(bank_cb[:, 0:128], xb16[:, sl], xb17[:, sl], start=True, stop=True))
    p.s_t.inc(t.transpose(bank_tr[:, 0:32], nb[:, sl], p.idn[0:32, 0:32]))
    p.s_t.inc(t.transpose(bank_tr[:, 32:64], dsb[:, sl], p.idn[0:32, 0:32]))
    tv = p.s_t.inc(t.transpose(bank_tr[:, 128:256], xb16[:, sl], p.idn))
    p.s_t.wait(v, tv)
    v.tensor_copy(p.nbt, bank_tr[:, 0:32])
    v.tensor_copy(p.dst, bank_tr[:, 32:64])
    v.tensor_copy(p.btm, bank_tr[:, 128:256])
    ev = p.s_v.inc(v.tensor_mul(p.cbm, bank_cb[:, 0:128], p.msk))
    for g4 in range(4):
        p.s_v.wait(t, ev)
        for k in range(4):
            fc = g4 * 4 + k
            tv = p.s_t.inc(t.transpose(
                bank_tr[:, k * 128 : (k + 1) * 128],
                xbcs[:, fc * LO + cc * LC : fc * LO + cc * LC + LC], p.idn))
        p.s_t.wait(v, tv)
        ev = p.s_v.inc(v.tensor_copy(xtm[:, g4 * 512 : (g4 + 1) * 512], bank_tr))

    bufsets = [
        (p.gb0, p.gb1, p.gb2, p.xd[:, 0:512], p.pb[0], p.s_bc),
        (p.big[:, 12288:13312], p.big[:, 13312:14336], p.big[:, 14336:15360],
         p.big[:, 15360:15872], p.pb[1], p.s_be),
    ]
    xb17 = xbcs[:, 17 * LO : 17 * LO + LO]
    bflat = p.bscr.rearrange("a b -> (a b)")
    g_tv = {}
    g_gv = {}
    g_vv = {}

    def build_group(gi):
        h0 = gi * 8
        b0, b1, b2, xd, sbank, bsem = bufsets[gi % 2]
        p.s_bw.wait(g, wv)
        if gi >= 2:
            p.s_t.wait(g, g_tv[gi - 2])
            p.s_v.wait(g, g_gv[gi - 2])
        bv = bsem.dma(g.dma_start(
            out=b0,
            in_=bflat[h0 * LC : h0 * LC + 1024].unsqueeze(0).broadcast_to([128, 1024])))
        bsem.wait(v, bv)
        if gi >= 2:
            p.s_t.wait(v, g_tv[gi - 2])
        b0v = b0.rearrange("p (h l) -> p h l", h=8)
        v.tensor_tensor(b0v, b0v, rep(p.nbt[:, h0 : h0 + 8], LC), ALU.add)
        mv = p.s_v.inc(v.tensor_scalar(b2, b0, 30.0, None, ALU.min))
        p.s_v.wait(s, mv)
        if gi >= 2:
            p.s_t.wait(s, g_tv[gi - 2])
        sv = p.s_s.inc(s.activation(b1, b2, AF.Exp))
        p.s_s.wait(v, sv)
        b1v = b1.rearrange("p (h l) -> p h l", h=8)
        v.tensor_tensor(b1v, b1v, rep(p.cbm, 8).transpose([0, 2, 1]), ALU.mult)
        v.tensor_tensor(b0v, b0v, rep(p.nbt[:, h0 : h0 + 8], LC), ALU.subtract)
        mv = p.s_v.v
        p.s_v.wait(s, mv)
        sv = p.s_s.inc(s.activation(b0, b0, AF.Exp))
        p.s_s.wait(v, sv)
        for hh in range(8):
            h = h0 + hh
            v.tensor_scalar(xd[:, hh * 64 : hh * 64 + 64],
                            xtm[:, h * 64 : h * 64 + 64],
                            p.dst[:, h : h + 1], None, ALU.mult)
        b2v = b2.rearrange("p (h l) -> p h l", h=8)
        v.tensor_tensor(b2v, rep(xb17[:, sl], 8).transpose([0, 2, 1]),
                        b0.rearrange("p (h l) -> p h l", h=8), ALU.mult)
        g_vv[gi] = p.s_v.v

    def mm_group(gi):
        h0 = gi * 8
        b0, b1, b2, xd, sbank, bsem = bufsets[gi % 2]
        ybank = p.pb[4 + gi]
        p.s_v.wait(t, g_vv[gi])
        for hh in range(8):
            h = h0 + hh
            ypart = (hh % 2) * 64
            yfree = (hh // 2) * 128
            yout = ybank[ypart : ypart + 64, yfree : yfree + 128]
            t.matmul(yout, xtm[:, h * 64 : h * 64 + 64],
                     b1[:, hh * 128 : hh * 128 + 128],
                     start=(hh < 2), stop=False, skip_group_check=True)
            t.matmul(yout, p.state[:, h * 64 : h * 64 + 64],
                     b2[:, hh * 128 : hh * 128 + 128],
                     start=False, stop=(hh >= 6), skip_group_check=True)
            mm = t.matmul(sbank[:, hh * 64 : hh * 64 + 64], p.btm,
                          xd[:, hh * 64 : hh * 64 + 64],
                          start=(hh == 0), stop=(hh == 7))
        g_tv[gi] = tv = p.s_t.inc(mm)
        p.s_t.wait(v, tv)
        for hh in range(8):
            h = h0 + hh
            stsl = p.state[:, h * 64 : h * 64 + 64]
            v.tensor_scalar(stsl, stsl, b0[:, hh * 128 + 127 : hh * 128 + 128],
                            None, ALU.mult)
        g_gv[gi] = p.s_v.inc(v.tensor_add(p.state[:, h0 * 64 : h0 * 64 + 512],
                                          p.state[:, h0 * 64 : h0 * 64 + 512], sbank))

    for gi in range(4):
        build_group(gi)
        if gi >= 1:
            mm_group(gi - 1)
    mm_group(3)

    p.s_zc.wait(s, zv)
    p.s_zc.wait(v, zv)
    p.s_yw.wait(v)
    sv = p.s_s.inc(s.activation(zc, zc, AF.Silu))
    p.s_s.wait(v, sv)
    for fc in range(16):
        ybank = p.pb[4 + fc // 4]
        yfree = (fc % 4) * 128
        dst_y = p.yoct[:, fc * LO + cc * LC : fc * LO + cc * LC + LC]
        v.scalar_tensor_tensor(p.tmpv[:, 0:128],
                               xbcs[:, fc * LO + cc * LC : fc * LO + cc * LC + LC],
                               p.dr[:, fc : fc + 1],
                               ybank[:, yfree : yfree + 128], ALU.mult, ALU.add)
        v.tensor_mul(dst_y, p.tmpv[:, 0:128], zc[:, fc * 128 : fc * 128 + 128])
    yv = p.s_v.v
    p.s_v.wait(s, yv)
    sv = p.s_s.inc(s.activation(p.ysq.rearrange("p (fc l) -> p fc l", fc=16),
                                p.yoct.rearrange("p (fc l) -> p fc l", fc=16)[:, :, cc * LC : cc * LC + LC],
                                AF.Square))
    p.s_s.wait(t, sv)
    p.s_v.wait(t)
    stats = p.pb[1]
    for fc in range(16):
        mm = t.matmul(stats[0:1, 0:128], p.ones[:, 0:1],
                      p.ysq[:, fc * 128 : fc * 128 + 128],
                      start=(fc == 0), stop=(fc == 15))
    tv = p.s_t.inc(mm)
    p.s_t.wait(v, tv)
    v.tensor_scalar(p.rows[0:1, 8:136], stats[0:1, 0:128], 1.0 / DIN, EPS,
                    ALU.mult, ALU.add)
    rv = p.s_v.inc(v.reciprocal(p.rows[0:1, 8:136], p.rows[0:1, 8:136]))
    p.s_v.wait(s, rv)
    sv = p.s_s.inc(s.activation(p.rows[0:1, 8:136], p.rows[0:1, 8:136], AF.Sqrt))
    p.s_s.wait(v, sv)
    p.s_v.inc(v.tensor_copy(p.rstd_all[0:1, tg : tg + LC], p.rows[0:1, 8:136]))


def _outproj_ln(p):
    sy, t, v, s, g = p.sy, p.t, p.v, p.s, p.g
    _fence_compute(p)
    wo4 = p.big[:, 0:8192]
    ybuf = p.big[:, 8192:12288]
    prev_tv = 0
    for ocg in range(2):
        wl = 0
        if ocg == 1:
            sy.wait_ge(p.s_t.h, prev_tv)
        for j in range(4):
            oc = ocg * 4 + j
            wj = wo4[:, j * 2048 : j * 2048 + 2048]
            src = p.w_out_t[:, oc * 128 : (oc + 1) * 128].rearrange(
                "(kc kp) o -> kp kc o", kp=128)
            wl = p.s_wa.dma(sy.dma_start(
                out=wj.rearrange("kp (kc o) -> kp kc o", kc=16), in_=src))
        p.s_wa.wait(v, wl)
        for j in range(4):
            wj = wo4[:, j * 2048 : j * 2048 + 2048]
            v.tensor_tensor(wj.rearrange("p (kc o) -> p kc o", kc=16),
                            wj.rearrange("p (kc o) -> p kc o", kc=16),
                            rep(p.nw, 128), ALU.mult)
        wsv = p.s_v.v

        for sub in range(8):
            ts0 = sub * 256
            if not (ocg == 0 and sub == 0):
                sy.wait_ge(p.s_t.h, prev_tv)
            yv = p.s_yb.dma(sy.dma_start(
                out=ybuf.rearrange("p (fc l) -> p fc l", fc=16),
                in_=p.y_hbm[:, :, ts0 : ts0 + 256].transpose([1, 0, 2])))
            p.s_v.wait(g)
            wv = p.s_bw.dma(g.dma_start(out=p.bscr[17:18, 0:256],
                                        in_=p.rstd_all[0:1, ts0 : ts0 + 256]))
            p.s_bw.wait(g, wv)
            bv = p.s_bc.dma(g.dma_start(
                out=p.bc1[:, 0:256], in_=p.bscr[17:18, 0:256].broadcast_to([128, 256])))
            for j in range(4):
                oc = ocg * 4 + j
                bank = p.pb[2 + (j % 2)]
                p.s_v.wait(t, wsv)
                p.s_yb.wait(t, yv)
                if not (ocg == 0 and sub == 0 and j < 2):
                    p.s_v.wait(t)
                for kc in range(16):
                    mm = t.matmul(bank[:, 0:256],
                                  wo4[:, j * 2048 + kc * 128 : j * 2048 + kc * 128 + 128],
                                  ybuf[:, kc * 256 : (kc + 1) * 256],
                                  start=(kc == 0), stop=(kc == 15))
                tv = p.s_t.inc(mm)
                prev_tv = tv
                p.s_t.wait(v, tv)
                p.s_bc.wait(v, bv)
                rsl = p.R[:, oc * T + ts0 : oc * T + ts0 + 256]
                v.tensor_mul(p.tmpv[:, 0:256], bank[:, 0:256], p.bc1[:, 0:256])
                p.s_v.inc(v.tensor_add(rsl, rsl, p.tmpv[:, 0:256]))

    _dbg(p, "x1pre", p.R, [128, 8 * T])
    _post_ln(p, ln_idx=0, absmax_acc=(p.upto >= 2))


def _post_ln(p, ln_idx, absmax_acc=False, gamma=None, beta=None):
    sy, t, v, s, g = p.sy, p.t, p.v, p.s, p.g
    scope(p, f"post_ln{ln_idx}")
    _fence_compute(p)
    if absmax_acc:
        v.memset(p.sml[:, 3:4], 0.0)
    sbanks = [p.pb[1], p.pb[4]]
    mrow = p.dt_raw[0:1, 0:2048]
    rrow = p.rstd_all[0:1, 0:2048]
    st_ck = {}
    ev_ck = {}
    for sub in range(8):
        ts0 = sub * 256
        stats = sbanks[sub % 2]
        ssl = sub % 2
        sqb = p.ysq[:, ssl * 1024 : ssl * 1024 + 1024]
        if sub >= 2:
            p.s_t.wait(s, st_ck[sub - 2])
        for half in range(2):
            if half == 1:
                p.s_t.wait(s, tv)
            for k in range(4):
                oc = half * 4 + k
                rsl = p.R[:, oc * T + ts0 : oc * T + ts0 + 256]
                s.activation(sqb[:, k * 256 : k * 256 + 256], rsl, AF.Square)
            sv = p.s_s.v
            p.s_s.wait(t, sv)
            if sub >= 2 and half == 0:
                p.s_v.wait(t, ev_ck[sub - 2])
            for k in range(4):
                oc = half * 4 + k
                rsl2 = p.R[:, oc * T + ts0 : oc * T + ts0 + 256]
                p.s_t.inc(t.matmul(stats[0:1, 0:256], p.ones[:, 0:1], rsl2,
                                   start=(oc == 0), stop=False,
                                   skip_group_check=True))
                tv = p.s_t.inc(t.matmul(
                    stats[0:1, 256:512], p.ones[:, 0:1],
                    sqb[:, k * 256 : k * 256 + 256],
                    start=False, stop=(oc == 7), skip_group_check=True))
        st_ck[sub] = tv
        p.s_t.wait(v, tv)
        v.tensor_scalar(mrow[:, ts0 : ts0 + 256], stats[0:1, 0:256], 1.0 / D,
                        None, ALU.mult)
        v.tensor_scalar(rrow[:, ts0 : ts0 + 256], stats[0:1, 256:512], 1.0 / D,
                        None, ALU.mult)
        ev_ck[sub] = p.s_v.v
    v.tensor_mul(p.tmpv[0:1, 0:1024], mrow[:, 0:1024], mrow[:, 0:1024])
    v.tensor_sub(rrow[:, 0:1024], rrow[:, 0:1024], p.tmpv[0:1, 0:1024])
    v.tensor_mul(p.tmpv[0:1, 0:1024], mrow[:, 1024:2048], mrow[:, 1024:2048])
    v.tensor_sub(rrow[:, 1024:2048], rrow[:, 1024:2048], p.tmpv[0:1, 0:1024])
    v.tensor_scalar(rrow, rrow, 1.0, EPS, ALU.mult, ALU.add)
    rv = p.s_v.inc(v.reciprocal(rrow, rrow))
    p.s_v.wait(s, rv)
    sv = p.s_s.inc(s.activation(rrow, rrow, AF.Sqrt))
    p.s_s.wait(g, sv)
    p.s_v.wait(g)
    bflat = p.bscr.rearrange("a b -> (a b)")
    wv = p.s_bw.dma(g.dma_start(out=bflat[16 * 512 : 16 * 512 + 2048], in_=mrow))
    wv = p.s_bw.dma(g.dma_start(out=bflat[20 * 512 : 20 * 512 + 2048], in_=rrow))
    p.s_bw.wait(g, wv)
    p.s_bc.dma(g.dma_start(
        out=p.bc1, in_=bflat[16 * 512 : 16 * 512 + 2048].unsqueeze(0).broadcast_to([128, 2048])))
    bv = p.s_bc.dma(g.dma_start(
        out=p.bc2, in_=bflat[20 * 512 : 20 * 512 + 2048].unsqueeze(0).broadcast_to([128, 2048])))
    p.s_bc.wait(v, bv)
    for oc in range(8):
        rsl = p.R[:, oc * T : (oc + 1) * T]
        v.tensor_sub(rsl, rsl, p.bc1)
        v.tensor_mul(rsl, rsl, p.bc2)
        if gamma is not None:
            v.tensor_scalar(rsl, rsl, gamma[:, oc : oc + 1], beta[:, oc : oc + 1],
                            ALU.mult, ALU.add)
        if absmax_acc:
            v.tensor_reduce(p.tmpv[:, 0:1], rsl, mybir.AxisListType.X, ALU.max,
                            apply_absolute_value=True)
            v.tensor_max(p.sml[:, 3:4], p.sml[:, 3:4], p.tmpv[:, 0:1])



def _allreduce_max(p, cols, out_cols):
    sy, t, v, s, g = p.sy, p.t, p.v, p.s, p.g
    k = len(cols)
    stats = p.pb[1]
    p.s_v.wait(t)
    p.s_gc.wait(t)
    for i, c in enumerate(cols):
        tv = p.s_t.inc(t.transpose(stats[0:1, i * 128 : i * 128 + 128],
                                   p.sml[:, c : c + 1], p.idn))
    p.s_t.wait(v, tv)
    v.memset(p.rows[0:1, 640:644], 0.0)
    for i in range(k):
        v.tensor_reduce(p.rows[0:1, 640 + i : 641 + i], stats[0:1, i * 128 : i * 128 + 128],
                        mybir.AxisListType.X, ALU.max)
    vv = p.s_v.v
    p.s_v.wait(g, vv)
    p.s_cc.wait(g)
    p.s_ci.dma(g.dma_start(out=p.cc_in.ap()[0:1, 0:4], in_=p.rows[0:1, 640:644]))
    p.s_ci.wait(g)
    p.s_co.wait(g)
    cc = g.collective_compute(
        "AllReduce", ALU.max, replica_groups=[list(range(NCORES))],
        ins=[p.cc_in.ap().opt()], outs=[p.cc_out.ap().opt()])
    p.s_cc.inc(cc)
    p.s_cc.wait(g)
    bv = p.s_co.dma(g.dma_start(
        out=p.tmpv[:, 0:4], in_=p.cc_out.ap()[0:1, 0:4].broadcast_to([128, 4])))
    p.s_co.wait(v, bv)
    for i, oc in enumerate(out_cols):
        v.tensor_copy(p.sml[:, oc : oc + 1], p.tmpv[:, i : i + 1])


def _scale_recip(p, mx_col, s_col, r_col):
    v = p.v
    v.tensor_scalar(p.sml[:, s_col : s_col + 1], p.sml[:, mx_col : mx_col + 1],
                    float(np.float32(1.0 / QMAX)), 1e-12, ALU.mult, ALU.max)
    v.reciprocal(p.sml[:, r_col : r_col + 1], p.sml[:, s_col : s_col + 1])


def _lpbq_dequant(p, wraw, ncols, sbuf, rbuf):
    v = p.v
    nb = ncols // 32
    v3 = wraw.rearrange("p (b e) -> p b e", b=nb)
    v.tensor_reduce(sbuf, v3, mybir.AxisListType.X, ALU.max, apply_absolute_value=True)
    v.tensor_scalar(sbuf, sbuf, float(np.float32(1.0 / 7.0)), 1e-12, ALU.mult, ALU.max)
    v.reciprocal(rbuf, sbuf)
    v.tensor_tensor(v3.bitcast(I32), v3, rep(rbuf, 32), ALU.mult)
    v.tensor_tensor(v3, v3.bitcast(I32), rep(sbuf, 32), ALU.mult)


def _ffn0(p):
    sy, t, v, s, g = p.sy, p.t, p.v, p.s, p.g
    _fence_compute(p)
    scope(p, 'ffn0_arA')
    _allreduce_max(p, [3], [3])
    _scale_recip(p, 3, 4, 5)

    p.s_v.wait(g)
    p.s_s.wait(g)
    for kc in range(8):
        p.s_ea.dma(g.dma_start(out=p.y_hbm[kc, :, :], in_=p.R[:, kc * T : (kc + 1) * T]))
    p.s_ea.wait(v)
    xqh = p.R
    xql = p.big
    for kc in range(8):
        xr = p.R[:, kc * T : (kc + 1) * T]
        dst = xql[:, kc * T : (kc + 1) * T]
        hi = p.xqh_r[:, kc * T : (kc + 1) * T]
        lo = p.xql_r[:, kc * T : (kc + 1) * T]
        v.tensor_scalar(dst.bitcast(I32), xr, p.sml[:, 5:6], None, ALU.mult)
        v.tensor_scalar(dst, dst.bitcast(I32), p.sml[:, 4:5], None, ALU.mult)
        v.tensor_copy(hi, dst)
        v.tensor_tensor(lo, dst, hi.bitcast(F32), ALU.subtract)
    xqv = p.s_v.v

    scope(p, 'ffn0_p1')
    v.memset(p.sml[:, 6:7], 0.0)
    v.memset(p.sml[:, 7:8], 0.0)
    wraw2 = p.yoct[:, 0:2048]
    wbuf = p.dt_raw[:, 0:2048]
    whbuf = p.rstd_all[:, 0:2048]
    sc2 = p.state[:, 0:128]
    stg = p.ysq[:, 0:2048]
    wsem = [p.s_wa, p.s_wb]
    esem = [p.s_ea, p.s_eb]
    tm_ck = {}
    sc_tr = {}
    sc_ev = {}
    sp_ck = {}
    dq_ck = {}
    tr_ck = {}
    wsp_ck = {}
    units = [(m, oc) for oc in range(24) for m in range(2)]
    bank_wt = p.pb[4]

    def p1_stage_a(ui):
        m, oc = units[ui]
        wsl = ui % 2
        wraw = wraw2[:, wsl * 1024 : wsl * 1024 + 1024]
        wb = wbuf[:, wsl * 1024 : wsl * 1024 + 1024]
        wsrc = (p.w_up if m == 0 else p.w_gate)
        if ui >= 2:
            p.s_v.wait(sy, dq_ck[ui - 2])
            sy.wait_ge(p.s_t.h, tr_ck[ui - 2])
        wl = wsem[wsl].dma(sy.dma_start(
            out=wraw, in_=wsrc[oc * 128 : (oc + 1) * 128, :]))
        wsem[wsl].wait(v, wl)
        _lpbq_dequant(p, wraw, 1024, sc2[:, wsl * 64 : wsl * 64 + 32],
                      sc2[:, wsl * 64 + 32 : wsl * 64 + 64])
        dq_ck[ui] = p.s_v.v
        p.s_v.wait(t, dq_ck[ui])
        if ui >= 2:
            tmp = tm_ck.get((ui - 2, 3))
            p.s_s.wait(t, sc_tr.get(ui - 1, 0))
        wh = p.whb_r[:, wsl * 1024 : wsl * 1024 + 1024]
        wlo = p.wlb_r[:, wsl * 1024 : wsl * 1024 + 1024]
        for kk in range(2):
            if ui >= 1 or kk == 1:
                p.s_s.wait(t, sc_tr.get(ui if kk == 1 else ui - 1, 0))
            for k4 in range(4):
                kc = kk * 4 + k4
                tv = p.s_t.inc(t.transpose(bank_wt[:, k4 * 128 : k4 * 128 + 128],
                                           wraw[:, kc * 128 : kc * 128 + 128], p.idn))
            tr_ck[ui] = tv
            p.s_t.wait(s, tv)
            if ui >= 2 and kk == 0:
                p.s_t.wait(s, tm_ck[(ui - 2, 3)])
            s.copy(wb[:, kk * 512 : kk * 512 + 512], bank_wt[:, 0:512])
            sc_tr[ui] = p.s_s.v
        p.s_s.wait(v, sc_tr[ui])
        if ui >= 2:
            p.s_t.wait(v, tm_ck[(ui - 2, 3)])
        v.tensor_copy(wh, wb)
        v.tensor_tensor(wlo, wb, wh.bitcast(F32), ALU.subtract)
        wsp_ck[ui] = p.s_v.v

    def p1_stage_b(ui):
        m, oc = units[ui]
        wsl = ui % 2
        wb = p.wlb_r[:, wsl * 1024 : wsl * 1024 + 1024]
        wh = p.whb_r[:, wsl * 1024 : wsl * 1024 + 1024]
        for tt in range(4):
            i = ui * 4 + tt
            bank = p.pb[i % 2]
            p.s_v.wait(t, wsp_ck[ui])
            if ui == 0 and tt == 0:
                p.s_v.wait(t, xqv)
            if i >= 2:
                p.s_v.wait(t, vr_ck[i - 2])
            for kc in range(8):
                xh = p.xqh_r[:, kc * T + tt * 512 : kc * T + tt * 512 + 512]
                xl = p.xql_r[:, kc * T + tt * 512 : kc * T + tt * 512 + 512]
                t.matmul(bank, wh[:, kc * 128 : (kc + 1) * 128], xh,
                         start=(kc == 0), stop=False)
                t.matmul(bank, wh[:, kc * 128 : (kc + 1) * 128], xl,
                         start=False, stop=False)
                mm = t.matmul(bank, wb[:, kc * 128 : (kc + 1) * 128], xh,
                              start=False, stop=(kc == 7))
            tm_ck[(ui, tt)] = p.s_t.inc(mm)
            ssl = i % 2
            te = stg[:, ssl * 512 : ssl * 512 + 512]
            p.s_t.wait(v, tm_ck[(ui, tt)])
            if i >= 2:
                esem[ssl].wait(v, sp_ck[i - 2])
            v.tensor_copy(te, bank)
            v.tensor_reduce(p.tmpv[:, 4 + ssl : 5 + ssl], bank, mybir.AxisListType.X,
                            ALU.max, apply_absolute_value=True)
            acc = p.sml[:, 6 + m : 7 + m]
            v.tensor_max(acc, acc, p.tmpv[:, 4 + ssl : 5 + ssl])
            vr_ck[i] = sc_ev[i] = p.s_v.v
            p.s_v.wait(g, sc_ev[i])
            dst = (p.up_hbm if m == 0 else p.gate_hbm)
            sp_ck[i] = esem[ssl].dma(
                g.dma_start(out=dst[oc, :, tt * 512 : tt * 512 + 512], in_=te))

    gr_ck = {}
    vr_ck = {}
    for ui in range(48):
        p1_stage_a(ui)
        if ui >= 1:
            p1_stage_b(ui - 1)
    p1_stage_b(47)

    scope(p, 'ffn0_wdq')
    p.s_ea.wait(v)
    p.s_eb.wait(v)
    for oc in range(8):
        wsl = oc % 2
        wraw = wraw2[:, wsl * 1024 : wsl * 1024 + 1024]
        for cp in range(3):
            p.s_v.wait(sy)
            wl = wsem[wsl].dma(sy.dma_start(
                out=wraw, in_=p.w_down[oc * 128 : (oc + 1) * 128,
                                       cp * 1024 : cp * 1024 + 1024]))
            wsem[wsl].wait(v, wl)
            _lpbq_dequant(p, wraw, 1024, sc2[:, wsl * 64 : wsl * 64 + 32],
                          sc2[:, wsl * 64 + 32 : wsl * 64 + 64])
            dqv = p.s_v.v
            p.s_v.wait(t, dqv)
            p.s_ea.wait(v)
            p.s_eb.wait(v)
            for kk in range(2):
                p.s_v.wait(t)
                for k4 in range(4):
                    kc = kk * 4 + k4
                    tv = p.s_t.inc(t.transpose(bank_wt[:, k4 * 128 : k4 * 128 + 128],
                                               wraw[:, kc * 128 : kc * 128 + 128], p.idn))
                p.s_t.wait(v, tv)
                v.tensor_copy(stg[:, kk * 512 : kk * 512 + 512], bank_wt)
            vv = p.s_v.v
            p.s_v.wait(g, vv)
            esem[wsl].dma(g.dma_start(
                out=p.wdq_hbm[cp * 1024 : (cp + 1) * 1024, oc * 128 : (oc + 1) * 128]
                .rearrange("(kc kp) o -> kp kc o", kp=128),
                in_=stg[:, 0:1024].rearrange("kp (kc o) -> kp kc o", kc=8)))

    scope(p, 'ffn0_arB')
    _allreduce_max(p, [6, 7], [6, 7])
    _scale_recip(p, 6, 10, 11)
    _scale_recip(p, 7, 12, 13)

    scope(p, 'ffn0_p2')
    p.s_ea.wait(sy)
    p.s_eb.wait(sy)
    v.memset(p.sml[:, 8:9], 0.0)
    rsem = [p.s_ra, p.s_rb]
    ld_ck = {}
    cv_ck = {}
    sp_ck = {}
    for oc in range(24):
        ssl = oc % 2
        base = ssl * 8192
        graw = p.big[:, base : base + 2048]
        gq = p.big[:, base + 2048 : base + 4096]
        sq = p.big[:, base + 4096 : base + 6144]
        g2 = p.big[:, base + 6144 : base + 8192]
        if oc >= 2:
            p.s_v.wait(sy, cv_ck[oc - 2])
        ld_ck[oc] = rsem[ssl].dma(sy.dma_start(out=graw, in_=p.gate_hbm[oc, :, :]))
        rsem[ssl].wait(v, ld_ck[oc])
        if oc >= 2:
            esem[ssl].wait(v, sp_ck[oc - 2])
        v.tensor_scalar(gq.bitcast(I32), graw, p.sml[:, 13:14], None, ALU.mult)
        v.tensor_scalar(gq, gq.bitcast(I32), p.sml[:, 12:13], None, ALU.mult)
        sv = p.s_v.v
        p.s_v.wait(s, sv)
        av = p.s_s.inc(s.activation(sq, gq, AF.Sigmoid))
        p.s_s.wait(v, av)
        v.tensor_scalar(sq.bitcast(I32), sq, 65536.0, None, ALU.mult)
        v.tensor_scalar(sq, sq.bitcast(I32), 65535.0, float(2.0 ** -16),
                        ALU.min, ALU.mult)
        v.tensor_mul(g2, gq, sq)
        v.tensor_reduce(p.tmpv[:, 0:1], g2, mybir.AxisListType.X, ALU.max,
                        apply_absolute_value=True)
        v.tensor_max(p.sml[:, 8:9], p.sml[:, 8:9], p.tmpv[:, 0:1])
        cv_ck[oc] = p.s_v.v
        p.s_v.wait(g, cv_ck[oc])
        sp_ck[oc] = esem[ssl].dma(g.dma_start(out=p.g2_hbm[oc, :, :], in_=g2))

    scope(p, 'ffn0_arC')
    scope(p, 'ffn0_arC')
    _allreduce_max(p, [8], [8])
    _scale_recip(p, 8, 14, 15)

    scope(p, 'ffn0_p3')
    p.s_ea.wait(sy)
    p.s_eb.wait(sy)
    v.memset(p.sml[:, 9:10], 0.0)
    for oc in range(24):
        ssl = oc % 2
        base = ssl * 8192
        g2r = p.big[:, base : base + 2048]
        upr = p.big[:, base + 2048 : base + 4096]
        orw = p.big[:, base + 4096 : base + 6144]
        if oc >= 2:
            p.s_v.wait(sy, cv_ck[oc - 2])
        rsem[ssl].dma(sy.dma_start(out=g2r, in_=p.g2_hbm[oc, :, :]))
        ld_ck[oc] = rsem[ssl].dma(sy.dma_start(out=upr, in_=p.up_hbm[oc, :, :]))
        rsem[ssl].wait(v, ld_ck[oc])
        if oc >= 2:
            esem[ssl].wait(v, sp_ck[oc - 2])
        v.tensor_scalar(g2r.bitcast(I32), g2r, p.sml[:, 15:16], None, ALU.mult)
        v.tensor_scalar(g2r, g2r.bitcast(I32), p.sml[:, 14:15], None, ALU.mult)
        v.tensor_scalar(upr.bitcast(I32), upr, p.sml[:, 11:12], None, ALU.mult)
        v.tensor_scalar(upr, upr.bitcast(I32), p.sml[:, 10:11], None, ALU.mult)
        v.tensor_mul(orw, g2r, upr)
        v.tensor_reduce(p.tmpv[:, 0:1], orw, mybir.AxisListType.X, ALU.max,
                        apply_absolute_value=True)
        v.tensor_max(p.sml[:, 9:10], p.sml[:, 9:10], p.tmpv[:, 0:1])
        cv_ck[oc] = p.s_v.v
        p.s_v.wait(g, cv_ck[oc])
        sp_ck[oc] = esem[ssl].dma(g.dma_start(out=p.o_hbm[oc, :, :], in_=orw))

    scope(p, 'ffn0_arD')
    scope(p, 'ffn0_arD')
    _allreduce_max(p, [9], [9])
    _scale_recip(p, 9, 16, 17)

    scope(p, 'ffn0_p4')
    p.s_ea.wait(sy)
    p.s_eb.wait(sy)
    p.s_v.wait(sy)
    for kc in range(8):
        p.s_rr.dma(sy.dma_start(out=p.R[:, kc * T : (kc + 1) * T], in_=p.y_hbm[kc, :, :]))
    p.s_rr.wait(v)
    oq = p.big[:, 0:12288]
    wdbs = [p.big[:, 12288 : 12288 + 3072], p.yoct[:, 0:3072]]
    prev_tv = p.s_t.v
    dtm_ck = {}
    for tt in range(4):
        if tt > 0:
            p.s_v.wait(sy)
            sy.wait_ge(p.s_t.h, prev_tv)
        for kc in range(24):
            ssl = kc % 2
            rsem[ssl].dma(sy.dma_start(
                out=oq[:, kc * 512 : kc * 512 + 512],
                in_=p.o_hbm[kc, :, tt * 512 : tt * 512 + 512]))
        rsem[0].wait(v)
        rsem[1].wait(v)
        for kc in range(24):
            dst = oq[:, kc * 512 : kc * 512 + 512]
            v.tensor_scalar(dst.bitcast(I32), dst, p.sml[:, 17:18], None, ALU.mult)
            v.tensor_scalar(dst, dst.bitcast(I32), p.sml[:, 16:17], None, ALU.mult)
        qv = p.s_v.v
        for oc in range(8):
            u4 = tt * 8 + oc
            wdb = wdbs[u4 % 2]
            wsm = [p.s_wa, p.s_wb][u4 % 2]
            if u4 >= 2:
                sy.wait_ge(p.s_t.h, dtm_ck[u4 - 2])
            src = p.wdq_hbm[:, oc * 128 : (oc + 1) * 128].rearrange(
                "(kc kp) o -> kp kc o", kp=128)
            wl = wsm.dma(sy.dma_start(
                out=wdb.rearrange("kp (kc o) -> kp kc o", kc=24), in_=src))
            bank = p.pb[2 + oc % 2]
            wsm.wait(t, wl)
            p.s_v.wait(t, qv)
            if not (tt == 0 and oc < 2):
                p.s_v.wait(t)
            for kc in range(24):
                mm = t.matmul(bank[:, 0:512],
                              wdb[:, kc * 128 : kc * 128 + 128],
                              oq[:, kc * 512 : kc * 512 + 512],
                              start=(kc == 0), stop=(kc == 23))
            tv = p.s_t.inc(mm)
            dtm_ck[u4] = tv
            prev_tv = tv
            p.s_t.wait(v, tv)
            rsl = p.R[:, oc * T + tt * 512 : oc * T + tt * 512 + 512]
            v.tensor_add(rsl, rsl, bank[:, 0:512])

    _dbg(p, "x2pre", p.R, [128, 8 * T])
    _post_ln(p, ln_idx=1)


def _l1_attn(p):
    sy, t, v, s, g = p.sy, p.t, p.v, p.s, p.g
    scope(p, "l1_attn")
    _fence_compute(p)
    LQ4 = 512
    abuf = p.big[:, 0:4096]
    bbuf = p.big[:, 4096:8192]
    hbuf = p.big[:, 8192:12288]
    wzb = p.big[:, 12288:13312]
    whb = p.big[:, 13312:14336]

    def shslot(fc, d):
        if d == 0:
            if fc < 8:
                return (p.yoct[:, fc * 512 : fc * 512 + 512] if fc < 8 else None)
        return None

    bzv = p.cw[:, 0:8]
    bhv = p.cw[:, 8:16]
    dcv = p.cw[:, 16:24]

    lv = p.s_ld.dma(sy.dma_start(
        out=bzv.rearrange("p (j o) -> p j o", o=1),
        in_=p.bz.rearrange("(j p) o -> p j o", p=128),
        allow_slow_non_contiguous=True))
    lv = p.s_ld.dma(sy.dma_start(
        out=bhv.rearrange("p (j o) -> p j o", o=1),
        in_=p.bh.rearrange("(j p) o -> p j o", p=128),
        allow_slow_non_contiguous=True))
    lv = p.s_ld.dma(sy.dma_start(
        out=dcv.rearrange("p (j o) -> p j o", o=1),
        in_=p.log_d.rearrange("(j p) o -> p j o", p=128),
        allow_slow_non_contiguous=True))
    p.mdf = p.gb0[:, 0:128]
    p.mwpb = p.gb0[:, 128:256]
    p.mwnb = p.gb0[:, 256:384]
    p.s_ld.dma(sy.dma_start(out=p.mdf, in_=p.mdiff))
    p.s_ld.dma(sy.dma_start(out=p.mwpb, in_=p.mwp))
    p.s_ld.dma(sy.dma_start(out=p.mwnb, in_=p.mwn))
    p.s_ld.wait(t)
    p.s_ld.wait(s)
    av = p.s_s.inc(s.activation(dcv, dcv, AF.Exp))
    v.memset(p.sml[:, 20:28], 0.0)
    wsem = [p.s_wa, p.s_wb]
    rsem = [p.s_ra, p.s_rb]
    wz2 = p.big[:, 14336:15360]
    wh2 = p.big[:, 15360:16384]
    tm_ck = {}
    sc_ck = {}
    scan_ck = 0
    diff_ck = 0
    for q in range(4):
        tq = q * LQ4
        for oc in range(8):
            u = q * 8 + oc
            wzsl = wzb if u % 2 == 0 else wz2
            whsl = whb if u % 2 == 0 else wh2
            if u >= 2:
                sy.wait_ge(p.s_t.h, tm_ck[u - 2])
            zsem = [p.s_wa, p.s_wc][u % 2]
            hsem = [p.s_wb, p.s_wd][u % 2]
            wl = zsem.dma(sy.dma_start(
                out=wzsl.rearrange("kp (kc o) -> kp kc o", kc=8),
                in_=p.wz[:, oc * 128 : (oc + 1) * 128].rearrange("(kc kp) o -> kp kc o", kp=128)))
            wl2 = hsem.dma(sy.dma_start(
                out=whsl.rearrange("kp (kc o) -> kp kc o", kc=8),
                in_=p.wh[:, oc * 128 : (oc + 1) * 128].rearrange("(kc kp) o -> kp kc o", kp=128)))
            zsem.wait(t, wl)
            hsem.wait(t, wl2)
            bz_bank = p.pb[2 + (u % 2)]
            bh_bank = p.pb[6 + (u % 2)]
            if u >= 2:
                p.s_s.wait(t, sc_ck[u - 2])
            if oc == 0:
                p.s_v.wait(t, diff_ck)
                p.s_v.wait(s, scan_ck)
            for kc in range(8):
                mm = t.matmul(bz_bank, wzsl[:, kc * 128 : kc * 128 + 128],
                              p.R[:, kc * T + tq : kc * T + tq + LQ4],
                              start=(kc == 0), stop=(kc == 7))
            tvz = p.s_t.inc(mm)
            for kc in range(8):
                mm = t.matmul(bh_bank, whsl[:, kc * 128 : kc * 128 + 128],
                              p.R[:, kc * T + tq : kc * T + tq + LQ4],
                              start=(kc == 0), stop=(kc == 7))
            tm_ck[u] = tvh = p.s_t.inc(mm)
            zsl = abuf[:, oc * 512 : oc * 512 + 512]
            hsl = bbuf[:, oc * 512 : oc * 512 + 512]
            p.s_t.wait(s, tvz)
            s.activation(zsl, bz_bank, AF.Sigmoid, bias=bzv[:, oc : oc + 1])
            p.s_t.wait(s, tvh)
            sv = p.s_s.inc(s.activation(hsl, bh_bank, AF.Identity, bias=bhv[:, oc : oc + 1]))
            sc_ck[u] = sv
            p.s_s.wait(v, sv)
            v.tensor_mul(hsl, zsl, hsl)
            v.tensor_scalar(zsl, zsl, -1.0, 1.0, ALU.mult, ALU.add)
        if "no_scan" in p.dbg:
            for fc in range(8):
                v.tensor_copy(hbuf[:, fc * 512 : fc * 512 + 512],
                              bbuf[:, fc * 512 : fc * 512 + 512])
        rsem[0].wait(v)
        rsem[1].wait(v)
        for fc in range(8 if "no_scan" not in p.dbg else 0):
            v.tensor_tensor_scan(hbuf[:, fc * 512 : fc * 512 + 512],
                                 abuf[:, fc * 512 : fc * 512 + 512],
                                 bbuf[:, fc * 512 : fc * 512 + 512],
                                 p.sml[:, 20 + fc : 21 + fc], ALU.mult, ALU.add)
            v.tensor_copy(p.sml[:, 20 + fc : 21 + fc],
                          hbuf[:, fc * 512 + 511 : fc * 512 + 512])
        hv = p.s_v.v
        scan_ck = hv
        p.s_v.wait(t, hv)
        p.s_s.wait(t)
        for fc in range(8):
            hsl = hbuf[:, fc * 512 : fc * 512 + 512]
            prv = hbuf[:, ((fc - 1) % 8) * 512 : ((fc - 1) % 8) * 512 + 512]
            nxt = hbuf[:, ((fc + 1) % 8) * 512 : ((fc + 1) % 8) * 512 + 512]
            bank = p.pb[fc]
            t.matmul(bank, p.mdf, hsl, start=True, stop=False)
            t.matmul(bank, p.mwpb, prv, start=False, stop=False)
            dtv = p.s_t.inc(t.matmul(bank, p.mwnb, nxt, start=False, stop=True))
        p.s_t.wait(v, dtv)
        for fc in range(8):
            hsl = hbuf[:, fc * 512 : fc * 512 + 512]
            v.scalar_tensor_tensor(hsl, p.pb[fc], dcv[:, fc : fc + 1], hsl,
                                   ALU.mult, ALU.add)
            rsl = p.R[:, fc * T + tq : fc * T + tq + LQ4]
            v.tensor_add(rsl, rsl, hsl)
        diff_ck = p.s_v.v

    _dbg(p, "x3pre", p.R, [128, 8 * T])
    _post_ln(p, ln_idx=2)


def _l1_ffn(p):
    sy, t, v, s, g = p.sy, p.t, p.v, p.s, p.g
    scope(p, "l1_ffn")
    _fence_compute(p)
    wd1 = p.big[:, 0:8192]
    y1 = p.big[:, 8192:12288]
    srst = p.big[:, 12288:14336]
    lra = p.big[:, 14336:14848]
    lab = p.big[:, 14848:15104]
    lbb = p.big[:, 15104:16128]

    def wd2(kc):
        if kc < 4:
            return p.yoct[:, kc * 1024 : kc * 1024 + 1024]
        if kc < 6:
            return p.ysq[:, (kc - 4) * 1024 : (kc - 4) * 1024 + 1024]
        return p.state[:, (kc - 6) * 1024 : (kc - 6) * 1024 + 1024]

    qb1v = p.cw[:, 24:32]
    qb2v = p.cw[:, 32:40]
    gamv = p.cw[:, 40:48]
    betv = p.cw[:, 48:56]
    wsem = [p.s_wa, p.s_wb]
    stats = p.pb[1]
    p.s_v.wait(g)
    p.s_s.wait(g)
    for fc in range(8):
        p.s_ea.dma(g.dma_start(out=p.z_hbm[fc, :, :], in_=p.R[:, fc * T : (fc + 1) * T]))
    p.s_ea.wait(v)
    p.prev_l1sc = [None] * 8
    p.prev_sqck = [None] * 8
    p.prev_vend = [None, None]
    p.last_rows_ck = 0
    for blk in range(3):
        i0, i1 = 2 * blk, 2 * blk + 1
        p.s_s.wait(sy)
        p.s_v.wait(sy)
        p.s_t.wait(sy)
        lv = p.s_ld.dma(sy.dma_start(
            out=qb1v.rearrange("p (j o) -> p j o", o=1),
            in_=p.qb[:, i0 : i0 + 1].rearrange("(j p) o -> p j o", p=128),
            allow_slow_non_contiguous=True))
        lv = p.s_ld.dma(sy.dma_start(
            out=qb2v.rearrange("p (j o) -> p j o", o=1),
            in_=p.qb[:, i1 : i1 + 1].rearrange("(j p) o -> p j o", p=128),
            allow_slow_non_contiguous=True))
        lv = p.s_ld.dma(sy.dma_start(
            out=gamv.rearrange("p (j o) -> p j o", o=1),
            in_=p.gam[:, blk : blk + 1].rearrange("(j p) o -> p j o", p=128),
            allow_slow_non_contiguous=True))
        lv = p.s_ld.dma(sy.dma_start(
            out=betv.rearrange("p (j o) -> p j o", o=1),
            in_=p.bet[:, blk : blk + 1].rearrange("(j p) o -> p j o", p=128),
            allow_slow_non_contiguous=True))
        lv = p.s_ld.dma(sy.dma_start(
            out=lab.rearrange("p (kc r) -> p kc r", kc=8),
            in_=p.la_t[i1].rearrange("(kc p) r -> p kc r", p=128)))
        lv = p.s_ld.dma(sy.dma_start(out=lbb[0:32, :], in_=p.lb_t[i1]))
        lab1 = p.dt_raw[:, 0:256]
        lbb1 = p.dt_raw[:, 256:1280]
        lv = p.s_ld.dma(sy.dma_start(
            out=lab1.rearrange("p (kc r) -> p kc r", kc=8),
            in_=p.la_t[i0].rearrange("(kc p) r -> p kc r", p=128)))
        lv = p.s_ld.dma(sy.dma_start(out=lbb1[0:32, :], in_=p.lb_t[i0]))

        p.s_t.wait(sy)
        for un in range(16):
            mat, kc = divmod(un, 8)
            dstw = wd1[:, kc * 1024 : kc * 1024 + 1024] if mat == 0 else wd2(kc)
            ssl = un % 2
            srb = srst[:, ssl * 1024 : ssl * 1024 + 1024]
            qi = i0 if mat == 0 else i1
            if un >= 2:
                p.s_v.wait(sy)
            wl = wsem[ssl].dma(sy.dma_start(
                out=dstw.bitcast(I32), in_=p.qw_t[qi, kc * 128 : (kc + 1) * 128, :]))
            wl = wsem[ssl].dma(sy.dma_start(
                out=srb, in_=p.srep_t[qi, kc * 128 : (kc + 1) * 128, :]))
            wsem[ssl].wait(v, wl)
            v.tensor_scalar(dstw, dstw.bitcast(I32), float(np.float32(1.0 / 7.5)),
                            -1.0, ALU.mult, ALU.add)
            v.tensor_mul(dstw, dstw, srb)
        dqv = p.s_v.v
        p.s_ld.wait(t)
        p.s_ld.wait(v)
        p.s_ld.wait(s)
        p.s_v.wait(t, dqv)

        stats2 = p.pb[4]
        l1sc = {}
        sqck = {}
        l2mv = {}
        vend = {}
        rows_ck = {}
        bv_tt = {}

        def y1sl(par, j):
            if par == 0:
                return y1[:, j * 512 : j * 512 + 512]
            if j < 4:
                return p.rstd_all[:, j * 512 : j * 512 + 512]
            if j < 7:
                return p.gbs[:, 512 + (j - 4) * 512 : 1024 + (j - 4) * 512]
            return p.gbs[:, 2560:3072]

        lra1 = [lra[0:32, :], p.xd[0:32, 0:512]]
        lra2 = p.tmpv[0:32, 512:1024]

        def stage_f(tt):
            par = tt % 2
            ts0 = tt * 512
            if tt >= 2:
                p.s_v.wait(s, vend[tt - 2])
            elif p.prev_vend[tt] is not None:
                p.s_v.wait(s, p.prev_vend[tt])
            for kc in range(8):
                mm = t.matmul(p.pb[0][0:32, :], lab1[:, kc * 32 : kc * 32 + 32],
                              p.R[:, kc * T + ts0 : kc * T + ts0 + 512],
                              start=(kc == 0), stop=(kc == 7))
            ltv = p.s_t.inc(mm)
            p.s_t.wait(v, ltv)
            v.tensor_copy(lra1[par], p.pb[0][0:32, :])
            lrv = p.s_v.v
            p.s_v.wait(t, lrv)
            for oc in range(8):
                bank = p.pb[6 + oc % 2]
                if oc >= 2:
                    p.s_s.wait(t, l1sc[(tt, oc - 2)])
                elif tt >= 1:
                    p.s_s.wait(t, l1sc[(tt - 1, 6 + oc)])
                elif p.prev_l1sc[6 + oc] is not None:
                    p.s_s.wait(t, p.prev_l1sc[6 + oc])
                if oc == 0:
                    p.s_v.wait(t, p.last_rows_ck)
                for kc in range(8):
                    t.matmul(bank, wd1[:, kc * 1024 + oc * 128 : kc * 1024 + oc * 128 + 128],
                             p.R[:, kc * T + ts0 : kc * T + ts0 + 512],
                             start=(kc == 0), stop=False)
                mm = t.matmul(bank, lbb1[0:32, oc * 128 : oc * 128 + 128],
                              lra1[par], start=False, stop=True)
                tv = p.s_t.inc(mm)
                p.s_t.wait(s, tv)
                ysl = y1sl(par, oc)
                sv = p.s_s.inc(s.activation(ysl, bank,
                                            AF.Identity, bias=qb1v[:, oc : oc + 1]))
                l1sc[(tt, oc)] = sv
                ssl = oc % 2
                sq = srst[:, ssl * 1024 : ssl * 1024 + 512]
                if oc >= 2:
                    p.s_t.wait(s, sqck[(tt, oc - 2)])
                elif tt >= 1:
                    p.s_t.wait(s, sqck[(tt - 1, 6 + oc)])
                elif p.prev_sqck[6 + oc] is not None:
                    p.s_t.wait(s, p.prev_sqck[6 + oc])
                sv = p.s_s.inc(s.activation(sq, ysl, AF.Square))
                p.s_s.wait(t, l1sc[(tt, oc)])
                p.s_t.inc(t.matmul(stats[0:1, 0:512], p.ones[:, 0:1], ysl,
                                   start=(oc == 0), stop=(oc == 7)))
                p.s_s.wait(t, sv)
                sqck[(tt, oc)] = p.s_t.inc(t.matmul(
                    stats2[0:1, 0:512], p.ones[:, 0:1], sq,
                    start=(oc == 0), stop=(oc == 7)))

        def stage_ba(tt):
            p.s_t.wait(v, sqck[(tt, 7)])
            m = p.rows[0:1, 0:512]
            var = p.rows[0:1, 512:1024]
            v.tensor_scalar(m, stats[0:1, 0:512], 1.0 / D, None, ALU.mult)
            v.tensor_scalar(var, stats2[0:1, 0:512], 1.0 / D, None, ALU.mult)
            rows_ck[tt] = p.s_v.v
            p.last_rows_ck = rows_ck[tt]
            v.tensor_mul(p.tmpv[0:1, 0:512], m, m)
            v.tensor_sub(var, var, p.tmpv[0:1, 0:512])
            v.tensor_scalar(var, var, 1.0, EPS, ALU.mult, ALU.add)
            rv = p.s_v.inc(v.reciprocal(var, var))
            p.s_v.wait(s, rv)
            sv = p.s_s.inc(s.activation(var, var, AF.Sqrt))
            p.s_s.wait(g, sv)
            p.s_v.wait(g)
            wv = p.s_bw.dma(g.dma_start(out=p.bscr[16:17, 0:512], in_=m))
            wv = p.s_bw.dma(g.dma_start(out=p.bscr[18:19, 0:512], in_=var))
            p.s_bw.wait(g, wv)
            p.s_bc.dma(g.dma_start(out=p.bc1[:, 0:512],
                                   in_=p.bscr[16:17, 0:512].broadcast_to([128, 512])))
            bv_tt[tt] = p.s_bc.dma(g.dma_start(
                out=p.bc2[:, 0:512],
                in_=p.bscr[18:19, 0:512].broadcast_to([128, 512])))

        def stage_bb(tt):
            par = tt % 2
            ts0 = tt * 512
            p.s_bc.wait(v, bv_tt[tt])
            for oc in range(8):
                ysl = y1sl(par, oc)
                v.tensor_sub(ysl, ysl, p.bc1[:, 0:512])
                v.tensor_mul(ysl, ysl, p.bc2[:, 0:512])
                v.tensor_scalar(ysl, ysl, gamv[:, oc : oc + 1], betv[:, oc : oc + 1],
                                ALU.mult, ALU.add)
                gv = p.s_v.v
                p.s_v.wait(s, gv)
                sv = p.s_s.inc(s.activation(ysl, ysl, AF.Gelu))
                p.s_s.wait(t, sv)
                mm = t.matmul(p.pb[5][0:32, :], lab[:, oc * 32 : oc * 32 + 32],
                              ysl, start=(oc == 0), stop=(oc == 7))
            ltv = p.s_t.inc(mm)
            p.s_t.wait(v, ltv)
            v.tensor_copy(lra2, p.pb[5][0:32, :])
            lrv = p.s_v.v
            p.s_v.wait(t, lrv)
            for oc in range(8):
                bank = p.pb[2 + oc % 2]
                if oc >= 2:
                    p.s_v.wait(t, l2mv[(tt, oc - 2)])
                elif tt >= 1:
                    p.s_v.wait(t, l2mv[(tt - 1, 6 + oc)])
                for kc in range(8):
                    t.matmul(bank, wd2(kc)[:, oc * 128 : oc * 128 + 128],
                             y1sl(par, kc), start=(kc == 0), stop=False)
                mm = t.matmul(bank, lbb[0:32, oc * 128 : oc * 128 + 128],
                              lra2, start=False, stop=True)
                tv = p.s_t.inc(mm)
                p.s_t.wait(v, tv)
                rsl = p.R[:, oc * T + ts0 : oc * T + ts0 + 512]
                v.scalar_tensor_tensor(p.tmpv[:, 0:512], bank, 1.0, rsl,
                                       ALU.mult, ALU.add)
                v.tensor_scalar(rsl, p.tmpv[:, 0:512],
                                qb2v[:, oc : oc + 1], None, ALU.add)
                l2mv[(tt, oc)] = p.s_v.v
            vend[tt] = p.s_v.v

        stage_f(0)
        stage_ba(0)
        for tt in range(1, 4):
            stage_f(tt)
            stage_bb(tt - 1)
            stage_ba(tt)
        stage_bb(3)
        p.prev_l1sc = [l1sc[(3, oc)] for oc in range(8)]
        p.prev_sqck = [sqck[(3, oc)] for oc in range(8)]
        p.prev_vend = [vend[2], vend[3]]
        _dbg(p, f"rb{blk}", p.R, [128, 8 * T])

    p.s_t.wait(sy)
    p.s_v.wait(sy)
    p.s_s.wait(sy)
    for fc in range(8):
        lv = p.s_ra.dma(sy.dma_start(out=p.big[:, fc * T : fc * T + T],
                                     in_=p.z_hbm[fc, :, :]))
    p.s_ra.wait(v)
    for fc in range(8):
        rsl = p.R[:, fc * T : (fc + 1) * T]
        v.tensor_add(rsl, rsl, p.big[:, fc * T : fc * T + T])

    _post_ln(p, ln_idx=3)


def _stage_inputs(inputs, names):
    u = np.ascontiguousarray(np.asarray(inputs["L0_attn_u"], dtype=np.float32))
    w_in_t = np.zeros((D, DPROJ_PAD), np.float32)
    w_in_t[:, :DPROJ] = np.asarray(inputs["L0_attn_W_in"], np.float32).T
    common = {
        "w_in_t": w_in_t,
        "conv_w": np.ascontiguousarray(np.asarray(inputs["L0_attn_conv_w"], np.float32)),
        "conv_b": np.asarray(inputs["L0_attn_conv_b"], np.float32)[:, None].copy(),
        "dt_bias": np.asarray(inputs["L0_attn_dt_bias"], np.float32)[:, None].copy(),
        "a_log": np.asarray(inputs["L0_attn_A_log"], np.float32)[:, None].copy(),
        "d_rep": np.repeat(np.asarray(inputs["L0_attn_D"], np.float32), HP)[:, None].copy(),
        "nw_in": np.asarray(inputs["L0_attn_norm_weight"], np.float32)[:, None].copy(),
        "w_out_t": np.ascontiguousarray(np.asarray(inputs["L0_attn_W_out"], np.float32).T),
        "ident": np.eye(128, dtype=np.float32),
        "mask01": np.triu(np.ones((128, 128), np.float32)).copy(),
    }
    if "w_up" in names:
        common["w_up"] = np.ascontiguousarray(np.asarray(inputs["L0_ffn_w_up"], np.float32))
        common["w_gate"] = np.ascontiguousarray(np.asarray(inputs["L0_ffn_w_gate"], np.float32))
        common["w_down"] = np.ascontiguousarray(np.asarray(inputs["L0_ffn_w_down"], np.float32))
    if "wz" in names:
        common["wz"] = np.ascontiguousarray(np.asarray(inputs["L1_attn_Wz"], np.float32))
        common["wh"] = np.ascontiguousarray(np.asarray(inputs["L1_attn_Wh"], np.float32))
        common["bz"] = np.asarray(inputs["L1_attn_bz"], np.float32)[:, None].copy()
        common["bh"] = np.asarray(inputs["L1_attn_bh"], np.float32)[:, None].copy()
        common["log_d"] = np.asarray(inputs["L1_attn_log_D"], np.float32)[:, None].copy()
        md = np.zeros((128, 128), np.float32)
        for i_ in range(128):
            md[i_, i_] = -2.0
            if i_ + 1 < 128:
                md[i_, i_ + 1] = 1.0
                md[i_ + 1, i_] = 1.0
        common["mdiff"] = md
        mwp_ = np.zeros((128, 128), np.float32)
        mwp_[127, 0] = 1.0
        common["mwp"] = mwp_
        mwn_ = np.zeros((128, 128), np.float32)
        mwn_[0, 127] = 1.0
        common["mwn"] = mwn_
    if "qw_t" in names:
        qw = np.asarray(inputs["L1_ffn_qweights"], np.int32)
        common["qw_t"] = np.ascontiguousarray(qw.transpose(0, 2, 1))
        sc = np.asarray(inputs["L1_ffn_scales"], np.float32)
        common["srep_t"] = np.ascontiguousarray(
            np.repeat(sc, 16, axis=2).transpose(0, 2, 1))
        common["qb"] = np.ascontiguousarray(np.asarray(inputs["L1_ffn_biases"], np.float32).T)
        common["la_t"] = np.ascontiguousarray(
            np.asarray(inputs["L1_ffn_lora_a"], np.float32).transpose(0, 2, 1))
        common["lb_t"] = np.ascontiguousarray(
            np.asarray(inputs["L1_ffn_lora_b"], np.float32).transpose(0, 2, 1))
        common["gam"] = np.ascontiguousarray(np.asarray(inputs["L1_ffn_ln_gamma"], np.float32).T)
        common["bet"] = np.ascontiguousarray(np.asarray(inputs["L1_ffn_ln_beta"], np.float32).T)
    in_maps = []
    for c in range(NCORES):
        m = {k: vv for k, vv in common.items() if k in names}
        m["u"] = np.ascontiguousarray(u[c].T)
        in_maps.append(m)
    return in_maps


def kernel(**inputs):
    p = build(_FULL)
    in_maps = _stage_inputs(inputs, set(p.in_names))
    res = run_bass_kernel_spmd(p.nc, in_maps, core_ids=list(range(NCORES)))
    outs = [np.ascontiguousarray(res.results[c]["out"].T) for c in range(NCORES)]
    return np.stack(outs, axis=0)


# version 07
# speedup vs baseline: 1.2314x; 1.0080x over previous
import numpy as np
import concourse.bass as bass
import concourse.mybir as mybir
from concourse.bass_utils import run_bass_kernel_spmd

F32 = mybir.dt.float32
F32R = mybir.dt.float32r
I32 = mybir.dt.int32
AF = mybir.ActivationFunctionType
ALU = mybir.AluOpType

NCORES = 8
T = 2048
D = 1024
DIN = 2048
NH = 32
HP = 64
DSTATE = 128
CONV_DIM = DIN + 2 * DSTATE
DPROJ = 2 * DIN + 2 * DSTATE + NH
DPROJ_PAD = 35 * 128
LO = 256
LC = 128
NO = T // LO
IFF = 3072
QMAX = 32767.0
EPS = 1e-5

_FULL = 4


class RecInstr:
    __slots__ = ("incs",)

    def __init__(self):
        self.incs = []

    def then_inc(self, sem_h, n):
        self.incs.append((sem_h, n))
        return self


_NOCHAIN = {"wait_ge", "dma_start", "indirect_dma_start", "collective_compute",
            "sem_inc", "sem_clear", "nop"}


def scope(p, name):
    for rec in (p.sy, p.t, p.v, p.s, p.g):
        rec.cmds.append(("__scope__", (name,), {}, RecInstr()))


class Rec:

    def __init__(self, name, chain_sem=None):
        self.name = name
        self.cmds = []
        self.chain = chain_sem
        self.ccount = 0

    def __getattr__(self, attr):
        def call(*a, **k):
            h = RecInstr()
            if self.chain is not None and attr not in _NOCHAIN:
                if self.ccount > 0:
                    self.cmds.append(("wait_ge", (self.chain, self.ccount), {}, RecInstr()))
                h.incs.append((self.chain, 1))
                self.ccount += 1
            self.cmds.append((attr, a, k, h))
            return h
        return call


def _replay(nc, eng, cmds):
    import contextlib
    stack = contextlib.ExitStack()
    with stack:
        for attr, a, k, h in cmds:
            if attr == "__scope__":
                stack.close()
                stack = contextlib.ExitStack()
                if a[0] is not None:
                    stack.enter_context(nc.named_scope(a[0]))
                continue
            r = getattr(eng, attr)(*a, **k)
            for sem_h, n in h.incs:
                r.then_inc(sem_h, n)


class Sem:
    def __init__(self, nc, name):
        self.h = nc.alloc_semaphore(name)
        self.v = 0

    def inc(self, instr, n=1):
        instr.then_inc(self.h, n)
        self.v += n
        return self.v

    def dma(self, instr):
        return self.inc(instr, 16)

    def wait(self, eng, val=None):
        eng.wait_ge(self.h, self.v if val is None else val)


class ChainSem:

    def __init__(self, rec):
        self.rec = rec
        self.h = rec.chain

    @property
    def v(self):
        return self.rec.ccount

    def inc(self, instr, n=1):
        return self.rec.ccount

    def wait(self, eng, val=None):
        eng.wait_ge(self.h, self.rec.ccount if val is None else val)


def rep(ap, k):
    return ap.unsqueeze(2).broadcast_to([ap.shape[0], ap.shape[1], k])


class P:
    pass


def _alloc(p, upto, dbg):
    nc = p.nc
    p.upto = upto
    p.dbg = set(dbg)

    def ein(name, shape, dt=F32):
        return nc.dram_tensor(name, shape, dt, kind="ExternalInput").ap()

    p.in_names = []

    def reg(name, shape, dt=F32):
        p.in_names.append(name)
        return ein(name, shape, dt)

    p.u = reg("u", [D, T])
    p.w_in_t = reg("w_in_t", [D, DPROJ_PAD])
    p.conv_w = reg("conv_w", [CONV_DIM, 4])
    p.conv_b = reg("conv_b", [CONV_DIM, 1])
    p.dt_bias = reg("dt_bias", [NH, 1])
    p.a_log = reg("a_log", [NH, 1])
    p.d_rep = reg("d_rep", [DIN, 1])
    p.nw_in = reg("nw_in", [DIN, 1])
    p.w_out_t = reg("w_out_t", [DIN, D])
    p.ident = reg("ident", [128, 128])
    p.mask01 = reg("mask01", [128, 128])
    if upto >= 2:
        p.w_up = reg("w_up", [IFF, D])
        p.w_gate = reg("w_gate", [IFF, D])
        p.w_down = reg("w_down", [D, IFF])
    if upto >= 3:
        p.wz = reg("wz", [D, D])
        p.wh = reg("wh", [D, D])
        p.bz = reg("bz", [D, 1])
        p.bh = reg("bh", [D, 1])
        p.log_d = reg("log_d", [D, 1])
        p.mdiff = reg("mdiff", [128, 128])
        p.mwp = reg("mwp", [128, 128])
        p.mwn = reg("mwn", [128, 128])
    if upto >= 4:
        p.qw_t = reg("qw_t", [6, D, D], I32)
        p.srep_t = reg("srep_t", [6, D, D])
        p.qb = reg("qb", [D, 6])
        p.la_t = reg("la_t", [6, D, 32])
        p.lb_t = reg("lb_t", [6, 32, D])
        p.gam = reg("gam", [D, 3])
        p.bet = reg("bet", [D, 3])

    p.out = nc.dram_tensor("out", [D, T], F32, kind="ExternalOutput").ap()

    p.z_hbm = nc.dram_tensor("z_hbm", [16, 128, T], F32).ap()
    p.xbc_hbm = nc.dram_tensor("xbc_hbm", [18, 128, T], F32).ap()
    p.y_hbm = nc.dram_tensor("y_hbm", [16, 128, T], F32).ap()
    p.bscr = nc.dram_tensor("bscr", [34, 512], F32).ap()
    if upto >= 2:
        p.up_hbm = nc.dram_tensor("up_hbm", [24, 128, T], F32).ap()
        p.gate_hbm = nc.dram_tensor("gate_hbm", [24, 128, T], F32).ap()
        p.g2_hbm = nc.dram_tensor("g2_hbm", [24, 128, T], F32).ap()
        p.o_hbm = nc.dram_tensor("o_hbm", [24, 128, T], F32).ap()
        p.wdq_hbm = nc.dram_tensor("wdq_hbm", [IFF, D], F32).ap()
        p.cc_in = nc.dram_tensor("cc_in", [1, 4], F32)
        p.cc_out = nc.dram_tensor("cc_out", [1, 4], F32)

    def sb(name, free):
        return nc.alloc_sbuf_tensor(name, [128, free], F32).ap()

    def addr():
        return (nc.sbuf_base + 31) // 32 * 32

    a_R = addr()
    p.R = sb("R", 8 * T)
    a_big = addr()
    p.big = sb("big", 16384)
    p.gbs = sb("gbs", 4096)
    p.gb0 = p.gbs[:, 0:1024]
    p.gb1 = p.gbs[:, 1024:2048]
    p.gb2 = p.gbs[:, 2048:3072]
    p.xd = sb("xd", 512)
    p.state = sb("state", DIN)
    a_dtr = addr()
    p.dt_raw = sb("dt_raw", T)
    a_rsa = addr()
    p.rstd_all = sb("rstd_all", T)
    p.ysq = sb("ysq", DIN)
    p.yoct = sb("yoct", 16 * LO)
    p.idn = sb("idn", 128)
    p.msk = sb("msk", 128)
    p.nbt = sb("nbt", 32)
    p.dst = sb("dst", 32)
    p.btm = sb("btm", 128)
    p.cbm = sb("cbm", 128)
    p.ones = sb("ones", 8)
    p.sml = sb("sml", 64)
    p.rows = sb("rows", 1024)
    p.bc1 = p.gbs[:, 0:2048]
    p.bc2 = p.gbs[:, 2048:4096]
    p.tmpv = sb("tmpv", 1024)
    p.cw = sb("cw", 96)
    p.dr = sb("dr", 16)
    p.nw = sb("nw", 16)

    p.pb = [nc.alloc_psum_tensor(f"pb{i}", [128, 512], F32).ap() for i in range(8)]

    p.xqh_r = nc.alloc_sbuf_tensor_at("xqh_r", [128, 8 * T], F32R, offset=a_R).ap()
    p.xql_r = nc.alloc_sbuf_tensor_at("xql_r", [128, 16384], F32R, offset=a_big).ap()
    p.wlb_r = nc.alloc_sbuf_tensor_at("wlb_r", [128, T], F32R, offset=a_dtr).ap()
    p.whb_r = nc.alloc_sbuf_tensor_at("whb_r", [128, T], F32R, offset=a_rsa).ap()

    for s in ["ld", "t", "o2", "wa", "wb", "wc", "wd", "ea", "eb", "ra", "rb",
              "zc", "bw", "bc", "be", "yw", "yb", "cc", "ci", "co", "rr"]:
        setattr(p, "s_" + s, Sem(nc, "sem_" + s))


def build(upto=_FULL, dbg=()):
    p = P()
    p.nc = bass.Bass("TRN2", target_bir_lowering=False)
    _alloc(p, upto, dbg)
    vch = p.nc.alloc_semaphore("sem_vch")
    sch = p.nc.alloc_semaphore("sem_sch")
    gch = p.nc.alloc_semaphore("sem_gch")
    p.sy, p.t, p.v, p.s, p.g = (Rec("sy"), Rec("t"), Rec("v", vch), Rec("s", sch),
                                Rec("g", gch))
    p.s_v = ChainSem(p.v)
    p.s_s = ChainSem(p.s)
    p.s_gc = ChainSem(p.g)
    _emit(p)
    with p.nc.Block() as block:
        @block.sync
        def _(e):
            _replay(p.nc, e, p.sy.cmds)

        @block.tensor
        def _(e):
            _replay(p.nc, e, p.t.cmds)

        @block.vector
        def _(e):
            _replay(p.nc, e, p.v.cmds)

        @block.scalar
        def _(e):
            _replay(p.nc, e, p.s.cmds)

        @block.gpsimd
        def _(e):
            _replay(p.nc, e, p.g.cmds)
    return p


def _dbg(p, name, ap, shape):
    if name in p.dbg:
        o = p.nc.dram_tensor("dbg_" + name, shape, F32, kind="ExternalOutput").ap()
        p.s_v.wait(p.sy)
        p.s_s.wait(p.sy)
        p.s_o2.dma(p.sy.dma_start(out=o, in_=ap))


def _fence_compute(p):
    for e in (p.t, p.v, p.s, p.sy, p.g):
        for s in (p.s_t, p.s_v, p.s_s, p.s_gc, p.s_ld, p.s_wa, p.s_wb, p.s_wc,
                  p.s_wd, p.s_ea, p.s_eb, p.s_ra, p.s_rb, p.s_zc, p.s_bw, p.s_bc,
                  p.s_yw, p.s_yb):
            s.wait(e)


def _emit(p):
    sy, t, v, s, g = p.sy, p.t, p.v, p.s, p.g

    p.s_ld.dma(sy.dma_start(out=p.idn, in_=p.ident))
    p.s_ld.dma(sy.dma_start(out=p.msk, in_=p.mask01))
    for fc in range(8):
        p.s_ld.dma(sy.dma_start(out=p.R[:, fc * T : (fc + 1) * T],
                                in_=p.u[fc * 128 : (fc + 1) * 128, :]))
    p.s_ld.dma(sy.dma_start(out=p.sml[0:NH, 0:1], in_=p.dt_bias))
    p.s_ld.dma(sy.dma_start(out=p.sml[0:NH, 1:2], in_=p.a_log))
    p.s_ld.dma(sy.dma_start(
        out=p.cw[:, 0:72].rearrange("p (j w) -> p j w", j=18),
        in_=p.conv_w.rearrange("(j p) w -> p j w", p=128),
        allow_slow_non_contiguous=True))
    p.s_ld.dma(sy.dma_start(
        out=p.cw[:, 72:90].rearrange("p (j o) -> p j o", o=1),
        in_=p.conv_b.rearrange("(j p) o -> p j o", p=128),
        allow_slow_non_contiguous=True))
    p.s_ld.dma(sy.dma_start(
        out=p.dr.rearrange("p (j o) -> p j o", o=1),
        in_=p.d_rep.rearrange("(j p) o -> p j o", p=128),
        allow_slow_non_contiguous=True))
    p.s_ld.dma(sy.dma_start(
        out=p.nw.rearrange("p (j o) -> p j o", o=1),
        in_=p.nw_in.rearrange("(j p) o -> p j o", p=128),
        allow_slow_non_contiguous=True))
    p.s_ld.wait(v)
    p.s_v.inc(v.memset(p.ones, 1.0))
    p.s_ld.wait(s)
    xv = p.s_s.inc(s.activation(p.sml[0:NH, 2:3], p.sml[0:NH, 1:2], AF.Exp))
    p.s_s.wait(s, xv)
    p.s_s.inc(s.mul(p.sml[0:NH, 2:3], p.sml[0:NH, 2:3], -1.0))
    p.negA = p.sml[0:NH, 2:3]
    p.dtbv = p.sml[0:NH, 0:1]
    p.s_v.wait(t)
    p.s_s.wait(t)

    _mamba(p)
    if p.upto >= 2:
        _ffn0(p)
    if p.upto >= 3:
        _l1_attn(p)
    if p.upto >= 4:
        _l1_ffn(p)

    p.s_v.wait(sy)
    p.s_s.wait(sy)
    for fc in range(8):
        p.s_o2.dma(sy.dma_start(out=p.out[fc * 128 : (fc + 1) * 128, :],
                                in_=p.R[:, fc * T : (fc + 1) * T]))
    p.s_o2.wait(sy)


def _mamba(p):
    sy, t, v, s, g = p.sy, p.t, p.v, p.s, p.g

    scope(p, 'mamba_inproj')
    wbuf = p.big[:, 0:2048]
    tmpe = p.big[:, 2048:3072]
    wl_ck = []
    tm_ck = [0] * 200
    ev_ck = [0] * 200
    sp_ck = [0] * 200
    wsem = [p.s_wa, p.s_wb]
    esem = [p.s_ea, p.s_eb]
    p.s_v.inc(v.memset(p.state, 0.0))
    for oc in range(35):
        wb = wbuf[:, (oc % 2) * 1024 : (oc % 2) * 1024 + 1024]
        if oc >= 2:
            sy.wait_ge(p.s_t.h, tm_ck[(oc - 2) * 4 + 3])
        src = p.w_in_t[:, oc * 128 : (oc + 1) * 128].rearrange("(kc kp) o -> kp kc o", kp=128)
        wv = wsem[oc % 2].dma(sy.dma_start(out=wb.rearrange("kp (kc o) -> kp kc o", kc=8), in_=src))
        wl_ck.append(wv)
        for tt in range(4):
            i = oc * 4 + tt
            bank = p.pb[i % 2]
            wsem[oc % 2].wait(t, wl_ck[oc])
            if i >= 2:
                p.s_v.wait(t, ev_ck[i - 2])
            for kc in range(8):
                mm = t.matmul(bank, wb[:, kc * 128 : (kc + 1) * 128],
                              p.R[:, kc * T + tt * 512 : kc * T + tt * 512 + 512],
                              start=(kc == 0), stop=(kc == 7))
            tm_ck[i] = p.s_t.inc(mm)
            p.s_t.wait(v, tm_ck[i])
            if oc < 34:
                slot = i % 2
                te = tmpe[:, slot * 512 : slot * 512 + 512]
                if i >= 2:
                    esem[slot].wait(v, sp_ck[i - 2])
                ev_ck[i] = p.s_v.inc(v.tensor_copy(te, bank))
                p.s_v.wait(g, ev_ck[i])
                dst = (p.z_hbm if oc < 16 else p.xbc_hbm)
                row = oc if oc < 16 else oc - 16
                sp_ck[i] = esem[slot].dma(
                    g.dma_start(out=dst[row, :, tt * 512 : tt * 512 + 512], in_=te))
            else:
                ev_ck[i] = p.s_v.inc(v.tensor_copy(p.dt_raw[:, tt * 512 : tt * 512 + 512], bank))

    p.s_ea.wait(sy)
    p.s_eb.wait(sy)
    _dbg(p, "dt_raw", p.dt_raw, [128, T])

    scope(p, 'mamba_chunks')
    xbcs = p.big[:, 0 : 18 * LO]
    zc = p.big[:, 4608 : 4608 + 2048]
    xtm = p.big[:, 6656 : 6656 + 2048]
    dtb = p.big[0:NH, 8704 : 8704 + 8 * LO]
    raw = p.big[:, 10752 : 10752 + 2 * 260]
    for o in range(NO):
        to = o * LO
        dts = dtb[:, 0 * LO : 1 * LO]
        lndt = dtb[:, 1 * LO : 2 * LO]
        dA = dtb[:, 2 * LO : 3 * LO]
        cA = dtb[:, 3 * LO : 4 * LO]
        nb = dtb[:, 4 * LO : 5 * LO]
        dsb = dtb[:, 5 * LO : 6 * LO]
        t32 = dtb[:, 6 * LO : 7 * LO]
        p.s_v.wait(s)
        p.s_t.wait(s)
        av = p.s_s.inc(s.activation(t32, p.dt_raw[0:NH, to : to + LO], AF.Exp, bias=p.dtbv))
        p.s_s.wait(s, av)
        av = p.s_s.inc(s.activation(dts, t32, AF.Ln, bias=1.0))
        p.s_s.wait(s, av)
        sv = p.s_s.inc(s.activation(lndt, dts, AF.Ln))
        p.s_s.wait(v, sv)
        vv = p.s_v.inc(v.tensor_scalar(dA, dts, p.negA, None, ALU.mult))
        p.s_v.wait(v, vv)
        for cc in range(2):
            sl = slice(cc * LC, (cc + 1) * LC)
            vv = p.s_v.inc(v.tensor_tensor_scan(
                cA[:, sl], p.ones[0:NH, 0:1].broadcast_to([NH, LC]),
                dA[:, sl], 0.0, ALU.mult, ALU.add))
        p.s_v.wait(v, vv)
        v.tensor_sub(nb, lndt, cA)
        for cc in range(2):
            sl = slice(cc * LC, (cc + 1) * LC)
            v.tensor_scalar(t32[:, sl], cA[:, sl], -1.0,
                            cA[:, cc * LC + LC - 1 : cc * LC + LC], ALU.mult, ALU.add)
            v.tensor_add(t32[:, sl], t32[:, sl], lndt[:, sl])
        dv = p.s_v.v
        p.s_v.wait(s, dv)
        p.s_s.inc(s.activation(dsb, t32, AF.Exp))

        cv_ck = [0] * 18
        rsem = [p.s_ra, p.s_rb]
        for j in range(18):
            slot = j % 2
            rb = raw[:, slot * 260 : slot * 260 + 260]
            if j >= 2:
                p.s_v.wait(sy, cv_ck[j - 2])
            if o == 0:
                lv = rsem[slot].dma(sy.dma_start(out=rb[:, 3:259], in_=p.xbc_hbm[j, :, 0:LO]))
            else:
                lv = rsem[slot].dma(sy.dma_start(out=rb[:, 0:259],
                                                 in_=p.xbc_hbm[j, :, to - 3 : to + LO]))
            rsem[slot].wait(v, lv)
            if o == 0:
                v.memset(rb[:, 0:3], 0.0)
            xj = xbcs[:, j * LO : (j + 1) * LO]
            v.tensor_scalar(xj, rb[:, 0:LO], p.cw[:, 4 * j : 4 * j + 1],
                            p.cw[:, 72 + j : 73 + j], ALU.mult, ALU.add)
            for w in range(1, 4):
                v.scalar_tensor_tensor(xj, rb[:, w : w + LO], p.cw[:, 4 * j + w : 4 * j + w + 1],
                                       xj, ALU.mult, ALU.add)
            cv_ck[j] = p.s_v.v
            p.s_v.wait(s, cv_ck[j])
            p.s_s.inc(s.activation(xj, xj, AF.Silu))

        _dbg(p, f"xbcs_o{o}", xbcs, [128, 18 * LO])

        for cc in range(2):
            _mamba_chunk(p, o, cc, xbcs, zc, xtm, cA, nb, dsb)

        p.s_v.wait(g)
        p.s_yw.dma(g.dma_start(
            out=p.y_hbm[:, :, to : to + LO].transpose([1, 0, 2]),
            in_=p.yoct.rearrange("p (fc l) -> p fc l", fc=16)))

    p.s_yw.wait(sy)
    _dbg(p, "rstd_all", p.rstd_all, [128, T])

    scope(p, 'mamba_outproj')
    _outproj_ln(p)


def _mamba_chunk(p, o, cc, xbcs, zc, xtm, cA, nb, dsb):
    sy, t, v, s, g = p.sy, p.t, p.v, p.s, p.g
    tg = o * LO + cc * LC
    sl = slice(cc * LC, (cc + 1) * LC)

    p.s_v.wait(sy)
    p.s_s.wait(sy)
    zv = p.s_zc.dma(sy.dma_start(
        out=zc.rearrange("p (fc l) -> p fc l", fc=16),
        in_=p.z_hbm[:, :, tg : tg + LC].transpose([1, 0, 2])))

    p.s_v.wait(g)
    p.s_s.wait(g)
    bflat = p.bscr.rearrange("a b -> (a b)")
    wv = p.s_bw.dma(g.dma_start(
        out=bflat[0 : NH * LC].rearrange("(h l) -> h l", l=LC), in_=cA[:, sl]))

    bank_tr = p.pb[2]
    bank_cb = p.pb[3]
    xb16 = xbcs[:, 16 * LO : 16 * LO + LO]
    xb17 = xbcs[:, 17 * LO : 17 * LO + LO]
    p.s_s.wait(t)
    p.s_v.wait(t)
    p.s_t.inc(t.matmul(bank_cb[:, 0:128], xb16[:, sl], xb17[:, sl], start=True, stop=True))
    p.s_t.inc(t.transpose(bank_tr[:, 0:32], nb[:, sl], p.idn[0:32, 0:32]))
    p.s_t.inc(t.transpose(bank_tr[:, 32:64], dsb[:, sl], p.idn[0:32, 0:32]))
    tv = p.s_t.inc(t.transpose(bank_tr[:, 128:256], xb16[:, sl], p.idn))
    p.s_t.wait(v, tv)
    v.tensor_copy(p.nbt, bank_tr[:, 0:32])
    v.tensor_copy(p.dst, bank_tr[:, 32:64])
    v.tensor_copy(p.btm, bank_tr[:, 128:256])
    ev = p.s_v.inc(v.tensor_mul(p.cbm, bank_cb[:, 0:128], p.msk))
    for g4 in range(4):
        p.s_v.wait(t, ev)
        for k in range(4):
            fc = g4 * 4 + k
            tv = p.s_t.inc(t.transpose(
                bank_tr[:, k * 128 : (k + 1) * 128],
                xbcs[:, fc * LO + cc * LC : fc * LO + cc * LC + LC], p.idn))
        p.s_t.wait(v, tv)
        ev = p.s_v.inc(v.tensor_copy(xtm[:, g4 * 512 : (g4 + 1) * 512], bank_tr))

    bufsets = [
        (p.gb0, p.gb1, p.gb2, p.xd[:, 0:512], p.pb[0], p.s_bc),
        (p.big[:, 12288:13312], p.big[:, 13312:14336], p.big[:, 14336:15360],
         p.big[:, 15360:15872], p.pb[1], p.s_be),
    ]
    xb17 = xbcs[:, 17 * LO : 17 * LO + LO]
    bflat = p.bscr.rearrange("a b -> (a b)")
    g_tv = {}
    g_gv = {}
    g_vv = {}

    def build_group(gi):
        h0 = gi * 8
        b0, b1, b2, xd, sbank, bsem = bufsets[gi % 2]
        p.s_bw.wait(g, wv)
        if gi >= 2:
            p.s_t.wait(g, g_tv[gi - 2])
            p.s_v.wait(g, g_gv[gi - 2])
        bv = bsem.dma(g.dma_start(
            out=b0,
            in_=bflat[h0 * LC : h0 * LC + 1024].unsqueeze(0).broadcast_to([128, 1024])))
        bsem.wait(v, bv)
        if gi >= 2:
            p.s_t.wait(v, g_tv[gi - 2])
        b0v = b0.rearrange("p (h l) -> p h l", h=8)
        v.tensor_tensor(b0v, b0v, rep(p.nbt[:, h0 : h0 + 8], LC), ALU.add)
        mv = p.s_v.inc(v.tensor_scalar(b2, b0, 30.0, None, ALU.min))
        p.s_v.wait(s, mv)
        if gi >= 2:
            p.s_t.wait(s, g_tv[gi - 2])
        sv = p.s_s.inc(s.activation(b1, b2, AF.Exp))
        p.s_s.wait(v, sv)
        b1v = b1.rearrange("p (h l) -> p h l", h=8)
        v.tensor_tensor(b1v, b1v, rep(p.cbm, 8).transpose([0, 2, 1]), ALU.mult)
        v.tensor_tensor(b0v, b0v, rep(p.nbt[:, h0 : h0 + 8], LC), ALU.subtract)
        mv = p.s_v.v
        p.s_v.wait(s, mv)
        sv = p.s_s.inc(s.activation(b0, b0, AF.Exp))
        p.s_s.wait(v, sv)
        for hh in range(8):
            h = h0 + hh
            v.tensor_scalar(xd[:, hh * 64 : hh * 64 + 64],
                            xtm[:, h * 64 : h * 64 + 64],
                            p.dst[:, h : h + 1], None, ALU.mult)
        b2v = b2.rearrange("p (h l) -> p h l", h=8)
        v.tensor_tensor(b2v, rep(xb17[:, sl], 8).transpose([0, 2, 1]),
                        b0.rearrange("p (h l) -> p h l", h=8), ALU.mult)
        g_vv[gi] = p.s_v.v

    def mm_group(gi):
        h0 = gi * 8
        b0, b1, b2, xd, sbank, bsem = bufsets[gi % 2]
        ybank = p.pb[4 + gi]
        p.s_v.wait(t, g_vv[gi])
        for hh in range(8):
            h = h0 + hh
            ypart = (hh % 2) * 64
            yfree = (hh // 2) * 128
            yout = ybank[ypart : ypart + 64, yfree : yfree + 128]
            t.matmul(yout, xtm[:, h * 64 : h * 64 + 64],
                     b1[:, hh * 128 : hh * 128 + 128],
                     start=(hh < 2), stop=False, skip_group_check=True)
            t.matmul(yout, p.state[:, h * 64 : h * 64 + 64],
                     b2[:, hh * 128 : hh * 128 + 128],
                     start=False, stop=(hh >= 6), skip_group_check=True)
            mm = t.matmul(sbank[:, hh * 64 : hh * 64 + 64], p.btm,
                          xd[:, hh * 64 : hh * 64 + 64],
                          start=(hh == 0), stop=(hh == 7))
        g_tv[gi] = tv = p.s_t.inc(mm)
        p.s_t.wait(v, tv)
        for hh in range(8):
            h = h0 + hh
            stsl = p.state[:, h * 64 : h * 64 + 64]
            v.tensor_scalar(stsl, stsl, b0[:, hh * 128 + 127 : hh * 128 + 128],
                            None, ALU.mult)
        g_gv[gi] = p.s_v.inc(v.tensor_add(p.state[:, h0 * 64 : h0 * 64 + 512],
                                          p.state[:, h0 * 64 : h0 * 64 + 512], sbank))

    for gi in range(4):
        build_group(gi)
        if gi >= 1:
            mm_group(gi - 1)
    mm_group(3)

    p.s_zc.wait(s, zv)
    p.s_zc.wait(v, zv)
    p.s_yw.wait(v)
    sv = p.s_s.inc(s.activation(zc, zc, AF.Silu))
    p.s_s.wait(v, sv)
    for fc in range(16):
        ybank = p.pb[4 + fc // 4]
        yfree = (fc % 4) * 128
        dst_y = p.yoct[:, fc * LO + cc * LC : fc * LO + cc * LC + LC]
        v.scalar_tensor_tensor(p.tmpv[:, 0:128],
                               xbcs[:, fc * LO + cc * LC : fc * LO + cc * LC + LC],
                               p.dr[:, fc : fc + 1],
                               ybank[:, yfree : yfree + 128], ALU.mult, ALU.add)
        v.tensor_mul(dst_y, p.tmpv[:, 0:128], zc[:, fc * 128 : fc * 128 + 128])
    yv = p.s_v.v
    p.s_v.wait(s, yv)
    sv = p.s_s.inc(s.activation(p.ysq.rearrange("p (fc l) -> p fc l", fc=16),
                                p.yoct.rearrange("p (fc l) -> p fc l", fc=16)[:, :, cc * LC : cc * LC + LC],
                                AF.Square))
    p.s_s.wait(t, sv)
    p.s_v.wait(t)
    stats = p.pb[1]
    for fc in range(16):
        mm = t.matmul(stats[0:1, 0:128], p.ones[:, 0:1],
                      p.ysq[:, fc * 128 : fc * 128 + 128],
                      start=(fc == 0), stop=(fc == 15))
    tv = p.s_t.inc(mm)
    p.s_t.wait(v, tv)
    v.tensor_scalar(p.rows[0:1, 8:136], stats[0:1, 0:128], 1.0 / DIN, EPS,
                    ALU.mult, ALU.add)
    rv = p.s_v.inc(v.reciprocal(p.rows[0:1, 8:136], p.rows[0:1, 8:136]))
    p.s_v.wait(s, rv)
    sv = p.s_s.inc(s.activation(p.rows[0:1, 8:136], p.rows[0:1, 8:136], AF.Sqrt))
    p.s_s.wait(v, sv)
    p.s_v.inc(v.tensor_copy(p.rstd_all[0:1, tg : tg + LC], p.rows[0:1, 8:136]))


def _outproj_ln(p):
    sy, t, v, s, g = p.sy, p.t, p.v, p.s, p.g
    _fence_compute(p)
    wo4 = p.big[:, 0:8192]
    ybuf = p.big[:, 8192:12288]
    prev_tv = 0
    for ocg in range(2):
        wl = 0
        if ocg == 1:
            sy.wait_ge(p.s_t.h, prev_tv)
        for j in range(4):
            oc = ocg * 4 + j
            wj = wo4[:, j * 2048 : j * 2048 + 2048]
            src = p.w_out_t[:, oc * 128 : (oc + 1) * 128].rearrange(
                "(kc kp) o -> kp kc o", kp=128)
            wl = p.s_wa.dma(sy.dma_start(
                out=wj.rearrange("kp (kc o) -> kp kc o", kc=16), in_=src))
        p.s_wa.wait(v, wl)
        for j in range(4):
            wj = wo4[:, j * 2048 : j * 2048 + 2048]
            v.tensor_tensor(wj.rearrange("p (kc o) -> p kc o", kc=16),
                            wj.rearrange("p (kc o) -> p kc o", kc=16),
                            rep(p.nw, 128), ALU.mult)
        wsv = p.s_v.v

        for sub in range(8):
            ts0 = sub * 256
            if not (ocg == 0 and sub == 0):
                sy.wait_ge(p.s_t.h, prev_tv)
            yv = p.s_yb.dma(sy.dma_start(
                out=ybuf.rearrange("p (fc l) -> p fc l", fc=16),
                in_=p.y_hbm[:, :, ts0 : ts0 + 256].transpose([1, 0, 2])))
            p.s_v.wait(g)
            wv = p.s_bw.dma(g.dma_start(out=p.bscr[17:18, 0:256],
                                        in_=p.rstd_all[0:1, ts0 : ts0 + 256]))
            p.s_bw.wait(g, wv)
            bv = p.s_bc.dma(g.dma_start(
                out=p.bc1[:, 0:256], in_=p.bscr[17:18, 0:256].broadcast_to([128, 256])))
            for j in range(4):
                oc = ocg * 4 + j
                bank = p.pb[2 + (j % 2)]
                p.s_v.wait(t, wsv)
                p.s_yb.wait(t, yv)
                if not (ocg == 0 and sub == 0 and j < 2):
                    p.s_v.wait(t)
                for kc in range(16):
                    mm = t.matmul(bank[:, 0:256],
                                  wo4[:, j * 2048 + kc * 128 : j * 2048 + kc * 128 + 128],
                                  ybuf[:, kc * 256 : (kc + 1) * 256],
                                  start=(kc == 0), stop=(kc == 15))
                tv = p.s_t.inc(mm)
                prev_tv = tv
                p.s_t.wait(v, tv)
                p.s_bc.wait(v, bv)
                rsl = p.R[:, oc * T + ts0 : oc * T + ts0 + 256]
                v.tensor_mul(p.tmpv[:, 0:256], bank[:, 0:256], p.bc1[:, 0:256])
                p.s_v.inc(v.tensor_add(rsl, rsl, p.tmpv[:, 0:256]))

    _dbg(p, "x1pre", p.R, [128, 8 * T])
    _post_ln(p, ln_idx=0, absmax_acc=(p.upto >= 2))


def _post_ln(p, ln_idx, absmax_acc=False, gamma=None, beta=None):
    sy, t, v, s, g = p.sy, p.t, p.v, p.s, p.g
    scope(p, f"post_ln{ln_idx}")
    _fence_compute(p)
    if absmax_acc:
        v.memset(p.sml[:, 3:4], 0.0)
    sbanks = [p.pb[1], p.pb[4]]
    mrow = p.dt_raw[0:1, 0:2048]
    rrow = p.rstd_all[0:1, 0:2048]
    st_ck = {}
    ev_ck = {}
    for sub in range(8):
        ts0 = sub * 256
        stats = sbanks[sub % 2]
        ssl = sub % 2
        sqb = p.ysq[:, ssl * 1024 : ssl * 1024 + 1024]
        if sub >= 2:
            p.s_t.wait(s, st_ck[sub - 2])
        for half in range(2):
            if half == 1:
                p.s_t.wait(s, tv)
            for k in range(4):
                oc = half * 4 + k
                rsl = p.R[:, oc * T + ts0 : oc * T + ts0 + 256]
                s.activation(sqb[:, k * 256 : k * 256 + 256], rsl, AF.Square)
            sv = p.s_s.v
            p.s_s.wait(t, sv)
            if sub >= 2 and half == 0:
                p.s_v.wait(t, ev_ck[sub - 2])
            for k in range(4):
                oc = half * 4 + k
                rsl2 = p.R[:, oc * T + ts0 : oc * T + ts0 + 256]
                p.s_t.inc(t.matmul(stats[0:1, 0:256], p.ones[:, 0:1], rsl2,
                                   start=(oc == 0), stop=False,
                                   skip_group_check=True))
                tv = p.s_t.inc(t.matmul(
                    stats[0:1, 256:512], p.ones[:, 0:1],
                    sqb[:, k * 256 : k * 256 + 256],
                    start=False, stop=(oc == 7), skip_group_check=True))
        st_ck[sub] = tv
        p.s_t.wait(v, tv)
        v.tensor_scalar(mrow[:, ts0 : ts0 + 256], stats[0:1, 0:256], 1.0 / D,
                        None, ALU.mult)
        v.tensor_scalar(rrow[:, ts0 : ts0 + 256], stats[0:1, 256:512], 1.0 / D,
                        None, ALU.mult)
        ev_ck[sub] = p.s_v.v
    v.tensor_mul(p.tmpv[0:1, 0:1024], mrow[:, 0:1024], mrow[:, 0:1024])
    v.tensor_sub(rrow[:, 0:1024], rrow[:, 0:1024], p.tmpv[0:1, 0:1024])
    v.tensor_mul(p.tmpv[0:1, 0:1024], mrow[:, 1024:2048], mrow[:, 1024:2048])
    v.tensor_sub(rrow[:, 1024:2048], rrow[:, 1024:2048], p.tmpv[0:1, 0:1024])
    v.tensor_scalar(rrow, rrow, 1.0, EPS, ALU.mult, ALU.add)
    rv = p.s_v.inc(v.reciprocal(rrow, rrow))
    p.s_v.wait(s, rv)
    sv = p.s_s.inc(s.activation(rrow, rrow, AF.Sqrt))
    p.s_s.wait(g, sv)
    p.s_v.wait(g)
    bflat = p.bscr.rearrange("a b -> (a b)")
    wv = p.s_bw.dma(g.dma_start(out=bflat[16 * 512 : 16 * 512 + 2048], in_=mrow))
    wv = p.s_bw.dma(g.dma_start(out=bflat[20 * 512 : 20 * 512 + 2048], in_=rrow))
    p.s_bw.wait(g, wv)
    p.s_bc.dma(g.dma_start(
        out=p.bc1, in_=bflat[16 * 512 : 16 * 512 + 2048].unsqueeze(0).broadcast_to([128, 2048])))
    bv = p.s_bc.dma(g.dma_start(
        out=p.bc2, in_=bflat[20 * 512 : 20 * 512 + 2048].unsqueeze(0).broadcast_to([128, 2048])))
    p.s_bc.wait(v, bv)
    for oc in range(8):
        rsl = p.R[:, oc * T : (oc + 1) * T]
        v.tensor_sub(rsl, rsl, p.bc1)
        v.tensor_mul(rsl, rsl, p.bc2)
        if gamma is not None:
            v.tensor_scalar(rsl, rsl, gamma[:, oc : oc + 1], beta[:, oc : oc + 1],
                            ALU.mult, ALU.add)
        if absmax_acc:
            v.tensor_reduce(p.tmpv[:, 0:1], rsl, mybir.AxisListType.X, ALU.max,
                            apply_absolute_value=True)
            v.tensor_max(p.sml[:, 3:4], p.sml[:, 3:4], p.tmpv[:, 0:1])



def _allreduce_max(p, cols, out_cols):
    sy, t, v, s, g = p.sy, p.t, p.v, p.s, p.g
    k = len(cols)
    stats = p.pb[1]
    p.s_v.wait(t)
    p.s_gc.wait(t)
    for i, c in enumerate(cols):
        tv = p.s_t.inc(t.transpose(stats[0:1, i * 128 : i * 128 + 128],
                                   p.sml[:, c : c + 1], p.idn))
    p.s_t.wait(v, tv)
    v.memset(p.rows[0:1, 640:644], 0.0)
    for i in range(k):
        v.tensor_reduce(p.rows[0:1, 640 + i : 641 + i], stats[0:1, i * 128 : i * 128 + 128],
                        mybir.AxisListType.X, ALU.max)
    vv = p.s_v.v
    p.s_v.wait(g, vv)
    p.s_cc.wait(g)
    p.s_ci.dma(g.dma_start(out=p.cc_in.ap()[0:1, 0:4], in_=p.rows[0:1, 640:644]))
    p.s_ci.wait(g)
    p.s_co.wait(g)
    cc = g.collective_compute(
        "AllReduce", ALU.max, replica_groups=[list(range(NCORES))],
        ins=[p.cc_in.ap().opt()], outs=[p.cc_out.ap().opt()])
    p.s_cc.inc(cc)
    p.s_cc.wait(g)
    bv = p.s_co.dma(g.dma_start(
        out=p.tmpv[:, 0:4], in_=p.cc_out.ap()[0:1, 0:4].broadcast_to([128, 4])))
    p.s_co.wait(v, bv)
    for i, oc in enumerate(out_cols):
        v.tensor_copy(p.sml[:, oc : oc + 1], p.tmpv[:, i : i + 1])


def _scale_recip(p, mx_col, s_col, r_col):
    v = p.v
    v.tensor_scalar(p.sml[:, s_col : s_col + 1], p.sml[:, mx_col : mx_col + 1],
                    float(np.float32(1.0 / QMAX)), 1e-12, ALU.mult, ALU.max)
    v.reciprocal(p.sml[:, r_col : r_col + 1], p.sml[:, s_col : s_col + 1])


def _lpbq_dequant(p, wraw, ncols, sbuf, rbuf):
    v = p.v
    nb = ncols // 32
    v3 = wraw.rearrange("p (b e) -> p b e", b=nb)
    v.tensor_reduce(sbuf, v3, mybir.AxisListType.X, ALU.max, apply_absolute_value=True)
    v.tensor_scalar(sbuf, sbuf, float(np.float32(1.0 / 7.0)), 1e-12, ALU.mult, ALU.max)
    v.reciprocal(rbuf, sbuf)
    v.tensor_tensor(v3.bitcast(I32), v3, rep(rbuf, 32), ALU.mult)
    v.tensor_tensor(v3, v3.bitcast(I32), rep(sbuf, 32), ALU.mult)


def _ffn0(p):
    sy, t, v, s, g = p.sy, p.t, p.v, p.s, p.g
    _fence_compute(p)
    scope(p, 'ffn0_arA')
    _allreduce_max(p, [3], [3])
    _scale_recip(p, 3, 4, 5)

    p.s_v.wait(g)
    p.s_s.wait(g)
    for kc in range(8):
        p.s_ea.dma(g.dma_start(out=p.y_hbm[kc, :, :], in_=p.R[:, kc * T : (kc + 1) * T]))
    p.s_ea.wait(v)
    xqh = p.R
    xql = p.big
    for kc in range(8):
        xr = p.R[:, kc * T : (kc + 1) * T]
        dst = xql[:, kc * T : (kc + 1) * T]
        hi = p.xqh_r[:, kc * T : (kc + 1) * T]
        lo = p.xql_r[:, kc * T : (kc + 1) * T]
        v.tensor_scalar(dst.bitcast(I32), xr, p.sml[:, 5:6], None, ALU.mult)
        v.tensor_scalar(dst, dst.bitcast(I32), p.sml[:, 4:5], None, ALU.mult)
        v.tensor_copy(hi, dst)
        v.tensor_tensor(lo, dst, hi.bitcast(F32), ALU.subtract)
    xqv = p.s_v.v

    scope(p, 'ffn0_p1')
    v.memset(p.sml[:, 6:7], 0.0)
    v.memset(p.sml[:, 7:8], 0.0)
    wraw2 = p.yoct[:, 0:2048]
    wbuf = p.dt_raw[:, 0:2048]
    whbuf = p.rstd_all[:, 0:2048]
    sc2 = p.state[:, 0:128]
    stg = p.ysq[:, 0:2048]
    wsem = [p.s_wa, p.s_wb]
    esem = [p.s_ea, p.s_eb]
    tm_ck = {}
    sc_tr = {}
    sc_ev = {}
    sp_ck = {}
    dq_ck = {}
    tr_ck = {}
    wsp_ck = {}
    units = [(m, oc) for oc in range(24) for m in range(2)]
    bank_wt = p.pb[4]

    def p1_stage_a(ui):
        m, oc = units[ui]
        wsl = ui % 2
        wraw = wraw2[:, wsl * 1024 : wsl * 1024 + 1024]
        wb = wbuf[:, wsl * 1024 : wsl * 1024 + 1024]
        wsrc = (p.w_up if m == 0 else p.w_gate)
        if ui >= 2:
            p.s_v.wait(sy, dq_ck[ui - 2])
            sy.wait_ge(p.s_t.h, tr_ck[ui - 2])
        wl = wsem[wsl].dma(sy.dma_start(
            out=wraw, in_=wsrc[oc * 128 : (oc + 1) * 128, :]))
        wsem[wsl].wait(v, wl)
        _lpbq_dequant(p, wraw, 1024, sc2[:, wsl * 64 : wsl * 64 + 32],
                      sc2[:, wsl * 64 + 32 : wsl * 64 + 64])
        dq_ck[ui] = p.s_v.v
        p.s_v.wait(t, dq_ck[ui])
        if ui >= 2:
            tmp = tm_ck.get((ui - 2, 3))
            p.s_s.wait(t, sc_tr.get(ui - 1, 0))
        wh = p.whb_r[:, wsl * 1024 : wsl * 1024 + 1024]
        wlo = p.wlb_r[:, wsl * 1024 : wsl * 1024 + 1024]
        for kk in range(2):
            if ui >= 1 or kk == 1:
                p.s_s.wait(t, sc_tr.get(ui if kk == 1 else ui - 1, 0))
            for k4 in range(4):
                kc = kk * 4 + k4
                tv = p.s_t.inc(t.transpose(bank_wt[:, k4 * 128 : k4 * 128 + 128],
                                           wraw[:, kc * 128 : kc * 128 + 128], p.idn))
            tr_ck[ui] = tv
            p.s_t.wait(s, tv)
            if ui >= 2 and kk == 0:
                p.s_t.wait(s, tm_ck[(ui - 2, 3)])
            s.copy(wb[:, kk * 512 : kk * 512 + 512], bank_wt[:, 0:512])
            sc_tr[ui] = p.s_s.v
        p.s_s.wait(v, sc_tr[ui])
        if ui >= 2:
            p.s_t.wait(v, tm_ck[(ui - 2, 3)])
        v.tensor_copy(wh, wb)
        v.tensor_tensor(wlo, wb, wh.bitcast(F32), ALU.subtract)
        wsp_ck[ui] = p.s_v.v

    def p1_stage_b(ui):
        m, oc = units[ui]
        wsl = ui % 2
        wb = p.wlb_r[:, wsl * 1024 : wsl * 1024 + 1024]
        wh = p.whb_r[:, wsl * 1024 : wsl * 1024 + 1024]
        for tt in range(4):
            i = ui * 4 + tt
            bank = p.pb[i % 2]
            p.s_v.wait(t, wsp_ck[ui])
            if ui == 0 and tt == 0:
                p.s_v.wait(t, xqv)
            if i >= 2:
                p.s_v.wait(t, vr_ck[i - 2])
            for kc in range(8):
                xh = p.xqh_r[:, kc * T + tt * 512 : kc * T + tt * 512 + 512]
                xl = p.xql_r[:, kc * T + tt * 512 : kc * T + tt * 512 + 512]
                t.matmul(bank, wh[:, kc * 128 : (kc + 1) * 128], xh,
                         start=(kc == 0), stop=False)
                t.matmul(bank, wh[:, kc * 128 : (kc + 1) * 128], xl,
                         start=False, stop=False)
                mm = t.matmul(bank, wb[:, kc * 128 : (kc + 1) * 128], xh,
                              start=False, stop=(kc == 7))
            tm_ck[(ui, tt)] = p.s_t.inc(mm)
            ssl = i % 2
            te = stg[:, ssl * 512 : ssl * 512 + 512]
            p.s_t.wait(v, tm_ck[(ui, tt)])
            if i >= 2:
                esem[ssl].wait(v, sp_ck[i - 2])
            v.tensor_copy(te, bank)
            v.tensor_reduce(p.tmpv[:, 4 + ssl : 5 + ssl], bank, mybir.AxisListType.X,
                            ALU.max, apply_absolute_value=True)
            acc = p.sml[:, 6 + m : 7 + m]
            v.tensor_max(acc, acc, p.tmpv[:, 4 + ssl : 5 + ssl])
            vr_ck[i] = sc_ev[i] = p.s_v.v
            p.s_v.wait(g, sc_ev[i])
            dst = (p.up_hbm if m == 0 else p.gate_hbm)
            sp_ck[i] = esem[ssl].dma(
                g.dma_start(out=dst[oc, :, tt * 512 : tt * 512 + 512], in_=te))

    gr_ck = {}
    vr_ck = {}
    for ui in range(48):
        p1_stage_a(ui)
        if ui >= 1:
            p1_stage_b(ui - 1)
    p1_stage_b(47)

    scope(p, 'ffn0_wdq')
    p.s_ea.wait(v)
    p.s_eb.wait(v)
    for oc in range(8):
        wsl = oc % 2
        wraw = wraw2[:, wsl * 1024 : wsl * 1024 + 1024]
        for cp in range(3):
            p.s_v.wait(sy)
            wl = wsem[wsl].dma(sy.dma_start(
                out=wraw, in_=p.w_down[oc * 128 : (oc + 1) * 128,
                                       cp * 1024 : cp * 1024 + 1024]))
            wsem[wsl].wait(v, wl)
            _lpbq_dequant(p, wraw, 1024, sc2[:, wsl * 64 : wsl * 64 + 32],
                          sc2[:, wsl * 64 + 32 : wsl * 64 + 64])
            dqv = p.s_v.v
            p.s_v.wait(t, dqv)
            p.s_ea.wait(v)
            p.s_eb.wait(v)
            for kk in range(2):
                p.s_v.wait(t)
                for k4 in range(4):
                    kc = kk * 4 + k4
                    tv = p.s_t.inc(t.transpose(bank_wt[:, k4 * 128 : k4 * 128 + 128],
                                               wraw[:, kc * 128 : kc * 128 + 128], p.idn))
                p.s_t.wait(v, tv)
                v.tensor_copy(stg[:, kk * 512 : kk * 512 + 512], bank_wt)
            vv = p.s_v.v
            p.s_v.wait(g, vv)
            esem[wsl].dma(g.dma_start(
                out=p.wdq_hbm[cp * 1024 : (cp + 1) * 1024, oc * 128 : (oc + 1) * 128]
                .rearrange("(kc kp) o -> kp kc o", kp=128),
                in_=stg[:, 0:1024].rearrange("kp (kc o) -> kp kc o", kc=8)))

    scope(p, 'ffn0_arB')
    _allreduce_max(p, [6, 7], [6, 7])
    _scale_recip(p, 6, 10, 11)
    _scale_recip(p, 7, 12, 13)

    scope(p, 'ffn0_p2')
    p.s_ea.wait(sy)
    p.s_eb.wait(sy)
    v.memset(p.sml[:, 8:9], 0.0)
    rsem = [p.s_ra, p.s_rb]
    ld_ck = {}
    cv_ck = {}
    sp_ck = {}
    for oc in range(24):
        ssl = oc % 2
        base = ssl * 8192
        graw = p.big[:, base : base + 2048]
        gq = p.big[:, base + 2048 : base + 4096]
        sq = p.big[:, base + 4096 : base + 6144]
        g2 = p.big[:, base + 6144 : base + 8192]
        if oc >= 2:
            p.s_v.wait(sy, cv_ck[oc - 2])
        ld_ck[oc] = rsem[ssl].dma(sy.dma_start(out=graw, in_=p.gate_hbm[oc, :, :]))
        rsem[ssl].wait(v, ld_ck[oc])
        if oc >= 2:
            esem[ssl].wait(v, sp_ck[oc - 2])
        v.tensor_scalar(gq.bitcast(I32), graw, p.sml[:, 13:14], None, ALU.mult)
        v.tensor_scalar(gq, gq.bitcast(I32), p.sml[:, 12:13], None, ALU.mult)
        sv = p.s_v.v
        p.s_v.wait(s, sv)
        av = p.s_s.inc(s.activation(sq, gq, AF.Sigmoid))
        p.s_s.wait(v, av)
        v.tensor_scalar(sq.bitcast(I32), sq, 65536.0, None, ALU.mult)
        v.tensor_scalar(sq, sq.bitcast(I32), 65535.0, float(2.0 ** -16),
                        ALU.min, ALU.mult)
        v.tensor_mul(g2, gq, sq)
        v.tensor_reduce(p.tmpv[:, 0:1], g2, mybir.AxisListType.X, ALU.max,
                        apply_absolute_value=True)
        v.tensor_max(p.sml[:, 8:9], p.sml[:, 8:9], p.tmpv[:, 0:1])
        cv_ck[oc] = p.s_v.v
        p.s_v.wait(g, cv_ck[oc])
        sp_ck[oc] = esem[ssl].dma(g.dma_start(out=p.g2_hbm[oc, :, :], in_=g2))

    scope(p, 'ffn0_arC')
    scope(p, 'ffn0_arC')
    _allreduce_max(p, [8], [8])
    _scale_recip(p, 8, 14, 15)

    scope(p, 'ffn0_p3')
    p.s_ea.wait(sy)
    p.s_eb.wait(sy)
    v.memset(p.sml[:, 9:10], 0.0)
    for oc in range(24):
        ssl = oc % 2
        base = ssl * 8192
        g2r = p.big[:, base : base + 2048]
        upr = p.big[:, base + 2048 : base + 4096]
        orw = p.big[:, base + 4096 : base + 6144]
        if oc >= 2:
            p.s_v.wait(sy, cv_ck[oc - 2])
        rsem[ssl].dma(sy.dma_start(out=g2r, in_=p.g2_hbm[oc, :, :]))
        ld_ck[oc] = rsem[ssl].dma(sy.dma_start(out=upr, in_=p.up_hbm[oc, :, :]))
        rsem[ssl].wait(v, ld_ck[oc])
        if oc >= 2:
            esem[ssl].wait(v, sp_ck[oc - 2])
        v.tensor_scalar(g2r.bitcast(I32), g2r, p.sml[:, 15:16], None, ALU.mult)
        v.tensor_scalar(g2r, g2r.bitcast(I32), p.sml[:, 14:15], None, ALU.mult)
        v.tensor_scalar(upr.bitcast(I32), upr, p.sml[:, 11:12], None, ALU.mult)
        v.tensor_scalar(upr, upr.bitcast(I32), p.sml[:, 10:11], None, ALU.mult)
        v.tensor_mul(orw, g2r, upr)
        v.tensor_reduce(p.tmpv[:, 0:1], orw, mybir.AxisListType.X, ALU.max,
                        apply_absolute_value=True)
        v.tensor_max(p.sml[:, 9:10], p.sml[:, 9:10], p.tmpv[:, 0:1])
        cv_ck[oc] = p.s_v.v
        p.s_v.wait(g, cv_ck[oc])
        sp_ck[oc] = esem[ssl].dma(g.dma_start(out=p.o_hbm[oc, :, :], in_=orw))

    scope(p, 'ffn0_arD')
    scope(p, 'ffn0_arD')
    _allreduce_max(p, [9], [9])
    _scale_recip(p, 9, 16, 17)

    scope(p, 'ffn0_p4')
    p.s_ea.wait(sy)
    p.s_eb.wait(sy)
    p.s_v.wait(sy)
    for kc in range(8):
        p.s_rr.dma(sy.dma_start(out=p.R[:, kc * T : (kc + 1) * T], in_=p.y_hbm[kc, :, :]))
    p.s_rr.wait(v)
    oq = p.big[:, 0:12288]
    wdbs = [p.big[:, 12288 : 12288 + 3072], p.yoct[:, 0:3072]]
    prev_tv = p.s_t.v
    dtm_ck = {}
    for tt in range(4):
        if tt > 0:
            p.s_v.wait(sy)
            sy.wait_ge(p.s_t.h, prev_tv)
        for kc in range(24):
            ssl = kc % 2
            rsem[ssl].dma(sy.dma_start(
                out=oq[:, kc * 512 : kc * 512 + 512],
                in_=p.o_hbm[kc, :, tt * 512 : tt * 512 + 512]))
        rsem[0].wait(v)
        rsem[1].wait(v)
        for kc in range(24):
            dst = oq[:, kc * 512 : kc * 512 + 512]
            v.tensor_scalar(dst.bitcast(I32), dst, p.sml[:, 17:18], None, ALU.mult)
            v.tensor_scalar(dst, dst.bitcast(I32), p.sml[:, 16:17], None, ALU.mult)
        qv = p.s_v.v
        for oc in range(8):
            u4 = tt * 8 + oc
            wdb = wdbs[u4 % 2]
            wsm = [p.s_wa, p.s_wb][u4 % 2]
            if u4 >= 2:
                sy.wait_ge(p.s_t.h, dtm_ck[u4 - 2])
            src = p.wdq_hbm[:, oc * 128 : (oc + 1) * 128].rearrange(
                "(kc kp) o -> kp kc o", kp=128)
            wl = wsm.dma(sy.dma_start(
                out=wdb.rearrange("kp (kc o) -> kp kc o", kc=24), in_=src))
            bank = p.pb[2 + oc % 2]
            wsm.wait(t, wl)
            p.s_v.wait(t, qv)
            if not (tt == 0 and oc < 2):
                p.s_v.wait(t)
            for kc in range(24):
                mm = t.matmul(bank[:, 0:512],
                              wdb[:, kc * 128 : kc * 128 + 128],
                              oq[:, kc * 512 : kc * 512 + 512],
                              start=(kc == 0), stop=(kc == 23))
            tv = p.s_t.inc(mm)
            dtm_ck[u4] = tv
            prev_tv = tv
            p.s_t.wait(v, tv)
            rsl = p.R[:, oc * T + tt * 512 : oc * T + tt * 512 + 512]
            v.tensor_add(rsl, rsl, bank[:, 0:512])

    _dbg(p, "x2pre", p.R, [128, 8 * T])
    _post_ln(p, ln_idx=1)


def _l1_attn(p):
    sy, t, v, s, g = p.sy, p.t, p.v, p.s, p.g
    scope(p, "l1_attn")
    _fence_compute(p)
    LQ4 = 512
    abuf = p.big[:, 0:4096]
    bbuf = p.big[:, 4096:8192]
    hbuf = p.big[:, 8192:12288]
    wzb = p.big[:, 12288:13312]
    whb = p.big[:, 13312:14336]

    def shslot(fc, d):
        if d == 0:
            if fc < 8:
                return (p.yoct[:, fc * 512 : fc * 512 + 512] if fc < 8 else None)
        return None

    bzv = p.cw[:, 0:8]
    bhv = p.cw[:, 8:16]
    dcv = p.cw[:, 16:24]

    lv = p.s_ld.dma(sy.dma_start(
        out=bzv.rearrange("p (j o) -> p j o", o=1),
        in_=p.bz.rearrange("(j p) o -> p j o", p=128),
        allow_slow_non_contiguous=True))
    lv = p.s_ld.dma(sy.dma_start(
        out=bhv.rearrange("p (j o) -> p j o", o=1),
        in_=p.bh.rearrange("(j p) o -> p j o", p=128),
        allow_slow_non_contiguous=True))
    lv = p.s_ld.dma(sy.dma_start(
        out=dcv.rearrange("p (j o) -> p j o", o=1),
        in_=p.log_d.rearrange("(j p) o -> p j o", p=128),
        allow_slow_non_contiguous=True))
    p.mdf = p.gb0[:, 0:128]
    p.mwpb = p.gb0[:, 128:256]
    p.mwnb = p.gb0[:, 256:384]
    p.s_ld.dma(sy.dma_start(out=p.mdf, in_=p.mdiff))
    p.s_ld.dma(sy.dma_start(out=p.mwpb, in_=p.mwp))
    p.s_ld.dma(sy.dma_start(out=p.mwnb, in_=p.mwn))
    p.s_ld.wait(t)
    p.s_ld.wait(s)
    av = p.s_s.inc(s.activation(dcv, dcv, AF.Exp))
    v.memset(p.sml[:, 20:28], 0.0)
    wsem = [p.s_wa, p.s_wb]
    rsem = [p.s_ra, p.s_rb]
    wz2 = p.big[:, 14336:15360]
    wh2 = p.big[:, 15360:16384]
    tm_ck = {}
    sc_ck = {}
    scan_ck = 0
    diff_ck = 0
    for q in range(4):
        tq = q * LQ4
        for oc in range(8):
            u = q * 8 + oc
            wzsl = wzb if u % 2 == 0 else wz2
            whsl = whb if u % 2 == 0 else wh2
            if u >= 2:
                sy.wait_ge(p.s_t.h, tm_ck[u - 2])
            zsem = [p.s_wa, p.s_wc][u % 2]
            hsem = [p.s_wb, p.s_wd][u % 2]
            wl = zsem.dma(sy.dma_start(
                out=wzsl.rearrange("kp (kc o) -> kp kc o", kc=8),
                in_=p.wz[:, oc * 128 : (oc + 1) * 128].rearrange("(kc kp) o -> kp kc o", kp=128)))
            wl2 = hsem.dma(sy.dma_start(
                out=whsl.rearrange("kp (kc o) -> kp kc o", kc=8),
                in_=p.wh[:, oc * 128 : (oc + 1) * 128].rearrange("(kc kp) o -> kp kc o", kp=128)))
            zsem.wait(t, wl)
            hsem.wait(t, wl2)
            bz_bank = p.pb[2 + (u % 2)]
            bh_bank = p.pb[6 + (u % 2)]
            if u >= 2:
                p.s_s.wait(t, sc_ck[u - 2])
            if oc == 0:
                p.s_v.wait(t, diff_ck)
                p.s_v.wait(s, scan_ck)
            for kc in range(8):
                mm = t.matmul(bz_bank, wzsl[:, kc * 128 : kc * 128 + 128],
                              p.R[:, kc * T + tq : kc * T + tq + LQ4],
                              start=(kc == 0), stop=(kc == 7))
            tvz = p.s_t.inc(mm)
            for kc in range(8):
                mm = t.matmul(bh_bank, whsl[:, kc * 128 : kc * 128 + 128],
                              p.R[:, kc * T + tq : kc * T + tq + LQ4],
                              start=(kc == 0), stop=(kc == 7))
            tm_ck[u] = tvh = p.s_t.inc(mm)
            zsl = abuf[:, oc * 512 : oc * 512 + 512]
            hsl = bbuf[:, oc * 512 : oc * 512 + 512]
            p.s_t.wait(s, tvz)
            s.activation(zsl, bz_bank, AF.Sigmoid, bias=bzv[:, oc : oc + 1])
            p.s_t.wait(s, tvh)
            sv = p.s_s.inc(s.activation(hsl, bh_bank, AF.Identity, bias=bhv[:, oc : oc + 1]))
            sc_ck[u] = sv
            p.s_s.wait(v, sv)
            v.tensor_mul(hsl, zsl, hsl)
            v.tensor_scalar(zsl, zsl, -1.0, 1.0, ALU.mult, ALU.add)
        if "no_scan" in p.dbg:
            for fc in range(8):
                v.tensor_copy(hbuf[:, fc * 512 : fc * 512 + 512],
                              bbuf[:, fc * 512 : fc * 512 + 512])
        rsem[0].wait(v)
        rsem[1].wait(v)
        for fc in range(8 if "no_scan" not in p.dbg else 0):
            v.tensor_tensor_scan(hbuf[:, fc * 512 : fc * 512 + 512],
                                 abuf[:, fc * 512 : fc * 512 + 512],
                                 bbuf[:, fc * 512 : fc * 512 + 512],
                                 p.sml[:, 20 + fc : 21 + fc], ALU.mult, ALU.add)
            v.tensor_copy(p.sml[:, 20 + fc : 21 + fc],
                          hbuf[:, fc * 512 + 511 : fc * 512 + 512])
        hv = p.s_v.v
        scan_ck = hv
        p.s_v.wait(t, hv)
        p.s_s.wait(t)
        for fc in range(8):
            hsl = hbuf[:, fc * 512 : fc * 512 + 512]
            prv = hbuf[:, ((fc - 1) % 8) * 512 : ((fc - 1) % 8) * 512 + 512]
            nxt = hbuf[:, ((fc + 1) % 8) * 512 : ((fc + 1) % 8) * 512 + 512]
            bank = p.pb[fc]
            t.matmul(bank, p.mdf, hsl, start=True, stop=False)
            t.matmul(bank, p.mwpb, prv, start=False, stop=False)
            dtv = p.s_t.inc(t.matmul(bank, p.mwnb, nxt, start=False, stop=True))
        p.s_t.wait(v, dtv)
        for fc in range(8):
            hsl = hbuf[:, fc * 512 : fc * 512 + 512]
            v.scalar_tensor_tensor(hsl, p.pb[fc], dcv[:, fc : fc + 1], hsl,
                                   ALU.mult, ALU.add)
            rsl = p.R[:, fc * T + tq : fc * T + tq + LQ4]
            v.tensor_add(rsl, rsl, hsl)
        diff_ck = p.s_v.v

    _dbg(p, "x3pre", p.R, [128, 8 * T])
    _post_ln(p, ln_idx=2)


def _l1_ffn(p):
    sy, t, v, s, g = p.sy, p.t, p.v, p.s, p.g
    scope(p, "l1_ffn")
    _fence_compute(p)
    wd1 = p.big[:, 0:8192]
    y1 = p.big[:, 8192:12288]
    srst = p.big[:, 12288:14336]
    lra = p.big[:, 14336:14848]
    lab = p.big[:, 14848:15104]
    lbb = p.big[:, 15104:16128]

    def wd2(kc):
        if kc < 4:
            return p.yoct[:, kc * 1024 : kc * 1024 + 1024]
        if kc < 6:
            return p.ysq[:, (kc - 4) * 1024 : (kc - 4) * 1024 + 1024]
        return p.state[:, (kc - 6) * 1024 : (kc - 6) * 1024 + 1024]

    qb1v = p.cw[:, 24:32]
    qb2v = p.cw[:, 32:40]
    gamv = p.cw[:, 40:48]
    betv = p.cw[:, 48:56]
    wsem = [p.s_wa, p.s_wb]
    stats = p.pb[1]
    p.s_v.wait(g)
    p.s_s.wait(g)
    for fc in range(8):
        p.s_ea.dma(g.dma_start(out=p.z_hbm[fc, :, :], in_=p.R[:, fc * T : (fc + 1) * T]))
    p.s_ea.wait(v)
    p.prev_l1sc = [None] * 8
    p.prev_sqck = [None] * 8
    p.prev_vend = [None, None]
    p.prev_f3ck = None
    p.last_rows_ck = 0
    p.lora2_ck = {}
    for blk in range(3):
        i0, i1 = 2 * blk, 2 * blk + 1
        lab1 = p.dt_raw[:, 0:256]
        lbb1 = p.dt_raw[:, 256:1280]
        if p.prev_f3ck is None:
            p.s_t.wait(sy)
        else:
            p.s_t.wait(sy, p.prev_f3ck)
        lv = p.s_ld.dma(sy.dma_start(
            out=qb1v.rearrange("p (j o) -> p j o", o=1),
            in_=p.qb[:, i0 : i0 + 1].rearrange("(j p) o -> p j o", p=128),
            allow_slow_non_contiguous=True))
        lv = p.s_ld.dma(sy.dma_start(
            out=lab1.rearrange("p (kc r) -> p kc r", kc=8),
            in_=p.la_t[i0].rearrange("(kc p) r -> p kc r", p=128)))
        lv = p.s_ld.dma(sy.dma_start(out=lbb1[0:32, :], in_=p.lb_t[i0]))
        p.s_ld.wait(t)
        p.s_ld.wait(v)
        p.s_ld.wait(s)
        dq_vv = {}
        for un in range(8):
            dstw = wd1[:, un * 1024 : un * 1024 + 1024]
            ssl = un % 2
            srb = srst[:, ssl * 1024 : ssl * 1024 + 1024]
            if un >= 2:
                p.s_v.wait(sy, dq_vv[un - 2])
            wl = wsem[ssl].dma(sy.dma_start(
                out=dstw.bitcast(I32), in_=p.qw_t[i0, un * 128 : (un + 1) * 128, :]))
            wl = wsem[ssl].dma(sy.dma_start(
                out=srb, in_=p.srep_t[i0, un * 128 : (un + 1) * 128, :]))
            wsem[ssl].wait(v, wl)
            v.tensor_scalar(dstw, dstw.bitcast(I32), float(np.float32(1.0 / 7.5)),
                            -1.0, ALU.mult, ALU.add)
            v.tensor_mul(dstw, dstw, srb)
            dq_vv[un] = p.s_v.v
        p.cur_dq1v = p.s_v.v

        p.s_s.wait(sy)
        p.s_v.wait(sy)
        p.s_t.wait(sy)
        lv = p.s_ld.dma(sy.dma_start(
            out=qb2v.rearrange("p (j o) -> p j o", o=1),
            in_=p.qb[:, i1 : i1 + 1].rearrange("(j p) o -> p j o", p=128),
            allow_slow_non_contiguous=True))
        lv = p.s_ld.dma(sy.dma_start(
            out=gamv.rearrange("p (j o) -> p j o", o=1),
            in_=p.gam[:, blk : blk + 1].rearrange("(j p) o -> p j o", p=128),
            allow_slow_non_contiguous=True))
        lv = p.s_ld.dma(sy.dma_start(
            out=betv.rearrange("p (j o) -> p j o", o=1),
            in_=p.bet[:, blk : blk + 1].rearrange("(j p) o -> p j o", p=128),
            allow_slow_non_contiguous=True))
        lv = p.s_ld.dma(sy.dma_start(
            out=lab.rearrange("p (kc r) -> p kc r", kc=8),
            in_=p.la_t[i1].rearrange("(kc p) r -> p kc r", p=128)))
        lv = p.s_ld.dma(sy.dma_start(out=lbb[0:32, :], in_=p.lb_t[i1]))
        p.late_lv = lv
        dq_vv2 = {}
        srb2 = p.tmpv[:, 0:1024]
        for un in range(8):
            dstw = wd2(un)
            if un >= 1:
                p.s_v.wait(sy, dq_vv2[un - 1])
            wl = wsem[un % 2].dma(sy.dma_start(
                out=dstw.bitcast(I32), in_=p.qw_t[i1, un * 128 : (un + 1) * 128, :]))
            wl = wsem[un % 2].dma(sy.dma_start(
                out=srb2, in_=p.srep_t[i1, un * 128 : (un + 1) * 128, :]))
            wsem[un % 2].wait(v, wl)
            v.tensor_scalar(dstw, dstw.bitcast(I32), float(np.float32(1.0 / 7.5)),
                            -1.0, ALU.mult, ALU.add)
            v.tensor_mul(dstw, dstw, srb2)
            dq_vv2[un] = p.s_v.v
        p.cur_dq2v = p.s_v.v

        stats2 = p.pb[4]
        l1sc = {}
        sqck = {}
        l2mv = {}
        vend = {}
        rows_ck = {}
        bv_tt = {}

        def y1sl(par, j):
            if par == 0:
                return y1[:, j * 512 : j * 512 + 512]
            if j < 4:
                return p.rstd_all[:, j * 512 : j * 512 + 512]
            if j < 7:
                return p.gbs[:, 512 + (j - 4) * 512 : 1024 + (j - 4) * 512]
            return p.gbs[:, 2560:3072]

        lra1 = [lra[0:32, :], p.xd[0:32, 0:512]]
        lra2 = p.tmpv[0:32, 512:1024]

        def bbf_part(pt, oc):
            par1 = pt % 2
            if oc == 0:
                p.s_bc.wait(v, bv_tt[pt])
                if pt == 0:
                    p.s_ld.wait(v, p.late_lv)
                    p.s_ld.wait(t, p.late_lv)
            ysl = y1sl(par1, oc)
            v.tensor_sub(ysl, ysl, p.bc1[:, 0:512])
            v.tensor_mul(ysl, ysl, p.bc2[:, 0:512])
            v.tensor_scalar(ysl, ysl, gamv[:, oc : oc + 1], betv[:, oc : oc + 1],
                            ALU.mult, ALU.add)
            gv = p.s_v.v
            p.s_v.wait(s, gv)
            sv = p.s_s.inc(s.activation(ysl, ysl, AF.Gelu))
            l1sc[("g", pt, oc)] = sv
            p.s_s.wait(t, sv)
            mm = t.matmul(p.pb[5][0:32, :], lab[:, oc * 32 : oc * 32 + 32],
                          ysl, start=(oc == 0), stop=(oc == 7))
            if oc == 7:
                p.lora2_ck[pt] = p.s_t.inc(mm)

        def stage_f(tt):
            par = tt % 2
            ts0 = tt * 512
            if tt >= 2:
                p.s_v.wait(s, vend[tt - 2])
            elif p.prev_vend[tt] is not None:
                p.s_v.wait(s, p.prev_vend[tt])
            for kc in range(8):
                mm = t.matmul(p.pb[0][0:32, :], lab1[:, kc * 32 : kc * 32 + 32],
                              p.R[:, kc * T + ts0 : kc * T + ts0 + 512],
                              start=(kc == 0), stop=(kc == 7))
            ltv = p.s_t.inc(mm)
            p.s_t.wait(v, ltv)
            v.tensor_copy(lra1[par], p.pb[0][0:32, :])
            lrv = p.s_v.v
            p.s_v.wait(t, lrv)
            for oc in range(8):
                bank = p.pb[6 + oc % 2]
                if oc >= 2:
                    p.s_s.wait(t, l1sc[(tt, oc - 2)])
                elif tt >= 1:
                    p.s_s.wait(t, l1sc[(tt - 1, 6 + oc)])
                elif p.prev_l1sc[6 + oc] is not None:
                    p.s_s.wait(t, p.prev_l1sc[6 + oc])
                if oc == 0:
                    p.s_v.wait(t, p.last_rows_ck)
                    if tt == 0:
                        p.s_v.wait(t, p.cur_dq1v)
                for kc in range(8):
                    t.matmul(bank, wd1[:, kc * 1024 + oc * 128 : kc * 1024 + oc * 128 + 128],
                             p.R[:, kc * T + ts0 : kc * T + ts0 + 512],
                             start=(kc == 0), stop=False)
                mm = t.matmul(bank, lbb1[0:32, oc * 128 : oc * 128 + 128],
                              lra1[par], start=False, stop=True)
                tv = p.s_t.inc(mm)
                p.s_t.wait(s, tv)
                ysl = y1sl(par, oc)
                sv = p.s_s.inc(s.activation(ysl, bank,
                                            AF.Identity, bias=qb1v[:, oc : oc + 1]))
                l1sc[(tt, oc)] = sv
                ssl = oc % 2
                sq = srst[:, ssl * 1024 : ssl * 1024 + 512]
                if oc >= 2:
                    p.s_t.wait(s, sqck[(tt, oc - 2)])
                elif tt >= 1:
                    p.s_t.wait(s, sqck[(tt - 1, 6 + oc)])
                elif p.prev_sqck[6 + oc] is not None:
                    p.s_t.wait(s, p.prev_sqck[6 + oc])
                if tt == 0 and oc < 2:
                    p.s_v.wait(s, p.cur_dq1v)
                sv = p.s_s.inc(s.activation(sq, ysl, AF.Square))
                p.s_s.wait(t, l1sc[(tt, oc)])
                p.s_t.inc(t.matmul(stats[0:1, 0:512], p.ones[:, 0:1], ysl,
                                   start=(oc == 0), stop=(oc == 7)))
                p.s_s.wait(t, sv)
                sqck[(tt, oc)] = p.s_t.inc(t.matmul(
                    stats2[0:1, 0:512], p.ones[:, 0:1], sq,
                    start=(oc == 0), stop=(oc == 7)))
                if tt >= 1:
                    bbf_part(tt - 1, oc)

        def stage_ba(tt):
            p.s_t.wait(v, sqck[(tt, 7)])
            m = p.rows[0:1, 0:512]
            var = p.rows[0:1, 512:1024]
            v.tensor_scalar(m, stats[0:1, 0:512], 1.0 / D, None, ALU.mult)
            v.tensor_scalar(var, stats2[0:1, 0:512], 1.0 / D, None, ALU.mult)
            rows_ck[tt] = p.s_v.v
            p.last_rows_ck = rows_ck[tt]
            v.tensor_mul(p.tmpv[0:1, 0:512], m, m)
            v.tensor_sub(var, var, p.tmpv[0:1, 0:512])
            v.tensor_scalar(var, var, 1.0, EPS, ALU.mult, ALU.add)
            rv = p.s_v.inc(v.reciprocal(var, var))
            p.s_v.wait(s, rv)
            sv = p.s_s.inc(s.activation(var, var, AF.Sqrt))
            p.s_s.wait(g, sv)
            p.s_v.wait(g)
            wv = p.s_bw.dma(g.dma_start(out=p.bscr[16:17, 0:512], in_=m))
            wv = p.s_bw.dma(g.dma_start(out=p.bscr[18:19, 0:512], in_=var))
            p.s_bw.wait(g, wv)
            p.s_bc.dma(g.dma_start(out=p.bc1[:, 0:512],
                                   in_=p.bscr[16:17, 0:512].broadcast_to([128, 512])))
            bv_tt[tt] = p.s_bc.dma(g.dma_start(
                out=p.bc2[:, 0:512],
                in_=p.bscr[18:19, 0:512].broadcast_to([128, 512])))

        def stage_btail(tt):
            par = tt % 2
            ts0 = tt * 512
            p.s_t.wait(v, p.lora2_ck[tt])
            v.tensor_copy(lra2, p.pb[5][0:32, :])
            lrv = p.s_v.v
            p.s_v.wait(t, lrv)
            for oc in range(8):
                bank = p.pb[2 + oc % 2]
                if oc >= 2:
                    p.s_v.wait(t, l2mv[(tt, oc - 2)])
                elif tt >= 1:
                    p.s_v.wait(t, l2mv[(tt - 1, 6 + oc)])
                if oc == 0 and tt == 0:
                    p.s_v.wait(t, p.cur_dq2v)
                    p.s_ld.wait(t, p.late_lv)
                    p.s_ld.wait(v, p.late_lv)
                for kc in range(8):
                    t.matmul(bank, wd2(kc)[:, oc * 128 : oc * 128 + 128],
                             y1sl(par, kc), start=(kc == 0), stop=False)
                mm = t.matmul(bank, lbb[0:32, oc * 128 : oc * 128 + 128],
                              lra2, start=False, stop=True)
                tv = p.s_t.inc(mm)
                p.s_t.wait(v, tv)
                rsl = p.R[:, oc * T + ts0 : oc * T + ts0 + 512]
                v.scalar_tensor_tensor(p.tmpv[:, 0:512], bank, 1.0, rsl,
                                       ALU.mult, ALU.add)
                v.tensor_scalar(rsl, p.tmpv[:, 0:512],
                                qb2v[:, oc : oc + 1], None, ALU.add)
                l2mv[(tt, oc)] = p.s_v.v
            vend[tt] = p.s_v.v

        stage_f(0)
        stage_ba(0)
        for tt in range(1, 4):
            stage_f(tt)
            stage_btail(tt - 1)
            stage_ba(tt)
        for oc in range(8):
            bbf_part(3, oc)
        stage_btail(3)
        p.prev_f3ck = sqck[(3, 7)]
        p.prev_l1sc = [l1sc[(3, oc)] for oc in range(8)]
        p.prev_sqck = [sqck[(3, oc)] for oc in range(8)]
        p.prev_vend = [vend[2], vend[3]]
        _dbg(p, f"rb{blk}", p.R, [128, 8 * T])

    p.s_t.wait(sy)
    p.s_v.wait(sy)
    p.s_s.wait(sy)
    for fc in range(8):
        lv = p.s_ra.dma(sy.dma_start(out=p.big[:, fc * T : fc * T + T],
                                     in_=p.z_hbm[fc, :, :]))
    p.s_ra.wait(v)
    for fc in range(8):
        rsl = p.R[:, fc * T : (fc + 1) * T]
        v.tensor_add(rsl, rsl, p.big[:, fc * T : fc * T + T])

    _post_ln(p, ln_idx=3)


def _stage_inputs(inputs, names):
    u = np.ascontiguousarray(np.asarray(inputs["L0_attn_u"], dtype=np.float32))
    w_in_t = np.zeros((D, DPROJ_PAD), np.float32)
    w_in_t[:, :DPROJ] = np.asarray(inputs["L0_attn_W_in"], np.float32).T
    common = {
        "w_in_t": w_in_t,
        "conv_w": np.ascontiguousarray(np.asarray(inputs["L0_attn_conv_w"], np.float32)),
        "conv_b": np.asarray(inputs["L0_attn_conv_b"], np.float32)[:, None].copy(),
        "dt_bias": np.asarray(inputs["L0_attn_dt_bias"], np.float32)[:, None].copy(),
        "a_log": np.asarray(inputs["L0_attn_A_log"], np.float32)[:, None].copy(),
        "d_rep": np.repeat(np.asarray(inputs["L0_attn_D"], np.float32), HP)[:, None].copy(),
        "nw_in": np.asarray(inputs["L0_attn_norm_weight"], np.float32)[:, None].copy(),
        "w_out_t": np.ascontiguousarray(np.asarray(inputs["L0_attn_W_out"], np.float32).T),
        "ident": np.eye(128, dtype=np.float32),
        "mask01": np.triu(np.ones((128, 128), np.float32)).copy(),
    }
    if "w_up" in names:
        common["w_up"] = np.ascontiguousarray(np.asarray(inputs["L0_ffn_w_up"], np.float32))
        common["w_gate"] = np.ascontiguousarray(np.asarray(inputs["L0_ffn_w_gate"], np.float32))
        common["w_down"] = np.ascontiguousarray(np.asarray(inputs["L0_ffn_w_down"], np.float32))
    if "wz" in names:
        common["wz"] = np.ascontiguousarray(np.asarray(inputs["L1_attn_Wz"], np.float32))
        common["wh"] = np.ascontiguousarray(np.asarray(inputs["L1_attn_Wh"], np.float32))
        common["bz"] = np.asarray(inputs["L1_attn_bz"], np.float32)[:, None].copy()
        common["bh"] = np.asarray(inputs["L1_attn_bh"], np.float32)[:, None].copy()
        common["log_d"] = np.asarray(inputs["L1_attn_log_D"], np.float32)[:, None].copy()
        md = np.zeros((128, 128), np.float32)
        for i_ in range(128):
            md[i_, i_] = -2.0
            if i_ + 1 < 128:
                md[i_, i_ + 1] = 1.0
                md[i_ + 1, i_] = 1.0
        common["mdiff"] = md
        mwp_ = np.zeros((128, 128), np.float32)
        mwp_[127, 0] = 1.0
        common["mwp"] = mwp_
        mwn_ = np.zeros((128, 128), np.float32)
        mwn_[0, 127] = 1.0
        common["mwn"] = mwn_
    if "qw_t" in names:
        qw = np.asarray(inputs["L1_ffn_qweights"], np.int32)
        common["qw_t"] = np.ascontiguousarray(qw.transpose(0, 2, 1))
        sc = np.asarray(inputs["L1_ffn_scales"], np.float32)
        common["srep_t"] = np.ascontiguousarray(
            np.repeat(sc, 16, axis=2).transpose(0, 2, 1))
        common["qb"] = np.ascontiguousarray(np.asarray(inputs["L1_ffn_biases"], np.float32).T)
        common["la_t"] = np.ascontiguousarray(
            np.asarray(inputs["L1_ffn_lora_a"], np.float32).transpose(0, 2, 1))
        common["lb_t"] = np.ascontiguousarray(
            np.asarray(inputs["L1_ffn_lora_b"], np.float32).transpose(0, 2, 1))
        common["gam"] = np.ascontiguousarray(np.asarray(inputs["L1_ffn_ln_gamma"], np.float32).T)
        common["bet"] = np.ascontiguousarray(np.asarray(inputs["L1_ffn_ln_beta"], np.float32).T)
    in_maps = []
    for c in range(NCORES):
        m = {k: vv for k, vv in common.items() if k in names}
        m["u"] = np.ascontiguousarray(u[c].T)
        in_maps.append(m)
    return in_maps


def kernel(**inputs):
    p = build(_FULL)
    in_maps = _stage_inputs(inputs, set(p.in_names))
    res = run_bass_kernel_spmd(p.nc, in_maps, core_ids=list(range(NCORES)))
    outs = [np.ascontiguousarray(res.results[c]["out"].T) for c in range(NCORES)]
    return np.stack(outs, axis=0)
